# version 02
# baseline (speedup 1.0000x reference)
import contextlib
import numpy as np
import ml_dtypes

import concourse.bass as bass
import concourse.mybir as mybir
from concourse.bass_utils import run_bass_kernel_spmd

F32 = mybir.dt.float32
BF16 = mybir.dt.bfloat16
AF = mybir.ActivationFunctionType
ALU = mybir.AluOpType
AX = mybir.AxisListType

TENSOR, VECTOR, SCALAR, GPSIMD, SYNC = "tensor", "vector", "scalar", "gpsimd", "sync"
ENGINES = (TENSOR, VECTOR, SCALAR, GPSIMD, SYNC)

EPS = 1e-5
P = 128
D = 1024
NT = D // P
HS = 64
NH = 16

BF = ml_dtypes.bfloat16



class Op:
    __slots__ = ("idx", "eng", "fn", "reads", "writes", "dma", "mm_out",
                 "marked", "waits", "tag", "coll", "inc_amt", "epos",
                 "semname", "semval", "waitlist", "markable", "lane")

    def __init__(self, idx, eng, fn, reads, writes, dma, mm_out, tag, coll,
                 markable=True):
        self.idx = idx
        self.eng = eng
        self.fn = fn
        self.reads = tuple(reads)
        self.writes = tuple(writes)
        self.dma = dma
        self.mm_out = mm_out
        self.marked = False
        self.waits = []
        self.tag = tag
        self.coll = coll
        self.inc_amt = 0
        self.epos = 0
        self.semname = None
        self.semval = 0
        self.waitlist = []
        self.markable = markable
        self.lane = -1

    def kind(self):
        return (self.dma, self.coll)


class Sched:
    def __init__(self):
        self.ops = []
        self.last_write = {}
        self.readers = {}
        self.pe_mm = {}
        self.drains = []
        self.estreams = {r: [] for r in ENGINES}
        self.floor = {}
        self.dma_counts = {}

    def _new(self, eng, fn, reads, writes, dma, mm_out, tag, coll, markable=True):
        o = Op(len(self.ops), eng, fn, reads, writes, dma, mm_out, tag, coll,
               markable)
        o.epos = len(self.estreams[eng])
        self.estreams[eng].append(o)
        self.ops.append(o)
        return o

    def op(self, eng, fn, r=(), w=(), dma=False, mm_out=None, tag="", coll=False):
        o = self._new(eng, fn, r, w, dma, mm_out, tag, coll)
        if dma or coll:
            o.marked = True
        if dma:
            c = self.dma_counts.get(eng, 0)
            self.dma_counts[eng] = c + 1
            o.lane = c % self.NLANES
        deps = []
        for reg in o.reads:
            p = self.last_write.get(reg)
            if p is not None:
                deps.append(p)
            self.readers.setdefault(reg, []).append(o)
        for reg in o.writes:
            p = self.last_write.get(reg)
            if p is not None:
                deps.append(p)
            for rd in self.readers.get(reg, ()):
                deps.append(rd)
            self.readers[reg] = []
            self.last_write[reg] = o
        if o.eng != TENSOR:
            for reg in o.reads:
                mmop = self.pe_mm.get(reg)
                if mmop is not None:
                    deps.append(self._drain_after(mmop))
        if mm_out is not None:
            self.pe_mm[mm_out] = o
        fl = self.floor.get(eng)
        if fl is not None:
            deps.append(fl)
        self._edges(o, deps)
        return o

    def _drain_after(self, mmop):
        for dop in self.drains:
            if dop.epos > mmop.epos:
                return dop
        dop = self._new(TENSOR, lambda e: e.drain(), (), (), False, None,
                        "drain", False)
        mmop.marked = True
        dop.marked = True
        self.drains.append(dop)
        return dop

    def _edges(self, o, deps):
        best = {}
        for p in deps:
            if p is o or p.idx >= o.idx:
                continue
            key = (p.eng, p.kind(), p.lane)
            cur = best.get(key)
            if cur is None or p.epos > cur.epos:
                best[key] = p
        for (engname, kindk, _lane), p in best.items():
            if kindk != (False, False) or (p.marked and not p.markable):
                o.waits.append(p)
                continue
            stream = self.estreams[engname]
            mop = None
            for cand in stream[p.epos:]:
                if cand.idx >= o.idx:
                    break
                if cand.marked and cand.markable and cand.kind() == kindk:
                    mop = cand
                    break
            if mop is None:
                assert p.markable, f"unmarkable producer {p.tag} for {o.tag}"
                p.marked = True
                mop = p
            o.waits.append(mop)

    def barrier(self, tag="bar"):
        lasts = []
        for e in ENGINES:
            ncomp = 0
            ndma = 0
            ncoll = 0
            for o in reversed(self.estreams[e]):
                if not o.markable:
                    continue
                if o.dma:
                    if ndma < self.NLANES:
                        lasts.append(o)
                        ndma += 1
                elif o.coll:
                    if ncoll < 1:
                        lasts.append(o)
                        ncoll += 1
                else:
                    if ncomp < 1:
                        lasts.append(o)
                        ncomp += 1
                if ncomp and ncoll and ndma >= self.NLANES:
                    break
        outs = []
        for engname in ENGINES:
            o = self._new(engname, lambda e: e.nop(), (), (), False, None,
                          tag, False, markable=False)
            fl = self.floor.get(engname)
            deps = [x for x in lasts if x.eng != engname]
            if fl is not None:
                deps.append(fl)
            self._edges(o, deps)
            o.marked = True
            self.floor[engname] = o
            outs.append(o)
        return outs

    NLANES = 8

    def plan(self):
        counts = {}
        lane_idx = {}
        gate = {}
        for engname in ENGINES:
            for o in self.estreams[engname]:
                if not o.marked:
                    continue
                if o.coll:
                    o.semname, o.inc_amt = "cc", 1
                elif o.dma:
                    o.semname = f"dma_{engname}{o.lane}"
                    o.inc_amt = 16
                else:
                    o.semname, o.inc_amt = "eng_" + engname, 1
                c = counts.get(o.semname, 0) + o.inc_amt
                counts[o.semname] = c
                o.semval = c
                if o.semval > o.inc_amt:
                    gate[o.idx] = (o.semname, o.semval - o.inc_amt)
        for o in self.ops:
            wl = {}
            g = gate.get(o.idx)
            if g is not None:
                wl[g[0]] = g[1]
            for mop in o.waits:
                assert mop.marked, (o.tag, mop.tag)
                k = mop.semname
                if wl.get(k, 0) < mop.semval:
                    wl[k] = mop.semval
            o.waitlist = sorted(wl.items())
        self.sem_totals = counts

    def emit(self, block, sems):
        def mk(engname):
            def run(eng_obj):
                for o in self.estreams[engname]:
                    for semname, val in o.waitlist:
                        eng_obj.wait_ge(sems[semname], val)
                    inst = o.fn(eng_obj)
                    if o.marked:
                        inst.then_inc(sems[o.semname], o.inc_amt)
            return run

        block.tensor(mk(TENSOR))
        block.vector(mk(VECTOR))
        block.scalar(mk(SCALAR))
        block.gpsimd(mk(GPSIMD))
        block.sync(mk(SYNC))

    def stats(self):
        per = {e: len(self.estreams[e]) for e in ENGINES}
        nmark = sum(1 for o in self.ops if o.marked)
        nwait = sum(len(o.waitlist) for o in self.ops)
        return (f"ops={len(self.ops)} per-engine={per} marked={nmark} "
                f"waits={nwait} sems={getattr(self, 'sem_totals', None)}")



class KCtx:

    def __init__(self, nc, sch, cfg):
        self.nc = nc
        self.sch = sch
        self.cfg = cfg
        self.es = contextlib.ExitStack()
        self.dram = {}
        self.pb = []
        self.pbT = None

    def dma(self, q, out_ap, in_ap, r=(), w=(), tag="dma", slow=False):
        return self.sch.op(q, lambda e, o=out_ap, i=in_ap, s=slow:
                           e.dma_start(out=o, in_=i,
                                       allow_slow_non_contiguous=s),
                           r=r, w=w, dma=True, tag=tag)

    def mm(self, lhsT, rhs, out_ap, bank, r=(), start=True, stop=True, tag="mm"):
        return self.sch.op(TENSOR,
                           lambda e, a=lhsT, b=rhs, o=out_ap, s=start, t=stop:
                           e.matmul(o, a, b, start=s, stop=t),
                           r=r, w=(bank,), mm_out=bank, tag=tag)

    def mmgroup(self, pairs, out_ap, bank, r=(), tag="mmg"):
        def fn(e, ps=tuple(pairs), o=out_ap):
            inst = None
            n = len(ps)
            for i, (a, b) in enumerate(ps):
                inst = e.matmul(o, a, b, start=(i == 0), stop=(i == n - 1))
            return inst
        return self.sch.op(TENSOR, fn, r=r, w=(bank,), mm_out=bank, tag=tag)

    def transpose(self, out_ap, in_ap, ident, bank, r=(), tag="tr"):
        return self.sch.op(TENSOR,
                           lambda e, o=out_ap, i=in_ap, d=ident: e.transpose(o, i, d),
                           r=r, w=(bank,), mm_out=bank, tag=tag)

    def act(self, out_ap, in_ap, func, r=(), w=(), bias=0.0, scale=1.0, tag="act"):
        return self.sch.op(SCALAR,
                           lambda e, o=out_ap, i=in_ap, f=func, b=bias, s=scale:
                           e.activation(o, i, f, bias=b, scale=s),
                           r=r, w=w, tag=tag)

    def ve(self, fn, r=(), w=(), tag="ve"):
        return self.sch.op(VECTOR, fn, r=r, w=w, tag=tag)

    def gp(self, fn, r=(), w=(), tag="gp"):
        return self.sch.op(GPSIMD, fn, r=r, w=w, tag=tag)

    def sbuf(self, es, name, shape, dtype):
        return es.enter_context(self.nc.sbuf_tensor(name, shape, dtype))


def dmajor_view(dram_handle, p=P):
    return dram_handle.ap().rearrange("(t p) n -> p t n", p=p)


def tokmajor_view(dram_handle, p=P):
    return dram_handle.ap().rearrange("(a p) d -> p a d", p=p)



def load_dmajor_chunk(kc, q, sbuf_t, dram_handle, dname, c0, c1, w, tag,
                      nt=NT, split=2):
    v = dmajor_view(dram_handle)
    ops = []
    step = nt // split
    for s in range(split):
        t0, t1 = s * step, (s + 1) * step
        ops.append(kc.dma(q, sbuf_t[:, t0:t1, c0:c1], v[:, t0:t1, c0:c1],
                          r=(dname,) if isinstance(dname, str) else dname,
                          w=(w + f"/{s}",) if split > 1 else (w,), tag=tag))
    return ops


def store_dmajor_chunk(kc, q, dram_handle, dname, sbuf_t, rname, c0, c1,
                       nt=NT, split=2):
    v = dmajor_view(dram_handle)
    step = nt // split
    for s in range(split):
        t0, t1 = s * step, (s + 1) * step
        kc.dma(q, v[:, t0:t1, c0:c1], sbuf_t[:, t0:t1, c0:c1],
               r=(rname,), w=(dname,), tag="st_" + dname)





def dmajor_view(dram_handle, p=P):
    return dram_handle.ap().rearrange("(t p) n -> p t n", p=p)


def tokmajor_view(dram_handle, p=P):
    return dram_handle.ap().rearrange("(a p) d -> p a d", p=p)


class Slot:

    def __init__(self, t, name):
        self.t = t
        self.name = name

    def __getitem__(self, k):
        return self.t[k]

    @property
    def r(self):
        return self.name


def bc_nt(col_ap, C):
    return col_ap.broadcast_to([P, NT, C])


class LN:

    def __init__(self, kc, ones_bf, xb, sq, srow, tf, bank_a, bank_b, C,
                 epsc=None):
        self.kc, self.ones = kc, ones_bf
        self.xb, self.sq, self.srow, self.tf = xb, sq, srow, tf
        self.ba, self.bb = bank_a, bank_b
        self.C = C
        self.epsc = epsc

    def run(self, x_ap, xreg, out_ap, outreg, wcol=None, out2=None, out2reg=None):
        kc, C = self.kc, self.C
        xbs, sqs, srow, tfs = self.xb, self.sq, self.srow, self.tf
        xb = v3(xbs, C)
        sq = v3(sqs, C)
        tf = v3(tfs, C)
        (ban, ba), (bbn, bb) = self.ba, self.bb
        mb = srow[:, 0:C]
        vb = srow[:, C:2 * C]
        tp = srow[:, 2 * C:3 * C]
        rb = srow[:, 3 * C:4 * C]
        kc.ve(lambda e: e.tensor_copy(xb[:, :, 0:C], x_ap), r=(xreg,),
              w=(xbs.r,), tag="ln_cast")
        kc.act(sq[:, :, 0:C], xb[:, :, 0:C], AF.Square, r=(xbs.r,), w=(sqs.r,),
               tag="ln_sq")
        kc.mmgroup([(self.ones, xb[:, t, 0:C]) for t in range(NT)],
                   ba[:, 0:C], ban, r=(xbs.r,), tag="ln_sum")
        kc.mmgroup([(self.ones, sq[:, t, 0:C]) for t in range(NT)],
                   bb[:, 0:C], bbn, r=(sqs.r,), tag="ln_sumsq")
        kc.ve(lambda e: e.tensor_scalar_mul(mb, ba[:, 0:C], 1.0 / D),
              r=(ban,), w=(srow.r,), tag="ln_mean")
        kc.ve(lambda e: e.scalar_tensor_tensor(tp, mb, -1.0, mb, ALU.mult,
                                               ALU.mult),
              r=(srow.r,), w=(srow.r,), tag="ln_msq")
        kc.ve(lambda e: e.scalar_tensor_tensor(vb, bb[:, 0:C], 1.0 / D, tp,
                                               ALU.mult, ALU.add),
              r=(bbn, srow.r), w=(srow.r,), tag="ln_var")
        kc.act(tp, vb, AF.Sqrt, bias=self.epsc, r=(srow.r, "CCOL"),
               w=(srow.r,), tag="ln_sdv")
        kc.ve(lambda e: e.reciprocal(rb, tp), r=(srow.r,), w=(srow.r,),
              tag="ln_rstd")
        mb_bc = mb.unsqueeze(1).broadcast_to([P, NT, C])
        rb_bc = rb.unsqueeze(1).broadcast_to([P, NT, C])
        kc.ve(lambda e: e.tensor_sub(tf[:, :, 0:C], x_ap, mb_bc),
              r=(xreg, srow.r), w=(tfs.r,), tag="ln_sub")
        if wcol is None:
            kc.ve(lambda e: e.tensor_mul(out_ap, tf[:, :, 0:C], rb_bc),
                  r=(tfs.r, srow.r), w=(outreg,), tag="ln_out")
        else:
            kc.ve(lambda e: e.tensor_mul(tf[:, :, 0:C], tf[:, :, 0:C], rb_bc),
                  r=(tfs.r, srow.r), w=(tfs.r,), tag="ln_nrm")
            kc.ve(lambda e: e.tensor_mul(out_ap, tf[:, :, 0:C],
                                         bc_nt(wcol, C)),
                  r=(tfs.r, "PPAR"), w=(outreg,), tag="ln_scale")


def v3(slot, C=256):
    return slot.t[:, :].rearrange("p (t c) -> p t c", c=C)



def build_program(cfg):
    NTOK = cfg["NTOK"]
    CH = cfg["CH"]
    NCH = NTOK // CH
    NSUB = NTOK // P
    LAYERS = cfg["LAYERS"]
    MCH = cfg["MCH"]
    NMCH = NTOK // MCH
    NEXP = cfg["NEXP"]
    DFF = cfg["DFF"]
    NFT = DFF // P
    dbg = "ExternalOutput" if cfg.get("debug") else "Internal"

    nc = bass.Bass("TRN2", target_bir_lowering=False)
    sch = Sched()
    kc = KCtx(nc, sch, cfg)

    ext_in = {}

    def din(name, shape, dt=F32):
        t = nc.dram_tensor(name, list(shape), dt, kind="ExternalInput")
        ext_in[name] = t
        return t

    x0 = din("x0", [D, NTOK])
    consts = din("consts", [P, 264])
    ppar = din("ppar", [P, 2 * 96 + 8])
    prows = din("prows", [1, 8 + P])
    pgn = din("pgn", [1, 4 * D])
    u2p = din("u2p", [P, 32])
    wWr = din("wWr", [2, D, D]); wWk = din("wWk", [2, D, D])
    wWv = din("wWv", [2, D, D]); wWo = din("wWo", [2, D, D])
    wWg = din("wWg", [2, D, D])
    wtm1 = din("wtm1", [2, D, 384])
    wtm2 = din("wtm2", [2, P, 5 * D])
    wtd1 = din("wtd1", [2, D, P])
    wtd2 = din("wtd2", [2, P, D])
    wcWk = din("wcWk", [2, D, 3 * D])
    wcWv = din("wcWv", [2, 3 * D, D])
    wcWr = din("wcWr", [2, D, D])
    wmWr = din("wmWr", [D, NEXP])
    wm1 = din("wm1", [NEXP, D, DFF])
    wb1 = din("wb1", [NEXP, P, NFT])
    wm2 = din("wm2", [NEXP, DFF, D])
    wb2 = din("wb2", [NEXP, D])

    out = nc.dram_tensor("out", [NTOK, D], F32, kind="ExternalOutput")

    hD = nc.dram_tensor("hD", [D, NTOK], F32, kind=dbg)
    x1D = nc.dram_tensor("x1D", [D, NTOK], F32, kind=dbg)
    xfD = nc.dram_tensor("xfD", [D, NTOK], BF16, kind=dbg)
    olocD = nc.dram_tensor("olocD", [NTOK, D], F32, kind="Internal")
    ggD = nc.dram_tensor("ggD", [NTOK, D], BF16, kind="Internal")
    qaD = nc.dram_tensor("qaD", [D, NTOK], BF16, kind="Internal")
    qbD = nc.dram_tensor("qbD", [D, NTOK], BF16, kind="Internal")
    cinU = nc.dram_tensor("cinU", [P, P], F32, kind="Internal")
    coutU = nc.dram_tensor("coutU", [2 * P, P], F32, kind="Internal")
    cinS = nc.dram_tensor("cinS", [P, 512], F32, kind="Internal")
    coutS = nc.dram_tensor("coutS", [2 * P, 512], F32, kind="Internal")

    es0 = kc.es
    pb = [kc.sbuf(es0, f"pb{i}", [P, 512], F32) if False else
          es0.enter_context(nc.psum_tensor(f"pb{i}", [P, 512], F32))
          for i in range(7)]
    pbT = es0.enter_context(nc.psum_tensor("pbT", [P, 1024], BF16))
    PB = [f"pb{i}" for i in range(7)]

    _uniq = [0]

    def S(name, shape, dt, es=es0):
        _uniq[0] += 1
        uname = f"{name}_{_uniq[0]}"
        t = es.enter_context(nc.sbuf_tensor(uname, shape, dt))
        return Slot(t, name)

    CID = S("CID", [P, P], BF16)
    CIDF = S("CIDF", [P, P], F32)
    CTRI = S("CTRI", [P, P], F32)
    CONE = S("CONE", [P, P], BF16)
    CCOL = S("CCOL", [P, 8], F32)
    PPAR = S("PPARs", [P, 200], F32)
    PROW = S("PROWs", [1, 8 + P], F32)
    U2S = S("U2S", [P, 32], BF16)
    SST = S("SST", [P, 512], F32)
    SBFS = S("SBFS", [P, 512], BF16)
    ACUM = S("ACUM", [P, 8], F32)
    ACUMC = S("ACUMC", [P, 16], F32)
    ACTAB = S("ACTAB", [P, 8 * NSUB], F32)
    BPREV = S("BPREV", [P, 8], F32)
    CCU = S("CCU", [P, 8], F32)
    CCG2 = S("CCG2", [P, 512], F32)
    SROW = S("SROW", [P, 4 * CH], F32)
    GST = S("GST", [P, 80], F32)
    RTMP = S("RTMP", [P, 16], F32)
    WTE = S("WTE", [P, (NTOK // P) * 8], F32)
    UB2 = S("UB2", [P, 16], BF16)
    HLAST = S("HLAST", [P, 8], F32)

    PAR_IDX = ["ln1", "ln2", "xm", "wm", "km", "vm", "rm", "gm", "ckm", "crm",
               "td", "tf"]

    def par(l, name):
        i = PAR_IDX.index(name)
        base = l * 96 + i * 8
        return PPAR[:, base:base + 8]

    def par_bc(l, name, C):
        return par(l, name).unsqueeze(2).broadcast_to([P, NT, C])

    def pcol(l, name, t):
        i = PAR_IDX.index(name)
        return PPAR[:, l * 96 + i * 8 + t: l * 96 + i * 8 + t + 1]

    LNOUTW = PPAR[:, 192:200]
    maskA = CCOL[:, 0:1]
    maskB = CCOL[:, 1:2]
    halfc = CCOL[:, 2:3]
    zeroc = CCOL[:, 3:4]
    epsc = CCOL[:, 5:6]
    neg1c = CCOL[:, 6:7]

    kc.dma(GPSIMD, CID[:, :], consts[:, 0:P], w=("CID",), tag="ld_cid")
    kc.dma(SYNC, CIDF[:, :], consts[:, 0:P], w=("CIDF",), tag="ld_cidf")
    kc.dma(SYNC, CTRI[:, :], consts[:, P:2 * P], w=("CTRI",), tag="ld_tri")
    kc.ve(lambda e: e.memset(CONE[:, :], 1.0), w=("CONE",), tag="ones")
    kc.dma(SYNC, CCOL[:, :], consts[:, 2 * P:2 * P + 8], w=("CCOL",), tag="ld_ccol")
    kc.dma(SYNC, PPAR[:, :], ppar[:, :], w=("PPAR",), tag="ld_par")
    kc.dma(SYNC, PROW[:, :], prows[:, :], w=("PROW",), tag="ld_prow")
    kc.dma(GPSIMD, U2S[:, :], u2p.ap(), w=("U2S",), tag="ld_u2")
    kc.ve(lambda e: e.memset(SST[:, :], 0.0), w=("SST",), tag="s0")
    kc.ve(lambda e: e.memset(SBFS[:, :], 0.0), w=("SBFS",), tag="sb0")
    kc.ve(lambda e: e.memset(ACUM[:, :], 1.0), w=("ACUM",), tag="ac0")
    kc.ve(lambda e: e.memset(SROW[:, 0:P], 0.0), w=("SROW",), tag="zsrow")
    kc.dma(GPSIMD, cinU.ap()[:, 8:P], SROW[:, 8:P], r=("SROW",), w=("cinU",),
           tag="zcinu")

    hDv = dmajor_view(hD)
    x0v = dmajor_view(x0)
    x1v = dmajor_view(x1D)
    xfv = dmajor_view(xfD)
    qav = dmajor_view(qaD)
    qbv = dmajor_view(qbD)
    olv = tokmajor_view(olocD)
    ggv = tokmajor_view(ggD)
    outv = tokmajor_view(out)

    def ld_dmaj(q, slot, C, view, dreg, c0, tag, f32=True):
        vv = v3(slot, C)
        for s2 in range(2):
            t0, t1 = s2 * 4, (s2 + 1) * 4
            kc.dma(q, vv[:, t0:t1, 0:C], view[:, t0:t1, c0:c0 + C],
                   r=(dreg,), w=(slot.r,), tag=tag)

    def st_dmaj(q, view, dreg, slot, C, c0, tag):
        vv = v3(slot, C)
        for s2 in range(2):
            t0, t1 = s2 * 4, (s2 + 1) * 4
            kc.dma(q, view[:, t0:t1, c0:c0 + C], vv[:, t0:t1, 0:C],
                   r=(slot.r,), w=(dreg,), tag=tag)

    def boundary_u(l, lnw_name, ph, UBB, tag):
        kc.ve(lambda e: e.tensor_copy(UBB[:, 0:8], HLAST[:, 0:8]),
              r=("HLAST",), w=(UBB.r,), tag=tag + "_ld")
        ubb = UBB[:, 0:8]
        ubbf = UBB[:, 8:16]
        kc.ve(lambda e: e.tensor_copy(UB2[:, 0:8], ubb), r=(UBB.r,),
              w=(UB2.r,), tag=tag + "_cast")
        kc.act(UB2[:, 8:16], UB2[:, 0:8], AF.Square, r=(UB2.r,), w=(UB2.r,),
               tag=tag + "_sq")
        kc.mmgroup([(CONE[:, :], UB2[:, t:t + 1]) for t in range(NT)],
                   pb[0][:, 0:1], PB[0], r=(UB2.r, "CONE"), tag=tag + "_sum")
        kc.mmgroup([(CONE[:, :], UB2[:, 8 + t:9 + t]) for t in range(NT)],
                   pb[1][:, 0:1], PB[1], r=(UB2.r, "CONE"), tag=tag + "_sumsq")
        kc.ve(lambda e: e.tensor_scalar_mul(RTMP[:, 0:1], pb[0][:, 0:1], 1.0 / D),
              r=(PB[0],), w=(RTMP.r,), tag=tag + "_m")
        kc.ve(lambda e: e.scalar_tensor_tensor(RTMP[:, 1:2], RTMP[:, 0:1], -1.0,
                                               RTMP[:, 0:1], ALU.mult, ALU.mult),
              r=(RTMP.r,), w=(RTMP.r,), tag=tag + "_msq")
        kc.ve(lambda e: e.scalar_tensor_tensor(RTMP[:, 2:3], pb[1][:, 0:1],
                                               1.0 / D, RTMP[:, 1:2], ALU.mult,
                                               ALU.add),
              r=(PB[1], RTMP.r), w=(RTMP.r,), tag=tag + "_var")
        kc.act(RTMP[:, 3:4], RTMP[:, 2:3], AF.Sqrt, bias=epsc,
               r=(RTMP.r, "CCOL"), w=(RTMP.r,), tag=tag + "_sdv")
        kc.ve(lambda e: e.reciprocal(RTMP[:, 4:5], RTMP[:, 3:4]),
              r=(RTMP.r,), w=(RTMP.r,), tag=tag + "_rstd")
        kc.ve(lambda e: e.tensor_scalar_sub(UBB[:, 0:8], ubb, RTMP[:, 0:1]),
              r=(UBB.r, RTMP.r), w=(UBB.r,), tag=tag + "_sub")
        kc.ve(lambda e: e.tensor_scalar_mul(UBB[:, 0:8], ubb, RTMP[:, 4:5]),
              r=(UBB.r, RTMP.r), w=(UBB.r,), tag=tag + "_nrm")
        if lnw_name is not None:
            kc.ve(lambda e: e.tensor_mul(UBB[:, 0:8], ubb, par(l, lnw_name)),
                  r=(UBB.r, "PPAR"), w=(UBB.r,), tag=tag + "_w")
        kc.dma(SYNC, cinU.ap()[:, 0:8], UBB[:, 0:8], r=(UBB.r,), w=("cinU",),
               tag=tag + "_stg")
        kc.sch.op(GPSIMD,
                  lambda e: e.collective_compute(
                      "AllGather", ALU.bypass,
                      replica_groups=[[0, 1], [2, 3], [4, 5], [6, 7]],
                      ins=[cinU.ap()], outs=[coutU.ap()]),
                  r=("cinU",), w=("coutU",), coll=True, tag=tag + "_cc")
        kc.dma(SYNC, CCU[:, 0:8], coutU.ap()[0:P, 0:8], r=("coutU",),
               w=("CCU",), tag=tag + "_g")
        kc.ve(lambda e: e.tensor_scalar_mul(BPREV[:, 0:8], CCU[:, 0:8], halfc),
              r=("CCU", "CCOL"), w=("BPREV",), tag=tag + "_bp")

    with contextlib.ExitStack() as pes:
        FXH = S("p0FXH", [P, NT * CH], F32, pes)
        LXB = S("p0LXB", [P, NT * CH], BF16, pes)
        LSQ = S("p0LSQ", [P, NT * CH], BF16, pes)
        LTF = S("p0LTF", [P, NT * CH], F32, pes)
        ln = LN(kc, CONE[:, :], LXB, LSQ, SROW, LTF,
                (PB[0], pb[0]), (PB[1], pb[1]), CH, epsc)
        for c in range(NCH):
            c0 = c * CH
            ld_dmaj(SYNC, FXH, CH, x0v, "x0", c0, "p0_ld")
            ln.run(v3(FXH, CH)[:, :, 0:CH], FXH.r,
                   v3(FXH, CH)[:, :, 0:CH], FXH.r)
            if c == NCH - 1:
                kc.ve(lambda e: e.tensor_copy(
                    HLAST[:, 0:8], v3(FXH, CH)[:, :, CH - 1:CH].squeeze(2)),
                      r=(FXH.r,), w=("HLAST",), tag="hlast")
            st_dmaj(SYNC, hDv, f"hD:{c}", FXH, CH, c0, "p0_st")
    sch.barrier("p0")

    def phase_tmA(l):
        with contextlib.ExitStack() as pes:
            WPA = S("WPA", [P, 4 * NT * D], BF16, pes)
            TM1 = S("TM1", [P, NT * 384], BF16, pes)
            TM2 = S("TM2", [P, 5 * D], BF16, pes)
            TD1 = S("TD1", [P, NT * P], BF16, pes)
            TD2 = S("TD2", [P, D], BF16, pes)
            FXH = S("FXH", [P, NT * CH], F32, pes)
            LOGW = S("LOGW", [P, NT * CH], F32, pes)
            PLAX = S("PLAX", [P, NT * CH], F32, pes)
            LTF = S("LTF", [P, NT * CH], F32, pes)
            LXB = S("LXB", [P, NT * CH], BF16, pes)
            LSQ = S("LSQ", [P, NT * CH], BF16, pes)
            U = S("U", [P, NT * CH], BF16, pes)
            SX = S("SX", [P, NT * CH], BF16, pes)
            EEXC = S("EEXC", [P, NT * CH], BF16, pes)
            EINC = S("EINC", [P, NT * CH], BF16, pes)
            KT = S("KT", [P, NT * CH], BF16, pes)
            RT = S("RT", [P, NT * CH], BF16, pes)
            QA = S("QA", [P, NT * CH], BF16, pes)
            QB = S("QB", [P, NT * CH], BF16, pes)
            RK = S("RK", [P, NT * CH], BF16, pes)
            KH = S("KH", [P, NT * CH], BF16, pes)
            PX = S("PXs", [P, NT * CH], BF16, pes)
            TA = S("TAs", [P, NT * CH], BF16, pes)
            TMPB = S("TMPB", [P, 3 * CH], BF16, pes)
            VT = S("VTs", [P, 2 * D], BF16, pes)
            GT = S("GTs", [P, 2 * D], BF16, pes)
            UBB = S("UBB", [P, 16], F32, pes)

            def wslab(i, t):
                return WPA[:, (i * NT + t) * D:(i * NT + t) * D + D]

            for i, wt in enumerate((wWr, wWk, wWv, wWg)):
                for t in range(NT):
                    kc.dma(GPSIMD, wslab(i, t), wt.ap()[l, t * P:(t + 1) * P, :],
                           w=(f"WPA{i}",), tag=f"ldw{i}")
            for t in range(NT):
                kc.dma(GPSIMD, TM1[:, t * 384:(t + 1) * 384],
                       wtm1.ap()[l, t * P:(t + 1) * P, :], w=("TM1",), tag="ldtm1")
                kc.dma(GPSIMD, TD1[:, t * P:(t + 1) * P],
                       wtd1.ap()[l, t * P:(t + 1) * P, :], w=("TD1",), tag="ldtd1")
            kc.dma(GPSIMD, TM2[:, :], wtm2.ap()[l, :, :], w=("TM2",), tag="ldtm2")
            kc.dma(GPSIMD, TD2[:, :], wtd2.ap()[l, :, :], w=("TD2",), tag="ldtd2")

            boundary_u(l, "ln1", None, UBB, f"ubA{l}")
            kc.ve(lambda e: e.memset(SST[:, :], 0.0), w=("SST",), tag="s0")
            kc.ve(lambda e: e.memset(SBFS[:, :], 0.0), w=("SBFS",), tag="sb0")
            kc.ve(lambda e: e.memset(ACUM[:, :], 1.0), w=("ACUM",), tag="ac0")

            ln = LN(kc, CONE[:, :], LXB, LSQ, SROW, LTF,
                    (PB[0], pb[0]), (PB[1], pb[1]), CH, epsc)
            fxh = v3(FXH, CH)
            uu = v3(U, CH)
            sx = v3(SX, CH)
            logw = v3(LOGW, CH)
            plax = v3(PLAX, CH)
            eexc = v3(EEXC, CH)
            einc = v3(EINC, CH)
            ktv = v3(KT, CH)
            rtv = v3(RT, CH)
            qa = v3(QA, CH)
            qb = v3(QB, CH)
            rk = v3(RK, CH)
            kh = v3(KH, CH)
            pxv = v3(PX, CH)
            tav = v3(TA, CH)
            ltf = v3(LTF, CH)
            vtv = VT.t[:, :].rearrange("p (s d) -> p s d", d=D)
            gtv = GT.t[:, :].rearrange("p (s d) -> p s d", d=D)

            for c in range(NCH):
                c0 = c * CH
                ld_dmaj(SYNC, FXH, CH, hDv, f"hD:{c}", c0, "tmA_ldh")
                ln.run(fxh[:, :, 0:CH], FXH.r, uu[:, :, 0:CH], U.r,
                       wcol=par(l, "ln1").unsqueeze(2))
                kc.ve(lambda e: e.tensor_sub(sx[:, :, 1:CH], uu[:, :, 0:CH - 1],
                                             uu[:, :, 1:CH]),
                      r=(U.r,), w=(SX.r,), tag="sx")
                kc.ve(lambda e: e.tensor_sub(sx[:, :, 0:1],
                                             BPREV[:, 0:8].unsqueeze(2),
                                             uu[:, :, 0:1]),
                      r=(U.r, "BPREV"), w=(SX.r,), tag="sx0")
                kc.ve(lambda e: e.tensor_copy(BPREV[:, 0:8],
                                              uu[:, :, CH - 1:CH].squeeze(2)),
                      r=(U.r,), w=("BPREV",), tag="bpup")
                kc.ve(lambda e: e.tensor_mul(pxv[:, :, 0:CH], sx[:, :, 0:CH],
                                             par_bc(l, "xm", CH)),
                      r=(SX.r, "PPAR"), w=(PX.r,), tag="xxx1")
                kc.ve(lambda e: e.tensor_add(pxv[:, :, 0:CH], pxv[:, :, 0:CH],
                                             uu[:, :, 0:CH]),
                      r=(PX.r, U.r), w=(PX.r,), tag="xxx2")
                for jp in range(3):
                    bk = 2 + (jp % 2)
                    kc.mmgroup([(TM1[:, t * 384 + jp * P: t * 384 + (jp + 1) * P],
                                 pxv[:, t, 0:CH]) for t in range(NT)],
                               pb[bk][:, 0:CH], PB[bk], r=("TM1", PX.r),
                               tag="tmp_mm")
                    kc.act(TMPB[:, jp * CH:(jp + 1) * CH], pb[bk][:, 0:CH],
                           AF.Tanh, r=(PB[bk],), w=("TMPB",), tag="tmp_tanh")
                MAAN = ["wm", "km", "vm", "rm", "gm"]
                for Pi in range(5):
                    for j in range(NT):
                        bk = 2 + (j % 2)
                        kc.mm(TM2[:, Pi * D + j * P: Pi * D + (j + 1) * P],
                              TMPB[:, (Pi // 2) * CH:(Pi // 2 + 1) * CH],
                              pb[bk][:, 0:CH], PB[bk], r=("TM2", "TMPB"),
                              tag="mix_mm")
                        kc.ve(lambda e, j=j, bk=bk, Pi=Pi:
                              e.scalar_tensor_tensor(
                                  ltf[:, j, 0:CH], pb[bk][:, 0:CH],
                                  pcol(l, MAAN[Pi], j), sx[:, j, 0:CH],
                                  ALU.add, ALU.mult),
                              r=(PB[bk], "PPAR", SX.r), w=(LTF.r,), tag="pxa")
                        kc.ve(lambda e, j=j: e.tensor_add(
                            pxv[:, j, 0:CH], ltf[:, j, 0:CH], uu[:, j, 0:CH]),
                              r=(LTF.r, U.r), w=(PX.r,), tag="pxb")
                    if Pi == 0:
                        kc.mmgroup([(TD1[:, t * P:(t + 1) * P], pxv[:, t, 0:CH])
                                    for t in range(NT)],
                                   pb[4][:, 0:CH], PB[4], r=("TD1", PX.r),
                                   tag="td1_mm")
                        kc.act(TMPB[:, 0:CH] if False else TA[:, 0:CH],
                               pb[4][:, 0:CH], AF.Tanh, r=(PB[4],), w=(TA.r,),
                               tag="td1_tanh")
                        for j in range(NT):
                            bk = 4 + (j % 2)
                            kc.mm(TD2[:, j * P:(j + 1) * P], TA[:, 0:CH],
                                  pb[bk][:, 0:CH], PB[bk], r=("TD2", TA.r),
                                  tag="td2_mm")
                            kc.act(logw[:, j, 0:CH], pb[bk][:, 0:CH], AF.Exp,
                                   bias=pcol(l, "td", j), r=(PB[bk], "PPAR"),
                                   w=(LOGW.r,), tag="logw")
                    elif Pi == 1:
                        for j in range(NT):
                            bk = 4 + (j % 2)
                            kc.mmgroup([(wslab(1, t)[:, j * P:(j + 1) * P],
                                         pxv[:, t, 0:CH]) for t in range(NT)],
                                       pb[bk][:, 0:CH], PB[bk],
                                       r=("WPA1", PX.r), tag="k_mm")
                            kc.sch.op(SCALAR,
                                      lambda e, j=j, bk=bk: e.copy(
                                          ktv[:, j, 0:CH], pb[bk][:, 0:CH]),
                                      r=(PB[bk],), w=(KT.r,), tag="k_ev")
                    elif Pi == 2:
                        for st in range(2):
                            for nh in range(2):
                                bk = 4 + nh
                                kc.mmgroup(
                                    [(pxv[:, t, st * P:(st + 1) * P],
                                      wslab(2, t)[:, nh * 512:(nh + 1) * 512])
                                     for t in range(NT)],
                                    pb[bk][:, 0:512], PB[bk],
                                    r=("WPA2", PX.r), tag="v_mm")
                                kc.sch.op(SCALAR,
                                          lambda e, st=st, nh=nh, bk=bk: e.copy(
                                              vtv[:, st, nh * 512:(nh + 1) * 512],
                                              pb[bk][:, 0:512]),
                                          r=(PB[bk],), w=(VT.r,), tag="v_ev")
                    elif Pi == 3:
                        for j in range(NT):
                            bk = 4 + (j % 2)
                            kc.mmgroup([(wslab(0, t)[:, j * P:(j + 1) * P],
                                         pxv[:, t, 0:CH]) for t in range(NT)],
                                       pb[bk][:, 0:CH], PB[bk],
                                       r=("WPA0", PX.r), tag="r_mm")
                            kc.act(rtv[:, j, 0:CH], pb[bk][:, 0:CH], AF.Sigmoid,
                                   r=(PB[bk],), w=(RT.r,), tag="r_ev")
                    elif Pi == 4:
                        for st in range(2):
                            for nh in range(2):
                                bk = 4 + nh
                                kc.mmgroup(
                                    [(pxv[:, t, st * P:(st + 1) * P],
                                      wslab(3, t)[:, nh * 512:(nh + 1) * 512])
                                     for t in range(NT)],
                                    pb[bk][:, 0:512], PB[bk],
                                    r=("WPA3", PX.r), tag="g_mm")
                                kc.act(gtv[:, st, nh * 512:(nh + 1) * 512],
                                       pb[bk][:, 0:512], AF.Silu,
                                       r=(PB[bk],), w=(GT.r,), tag="g_ev")
                for st in range(2):
                    kc.dma(SYNC, ggv[:, c * 2 + st, :], gtv[:, st, :],
                           r=(GT.r,), w=(f"ggD:{c}",), tag="g_sp")
                for s in range(2):
                    w0 = s * P
                    kc.ve(lambda e, w0=w0: e.memset(plax[:, :, w0:w0 + 1], 0.0),
                          w=(PLAX.r,), tag="plx0")
                    for tp in range(NT):
                        kc.ve(lambda e, w0=w0, tp=tp: e.tensor_tensor_scan(
                            plax[:, tp, w0 + 1:w0 + P],
                            logw[:, tp, w0:w0 + P - 1],
                            zeroc.broadcast_to([P, P - 1]),
                            0.0, ALU.add, ALU.add),
                              r=(LOGW.r, "CCOL"), w=(PLAX.r,), tag="scan")
                kc.ve(lambda e: e.tensor_add(logw[:, :, 0:CH], plax[:, :, 0:CH],
                                             logw[:, :, 0:CH]),
                      r=(PLAX.r, LOGW.r), w=(LOGW.r,), tag="plainc")
                for s in range(2):
                    w1 = s * P + P - 1
                    kc.act(ACUMC[:, s * 8:(s + 1) * 8].unsqueeze(2),
                           logw[:, :, w1:w1 + 1], AF.Exp, scale=neg1c,
                           r=(LOGW.r, "CCOL"), w=("ACUMC",), tag="aC")
                kc.act(eexc[:, :, 0:CH], plax[:, :, 0:CH], AF.Exp, scale=neg1c,
                       r=(PLAX.r, "CCOL"), w=(EEXC.r,), tag="eexc")
                kc.act(einc[:, :, 0:CH], logw[:, :, 0:CH], AF.Exp,
                       r=(LOGW.r,), w=(EINC.r,), tag="einc")
                kc.ve(lambda e: e.scalar_tensor_tensor(
                    v3(LXB, CH)[:, :, 0:CH], einc[:, :, 0:CH], maskA,
                    ktv[:, :, 0:CH], ALU.mult, ALU.mult),
                      r=(EINC.r, "CCOL", KT.r), w=(LXB.r,), tag="ka")
                kc.ve(lambda e: e.scalar_tensor_tensor(
                    v3(LSQ, CH)[:, :, 0:CH], einc[:, :, 0:CH], maskB,
                    ktv[:, :, 0:CH], ALU.mult, ALU.mult),
                      r=(EINC.r, "CCOL", KT.r), w=(LSQ.r,), tag="kb")
                kc.ve(lambda e: e.scalar_tensor_tensor(
                    qa[:, :, 0:CH], eexc[:, :, 0:CH], maskA,
                    rtv[:, :, 0:CH], ALU.mult, ALU.mult),
                      r=(EEXC.r, "CCOL", RT.r), w=(QA.r,), tag="qa")
                kc.ve(lambda e: e.scalar_tensor_tensor(
                    qb[:, :, 0:CH], eexc[:, :, 0:CH], maskB,
                    rtv[:, :, 0:CH], ALU.mult, ALU.mult),
                      r=(EEXC.r, "CCOL", RT.r), w=(QB.r,), tag="qb")
                kc.ve(lambda e: e.tensor_mul(rk[:, :, 0:CH], rtv[:, :, 0:CH],
                                             ktv[:, :, 0:CH]),
                      r=(RT.r, KT.r), w=(RK.r,), tag="rk")
                for s in range(2):
                    w0 = s * P
                    kc.ve(lambda e, s=s, w0=w0: e.tensor_mul(
                        tav[:, :, w0:w0 + P], einc[:, :, w0:w0 + P],
                        ACUMC[:, s * 8:(s + 1) * 8].unsqueeze(2)
                        .broadcast_to([P, NT, P])),
                          r=(EINC.r, "ACUMC"), w=(TA.r,), tag="est")
                    kc.ve(lambda e, w0=w0: e.tensor_mul(
                        kh[:, :, w0:w0 + P], ktv[:, :, w0:w0 + P],
                        tav[:, :, w0:w0 + P]),
                          r=(KT.r, TA.r), w=(KH.r,), tag="khat")
                for s2 in range(2):
                    t0, t1 = s2 * 4, (s2 + 1) * 4
                    kc.dma(SYNC, qav[:, t0:t1, c0:c0 + CH], qa[:, t0:t1, 0:CH],
                           r=(QA.r,), w=(f"qaD:{c}",), tag="qa_sp")
                    kc.dma(SYNC, qbv[:, t0:t1, c0:c0 + CH], qb[:, t0:t1, 0:CH],
                           r=(QB.r,), w=(f"qbD:{c}",), tag="qb_sp")
                for s in range(2):
                    sub = c * 2 + s
                    kc.ve(lambda e, sub=sub: e.tensor_copy(
                        ACTAB[:, sub * 8:(sub + 1) * 8], ACUM[:, 0:8]),
                          r=("ACUM",), w=("ACTAB",), tag="acst")
                    kc.ve(lambda e, s=s: e.tensor_mul(
                        ACUM[:, 0:8], ACUM[:, 0:8], ACUMC[:, s * 8:(s + 1) * 8]),
                          r=("ACUM", "ACUMC"), w=("ACUM",), tag="acup")
                for s in range(2):
                    w0 = s * P
                    for tp in range(NT):
                        kc.mm(v3(LXB, CH)[:, tp, w0:w0 + P],
                              qa[:, tp, w0:w0 + P],
                              pb[0][:, 0:P], PB[0], r=(LXB.r, QA.r), tag="ata")
                        kc.mm(v3(LSQ, CH)[:, tp, w0:w0 + P],
                              qb[:, tp, w0:w0 + P],
                              pb[1][:, 0:P], PB[1], r=(LSQ.r, QB.r), tag="atb")
                        kc.mm(rk[:, tp, w0:w0 + P],
                              U2S[:, l * 16 + tp * 2: l * 16 + tp * 2 + 2],
                              pb[6][:, s * 64:s * 64 + 2], PB[6],
                              r=(RK.r, "U2S"), tag="c2")
                        kc.ve(lambda e, tp=tp: e.tensor_mul(
                            TA[:, 0:P], pb[0][:, 0:P], CTRI[:, :]),
                              r=(PB[0], "CTRI"), w=(TA.r,), tag="mstra")
                        kc.ve(lambda e, tp=tp: e.tensor_mul(
                            TA[:, P:2 * P], pb[1][:, 0:P], CTRI[:, :]),
                              r=(PB[1], "CTRI"), w=(TA.r,), tag="mstrb")
                        for h in range(2):
                            kc.ve(lambda e, s=s, tp=tp, h=h: e.tensor_scalar(
                                PX[:, h * 64:(h + 1) * 64],
                                vtv[:, s, tp * P + h * 64: tp * P + (h + 1) * 64],
                                pb[6][:, s * 64 + h: s * 64 + h + 1], None,
                                ALU.mult),
                                  r=(VT.r, PB[6]), w=(PX.r,), tag="vc")
                        for h, (qv, msl) in enumerate(((qa, TA[:, 0:P]),
                                                      (qb, TA[:, P:2 * P]))):
                            kc.mmgroup(
                                [(msl, vtv[:, s, tp * P + h * 64:
                                           tp * P + (h + 1) * 64]),
                                 (qv[:, tp, w0:w0 + P],
                                  SBFS[:, tp * 64:(tp + 1) * 64]),
                                 (CID[:, :], PX[:, h * 64:(h + 1) * 64])],
                                pb[2][:, h * 64:(h + 1) * 64], PB[2],
                                r=(TA.r, VT.r, QA.r if h == 0 else QB.r,
                                   "SBFS", "CID", PX.r),
                                tag="o_mm")
                        kc.transpose(pbT[:, 0:P], kh[:, tp, w0:w0 + P],
                                     CID[:, :], "pbT", r=(KH.r, "CID"),
                                     tag="khT")
                        kc.ve(lambda e: e.tensor_copy(PX[:, P:2 * P],
                                                      pbT[:, 0:P]),
                              r=("pbT",), w=(PX.r,), tag="khT_ev")
                        kc.mm(PX[:, P:2 * P],
                              vtv[:, s, tp * P:(tp + 1) * P],
                              pb[3][:, 0:P], PB[3], r=(PX.r, VT.r), tag="ds")
                        kc.ve(lambda e, tp=tp: e.tensor_copy(
                            v3(LTF, CH)[:, 0, tp * P:(tp + 1) * P]
                            if False else LTF[:, tp * P:(tp + 1) * P],
                            pb[2][:, 0:P]),
                              r=(PB[2],), w=(LTF.r,), tag="o_ev")
                        sv = SST.t[:, :].rearrange("p (t v) -> p t v", v=64)
                        for h in range(2):
                            h0 = h * 64
                            kc.ve(lambda e, tp=tp, s=s, h0=h0: e.scalar_tensor_tensor(
                                sv[h0:h0 + 64, tp, :],
                                sv[h0:h0 + 64, tp, :],
                                ACUMC[h0:h0 + 64, s * 8 + tp: s * 8 + tp + 1],
                                pb[3][h0:h0 + 64, h0:h0 + 64],
                                ALU.mult, ALU.add),
                                  r=("SST", "ACUMC", PB[3]), w=("SST",),
                                  tag="sup")
                        kc.ve(lambda e, tp=tp: e.tensor_copy(
                            SBFS[:, tp * 64:(tp + 1) * 64], sv[:, tp, :]),
                              r=("SST",), w=("SBFS",), tag="sbf")
                    kc.dma(SYNC, olv[:, c * 2 + s, :], LTF[:, 0:D],
                           r=(LTF.r,), w=(f"olocD:{c}",), tag="o_sp")
            kc.dma(SYNC, cinS.ap()[:, :], SST[:, :], r=("SST",), w=("cinS",),
                   tag="st_stg")
            kc.sch.op(GPSIMD,
                      lambda e: e.collective_compute(
                          "AllGather", ALU.bypass,
                          replica_groups=[[0, 1], [2, 3], [4, 5], [6, 7]],
                          ins=[cinS.ap()], outs=[coutS.ap()]),
                      r=("cinS",), w=("coutS",), coll=True, tag="st_cc")
        sch.barrier(f"tmA{l}")

    def phase_tmB(l):
        with contextlib.ExitStack() as pes:
            WOS = S("WOS", [P, NT * D], BF16, pes)
            FXH = S("bFXH", [P, NT * CH], F32, pes)
            OLS = S("OLS", [P, 2 * D], F32, pes)
            OF = S("OFs", [P, D], F32, pes)
            OSQ = S("OSQ", [P, D], F32, pes)
            QAS = S("QAS", [P, NT * CH], BF16, pes)
            QBS = S("QBS", [P, NT * CH], BF16, pes)
            GS = S("GSs", [P, 2 * D], BF16, pes)
            SCB = S("SCB", [P, 2 * 512], BF16, pes)
            GTB = S("GTB", [P, D], BF16, pes)
            GTT = S("GTTs", [P, NT * CH], BF16, pes)
            GNREP = S("GNREP", [P, 2 * D], F32, pes)
            PGN = S("PGNs", [1, 4 * D], F32, pes)

            for t in range(NT):
                kc.dma(GPSIMD, WOS[:, t * D:(t + 1) * D],
                       wWo.ap()[l, t * P:(t + 1) * P, :], w=("WOS",), tag="ldwo")
            kc.dma(SYNC, PGN[:, :], pgn.ap()[:, :], w=("PGN",), tag="ldpgn")
            for i, base in enumerate((2 * l * D, (2 * l + 1) * D)):
                for nh in range(2):
                    kc.mm(PROW[0:1, NEXP:NEXP + P],
                          PGN[0:1, base + nh * 512: base + (nh + 1) * 512],
                          pb[i][:, 0:512], PB[i], r=("PROW", "PGN"),
                          tag="gnrep_mm")
                    kc.ve(lambda e, i=i, nh=nh: e.tensor_copy(
                        GNREP[:, i * D + nh * 512: i * D + (nh + 1) * 512],
                        pb[i][:, 0:512]),
                          r=(PB[i],), w=("GNREP",), tag="gnrep_ev")
            kc.dma(SYNC, CCG2[:, :], coutS.ap()[0:P, :], r=("coutS",),
                   w=("CCG2",), tag="g0s")
            qas = v3(QAS, CH)
            qbs = v3(QBS, CH)
            fxh = v3(FXH, CH)
            gtt = v3(GTT, CH)
            ccg2v = CCG2.t[:, :].rearrange("p (t v) -> p t v", v=64)
            scbv = SCB.t[:, :].rearrange("p (s x) -> p s x", x=512)
            for c in range(NCH):
                c0 = c * CH
                ld_dmaj(SYNC, FXH, CH, hDv, f"hD:{c}", c0, "tmB_ldh")
                for s2 in range(2):
                    t0, t1 = s2 * 4, (s2 + 1) * 4
                    kc.dma(SYNC, qas[:, t0:t1, 0:CH], qav[:, t0:t1, c0:c0 + CH],
                           r=(f"qaD:{c}",), w=(QAS.r,), tag="qa_ld")
                    kc.dma(SYNC, qbs[:, t0:t1, 0:CH], qbv[:, t0:t1, c0:c0 + CH],
                           r=(f"qbD:{c}",), w=(QBS.r,), tag="qb_ld")
                for s in range(2):
                    kc.dma(SYNC, OLS[:, s * D:(s + 1) * D], olv[:, c * 2 + s, :],
                           r=(f"olocD:{c}",), w=(OLS.r,), tag="ol_ld")
                    kc.dma(GPSIMD, GS[:, s * D:(s + 1) * D], ggv[:, c * 2 + s, :],
                           r=(f"ggD:{c}",), w=(GS.r,), tag="g_ld")
                    sub = c * 2 + s
                    kc.ve(lambda e, s=s, sub=sub: e.tensor_mul(
                        scbv[:, s, :].rearrange("p (t v) -> p t v", v=64),
                        ccg2v[:, :, :],
                        ACTAB[:, sub * 8:(sub + 1) * 8].unsqueeze(2)
                        .broadcast_to([P, NT, 64])),
                          r=("CCG2", "ACTAB"), w=(SCB.r,), tag="scb1")
                    kc.ve(lambda e, s=s: e.tensor_scalar_mul(
                        scbv[:, s, :], scbv[:, s, :], halfc),
                          r=(SCB.r, "CCOL"), w=(SCB.r,), tag="scb2")
                for s in range(2):
                    w0 = s * P
                    for tp in range(NT):
                        bk = 2 + (tp % 2)
                        for h, qv in enumerate((qas, qbs)):
                            kc.mm(qv[:, tp, w0:w0 + P],
                                  scbv[:, s, tp * 64:(tp + 1) * 64],
                                  pb[bk][:, h * 64:(h + 1) * 64], PB[bk],
                                  r=(QAS.r if h == 0 else QBS.r, SCB.r),
                                  tag="corr")
                        kc.ve(lambda e, s=s, tp=tp, bk=bk: e.tensor_add(
                            OF[:, tp * P:(tp + 1) * P],
                            OLS[:, s * D + tp * P: s * D + (tp + 1) * P],
                            pb[bk][:, 0:P]),
                              r=(OLS.r, PB[bk]), w=(OF.r,), tag="ofull")
                    ofv = OF.t[:, :].rearrange("p (g v) -> p g v", v=64)
                    kc.ve(lambda e: e.reduce_sum(GST[:, 0:16].unsqueeze(2),
                                                 ofv[:, :, :], AX.X),
                          r=(OF.r,), w=("GST",), tag="gn_sum")
                    kc.act(OSQ[:, 0:D], OF[:, 0:D], AF.Square, r=(OF.r,),
                           w=(OSQ.r,), tag="gn_sq")
                    kc.ve(lambda e: e.reduce_sum(GST[:, 16:32].unsqueeze(2),
                                                 OSQ.t[:, :].rearrange(
                                                     "p (g v) -> p g v", v=64),
                                                 AX.X),
                          r=(OSQ.r,), w=("GST",), tag="gn_sumsq")
                    kc.ve(lambda e: e.tensor_scalar_mul(GST[:, 0:16],
                                                        GST[:, 0:16], 1.0 / 64),
                          r=("GST",), w=("GST",), tag="gn_mean")
                    kc.ve(lambda e: e.scalar_tensor_tensor(
                        GST[:, 32:48], GST[:, 0:16], -1.0, GST[:, 0:16],
                        ALU.mult, ALU.mult),
                          r=("GST",), w=("GST",), tag="gn_msq")
                    kc.ve(lambda e: e.scalar_tensor_tensor(
                        GST[:, 48:64], GST[:, 16:32], 1.0 / 64, GST[:, 32:48],
                        ALU.mult, ALU.add),
                          r=("GST",), w=("GST",), tag="gn_var")
                    kc.act(GST[:, 32:48], GST[:, 48:64], AF.Sqrt,
                           bias=epsc, r=("GST", "CCOL"), w=("GST",),
                           tag="gn_sdv")
                    kc.ve(lambda e: e.reciprocal(GST[:, 48:64], GST[:, 32:48]),
                          r=("GST",), w=("GST",), tag="gn_rstd")
                    kc.ve(lambda e: e.tensor_sub(
                        ofv[:, :, :], ofv[:, :, :],
                        GST[:, 0:16].unsqueeze(2).broadcast_to([P, 16, 64])),
                          r=(OF.r, "GST"), w=(OF.r,), tag="gn_sub")
                    kc.ve(lambda e: e.tensor_mul(
                        ofv[:, :, :], ofv[:, :, :],
                        GST[:, 48:64].unsqueeze(2).broadcast_to([P, 16, 64])),
                          r=(OF.r, "GST"), w=(OF.r,), tag="gn_nrm")
                    kc.ve(lambda e: e.tensor_mul(OF[:, 0:D], OF[:, 0:D],
                                                 GNREP[:, 0:D]),
                          r=(OF.r, "GNREP"), w=(OF.r,), tag="gn_w")
                    kc.ve(lambda e: e.tensor_add(OF[:, 0:D], OF[:, 0:D],
                                                 GNREP[:, D:2 * D]),
                          r=(OF.r, "GNREP"), w=(OF.r,), tag="gn_b")
                    kc.ve(lambda e, s=s: e.tensor_mul(
                        GTB[:, 0:D], OF[:, 0:D], GS[:, s * D:(s + 1) * D]),
                          r=(OF.r, GS.r), w=(GTB.r,), tag="gate")
                    for t in range(NT):
                        kc.transpose(pbT[:, 0:P], GTB[:, t * P:(t + 1) * P],
                                     CID[:, :], "pbT", r=(GTB.r, "CID"),
                                     tag="gT")
                        kc.ve(lambda e, t=t, s=s: e.tensor_copy(
                            gtt[:, t, s * P:(s + 1) * P], pbT[:, 0:P]),
                              r=("pbT",), w=(GTT.r,), tag="gT_ev")
                for j in range(NT):
                    bk = 4 + (j % 2)
                    kc.mmgroup([(WOS[:, t * D + j * P: t * D + (j + 1) * P],
                                 gtt[:, t, 0:CH]) for t in range(NT)],
                               pb[bk][:, 0:CH], PB[bk], r=("WOS", GTT.r),
                               tag="wo_mm")
                    kc.ve(lambda e, j=j, bk=bk: e.tensor_add(
                        fxh[:, j, 0:CH], fxh[:, j, 0:CH], pb[bk][:, 0:CH]),
                          r=(FXH.r, PB[bk]), w=(FXH.r,), tag="hup")
                if c == NCH - 1:
                    kc.ve(lambda e: e.tensor_copy(
                        HLAST[:, 0:8], fxh[:, :, CH - 1:CH].squeeze(2)),
                          r=(FXH.r,), w=("HLAST",), tag="hlast")
                st_dmaj(SYNC, hDv, f"hD:{c}", FXH, CH, c0, "tmB_sth")
        sch.barrier(f"tmB{l}")

    def phase_cm(l):
        with contextlib.ExitStack() as pes:
            CWK = S("CWK", [P, NT * 3 * D], BF16, pes)
            CWV = S("CWV", [P, 24 * D], BF16, pes)
            CWR = S("CWR", [P, NT * D], BF16, pes)
            FXH = S("cFXH", [P, NT * CH], F32, pes)
            LXB = S("cLXB", [P, NT * CH], BF16, pes)
            LSQ = S("cLSQ", [P, NT * CH], BF16, pes)
            LTF = S("cLTF", [P, NT * CH], F32, pes)
            U2 = S("cU2", [P, NT * CH], BF16, pes)
            SX2 = S("cSX2", [P, NT * CH], BF16, pes)
            KX = S("cKX", [P, NT * CH], BF16, pes)
            RX = S("cRX", [P, NT * CH], BF16, pes)
            KK = S("cKK", [P, 24 * CH], BF16, pes)
            RG = S("cRG", [P, NT * CH], BF16, pes)
            TR = S("cTR", [P, CH], F32, pes)
            UBB = S("cUBB", [P, 16], F32, pes)

            for t in range(NT):
                kc.dma(GPSIMD, CWK[:, t * 3 * D:(t + 1) * 3 * D],
                       wcWk.ap()[l, t * P:(t + 1) * P, :], w=("CWK",), tag="ldck")
                kc.dma(GPSIMD, CWR[:, t * D:(t + 1) * D],
                       wcWr.ap()[l, t * P:(t + 1) * P, :], w=("CWR",), tag="ldcr")
            for f in range(24):
                kc.dma(GPSIMD, CWV[:, f * D:(f + 1) * D],
                       wcWv.ap()[l, f * P:(f + 1) * P, :], w=("CWV",), tag="ldcv")
            boundary_u(l, "ln2", None, UBB, f"ubC{l}")

            ln = LN(kc, CONE[:, :], LXB, LSQ, SROW, LTF,
                    (PB[0], pb[0]), (PB[1], pb[1]), CH, epsc)
            fxh = v3(FXH, CH)
            u2 = v3(U2, CH)
            sx2 = v3(SX2, CH)
            kx = v3(KX, CH)
            rx = v3(RX, CH)
            kkv = v3(KK, CH)
            rgv = v3(RG, CH)
            ltf = v3(LTF, CH)
            for c in range(NCH):
                c0 = c * CH
                ld_dmaj(SYNC, FXH, CH, hDv, f"hD:{c}", c0, "cm_ldh")
                ln.run(fxh[:, :, 0:CH], FXH.r, u2[:, :, 0:CH], U2.r,
                       wcol=par(l, "ln2").unsqueeze(2))
                kc.ve(lambda e: e.tensor_sub(sx2[:, :, 1:CH], u2[:, :, 0:CH - 1],
                                             u2[:, :, 1:CH]),
                      r=(U2.r,), w=(SX2.r,), tag="csx")
                kc.ve(lambda e: e.tensor_sub(sx2[:, :, 0:1],
                                             BPREV[:, 0:8].unsqueeze(2),
                                             u2[:, :, 0:1]),
                      r=(U2.r, "BPREV"), w=(SX2.r,), tag="csx0")
                kc.ve(lambda e: e.tensor_copy(BPREV[:, 0:8],
                                              u2[:, :, CH - 1:CH].squeeze(2)),
                      r=(U2.r,), w=("BPREV",), tag="cbpup")
                kc.ve(lambda e: e.tensor_mul(ltf[:, :, 0:CH], sx2[:, :, 0:CH],
                                             par_bc(l, "ckm", CH)),
                      r=(SX2.r, "PPAR"), w=(LTF.r,), tag="ckx1")
                kc.ve(lambda e: e.tensor_add(kx[:, :, 0:CH], ltf[:, :, 0:CH],
                                             u2[:, :, 0:CH]),
                      r=(LTF.r, U2.r), w=(KX.r,), tag="ckx2")
                kc.ve(lambda e: e.tensor_mul(ltf[:, :, 0:CH], sx2[:, :, 0:CH],
                                             par_bc(l, "crm", CH)),
                      r=(SX2.r, "PPAR"), w=(LTF.r,), tag="crx1")
                kc.ve(lambda e: e.tensor_add(rx[:, :, 0:CH], ltf[:, :, 0:CH],
                                             u2[:, :, 0:CH]),
                      r=(LTF.r, U2.r), w=(RX.r,), tag="crx2")
                for j in range(24):
                    bk = 2 + (j % 2)
                    kc.mmgroup([(CWK[:, t * 3 * D + j * P: t * 3 * D + (j + 1) * P],
                                 kx[:, t, 0:CH]) for t in range(NT)],
                               pb[bk][:, 0:CH], PB[bk], r=("CWK", KX.r),
                               tag="ck_mm")
                    kc.ve(lambda e, bk=bk: e.tensor_relu(TR[:, 0:CH],
                                                         pb[bk][:, 0:CH]),
                          r=(PB[bk],), w=(TR.r,), tag="ck_rl")
                    kc.act(kkv[:, j, 0:CH], TR[:, 0:CH], AF.Square,
                           r=(TR.r,), w=(KK.r,), tag="ck_sq")
                for j in range(NT):
                    bk = 4 + (j % 2)
                    kc.mmgroup([(CWR[:, t * D + j * P: t * D + (j + 1) * P],
                                 rx[:, t, 0:CH]) for t in range(NT)],
                               pb[bk][:, 0:CH], PB[bk], r=("CWR", RX.r),
                               tag="cr_mm")
                    kc.act(rgv[:, j, 0:CH], pb[bk][:, 0:CH], AF.Sigmoid,
                           r=(PB[bk],), w=(RG.r,), tag="cr_ev")
                for j in range(NT):
                    bk = 2 + (j % 2)
                    kc.mmgroup([(CWV[:, f * D + j * P: f * D + (j + 1) * P],
                                 kkv[:, f, 0:CH]) for f in range(24)],
                               pb[bk][:, 0:CH], PB[bk], r=("CWV", KK.r),
                               tag="cv_mm")
                    kc.ve(lambda e, j=j, bk=bk: e.tensor_mul(
                        ltf[:, j, 0:CH], pb[bk][:, 0:CH], rgv[:, j, 0:CH]),
                          r=(PB[bk], RG.r), w=(LTF.r,), tag="cv_gate")
                    kc.ve(lambda e, j=j: e.tensor_add(
                        fxh[:, j, 0:CH], fxh[:, j, 0:CH], ltf[:, j, 0:CH]),
                          r=(LTF.r, FXH.r), w=(FXH.r,), tag="chup")
                if c == NCH - 1:
                    kc.ve(lambda e: e.tensor_copy(
                        HLAST[:, 0:8], fxh[:, :, CH - 1:CH].squeeze(2)),
                          r=(FXH.r,), w=("HLAST",), tag="hlast")
                st_dmaj(SYNC, hDv, f"hD:{c}", FXH, CH, c0, "cm_sth")
        sch.barrier(f"cm{l}")

    def phase_x1():
        with contextlib.ExitStack() as pes:
            FXH = S("xFXH", [P, NT * CH], F32, pes)
            FX2 = S("xFX2", [P, NT * CH], F32, pes)
            LXB = S("xLXB", [P, NT * CH], BF16, pes)
            LSQ = S("xLSQ", [P, NT * CH], BF16, pes)
            LTF = S("xLTF", [P, NT * CH], F32, pes)
            XFB = S("xXFB", [P, NT * CH], BF16, pes)
            ln = LN(kc, CONE[:, :], LXB, LSQ, SROW, LTF,
                    (PB[0], pb[0]), (PB[1], pb[1]), CH, epsc)
            fxh = v3(FXH, CH)
            fx2 = v3(FX2, CH)
            xfb = v3(XFB, CH)
            for c in range(NCH):
                c0 = c * CH
                ld_dmaj(SYNC, FXH, CH, hDv, f"hD:{c}", c0, "x1_ldh")
                ld_dmaj(SYNC, FX2, CH, x0v, "x0", c0, "x1_ldx0")
                ln.run(fxh[:, :, 0:CH], FXH.r, fxh[:, :, 0:CH], FXH.r,
                       wcol=LNOUTW.unsqueeze(2))
                kc.ve(lambda e: e.tensor_add(fxh[:, :, 0:CH], fxh[:, :, 0:CH],
                                             fx2[:, :, 0:CH]),
                      r=(FXH.r, FX2.r), w=(FXH.r,), tag="x1add")
                st_dmaj(SYNC, x1v, f"x1D:{c}", FXH, CH, c0, "x1_st")
                ln.run(fxh[:, :, 0:CH], FXH.r, xfb[:, :, 0:CH], XFB.r)
                st_dmaj(SYNC, xfv, f"xfD:{c}", XFB, CH, c0, "xf_st")
        sch.barrier("x1")

    def phase_moe():
        with contextlib.ExitStack() as pes:
            WP1 = S("mWP1", [P, NT * DFF], BF16, pes)
            WP2 = S("mWP2", [P, NFT * D], BF16, pes)
            HT = S("mHT", [P, NFT * MCH], BF16, pes)
            XFS = S("mXFS", [P, NT * MCH], BF16, pes)
            ACC = S("mACC", [P, (MCH // P) * D], F32, pes)
            WRM = S("mWRM", [P, NT * NEXP], BF16, pes)
            B1C = S("mB1C", [P, NFT], F32, pes)
            B2S = S("mB2S", [NEXP, D], BF16, pes)
            RL = S("mRL", [P, 3 * NEXP], F32, pes)
            RWB = S("mRWB", [P, NEXP], BF16, pes)
            BRB = S("mBRB", [P, NEXP], F32, pes)

            ht = HT.t[:, :].rearrange("p (f c) -> p f c", c=MCH)
            xfs = v3(XFS, MCH)
            accv = ACC.t[:, :].rearrange("p (a d) -> p a d", d=D)
            NTT = NTOK // P
            x1sb = HT.t[:, 0:2 * NT * MCH].bitcast(F32)\
                .rearrange("p (t c) -> p t c", c=MCH)
            WTTa = HT.t[0:NEXP, (NFT - NTT // 4) * MCH:NFT * MCH]

            onesrow = PROW[0:1, NEXP:NEXP + P]
            kc.mm(onesrow, PROW[0:1, 0:NEXP], pb[6][:, 0:NEXP],
                  PB[6], r=("PROW",), tag="brb_mm")
            kc.ve(lambda e: e.tensor_copy(BRB[:, :], pb[6][:, 0:NEXP]),
                  r=(PB[6],), w=("BRB",), tag="brb_ev")
            for t in range(NT):
                kc.dma(GPSIMD, WRM[:, t * NEXP:(t + 1) * NEXP],
                       wmWr.ap()[t * P:(t + 1) * P, :], w=("WRM",), tag="ldwr")
            kc.dma(GPSIMD, B2S[:, :], wb2.ap()[:, :], w=("B2S",), tag="ldb2")

            for c in range(NMCH):
                c0 = c * MCH
                for s2 in range(2):
                    t0, t1 = s2 * 4, (s2 + 1) * 4
                    kc.dma(SYNC, xfs[:, t0:t1, 0:MCH], xfv[:, t0:t1, c0:c0 + MCH],
                           r=(f"xfD:{c0 // CH}", f"xfD:{(c0 + MCH - 1) // CH}"),
                           w=(XFS.r,), tag="rxf_ld")
                for ti in range(MCH // P):
                    tt = c * (MCH // P) + ti
                    kc.mmgroup([(xfs[:, t, ti * P:(ti + 1) * P],
                                 WRM[:, t * NEXP:(t + 1) * NEXP])
                                for t in range(NT)],
                               pb[6][:, 0:NEXP], PB[6], r=(XFS.r, "WRM"),
                               tag="rt_mm")
                    kc.ve(lambda e: e.tensor_add(RL[:, 0:NEXP], pb[6][:, 0:NEXP],
                                                 BRB[:, :]),
                          r=(PB[6], "BRB"), w=("mRL",), tag="rt_br")
                    kc.ve(lambda e: e.reduce_max(RTMP[:, 0:1], RL[:, 0:NEXP],
                                                 AX.X),
                          r=("mRL",), w=("RTMP",), tag="rt_m1")
                    kc.ve(lambda e: e.tensor_single_scalar(
                        RL[:, NEXP:2 * NEXP], RL[:, 0:NEXP], RTMP[:, 0:1],
                        ALU.is_ge),
                          r=("mRL", "RTMP"), w=("mRL",), tag="rt_t1")
                    kc.ve(lambda e: e.scalar_tensor_tensor(
                        RL[:, NEXP:2 * NEXP], RL[:, NEXP:2 * NEXP], -1e30,
                        RL[:, 0:NEXP], ALU.mult, ALU.add),
                          r=("mRL",), w=("mRL",), tag="rt_pen")
                    kc.ve(lambda e: e.reduce_max(RTMP[:, 1:2],
                                                 RL[:, NEXP:2 * NEXP], AX.X),
                          r=("mRL",), w=("RTMP",), tag="rt_m2")
                    kc.ve(lambda e: e.tensor_single_scalar(
                        RL[:, NEXP:2 * NEXP], RL[:, 0:NEXP], RTMP[:, 1:2],
                        ALU.is_ge),
                          r=("mRL", "RTMP"), w=("mRL",), tag="rt_keep")
                    kc.ve(lambda e: e.tensor_scalar_mul(RTMP[:, 2:3],
                                                        RTMP[:, 0:1], -1.0),
                          r=("RTMP",), w=("RTMP",), tag="rt_nm1")
                    kc.act(RL[:, 2 * NEXP:3 * NEXP], RL[:, 0:NEXP], AF.Exp,
                           bias=RTMP[:, 2:3], r=("mRL", "RTMP"), w=("mRL",),
                           tag="rt_exp")
                    kc.ve(lambda e: e.tensor_mul(RL[:, 2 * NEXP:3 * NEXP],
                                                 RL[:, 2 * NEXP:3 * NEXP],
                                                 RL[:, NEXP:2 * NEXP]),
                          r=("mRL",), w=("mRL",), tag="rt_w")
                    kc.ve(lambda e: e.reduce_sum(RTMP[:, 3:4],
                                                 RL[:, 2 * NEXP:3 * NEXP], AX.X),
                          r=("mRL",), w=("RTMP",), tag="rt_den")
                    kc.ve(lambda e: e.reciprocal(RTMP[:, 4:5], RTMP[:, 3:4]),
                          r=("RTMP",), w=("RTMP",), tag="rt_rd")
                    kc.ve(lambda e, tt=tt: e.tensor_scalar_mul(
                        WTE[:, tt * NEXP:(tt + 1) * NEXP],
                        RL[:, 2 * NEXP:3 * NEXP], RTMP[:, 4:5]),
                          r=("mRL", "RTMP"), w=("WTE",), tag="rt_fin")
                    kc.ve(lambda e, tt=tt: e.tensor_copy(
                        RWB[:, :], WTE[:, tt * NEXP:(tt + 1) * NEXP]),
                          r=("WTE",), w=("RWB",), tag="rt_cw")
                    kc.transpose(pbT[0:NEXP, 0:P], RWB[:, :], CID[:, :],
                                 "pbT", r=("RWB", "CID"), tag="rt_tr")
                    kc.ve(lambda e, tt=tt: e.tensor_copy(
                        WTTa[0:NEXP, tt * P:(tt + 1) * P], pbT[0:NEXP, 0:P]),
                          r=("pbT",), w=("mHT",), tag="rt_trev")

            for c in range(NMCH):
                c0 = c * MCH
                for s2 in range(2):
                    t0, t1 = s2 * 4, (s2 + 1) * 4
                    kc.dma(SYNC, x1sb[:, t0:t1, 0:MCH],
                           x1v[:, t0:t1, c0:c0 + MCH],
                           r=(f"x1D:{c0 // CH}", f"x1D:{(c0 + MCH - 1) // CH}"),
                           w=("mHT",), tag="mi_ldx1")
                for ti in range(MCH // P):
                    tt = c * (MCH // P) + ti
                    for t in range(NT):
                        bk = 2 + (t % 2)
                        kc.transpose(pb[bk][:, 0:P].bitcast(F32)
                                     if False else pb[bk][:, 0:P],
                                     x1sb[:, t, ti * P:(ti + 1) * P],
                                     CIDF[:, :], PB[bk], r=("mHT", "CIDF"),
                                     tag="mi_tr")
                        kc.ve(lambda e, ti=ti, t=t, bk=bk: e.tensor_copy(
                            accv[:, ti, t * P:(t + 1) * P], pb[bk][:, 0:P]),
                              r=(PB[bk],), w=("mACC",), tag="mi_ev")
                    for dh in range(2):
                        kc.mm(WTTa[0:NEXP, tt * P:(tt + 1) * P],
                              B2S[:, dh * 512:(dh + 1) * 512],
                              pb[4][:, 0:512], PB[4], r=("mHT", "B2S"),
                              tag="mi_b2")
                        kc.ve(lambda e, ti=ti, dh=dh: e.tensor_add(
                            accv[:, ti, dh * 512:(dh + 1) * 512],
                            accv[:, ti, dh * 512:(dh + 1) * 512],
                            pb[4][:, 0:512]),
                              r=(PB[4],), w=("mACC",), tag="mi_b2ev")
                kc.dma(SYNC, outv[:, c * (MCH // P):(c + 1) * (MCH // P), :],
                       accv[:, 0:MCH // P, :], r=("mACC",), w=(f"out:{c}",),
                       tag="mi_st")

            for e_i in range(NEXP):
                for t in range(NT):
                    kc.dma(GPSIMD, WP1[:, t * DFF:(t + 1) * DFF],
                           wm1.ap()[e_i, t * P:(t + 1) * P, :],
                           w=(f"mW1:{t}",), tag="ldw1")
                for f in range(NFT):
                    kc.dma(GPSIMD, WP2[:, f * D:(f + 1) * D],
                           wm2.ap()[e_i, f * P:(f + 1) * P, :],
                           w=(f"mW2:{f}",), tag="ldw2")
                kc.dma(SYNC, B1C[:, :], wb1.ap()[e_i, :, :], w=("mB1C",),
                       tag="ldb1")
                w1regs = tuple(f"mW1:{t}" for t in range(NT))
                w2regs = tuple(f"mW2:{f}" for f in range(NFT))
                for c in range(NMCH):
                    c0 = c * MCH
                    for s2 in range(2):
                        t0, t1 = s2 * 4, (s2 + 1) * 4
                        kc.dma(SYNC, xfs[:, t0:t1, 0:MCH],
                               xfv[:, t0:t1, c0:c0 + MCH],
                               r=(f"xfD:{c0 // CH}",
                                  f"xfD:{(c0 + MCH - 1) // CH}"),
                               w=(XFS.r,), tag="me_xf")
                    kc.dma(SYNC, accv[:, 0:MCH // P, :],
                           outv[:, c * (MCH // P):(c + 1) * (MCH // P), :],
                           r=(f"out:{c}",), w=("mACC",), tag="me_lda")
                    for f in range(NFT):
                        bk = 2 + (f % 2)
                        kc.mmgroup(
                            [(WP1[:, t * DFF + f * P: t * DFF + (f + 1) * P],
                              xfs[:, t, 0:MCH]) for t in range(NT)],
                            pb[bk][:, 0:MCH], PB[bk],
                            r=w1regs + (XFS.r,), tag="me_h")
                        kc.act(ht[:, f, 0:MCH], pb[bk][:, 0:MCH], AF.Gelu,
                               bias=B1C[:, f:f + 1], r=(PB[bk], "mB1C"),
                               w=("mHT",), tag="me_gelu")
                    for ti in range(MCH // P):
                        tt = c * (MCH // P) + ti
                        for dh in range(2):
                            bk = 4 + dh
                            kc.mmgroup(
                                [(ht[:, f, ti * P:(ti + 1) * P],
                                  WP2[:, f * D + dh * 512: f * D + (dh + 1) * 512])
                                 for f in range(NFT)],
                                pb[bk][:, 0:512], PB[bk],
                                r=w2regs + ("mHT",), tag="me_y")
                            kc.ve(lambda e, ti=ti, dh=dh, bk=bk, tt=tt, e_i=e_i:
                                  e.scalar_tensor_tensor(
                                      accv[:, ti, dh * 512:(dh + 1) * 512],
                                      pb[bk][:, 0:512],
                                      WTE[:, tt * NEXP + e_i: tt * NEXP + e_i + 1],
                                      accv[:, ti, dh * 512:(dh + 1) * 512],
                                      ALU.mult, ALU.add),
                                  r=(PB[bk], "WTE", "mACC"), w=("mACC",),
                                  tag="me_acc")
                    kc.dma(SYNC, outv[:, c * (MCH // P):(c + 1) * (MCH // P), :],
                           accv[:, 0:MCH // P, :], r=("mACC",),
                           w=(f"out:{c}",), tag="me_st")
        sch.barrier("moe")

    for l in range(LAYERS):
        phase_tmA(l)
        phase_tmB(l)
        phase_cm(l)
    phase_x1()
    phase_moe()
    sch.barrier("fin")

    sch.plan()
    if cfg.get("verbose"):
        print("SCHED:", sch.stats(), flush=True)
        print("sbuf bytes remaining:", nc.sbuf_bytes_remaining, flush=True)

    semnames = (["eng_" + e for e in ENGINES]
                + [f"dma_{e}{i}" for e in ENGINES for i in range(Sched.NLANES)]
                + ["cc"])
    sems = {}
    with contextlib.ExitStack() as ses:
        for sn in semnames:
            sems[sn] = ses.enter_context(nc.semaphore(sn))
        with nc.Block() as block:
            sch.emit(block, sems)
    kc.es.close()
    return nc



def _col8(vec):
    return np.ascontiguousarray(vec.reshape(NT, P).T)


def _pack_shared(i):
    f = np.float32
    L = 2
    ppar = np.zeros((P, 200), f)
    names = {"ln1": i["L0_attn_ln1_w"], "ln2": i["L0_attn_ln2_w"],
             "xm": i["L0_attn_x_maa"], "wm": i["L0_attn_w_maa"],
             "km": i["L0_attn_k_maa"], "vm": i["L0_attn_v_maa"],
             "rm": i["L0_attn_r_maa"], "gm": i["L0_attn_g_maa"],
             "ckm": i["L0_attn_ck_maa"], "crm": i["L0_attn_cr_maa"],
             "td": np.asarray(i["L0_attn_time_decay"], f).reshape(L, D),
             "tf": np.asarray(i["L0_attn_time_first"], f).reshape(L, D)}
    order = ["ln1", "ln2", "xm", "wm", "km", "vm", "rm", "gm", "ckm", "crm",
             "td", "tf"]
    for l in range(L):
        for pi, nm in enumerate(order):
            ppar[:, l * 96 + pi * 8:l * 96 + pi * 8 + 8] = \
                _col8(np.asarray(names[nm], f)[l])
    ppar[:, 192:200] = _col8(np.asarray(i["L0_attn_ln_out_w"], f))

    prows = np.zeros((1, 8 + P), f)
    prows[0, 0:8] = np.asarray(i["L0_ffn_br"], f)
    prows[0, 8:8 + P] = 1.0
    pgn = np.zeros((1, 4 * D), f)
    gnw = np.asarray(i["L0_attn_gn_w"], f)
    gnb = np.asarray(i["L0_attn_gn_b"], f)
    for l in range(L):
        pgn[0, 2 * l * D:(2 * l) * D + D] = gnw[l]
        pgn[0, (2 * l + 1) * D:(2 * l + 1) * D + D] = gnb[l]

    tf2 = np.asarray(i["L0_attn_time_first"], f).reshape(L, D)
    u2p = np.zeros((P, 32), f)
    for l in range(L):
        for tp in range(NT):
            blk = tf2[l, tp * P:(tp + 1) * P]
            u2p[0:64, l * 16 + tp * 2 + 0] = blk[0:64]
            u2p[64:128, l * 16 + tp * 2 + 1] = blk[64:128]

    tm1 = np.asarray(i["L0_attn_tm_w1"], f).reshape(L, D, 5, 32)
    wtm1 = np.zeros((L, D, 384), f)
    for j in range(5):
        jp, jh = j // 2, j % 2
        wtm1[:, :, jp * 128 + jh * 64:jp * 128 + jh * 64 + 32] = tm1[:, :, j]
    tm2 = np.asarray(i["L0_attn_tm_w2"], f)
    wtm2 = np.zeros((L, P, 5 * D), f)
    for j in range(5):
        jh = j % 2
        wtm2[:, jh * 64:jh * 64 + 32, j * D:(j + 1) * D] = tm2[:, j]
    td1 = np.asarray(i["L0_attn_td_w1"], f)
    wtd1 = np.zeros((L, D, P), f)
    wtd1[:, :, 0:64] = td1
    td2 = np.asarray(i["L0_attn_td_w2"], f)
    wtd2 = np.zeros((L, P, D), f)
    wtd2[:, 0:64, :] = td2

    b1 = np.asarray(i["L0_ffn_b1"], f)
    wb1 = np.ascontiguousarray(b1.reshape(8, 32, 128).transpose(0, 2, 1))

    shared = {
        "ppar": ppar, "prows": prows, "pgn": pgn, "u2p": u2p,
        "wWr": np.asarray(i["L0_attn_Wr"], f),
        "wWk": np.asarray(i["L0_attn_Wk"], f),
        "wWv": np.asarray(i["L0_attn_Wv"], f),
        "wWo": np.asarray(i["L0_attn_Wo"], f),
        "wWg": np.asarray(i["L0_attn_Wg"], f),
        "wtm1": wtm1, "wtm2": wtm2, "wtd1": wtd1, "wtd2": wtd2,
        "wcWk": np.asarray(i["L0_attn_cWk"], f),
        "wcWv": np.asarray(i["L0_attn_cWv"], f),
        "wcWr": np.asarray(i["L0_attn_cWr"], f),
        "wmWr": np.asarray(i["L0_ffn_Wr"], f),
        "wm1": np.asarray(i["L0_ffn_W1"], f),
        "wb1": wb1,
        "wm2": np.asarray(i["L0_ffn_W2"], f),
        "wb2": np.asarray(i["L0_ffn_b2"], f),
    }
    return shared


def _consts(is_second_half):
    f = np.float32
    c = np.zeros((P, 264), f)
    c[:, 0:P] = np.eye(P, dtype=f)
    ts = np.arange(P)
    c[:, P:2 * P] = (ts[:, None] < ts[None, :]).astype(f)
    c[0:64, 2 * P + 0] = 1.0
    c[64:128, 2 * P + 1] = 1.0
    c[:, 2 * P + 2] = 1.0 if is_second_half else 0.0
    c[:, 2 * P + 3] = 0.0
    c[:, 2 * P + 4] = 1.0
    c[:, 2 * P + 5] = EPS
    c[:, 2 * P + 6] = -1.0
    return c


_NC_CACHE = {}


def _get_nc(cfg_key):
    if cfg_key not in _NC_CACHE:
        cfg = dict(NTOK=4096, CH=256, LAYERS=2, MCH=512, NEXP=8, DFF=4096,
                   verbose=False)
        _NC_CACHE[cfg_key] = build_program(cfg)
    return _NC_CACHE[cfg_key]


def kernel(**inputs):
    x = np.asarray(inputs["L0_attn_x"], np.float32)
    B, T, _ = x.shape
    half = T // 2
    shared = _pack_shared(inputs)
    nc = _get_nc("full")
    in_maps = []
    cA = _consts(False)
    cB = _consts(True)
    for c in range(8):
        b, h = c // 2, c % 2
        m = dict(shared)
        m["x0"] = np.ascontiguousarray(x[b, h * half:(h + 1) * half, :].T)
        m["consts"] = cB if h else cA
        in_maps.append(m)
    res = run_bass_kernel_spmd(nc, in_maps, core_ids=list(range(8)))
    outf = np.empty((B, T, D), np.float32)
    for c in range(8):
        b, h = c // 2, c % 2
        outf[b, h * half:(h + 1) * half, :] = res.results[c]["out"]
    return outf


# version 03
# speedup vs baseline: 1.2071x; 1.2071x over previous
import contextlib
import numpy as np
import ml_dtypes

import concourse.bass as bass
import concourse.mybir as mybir
from concourse.bass_utils import run_bass_kernel_spmd

F32 = mybir.dt.float32
BF16 = mybir.dt.bfloat16
AF = mybir.ActivationFunctionType
ALU = mybir.AluOpType
AX = mybir.AxisListType

TENSOR, VECTOR, SCALAR, GPSIMD, SYNC = "tensor", "vector", "scalar", "gpsimd", "sync"
ENGINES = (TENSOR, VECTOR, SCALAR, GPSIMD, SYNC)

EPS = 1e-5
P = 128
D = 1024
NT = D // P
HS = 64
NH = 16

BF = ml_dtypes.bfloat16



class Op:
    __slots__ = ("idx", "eng", "fn", "reads", "writes", "dma", "mm_out",
                 "marked", "waits", "tag", "coll", "inc_amt", "epos",
                 "semname", "semval", "waitlist", "markable", "lane")

    def __init__(self, idx, eng, fn, reads, writes, dma, mm_out, tag, coll,
                 markable=True):
        self.idx = idx
        self.eng = eng
        self.fn = fn
        self.reads = tuple(reads)
        self.writes = tuple(writes)
        self.dma = dma
        self.mm_out = mm_out
        self.marked = False
        self.waits = []
        self.tag = tag
        self.coll = coll
        self.inc_amt = 0
        self.epos = 0
        self.semname = None
        self.semval = 0
        self.waitlist = []
        self.markable = markable
        self.lane = -1

    def kind(self):
        return (self.dma, self.coll)


class Sched:
    def __init__(self):
        self.ops = []
        self.last_write = {}
        self.readers = {}
        self.pe_mm = {}
        self.drains = []
        self.estreams = {r: [] for r in ENGINES}
        self.floor = {}
        self.dma_counts = {}

    def _new(self, eng, fn, reads, writes, dma, mm_out, tag, coll, markable=True):
        o = Op(len(self.ops), eng, fn, reads, writes, dma, mm_out, tag, coll,
               markable)
        o.epos = len(self.estreams[eng])
        self.estreams[eng].append(o)
        self.ops.append(o)
        return o

    def op(self, eng, fn, r=(), w=(), dma=False, mm_out=None, tag="", coll=False):
        o = self._new(eng, fn, r, w, dma, mm_out, tag, coll)
        if dma or coll:
            o.marked = True
        if dma:
            c = self.dma_counts.get(eng, 0)
            self.dma_counts[eng] = c + 1
            o.lane = c % self.NLANES
        deps = []
        for reg in o.reads:
            p = self.last_write.get(reg)
            if p is not None:
                deps.append(p)
            self.readers.setdefault(reg, []).append(o)
        for reg in o.writes:
            p = self.last_write.get(reg)
            if p is not None:
                deps.append(p)
            for rd in self.readers.get(reg, ()):
                deps.append(rd)
            self.readers[reg] = []
            self.last_write[reg] = o
        if o.eng != TENSOR:
            for reg in o.reads:
                mmop = self.pe_mm.get(reg)
                if mmop is not None:
                    deps.append(self._drain_after(mmop))
        if mm_out is not None:
            self.pe_mm[mm_out] = o
        fl = self.floor.get(eng)
        if fl is not None:
            deps.append(fl)
        self._edges(o, deps)
        return o

    def _drain_after(self, mmop):
        for dop in self.drains:
            if dop.epos > mmop.epos:
                return dop
        dop = self._new(TENSOR, lambda e: e.drain(), (), (), False, None,
                        "drain", False)
        mmop.marked = True
        dop.marked = True
        self.drains.append(dop)
        return dop

    def _edges(self, o, deps):
        best = {}
        for p in deps:
            if p is o or p.idx >= o.idx:
                continue
            key = (p.eng, p.kind(), p.lane)
            cur = best.get(key)
            if cur is None or p.epos > cur.epos:
                best[key] = p
        for (engname, kindk, _lane), p in best.items():
            if kindk != (False, False) or (p.marked and not p.markable):
                o.waits.append(p)
                continue
            stream = self.estreams[engname]
            mop = None
            for cand in stream[p.epos:]:
                if cand.idx >= o.idx:
                    break
                if cand.marked and cand.markable and cand.kind() == kindk:
                    mop = cand
                    break
            if mop is None:
                assert p.markable, f"unmarkable producer {p.tag} for {o.tag}"
                p.marked = True
                mop = p
            o.waits.append(mop)

    def barrier(self, tag="bar"):
        lasts = []
        for e in ENGINES:
            ncomp = 0
            ndma = 0
            ncoll = 0
            for o in reversed(self.estreams[e]):
                if not o.markable:
                    continue
                if o.dma:
                    if ndma < self.NLANES:
                        lasts.append(o)
                        ndma += 1
                elif o.coll:
                    if ncoll < 1:
                        lasts.append(o)
                        ncoll += 1
                else:
                    if ncomp < 1:
                        lasts.append(o)
                        ncomp += 1
                if ncomp and ncoll and ndma >= self.NLANES:
                    break
        outs = []
        for engname in ENGINES:
            o = self._new(engname, lambda e: e.nop(), (), (), False, None,
                          tag, False, markable=False)
            fl = self.floor.get(engname)
            deps = [x for x in lasts if x.eng != engname]
            if fl is not None:
                deps.append(fl)
            self._edges(o, deps)
            o.marked = True
            self.floor[engname] = o
            outs.append(o)
        return outs

    NLANES = 8

    def plan(self):
        counts = {}
        lane_idx = {}
        gate = {}
        for engname in ENGINES:
            for o in self.estreams[engname]:
                if not o.marked:
                    continue
                if o.coll:
                    o.semname, o.inc_amt = "cc", 1
                elif o.dma:
                    o.semname = f"dma_{engname}{o.lane}"
                    o.inc_amt = 16
                else:
                    o.semname, o.inc_amt = "eng_" + engname, 1
                c = counts.get(o.semname, 0) + o.inc_amt
                counts[o.semname] = c
                o.semval = c
                if o.semval > o.inc_amt:
                    gate[o.idx] = (o.semname, o.semval - o.inc_amt)
        for o in self.ops:
            wl = {}
            g = gate.get(o.idx)
            if g is not None:
                wl[g[0]] = g[1]
            for mop in o.waits:
                assert mop.marked, (o.tag, mop.tag)
                k = mop.semname
                if wl.get(k, 0) < mop.semval:
                    wl[k] = mop.semval
            o.waitlist = sorted(wl.items())
        self.sem_totals = counts

    def emit(self, block, sems):
        def mk(engname):
            def run(eng_obj):
                for o in self.estreams[engname]:
                    for semname, val in o.waitlist:
                        eng_obj.wait_ge(sems[semname], val)
                    inst = o.fn(eng_obj)
                    if o.marked:
                        inst.then_inc(sems[o.semname], o.inc_amt)
            return run

        block.tensor(mk(TENSOR))
        block.vector(mk(VECTOR))
        block.scalar(mk(SCALAR))
        block.gpsimd(mk(GPSIMD))
        block.sync(mk(SYNC))

    def stats(self):
        per = {e: len(self.estreams[e]) for e in ENGINES}
        nmark = sum(1 for o in self.ops if o.marked)
        nwait = sum(len(o.waitlist) for o in self.ops)
        return (f"ops={len(self.ops)} per-engine={per} marked={nmark} "
                f"waits={nwait} sems={getattr(self, 'sem_totals', None)}")



class KCtx:

    def __init__(self, nc, sch, cfg):
        self.nc = nc
        self.sch = sch
        self.cfg = cfg
        self.es = contextlib.ExitStack()
        self.dram = {}
        self.pb = []
        self.pbT = None

    def dma(self, q, out_ap, in_ap, r=(), w=(), tag="dma", slow=False):
        return self.sch.op(q, lambda e, o=out_ap, i=in_ap, s=slow:
                           e.dma_start(out=o, in_=i,
                                       allow_slow_non_contiguous=s),
                           r=r, w=w, dma=True, tag=tag)

    def mm(self, lhsT, rhs, out_ap, bank, r=(), start=True, stop=True, tag="mm"):
        return self.sch.op(TENSOR,
                           lambda e, a=lhsT, b=rhs, o=out_ap, s=start, t=stop:
                           e.matmul(o, a, b, start=s, stop=t),
                           r=r, w=(bank,), mm_out=bank, tag=tag)

    def mmgroup(self, pairs, out_ap, bank, r=(), tag="mmg"):
        def fn(e, ps=tuple(pairs), o=out_ap):
            inst = None
            n = len(ps)
            for i, (a, b) in enumerate(ps):
                inst = e.matmul(o, a, b, start=(i == 0), stop=(i == n - 1))
            return inst
        return self.sch.op(TENSOR, fn, r=r, w=(bank,), mm_out=bank, tag=tag)

    def transpose(self, out_ap, in_ap, ident, bank, r=(), tag="tr"):
        return self.sch.op(TENSOR,
                           lambda e, o=out_ap, i=in_ap, d=ident: e.transpose(o, i, d),
                           r=r, w=(bank,), mm_out=bank, tag=tag)

    def act(self, out_ap, in_ap, func, r=(), w=(), bias=0.0, scale=1.0, tag="act"):
        return self.sch.op(SCALAR,
                           lambda e, o=out_ap, i=in_ap, f=func, b=bias, s=scale:
                           e.activation(o, i, f, bias=b, scale=s),
                           r=r, w=w, tag=tag)

    def ve(self, fn, r=(), w=(), tag="ve"):
        return self.sch.op(VECTOR, fn, r=r, w=w, tag=tag)

    def gp(self, fn, r=(), w=(), tag="gp"):
        return self.sch.op(GPSIMD, fn, r=r, w=w, tag=tag)

    def sbuf(self, es, name, shape, dtype):
        return es.enter_context(self.nc.sbuf_tensor(name, shape, dtype))


class EvacPipe:

    def __init__(self, banks, lag=None):
        self.banks = banks
        self.lag = (len(banks) - 1) if lag is None else lag
        self.i = 0
        self.pend = []

    def step(self, mmfn, evfn):
        bk = self.banks[self.i % len(self.banks)]
        self.i += 1
        mmfn(bk)
        self.pend.append((evfn, bk))
        if len(self.pend) > self.lag:
            f, b = self.pend.pop(0)
            f(b)

    def flush(self):
        for f, b in self.pend:
            f(b)
        self.pend = []


def dmajor_view(dram_handle, p=P):
    return dram_handle.ap().rearrange("(t p) n -> p t n", p=p)


def tokmajor_view(dram_handle, p=P):
    return dram_handle.ap().rearrange("(a p) d -> p a d", p=p)



def load_dmajor_chunk(kc, q, sbuf_t, dram_handle, dname, c0, c1, w, tag,
                      nt=NT, split=2):
    v = dmajor_view(dram_handle)
    ops = []
    step = nt // split
    for s in range(split):
        t0, t1 = s * step, (s + 1) * step
        ops.append(kc.dma(q, sbuf_t[:, t0:t1, c0:c1], v[:, t0:t1, c0:c1],
                          r=(dname,) if isinstance(dname, str) else dname,
                          w=(w + f"/{s}",) if split > 1 else (w,), tag=tag))
    return ops


def store_dmajor_chunk(kc, q, dram_handle, dname, sbuf_t, rname, c0, c1,
                       nt=NT, split=2):
    v = dmajor_view(dram_handle)
    step = nt // split
    for s in range(split):
        t0, t1 = s * step, (s + 1) * step
        kc.dma(q, v[:, t0:t1, c0:c1], sbuf_t[:, t0:t1, c0:c1],
               r=(rname,), w=(dname,), tag="st_" + dname)





def dmajor_view(dram_handle, p=P):
    return dram_handle.ap().rearrange("(t p) n -> p t n", p=p)


def tokmajor_view(dram_handle, p=P):
    return dram_handle.ap().rearrange("(a p) d -> p a d", p=p)


class Slot:

    def __init__(self, t, name):
        self.t = t
        self.name = name

    def __getitem__(self, k):
        return self.t[k]

    @property
    def r(self):
        return self.name


def bc_nt(col_ap, C):
    return col_ap.broadcast_to([P, NT, C])


class LN:

    def __init__(self, kc, ones_bf, xb, sq, srow, tf, bank_a, bank_b, C,
                 epsc=None):
        self.kc, self.ones = kc, ones_bf
        self.xb, self.sq, self.srow, self.tf = xb, sq, srow, tf
        self.ba, self.bb = bank_a, bank_b
        self.C = C
        self.epsc = epsc

    def run(self, x_ap, xreg, out_ap, outreg, wcol=None, out2=None, out2reg=None):
        kc, C = self.kc, self.C
        xbs, sqs, srow, tfs = self.xb, self.sq, self.srow, self.tf
        xb = v3(xbs, C)
        sq = v3(sqs, C)
        tf = v3(tfs, C)
        (ban, ba), (bbn, bb) = self.ba, self.bb
        mb = srow[:, 0:C]
        vb = srow[:, C:2 * C]
        tp = srow[:, 2 * C:3 * C]
        rb = srow[:, 3 * C:4 * C]
        kc.ve(lambda e: e.tensor_copy(xb[:, :, 0:C], x_ap), r=(xreg,),
              w=(xbs.r,), tag="ln_cast")
        kc.act(sq[:, :, 0:C], xb[:, :, 0:C], AF.Square, r=(xbs.r,), w=(sqs.r,),
               tag="ln_sq")
        kc.mmgroup([(self.ones, xb[:, t, 0:C]) for t in range(NT)],
                   ba[:, 0:C], ban, r=(xbs.r,), tag="ln_sum")
        kc.mmgroup([(self.ones, sq[:, t, 0:C]) for t in range(NT)],
                   bb[:, 0:C], bbn, r=(sqs.r,), tag="ln_sumsq")
        kc.ve(lambda e: e.tensor_scalar_mul(mb, ba[:, 0:C], 1.0 / D),
              r=(ban,), w=(srow.r,), tag="ln_mean")
        kc.ve(lambda e: e.scalar_tensor_tensor(tp, mb, -1.0, mb, ALU.mult,
                                               ALU.mult),
              r=(srow.r,), w=(srow.r,), tag="ln_msq")
        kc.ve(lambda e: e.scalar_tensor_tensor(vb, bb[:, 0:C], 1.0 / D, tp,
                                               ALU.mult, ALU.add),
              r=(bbn, srow.r), w=(srow.r,), tag="ln_var")
        kc.act(tp, vb, AF.Sqrt, bias=self.epsc, r=(srow.r, "CCOL"),
               w=(srow.r,), tag="ln_sdv")
        kc.ve(lambda e: e.reciprocal(rb, tp), r=(srow.r,), w=(srow.r,),
              tag="ln_rstd")
        mb_bc = mb.unsqueeze(1).broadcast_to([P, NT, C])
        rb_bc = rb.unsqueeze(1).broadcast_to([P, NT, C])
        kc.ve(lambda e: e.tensor_sub(tf[:, :, 0:C], x_ap, mb_bc),
              r=(xreg, srow.r), w=(tfs.r,), tag="ln_sub")
        if wcol is None:
            kc.ve(lambda e: e.tensor_mul(out_ap, tf[:, :, 0:C], rb_bc),
                  r=(tfs.r, srow.r), w=(outreg,), tag="ln_out")
        else:
            kc.ve(lambda e: e.tensor_mul(tf[:, :, 0:C], tf[:, :, 0:C], rb_bc),
                  r=(tfs.r, srow.r), w=(tfs.r,), tag="ln_nrm")
            kc.ve(lambda e: e.tensor_mul(out_ap, tf[:, :, 0:C],
                                         bc_nt(wcol, C)),
                  r=(tfs.r, "PPAR"), w=(outreg,), tag="ln_scale")


def v3(slot, C=256):
    return slot.t[:, :].rearrange("p (t c) -> p t c", c=C)



def build_program(cfg):
    NTOK = cfg["NTOK"]
    CH = cfg["CH"]
    NCH = NTOK // CH
    NSUB = NTOK // P
    LAYERS = cfg["LAYERS"]
    MCH = cfg["MCH"]
    NMCH = NTOK // MCH
    NEXP = cfg["NEXP"]
    DFF = cfg["DFF"]
    NFT = DFF // P
    dbg = "ExternalOutput" if cfg.get("debug") else "Internal"

    nc = bass.Bass("TRN2", target_bir_lowering=False)
    sch = Sched()
    kc = KCtx(nc, sch, cfg)

    ext_in = {}

    def din(name, shape, dt=F32):
        t = nc.dram_tensor(name, list(shape), dt, kind="ExternalInput")
        ext_in[name] = t
        return t

    x0 = din("x0", [D, NTOK])
    consts = din("consts", [P, 264])
    ppar = din("ppar", [P, 2 * 96 + 8])
    prows = din("prows", [1, 8 + P])
    pgn = din("pgn", [1, 4 * D])
    u2p = din("u2p", [P, 32])
    wWr = din("wWr", [2, D, D]); wWk = din("wWk", [2, D, D])
    wWv = din("wWv", [2, D, D]); wWo = din("wWo", [2, D, D])
    wWg = din("wWg", [2, D, D])
    wtm1 = din("wtm1", [2, D, 384])
    wtm2 = din("wtm2", [2, P, 5 * D])
    wtd1 = din("wtd1", [2, D, P])
    wtd2 = din("wtd2", [2, P, D])
    wcWk = din("wcWk", [2, D, 3 * D])
    wcWv = din("wcWv", [2, 3 * D, D])
    wcWr = din("wcWr", [2, D, D])
    wmWr = din("wmWr", [D, NEXP])
    wm1 = din("wm1", [NEXP, D, DFF])
    wb1 = din("wb1", [NEXP, P, NFT])
    wm2 = din("wm2", [NEXP, DFF, D])
    wb2 = din("wb2", [NEXP, D])

    out = nc.dram_tensor("out", [NTOK, D], F32, kind="ExternalOutput")

    hD = nc.dram_tensor("hD", [D, NTOK], F32, kind=dbg)
    x1D = nc.dram_tensor("x1D", [D, NTOK], F32, kind=dbg)
    xfD = nc.dram_tensor("xfD", [D, NTOK], BF16, kind=dbg)
    olocD = nc.dram_tensor("olocD", [NTOK, D], F32, kind="Internal")
    ggD = nc.dram_tensor("ggD", [NTOK, D], BF16, kind="Internal")
    qaD = nc.dram_tensor("qaD", [D, NTOK], BF16, kind="Internal")
    qbD = nc.dram_tensor("qbD", [D, NTOK], BF16, kind="Internal")
    cinU = nc.dram_tensor("cinU", [P, P], F32, kind="Internal")
    coutU = nc.dram_tensor("coutU", [2 * P, P], F32, kind="Internal")
    cinS = nc.dram_tensor("cinS", [P, 512], F32, kind="Internal")
    coutS = nc.dram_tensor("coutS", [2 * P, 512], F32, kind="Internal")

    es0 = kc.es
    pb = [kc.sbuf(es0, f"pb{i}", [P, 512], F32) if False else
          es0.enter_context(nc.psum_tensor(f"pb{i}", [P, 512], F32))
          for i in range(7)]
    pbT = es0.enter_context(nc.psum_tensor("pbT", [P, 1024], BF16))
    PB = [f"pb{i}" for i in range(7)]

    _uniq = [0]

    def S(name, shape, dt, es=es0):
        _uniq[0] += 1
        uname = f"{name}_{_uniq[0]}"
        t = es.enter_context(nc.sbuf_tensor(uname, shape, dt))
        return Slot(t, name)

    CID = S("CID", [P, P], BF16)
    CIDF = S("CIDF", [P, P], F32)
    CTRI = S("CTRI", [P, P], F32)
    CONE = S("CONE", [P, P], BF16)
    CCOL = S("CCOL", [P, 8], F32)
    PPAR = S("PPARs", [P, 200], F32)
    PROW = S("PROWs", [1, 8 + P], F32)
    U2S = S("U2S", [P, 32], BF16)
    SST = S("SST", [P, 512], F32)
    SBFS = S("SBFS", [P, 512], BF16)
    ACUM = S("ACUM", [P, 8], F32)
    ACUMC = S("ACUMC", [P, 16], F32)
    ACTAB = S("ACTAB", [P, 8 * NSUB], F32)
    BPREV = S("BPREV", [P, 8], F32)
    CCU = S("CCU", [P, 8], F32)
    CCG2 = S("CCG2", [P, 512], F32)
    GST = S("GST", [P, 80], F32)
    RTMP = S("RTMP", [P, 16], F32)
    WTE = S("WTE", [P, (NTOK // P) * 8], F32)
    UB2 = S("UB2", [P, 16], BF16)
    HLAST = S("HLAST", [P, 8], F32)

    PAR_IDX = ["ln1", "ln2", "xm", "wm", "km", "vm", "rm", "gm", "ckm", "crm",
               "td", "tf"]

    def par(l, name):
        i = PAR_IDX.index(name)
        base = l * 96 + i * 8
        return PPAR[:, base:base + 8]

    def par_bc(l, name, C):
        return par(l, name).unsqueeze(2).broadcast_to([P, NT, C])

    def pcol(l, name, t):
        i = PAR_IDX.index(name)
        return PPAR[:, l * 96 + i * 8 + t: l * 96 + i * 8 + t + 1]

    LNOUTW = PPAR[:, 192:200]
    maskA = CCOL[:, 0:1]
    maskB = CCOL[:, 1:2]
    halfc = CCOL[:, 2:3]
    zeroc = CCOL[:, 3:4]
    epsc = CCOL[:, 5:6]
    neg1c = CCOL[:, 6:7]

    kc.dma(GPSIMD, CID[:, :], consts[:, 0:P], w=("CID",), tag="ld_cid")
    kc.dma(SYNC, CIDF[:, :], consts[:, 0:P], w=("CIDF",), tag="ld_cidf")
    kc.dma(SYNC, CTRI[:, :], consts[:, P:2 * P], w=("CTRI",), tag="ld_tri")
    kc.ve(lambda e: e.memset(CONE[:, :], 1.0), w=("CONE",), tag="ones")
    kc.dma(SYNC, CCOL[:, :], consts[:, 2 * P:2 * P + 8], w=("CCOL",), tag="ld_ccol")
    kc.dma(SYNC, PPAR[:, :], ppar[:, :], w=("PPAR",), tag="ld_par")
    kc.dma(SYNC, PROW[:, :], prows[:, :], w=("PROW",), tag="ld_prow")
    kc.dma(GPSIMD, U2S[:, :], u2p.ap(), w=("U2S",), tag="ld_u2")
    kc.ve(lambda e: e.memset(SST[:, :], 0.0), w=("SST",), tag="s0")
    kc.ve(lambda e: e.memset(SBFS[:, :], 0.0), w=("SBFS",), tag="sb0")
    kc.ve(lambda e: e.memset(ACUM[:, :], 1.0), w=("ACUM",), tag="ac0")
    kc.ve(lambda e: e.memset(CCG2[:, 0:P], 0.0), w=("CCG2",), tag="zsrow")
    kc.dma(GPSIMD, cinU.ap()[:, 8:P], CCG2[:, 8:P], r=("CCG2",), w=("cinU",),
           tag="zcinu")

    hDv = dmajor_view(hD)
    x0v = dmajor_view(x0)
    x1v = dmajor_view(x1D)
    xfv = dmajor_view(xfD)
    qav = dmajor_view(qaD)
    qbv = dmajor_view(qbD)
    olv = tokmajor_view(olocD)
    ggv = tokmajor_view(ggD)
    outv = tokmajor_view(out)

    def ld_dmaj(q, slot, C, view, dreg, c0, tag, f32=True):
        vv = v3(slot, C)
        for s2 in range(2):
            t0, t1 = s2 * 4, (s2 + 1) * 4
            kc.dma(q, vv[:, t0:t1, 0:C], view[:, t0:t1, c0:c0 + C],
                   r=(dreg,), w=(slot.r,), tag=tag)

    def st_dmaj(q, view, dreg, slot, C, c0, tag):
        vv = v3(slot, C)
        for s2 in range(2):
            t0, t1 = s2 * 4, (s2 + 1) * 4
            kc.dma(q, view[:, t0:t1, c0:c0 + C], vv[:, t0:t1, 0:C],
                   r=(slot.r,), w=(dreg,), tag=tag)

    def boundary_u(l, lnw_name, ph, UBB, tag):
        kc.ve(lambda e: e.tensor_copy(UBB[:, 0:8], HLAST[:, 0:8]),
              r=("HLAST",), w=(UBB.r,), tag=tag + "_ld")
        ubb = UBB[:, 0:8]
        ubbf = UBB[:, 8:16]
        kc.ve(lambda e: e.tensor_copy(UB2[:, 0:8], ubb), r=(UBB.r,),
              w=(UB2.r,), tag=tag + "_cast")
        kc.act(UB2[:, 8:16], UB2[:, 0:8], AF.Square, r=(UB2.r,), w=(UB2.r,),
               tag=tag + "_sq")
        kc.mmgroup([(CONE[:, :], UB2[:, t:t + 1]) for t in range(NT)],
                   pb[0][:, 0:1], PB[0], r=(UB2.r, "CONE"), tag=tag + "_sum")
        kc.mmgroup([(CONE[:, :], UB2[:, 8 + t:9 + t]) for t in range(NT)],
                   pb[1][:, 0:1], PB[1], r=(UB2.r, "CONE"), tag=tag + "_sumsq")
        kc.ve(lambda e: e.tensor_scalar_mul(RTMP[:, 0:1], pb[0][:, 0:1], 1.0 / D),
              r=(PB[0],), w=(RTMP.r,), tag=tag + "_m")
        kc.ve(lambda e: e.scalar_tensor_tensor(RTMP[:, 1:2], RTMP[:, 0:1], -1.0,
                                               RTMP[:, 0:1], ALU.mult, ALU.mult),
              r=(RTMP.r,), w=(RTMP.r,), tag=tag + "_msq")
        kc.ve(lambda e: e.scalar_tensor_tensor(RTMP[:, 2:3], pb[1][:, 0:1],
                                               1.0 / D, RTMP[:, 1:2], ALU.mult,
                                               ALU.add),
              r=(PB[1], RTMP.r), w=(RTMP.r,), tag=tag + "_var")
        kc.act(RTMP[:, 3:4], RTMP[:, 2:3], AF.Sqrt, bias=epsc,
               r=(RTMP.r, "CCOL"), w=(RTMP.r,), tag=tag + "_sdv")
        kc.ve(lambda e: e.reciprocal(RTMP[:, 4:5], RTMP[:, 3:4]),
              r=(RTMP.r,), w=(RTMP.r,), tag=tag + "_rstd")
        kc.ve(lambda e: e.tensor_scalar_sub(UBB[:, 0:8], ubb, RTMP[:, 0:1]),
              r=(UBB.r, RTMP.r), w=(UBB.r,), tag=tag + "_sub")
        kc.ve(lambda e: e.tensor_scalar_mul(UBB[:, 0:8], ubb, RTMP[:, 4:5]),
              r=(UBB.r, RTMP.r), w=(UBB.r,), tag=tag + "_nrm")
        if lnw_name is not None:
            kc.ve(lambda e: e.tensor_mul(UBB[:, 0:8], ubb, par(l, lnw_name)),
                  r=(UBB.r, "PPAR"), w=(UBB.r,), tag=tag + "_w")
        kc.dma(SYNC, cinU.ap()[:, 0:8], UBB[:, 0:8], r=(UBB.r,), w=("cinU",),
               tag=tag + "_stg")
        kc.sch.op(GPSIMD,
                  lambda e: e.collective_compute(
                      "AllGather", ALU.bypass,
                      replica_groups=[[0, 1], [2, 3], [4, 5], [6, 7]],
                      ins=[cinU.ap()], outs=[coutU.ap()]),
                  r=("cinU",), w=("coutU",), coll=True, tag=tag + "_cc")
        kc.dma(SYNC, CCU[:, 0:8], coutU.ap()[0:P, 0:8], r=("coutU",),
               w=("CCU",), tag=tag + "_g")
        kc.ve(lambda e: e.tensor_scalar_mul(BPREV[:, 0:8], CCU[:, 0:8], halfc),
              r=("CCU", "CCOL"), w=("BPREV",), tag=tag + "_bp")

    with contextlib.ExitStack() as pes:
        FXH = S("p0FXH", [P, NT * CH], F32, pes)
        LXB = S("p0LXB", [P, NT * CH], BF16, pes)
        LSQ = S("p0LSQ", [P, NT * CH], BF16, pes)
        LTF = S("p0LTF", [P, NT * CH], F32, pes)
        SROW = S("SROW", [P, 4 * CH], F32, pes)
        ln = LN(kc, CONE[:, :], LXB, LSQ, SROW, LTF,
                (PB[0], pb[0]), (PB[1], pb[1]), CH, epsc)
        for c in range(NCH):
            c0 = c * CH
            ld_dmaj(SYNC, FXH, CH, x0v, "x0", c0, "p0_ld")
            ln.run(v3(FXH, CH)[:, :, 0:CH], FXH.r,
                   v3(FXH, CH)[:, :, 0:CH], FXH.r)
            if c == NCH - 1:
                kc.ve(lambda e: e.tensor_copy(
                    HLAST[:, 0:8], v3(FXH, CH)[:, :, CH - 1:CH].squeeze(2)),
                      r=(FXH.r,), w=("HLAST",), tag="hlast")
            st_dmaj(SYNC, hDv, f"hD:{c}", FXH, CH, c0, "p0_st")
    sch.barrier("p0")

    def phase_tmA(l):
        with contextlib.ExitStack() as pes:
            WPA = S("WPA", [P, 4 * NT * D], BF16, pes)
            TM1 = S("TM1", [P, NT * 384], BF16, pes)
            TM2 = S("TM2", [P, 5 * D], BF16, pes)
            TD1 = S("TD1", [P, NT * P], BF16, pes)
            TD2 = S("TD2", [P, D], BF16, pes)
            FXH = S("FXH", [P, NT * CH], F32, pes)
            LOGW = S("LOGW", [P, NT * CH], F32, pes)
            PLAX = S("PLAX", [P, NT * CH], F32, pes)
            LTF = S("LTF", [P, NT * CH], F32, pes)
            LXB = S("LXB", [P, NT * CH], BF16, pes)
            LSQ = S("LSQ", [P, NT * CH], BF16, pes)
            U = S("U", [P, NT * CH], BF16, pes)
            SX = S("SX", [P, NT * CH], BF16, pes)
            EEXC = S("EEXC", [P, NT * CH], BF16, pes)
            EINC = S("EINC", [P, NT * CH], BF16, pes)
            KT = S("KT", [P, NT * CH], BF16, pes)
            RT = S("RT", [P, NT * CH], BF16, pes)
            QA = S("QA", [P, NT * CH], BF16, pes)
            QB = S("QB", [P, NT * CH], BF16, pes)
            RK = S("RK", [P, NT * CH], BF16, pes)
            KH = S("KH", [P, NT * CH], BF16, pes)
            PX = S("PXs", [P, NT * CH], BF16, pes)
            TA = S("TAs", [P, NT * CH], BF16, pes)
            TMPB = S("TMPB", [P, 3 * CH], BF16, pes)
            VT = S("VTs", [P, 2 * D], BF16, pes)
            GT = S("GTs", [P, 2 * D], BF16, pes)
            UBB = S("UBB", [P, 16], F32, pes)

            def wslab(i, t):
                return WPA[:, (i * NT + t) * D:(i * NT + t) * D + D]

            for i, wt in enumerate((wWr, wWk, wWv, wWg)):
                for t in range(NT):
                    kc.dma(GPSIMD, wslab(i, t), wt.ap()[l, t * P:(t + 1) * P, :],
                           w=(f"WPA{i}",), tag=f"ldw{i}")
            for t in range(NT):
                kc.dma(GPSIMD, TM1[:, t * 384:(t + 1) * 384],
                       wtm1.ap()[l, t * P:(t + 1) * P, :], w=("TM1",), tag="ldtm1")
                kc.dma(GPSIMD, TD1[:, t * P:(t + 1) * P],
                       wtd1.ap()[l, t * P:(t + 1) * P, :], w=("TD1",), tag="ldtd1")
            kc.dma(GPSIMD, TM2[:, :], wtm2.ap()[l, :, :], w=("TM2",), tag="ldtm2")
            kc.dma(GPSIMD, TD2[:, :], wtd2.ap()[l, :, :], w=("TD2",), tag="ldtd2")

            boundary_u(l, "ln1", None, UBB, f"ubA{l}")
            kc.ve(lambda e: e.memset(SST[:, :], 0.0), w=("SST",), tag="s0")
            kc.ve(lambda e: e.memset(SBFS[:, :], 0.0), w=("SBFS",), tag="sb0")
            kc.ve(lambda e: e.memset(ACUM[:, :], 1.0), w=("ACUM",), tag="ac0")

            SROW = S("SROW", [P, 4 * CH], F32, pes)
            ln = LN(kc, CONE[:, :], LXB, LSQ, SROW, LTF,
                    (PB[0], pb[0]), (PB[1], pb[1]), CH, epsc)
            fxh = v3(FXH, CH)
            uu = v3(U, CH)
            sx = v3(SX, CH)
            logw = v3(LOGW, CH)
            plax = v3(PLAX, CH)
            eexc = v3(EEXC, CH)
            einc = v3(EINC, CH)
            ktv = v3(KT, CH)
            rtv = v3(RT, CH)
            qa = v3(QA, CH)
            qb = v3(QB, CH)
            rk = v3(RK, CH)
            kh = v3(KH, CH)
            pxv = v3(PX, CH)
            tav = v3(TA, CH)
            ltf = v3(LTF, CH)
            vtv = VT.t[:, :].rearrange("p (s d) -> p s d", d=D)
            gtv = GT.t[:, :].rearrange("p (s d) -> p s d", d=D)

            for c in range(NCH):
                c0 = c * CH
                ld_dmaj(SYNC, FXH, CH, hDv, f"hD:{c}", c0, "tmA_ldh")
                ln.run(fxh[:, :, 0:CH], FXH.r, uu[:, :, 0:CH], U.r,
                       wcol=par(l, "ln1").unsqueeze(2))
                kc.ve(lambda e: e.tensor_sub(sx[:, :, 1:CH], uu[:, :, 0:CH - 1],
                                             uu[:, :, 1:CH]),
                      r=(U.r,), w=(SX.r,), tag="sx")
                kc.ve(lambda e: e.tensor_sub(sx[:, :, 0:1],
                                             BPREV[:, 0:8].unsqueeze(2),
                                             uu[:, :, 0:1]),
                      r=(U.r, "BPREV"), w=(SX.r,), tag="sx0")
                kc.ve(lambda e: e.tensor_copy(BPREV[:, 0:8],
                                              uu[:, :, CH - 1:CH].squeeze(2)),
                      r=(U.r,), w=("BPREV",), tag="bpup")
                kc.ve(lambda e: e.tensor_mul(pxv[:, :, 0:CH], sx[:, :, 0:CH],
                                             par_bc(l, "xm", CH)),
                      r=(SX.r, "PPAR"), w=(PX.r,), tag="xxx1")
                kc.ve(lambda e: e.tensor_add(pxv[:, :, 0:CH], pxv[:, :, 0:CH],
                                             uu[:, :, 0:CH]),
                      r=(PX.r, U.r), w=(PX.r,), tag="xxx2")
                pipe = EvacPipe([2, 3, 4, 5])
                for jp in range(3):
                    def mmf(bk, jp=jp):
                        kc.mmgroup(
                            [(TM1[:, t * 384 + jp * P: t * 384 + (jp + 1) * P],
                              pxv[:, t, 0:CH]) for t in range(NT)],
                            pb[bk][:, 0:CH], PB[bk], r=("TM1", PX.r),
                            tag="tmp_mm")

                    def evf(bk, jp=jp):
                        kc.act(TMPB[:, jp * CH:(jp + 1) * CH], pb[bk][:, 0:CH],
                               AF.Tanh, r=(PB[bk],), w=("TMPB",), tag="tmp_tanh")
                    pipe.step(mmf, evf)
                pipe.flush()
                MAAN = ["wm", "km", "vm", "rm", "gm"]
                for Pi in range(5):
                    for j in range(NT):
                        def mmf(bk, j=j, Pi=Pi):
                            kc.mm(TM2[:, Pi * D + j * P: Pi * D + (j + 1) * P],
                                  TMPB[:, (Pi // 2) * CH:(Pi // 2 + 1) * CH],
                                  pb[bk][:, 0:CH], PB[bk], r=("TM2", "TMPB"),
                                  tag="mix_mm")

                        def evf(bk, j=j, Pi=Pi):
                            kc.ve(lambda e, j=j, bk=bk, Pi=Pi:
                                  e.scalar_tensor_tensor(
                                      ltf[:, j, 0:CH], pb[bk][:, 0:CH],
                                      pcol(l, MAAN[Pi], j), sx[:, j, 0:CH],
                                      ALU.add, ALU.mult),
                                  r=(PB[bk], "PPAR", SX.r), w=(LTF.r,),
                                  tag="pxa")
                            kc.ve(lambda e, j=j: e.tensor_add(
                                pxv[:, j, 0:CH], ltf[:, j, 0:CH],
                                uu[:, j, 0:CH]),
                                  r=(LTF.r, U.r), w=(PX.r,), tag="pxb")
                        pipe.step(mmf, evf)
                    pipe.flush()
                    if Pi == 0:
                        def mmf(bk):
                            kc.mmgroup([(TD1[:, t * P:(t + 1) * P],
                                         pxv[:, t, 0:CH]) for t in range(NT)],
                                       pb[bk][:, 0:CH], PB[bk],
                                       r=("TD1", PX.r), tag="td1_mm")

                        def evf(bk):
                            kc.act(TA[:, 0:CH], pb[bk][:, 0:CH], AF.Tanh,
                                   r=(PB[bk],), w=(TA.r,), tag="td1_tanh")
                        pipe.step(mmf, evf)
                        pipe.flush()
                        for j in range(NT):
                            def mmf(bk, j=j):
                                kc.mm(TD2[:, j * P:(j + 1) * P], TA[:, 0:CH],
                                      pb[bk][:, 0:CH], PB[bk],
                                      r=("TD2", TA.r), tag="td2_mm")

                            def evf(bk, j=j):
                                kc.act(logw[:, j, 0:CH], pb[bk][:, 0:CH],
                                       AF.Exp, bias=pcol(l, "td", j),
                                       r=(PB[bk], "PPAR"), w=(LOGW.r,),
                                       tag="logw")
                            pipe.step(mmf, evf)
                    elif Pi == 1:
                        for j in range(NT):
                            def mmf(bk, j=j):
                                kc.mmgroup(
                                    [(wslab(1, t)[:, j * P:(j + 1) * P],
                                      pxv[:, t, 0:CH]) for t in range(NT)],
                                    pb[bk][:, 0:CH], PB[bk],
                                    r=("WPA1", PX.r), tag="k_mm")

                            def evf(bk, j=j):
                                kc.sch.op(SCALAR,
                                          lambda e, j=j, bk=bk: e.copy(
                                              ktv[:, j, 0:CH], pb[bk][:, 0:CH]),
                                          r=(PB[bk],), w=(KT.r,), tag="k_ev")
                            pipe.step(mmf, evf)
                    elif Pi == 2:
                        for st in range(2):
                            for nh in range(2):
                                def mmf(bk, st=st, nh=nh):
                                    kc.mmgroup(
                                        [(pxv[:, t, st * P:(st + 1) * P],
                                          wslab(2, t)[:, nh * 512:(nh + 1) * 512])
                                         for t in range(NT)],
                                        pb[bk][:, 0:512], PB[bk],
                                        r=("WPA2", PX.r), tag="v_mm")

                                def evf(bk, st=st, nh=nh):
                                    kc.sch.op(
                                        SCALAR,
                                        lambda e, st=st, nh=nh, bk=bk: e.copy(
                                            vtv[:, st, nh * 512:(nh + 1) * 512],
                                            pb[bk][:, 0:512]),
                                        r=(PB[bk],), w=(VT.r,), tag="v_ev")
                                pipe.step(mmf, evf)
                    elif Pi == 3:
                        for j in range(NT):
                            def mmf(bk, j=j):
                                kc.mmgroup(
                                    [(wslab(0, t)[:, j * P:(j + 1) * P],
                                      pxv[:, t, 0:CH]) for t in range(NT)],
                                    pb[bk][:, 0:CH], PB[bk],
                                    r=("WPA0", PX.r), tag="r_mm")

                            def evf(bk, j=j):
                                kc.act(rtv[:, j, 0:CH], pb[bk][:, 0:CH],
                                       AF.Sigmoid, r=(PB[bk],), w=(RT.r,),
                                       tag="r_ev")
                            pipe.step(mmf, evf)
                    elif Pi == 4:
                        for st in range(2):
                            for nh in range(2):
                                def mmf(bk, st=st, nh=nh):
                                    kc.mmgroup(
                                        [(pxv[:, t, st * P:(st + 1) * P],
                                          wslab(3, t)[:, nh * 512:(nh + 1) * 512])
                                         for t in range(NT)],
                                        pb[bk][:, 0:512], PB[bk],
                                        r=("WPA3", PX.r), tag="g_mm")

                                def evf(bk, st=st, nh=nh):
                                    kc.act(gtv[:, st, nh * 512:(nh + 1) * 512],
                                           pb[bk][:, 0:512], AF.Silu,
                                           r=(PB[bk],), w=(GT.r,), tag="g_ev")
                                pipe.step(mmf, evf)
                pipe.flush()
                for st in range(2):
                    kc.dma(SYNC, ggv[:, c * 2 + st, :], gtv[:, st, :],
                           r=(GT.r,), w=(f"ggD:{c}",), tag="g_sp")
                for s in range(2):
                    w0 = s * P
                    kc.ve(lambda e, w0=w0: e.memset(plax[:, :, w0:w0 + 1], 0.0),
                          w=(PLAX.r,), tag="plx0")
                    for tp in range(NT):
                        kc.ve(lambda e, w0=w0, tp=tp: e.tensor_tensor_scan(
                            plax[:, tp, w0 + 1:w0 + P],
                            logw[:, tp, w0:w0 + P - 1],
                            zeroc.broadcast_to([P, P - 1]),
                            0.0, ALU.add, ALU.add),
                              r=(LOGW.r, "CCOL"), w=(PLAX.r,), tag="scan")
                kc.ve(lambda e: e.tensor_add(logw[:, :, 0:CH], plax[:, :, 0:CH],
                                             logw[:, :, 0:CH]),
                      r=(PLAX.r, LOGW.r), w=(LOGW.r,), tag="plainc")
                for s in range(2):
                    w1 = s * P + P - 1
                    kc.act(ACUMC[:, s * 8:(s + 1) * 8].unsqueeze(2),
                           logw[:, :, w1:w1 + 1], AF.Exp, scale=neg1c,
                           r=(LOGW.r, "CCOL"), w=("ACUMC",), tag="aC")
                kc.act(eexc[:, :, 0:CH], plax[:, :, 0:CH], AF.Exp, scale=neg1c,
                       r=(PLAX.r, "CCOL"), w=(EEXC.r,), tag="eexc")
                kc.act(einc[:, :, 0:CH], logw[:, :, 0:CH], AF.Exp,
                       r=(LOGW.r,), w=(EINC.r,), tag="einc")
                kc.ve(lambda e: e.scalar_tensor_tensor(
                    v3(LXB, CH)[:, :, 0:CH], einc[:, :, 0:CH], maskA,
                    ktv[:, :, 0:CH], ALU.mult, ALU.mult),
                      r=(EINC.r, "CCOL", KT.r), w=(LXB.r,), tag="ka")
                kc.ve(lambda e: e.scalar_tensor_tensor(
                    v3(LSQ, CH)[:, :, 0:CH], einc[:, :, 0:CH], maskB,
                    ktv[:, :, 0:CH], ALU.mult, ALU.mult),
                      r=(EINC.r, "CCOL", KT.r), w=(LSQ.r,), tag="kb")
                kc.ve(lambda e: e.scalar_tensor_tensor(
                    qa[:, :, 0:CH], eexc[:, :, 0:CH], maskA,
                    rtv[:, :, 0:CH], ALU.mult, ALU.mult),
                      r=(EEXC.r, "CCOL", RT.r), w=(QA.r,), tag="qa")
                kc.ve(lambda e: e.scalar_tensor_tensor(
                    qb[:, :, 0:CH], eexc[:, :, 0:CH], maskB,
                    rtv[:, :, 0:CH], ALU.mult, ALU.mult),
                      r=(EEXC.r, "CCOL", RT.r), w=(QB.r,), tag="qb")
                kc.ve(lambda e: e.tensor_mul(rk[:, :, 0:CH], rtv[:, :, 0:CH],
                                             ktv[:, :, 0:CH]),
                      r=(RT.r, KT.r), w=(RK.r,), tag="rk")
                for s in range(2):
                    w0 = s * P
                    kc.ve(lambda e, s=s, w0=w0: e.tensor_mul(
                        tav[:, :, w0:w0 + P], einc[:, :, w0:w0 + P],
                        ACUMC[:, s * 8:(s + 1) * 8].unsqueeze(2)
                        .broadcast_to([P, NT, P])),
                          r=(EINC.r, "ACUMC"), w=(TA.r,), tag="est")
                    kc.ve(lambda e, w0=w0: e.tensor_mul(
                        kh[:, :, w0:w0 + P], ktv[:, :, w0:w0 + P],
                        tav[:, :, w0:w0 + P]),
                          r=(KT.r, TA.r), w=(KH.r,), tag="khat")
                for s2 in range(2):
                    t0, t1 = s2 * 4, (s2 + 1) * 4
                    kc.dma(SYNC, qav[:, t0:t1, c0:c0 + CH], qa[:, t0:t1, 0:CH],
                           r=(QA.r,), w=(f"qaD:{c}",), tag="qa_sp")
                    kc.dma(SYNC, qbv[:, t0:t1, c0:c0 + CH], qb[:, t0:t1, 0:CH],
                           r=(QB.r,), w=(f"qbD:{c}",), tag="qb_sp")
                for s in range(2):
                    sub = c * 2 + s
                    kc.ve(lambda e, sub=sub: e.tensor_copy(
                        ACTAB[:, sub * 8:(sub + 1) * 8], ACUM[:, 0:8]),
                          r=("ACUM",), w=("ACTAB",), tag="acst")
                    kc.ve(lambda e, s=s: e.tensor_mul(
                        ACUM[:, 0:8], ACUM[:, 0:8], ACUMC[:, s * 8:(s + 1) * 8]),
                          r=("ACUM", "ACUMC"), w=("ACUM",), tag="acup")
                for s in range(2):
                    w0 = s * P
                    for tp in range(NT):
                        kc.mm(v3(LXB, CH)[:, tp, w0:w0 + P],
                              qa[:, tp, w0:w0 + P],
                              pb[0][:, 0:P], PB[0], r=(LXB.r, QA.r), tag="ata")
                        kc.mm(v3(LSQ, CH)[:, tp, w0:w0 + P],
                              qb[:, tp, w0:w0 + P],
                              pb[1][:, 0:P], PB[1], r=(LSQ.r, QB.r), tag="atb")
                        kc.mm(rk[:, tp, w0:w0 + P],
                              U2S[:, l * 16 + tp * 2: l * 16 + tp * 2 + 2],
                              pb[6][:, s * 64:s * 64 + 2], PB[6],
                              r=(RK.r, "U2S"), tag="c2")
                        kc.ve(lambda e, tp=tp: e.tensor_mul(
                            TA[:, 0:P], pb[0][:, 0:P], CTRI[:, :]),
                              r=(PB[0], "CTRI"), w=(TA.r,), tag="mstra")
                        kc.ve(lambda e, tp=tp: e.tensor_mul(
                            TA[:, P:2 * P], pb[1][:, 0:P], CTRI[:, :]),
                              r=(PB[1], "CTRI"), w=(TA.r,), tag="mstrb")
                        for h in range(2):
                            kc.ve(lambda e, s=s, tp=tp, h=h: e.tensor_scalar(
                                PX[:, h * 64:(h + 1) * 64],
                                vtv[:, s, tp * P + h * 64: tp * P + (h + 1) * 64],
                                pb[6][:, s * 64 + h: s * 64 + h + 1], None,
                                ALU.mult),
                                  r=(VT.r, PB[6]), w=(PX.r,), tag="vc")
                        for h, (qv, msl) in enumerate(((qa, TA[:, 0:P]),
                                                      (qb, TA[:, P:2 * P]))):
                            kc.mmgroup(
                                [(msl, vtv[:, s, tp * P + h * 64:
                                           tp * P + (h + 1) * 64]),
                                 (qv[:, tp, w0:w0 + P],
                                  SBFS[:, tp * 64:(tp + 1) * 64]),
                                 (CID[:, :], PX[:, h * 64:(h + 1) * 64])],
                                pb[2][:, h * 64:(h + 1) * 64], PB[2],
                                r=(TA.r, VT.r, QA.r if h == 0 else QB.r,
                                   "SBFS", "CID", PX.r),
                                tag="o_mm")
                        kc.transpose(pbT[:, 0:P], kh[:, tp, w0:w0 + P],
                                     CID[:, :], "pbT", r=(KH.r, "CID"),
                                     tag="khT")
                        kc.ve(lambda e: e.tensor_copy(PX[:, P:2 * P],
                                                      pbT[:, 0:P]),
                              r=("pbT",), w=(PX.r,), tag="khT_ev")
                        kc.mm(PX[:, P:2 * P],
                              vtv[:, s, tp * P:(tp + 1) * P],
                              pb[3][:, 0:P], PB[3], r=(PX.r, VT.r), tag="ds")
                        kc.ve(lambda e, tp=tp: e.tensor_copy(
                            v3(LTF, CH)[:, 0, tp * P:(tp + 1) * P]
                            if False else LTF[:, tp * P:(tp + 1) * P],
                            pb[2][:, 0:P]),
                              r=(PB[2],), w=(LTF.r,), tag="o_ev")
                        sv = SST.t[:, :].rearrange("p (t v) -> p t v", v=64)
                        for h in range(2):
                            h0 = h * 64
                            kc.ve(lambda e, tp=tp, s=s, h0=h0: e.scalar_tensor_tensor(
                                sv[h0:h0 + 64, tp, :],
                                sv[h0:h0 + 64, tp, :],
                                ACUMC[h0:h0 + 64, s * 8 + tp: s * 8 + tp + 1],
                                pb[3][h0:h0 + 64, h0:h0 + 64],
                                ALU.mult, ALU.add),
                                  r=("SST", "ACUMC", PB[3]), w=("SST",),
                                  tag="sup")
                        kc.ve(lambda e, tp=tp: e.tensor_copy(
                            SBFS[:, tp * 64:(tp + 1) * 64], sv[:, tp, :]),
                              r=("SST",), w=("SBFS",), tag="sbf")
                    kc.dma(SYNC, olv[:, c * 2 + s, :], LTF[:, 0:D],
                           r=(LTF.r,), w=(f"olocD:{c}",), tag="o_sp")
            kc.dma(SYNC, cinS.ap()[:, :], SST[:, :], r=("SST",), w=("cinS",),
                   tag="st_stg")
            kc.sch.op(GPSIMD,
                      lambda e: e.collective_compute(
                          "AllGather", ALU.bypass,
                          replica_groups=[[0, 1], [2, 3], [4, 5], [6, 7]],
                          ins=[cinS.ap()], outs=[coutS.ap()]),
                      r=("cinS",), w=("coutS",), coll=True, tag="st_cc")
        sch.barrier(f"tmA{l}")

    def phase_tmB(l):
        with contextlib.ExitStack() as pes:
            WOS = S("WOS", [P, NT * D], BF16, pes)
            FXH = S("bFXH", [P, NT * CH], F32, pes)
            OLS = S("OLS", [P, 2 * D], F32, pes)
            OF = S("OFs", [P, D], F32, pes)
            OSQ = S("OSQ", [P, D], F32, pes)
            QAS = S("QAS", [P, NT * CH], BF16, pes)
            QBS = S("QBS", [P, NT * CH], BF16, pes)
            GS = S("GSs", [P, 2 * D], BF16, pes)
            SCB = S("SCB", [P, 2 * 512], BF16, pes)
            GTB = S("GTB", [P, D], BF16, pes)
            GTT = S("GTTs", [P, NT * CH], BF16, pes)
            GNREP = S("GNREP", [P, 2 * D], F32, pes)
            PGN = S("PGNs", [1, 4 * D], F32, pes)

            for t in range(NT):
                kc.dma(GPSIMD, WOS[:, t * D:(t + 1) * D],
                       wWo.ap()[l, t * P:(t + 1) * P, :], w=("WOS",), tag="ldwo")
            kc.dma(SYNC, PGN[:, :], pgn.ap()[:, :], w=("PGN",), tag="ldpgn")
            for i, base in enumerate((2 * l * D, (2 * l + 1) * D)):
                for nh in range(2):
                    kc.mm(PROW[0:1, NEXP:NEXP + P],
                          PGN[0:1, base + nh * 512: base + (nh + 1) * 512],
                          pb[i][:, 0:512], PB[i], r=("PROW", "PGN"),
                          tag="gnrep_mm")
                    kc.ve(lambda e, i=i, nh=nh: e.tensor_copy(
                        GNREP[:, i * D + nh * 512: i * D + (nh + 1) * 512],
                        pb[i][:, 0:512]),
                          r=(PB[i],), w=("GNREP",), tag="gnrep_ev")
            kc.dma(SYNC, CCG2[:, :], coutS.ap()[0:P, :], r=("coutS",),
                   w=("CCG2",), tag="g0s")
            qas = v3(QAS, CH)
            qbs = v3(QBS, CH)
            fxh = v3(FXH, CH)
            gtt = v3(GTT, CH)
            ccg2v = CCG2.t[:, :].rearrange("p (t v) -> p t v", v=64)
            scbv = SCB.t[:, :].rearrange("p (s x) -> p s x", x=512)
            for c in range(NCH):
                c0 = c * CH
                ld_dmaj(SYNC, FXH, CH, hDv, f"hD:{c}", c0, "tmB_ldh")
                for s2 in range(2):
                    t0, t1 = s2 * 4, (s2 + 1) * 4
                    kc.dma(SYNC, qas[:, t0:t1, 0:CH], qav[:, t0:t1, c0:c0 + CH],
                           r=(f"qaD:{c}",), w=(QAS.r,), tag="qa_ld")
                    kc.dma(SYNC, qbs[:, t0:t1, 0:CH], qbv[:, t0:t1, c0:c0 + CH],
                           r=(f"qbD:{c}",), w=(QBS.r,), tag="qb_ld")
                for s in range(2):
                    kc.dma(SYNC, OLS[:, s * D:(s + 1) * D], olv[:, c * 2 + s, :],
                           r=(f"olocD:{c}",), w=(OLS.r,), tag="ol_ld")
                    kc.dma(GPSIMD, GS[:, s * D:(s + 1) * D], ggv[:, c * 2 + s, :],
                           r=(f"ggD:{c}",), w=(GS.r,), tag="g_ld")
                    sub = c * 2 + s
                    kc.ve(lambda e, s=s, sub=sub: e.tensor_mul(
                        scbv[:, s, :].rearrange("p (t v) -> p t v", v=64),
                        ccg2v[:, :, :],
                        ACTAB[:, sub * 8:(sub + 1) * 8].unsqueeze(2)
                        .broadcast_to([P, NT, 64])),
                          r=("CCG2", "ACTAB"), w=(SCB.r,), tag="scb1")
                    kc.ve(lambda e, s=s: e.tensor_scalar_mul(
                        scbv[:, s, :], scbv[:, s, :], halfc),
                          r=(SCB.r, "CCOL"), w=(SCB.r,), tag="scb2")
                pipe = EvacPipe([2, 3, 4, 5])
                for s in range(2):
                    w0 = s * P
                    for tp in range(NT):
                        def mmf(bk, s=s, tp=tp, w0=w0):
                            for h, qv in enumerate((qas, qbs)):
                                kc.mm(qv[:, tp, w0:w0 + P],
                                      scbv[:, s, tp * 64:(tp + 1) * 64],
                                      pb[bk][:, h * 64:(h + 1) * 64], PB[bk],
                                      r=(QAS.r if h == 0 else QBS.r, SCB.r),
                                      tag="corr")

                        def evf(bk, s=s, tp=tp):
                            kc.ve(lambda e, s=s, tp=tp, bk=bk: e.tensor_add(
                                OF[:, tp * P:(tp + 1) * P],
                                OLS[:, s * D + tp * P: s * D + (tp + 1) * P],
                                pb[bk][:, 0:P]),
                                  r=(OLS.r, PB[bk]), w=(OF.r,), tag="ofull")
                        pipe.step(mmf, evf)
                    pipe.flush()
                    ofv = OF.t[:, :].rearrange("p (g v) -> p g v", v=64)
                    kc.ve(lambda e: e.reduce_sum(GST[:, 0:16].unsqueeze(2),
                                                 ofv[:, :, :], AX.X),
                          r=(OF.r,), w=("GST",), tag="gn_sum")
                    kc.act(OSQ[:, 0:D], OF[:, 0:D], AF.Square, r=(OF.r,),
                           w=(OSQ.r,), tag="gn_sq")
                    kc.ve(lambda e: e.reduce_sum(GST[:, 16:32].unsqueeze(2),
                                                 OSQ.t[:, :].rearrange(
                                                     "p (g v) -> p g v", v=64),
                                                 AX.X),
                          r=(OSQ.r,), w=("GST",), tag="gn_sumsq")
                    kc.ve(lambda e: e.tensor_scalar_mul(GST[:, 0:16],
                                                        GST[:, 0:16], 1.0 / 64),
                          r=("GST",), w=("GST",), tag="gn_mean")
                    kc.ve(lambda e: e.scalar_tensor_tensor(
                        GST[:, 32:48], GST[:, 0:16], -1.0, GST[:, 0:16],
                        ALU.mult, ALU.mult),
                          r=("GST",), w=("GST",), tag="gn_msq")
                    kc.ve(lambda e: e.scalar_tensor_tensor(
                        GST[:, 48:64], GST[:, 16:32], 1.0 / 64, GST[:, 32:48],
                        ALU.mult, ALU.add),
                          r=("GST",), w=("GST",), tag="gn_var")
                    kc.act(GST[:, 32:48], GST[:, 48:64], AF.Sqrt,
                           bias=epsc, r=("GST", "CCOL"), w=("GST",),
                           tag="gn_sdv")
                    kc.ve(lambda e: e.reciprocal(GST[:, 48:64], GST[:, 32:48]),
                          r=("GST",), w=("GST",), tag="gn_rstd")
                    kc.ve(lambda e: e.tensor_sub(
                        ofv[:, :, :], ofv[:, :, :],
                        GST[:, 0:16].unsqueeze(2).broadcast_to([P, 16, 64])),
                          r=(OF.r, "GST"), w=(OF.r,), tag="gn_sub")
                    kc.ve(lambda e: e.tensor_mul(
                        ofv[:, :, :], ofv[:, :, :],
                        GST[:, 48:64].unsqueeze(2).broadcast_to([P, 16, 64])),
                          r=(OF.r, "GST"), w=(OF.r,), tag="gn_nrm")
                    kc.ve(lambda e: e.tensor_mul(OF[:, 0:D], OF[:, 0:D],
                                                 GNREP[:, 0:D]),
                          r=(OF.r, "GNREP"), w=(OF.r,), tag="gn_w")
                    kc.ve(lambda e: e.tensor_add(OF[:, 0:D], OF[:, 0:D],
                                                 GNREP[:, D:2 * D]),
                          r=(OF.r, "GNREP"), w=(OF.r,), tag="gn_b")
                    kc.ve(lambda e, s=s: e.tensor_mul(
                        GTB[:, 0:D], OF[:, 0:D], GS[:, s * D:(s + 1) * D]),
                          r=(OF.r, GS.r), w=(GTB.r,), tag="gate")
                    for t in range(NT):
                        kc.transpose(pbT[:, t * P:(t + 1) * P],
                                     GTB[:, t * P:(t + 1) * P],
                                     CID[:, :], "pbT", r=(GTB.r, "CID"),
                                     tag="gT")
                    kc.ve(lambda e, s=s: e.tensor_copy(
                        gtt[:, :, s * P:(s + 1) * P],
                        pbT[:, 0:D].rearrange("p (t c) -> p t c", c=P)),
                          r=("pbT",), w=(GTT.r,), tag="gT_ev")
                for j in range(NT):
                    def mmf(bk, j=j):
                        kc.mmgroup([(WOS[:, t * D + j * P: t * D + (j + 1) * P],
                                     gtt[:, t, 0:CH]) for t in range(NT)],
                                   pb[bk][:, 0:CH], PB[bk], r=("WOS", GTT.r),
                                   tag="wo_mm")

                    def evf(bk, j=j):
                        kc.ve(lambda e, j=j, bk=bk: e.tensor_add(
                            fxh[:, j, 0:CH], fxh[:, j, 0:CH], pb[bk][:, 0:CH]),
                              r=(FXH.r, PB[bk]), w=(FXH.r,), tag="hup")
                    pipe.step(mmf, evf)
                pipe.flush()
                if c == NCH - 1:
                    kc.ve(lambda e: e.tensor_copy(
                        HLAST[:, 0:8], fxh[:, :, CH - 1:CH].squeeze(2)),
                          r=(FXH.r,), w=("HLAST",), tag="hlast")
                st_dmaj(SYNC, hDv, f"hD:{c}", FXH, CH, c0, "tmB_sth")
        sch.barrier(f"tmB{l}")

    def phase_cm(l):
        with contextlib.ExitStack() as pes:
            CWK = S("CWK", [P, NT * 3 * D], BF16, pes)
            CWV = S("CWV", [P, 24 * D], BF16, pes)
            CWR = S("CWR", [P, NT * D], BF16, pes)
            FXH = S("cFXH", [P, NT * CH], F32, pes)
            LXB = S("cLXB", [P, NT * CH], BF16, pes)
            LSQ = S("cLSQ", [P, NT * CH], BF16, pes)
            LTF = S("cLTF", [P, NT * CH], F32, pes)
            U2 = S("cU2", [P, NT * CH], BF16, pes)
            SX2 = S("cSX2", [P, NT * CH], BF16, pes)
            KX = S("cKX", [P, NT * CH], BF16, pes)
            RX = S("cRX", [P, NT * CH], BF16, pes)
            KK = S("cKK", [P, 24 * CH], BF16, pes)
            RG = S("cRG", [P, NT * CH], BF16, pes)
            TR = S("cTR", [P, CH], F32, pes)
            UBB = S("cUBB", [P, 16], F32, pes)

            for t in range(NT):
                kc.dma(GPSIMD, CWK[:, t * 3 * D:(t + 1) * 3 * D],
                       wcWk.ap()[l, t * P:(t + 1) * P, :], w=("CWK",), tag="ldck")
                kc.dma(GPSIMD, CWR[:, t * D:(t + 1) * D],
                       wcWr.ap()[l, t * P:(t + 1) * P, :], w=("CWR",), tag="ldcr")
            for f in range(24):
                kc.dma(GPSIMD, CWV[:, f * D:(f + 1) * D],
                       wcWv.ap()[l, f * P:(f + 1) * P, :], w=("CWV",), tag="ldcv")
            boundary_u(l, "ln2", None, UBB, f"ubC{l}")

            SROW = S("SROW", [P, 4 * CH], F32, pes)
            ln = LN(kc, CONE[:, :], LXB, LSQ, SROW, LTF,
                    (PB[0], pb[0]), (PB[1], pb[1]), CH, epsc)
            fxh = v3(FXH, CH)
            u2 = v3(U2, CH)
            sx2 = v3(SX2, CH)
            kx = v3(KX, CH)
            rx = v3(RX, CH)
            kkv = v3(KK, CH)
            rgv = v3(RG, CH)
            ltf = v3(LTF, CH)
            for c in range(NCH):
                c0 = c * CH
                ld_dmaj(SYNC, FXH, CH, hDv, f"hD:{c}", c0, "cm_ldh")
                ln.run(fxh[:, :, 0:CH], FXH.r, u2[:, :, 0:CH], U2.r,
                       wcol=par(l, "ln2").unsqueeze(2))
                kc.ve(lambda e: e.tensor_sub(sx2[:, :, 1:CH], u2[:, :, 0:CH - 1],
                                             u2[:, :, 1:CH]),
                      r=(U2.r,), w=(SX2.r,), tag="csx")
                kc.ve(lambda e: e.tensor_sub(sx2[:, :, 0:1],
                                             BPREV[:, 0:8].unsqueeze(2),
                                             u2[:, :, 0:1]),
                      r=(U2.r, "BPREV"), w=(SX2.r,), tag="csx0")
                kc.ve(lambda e: e.tensor_copy(BPREV[:, 0:8],
                                              u2[:, :, CH - 1:CH].squeeze(2)),
                      r=(U2.r,), w=("BPREV",), tag="cbpup")
                kc.ve(lambda e: e.tensor_mul(ltf[:, :, 0:CH], sx2[:, :, 0:CH],
                                             par_bc(l, "ckm", CH)),
                      r=(SX2.r, "PPAR"), w=(LTF.r,), tag="ckx1")
                kc.ve(lambda e: e.tensor_add(kx[:, :, 0:CH], ltf[:, :, 0:CH],
                                             u2[:, :, 0:CH]),
                      r=(LTF.r, U2.r), w=(KX.r,), tag="ckx2")
                kc.ve(lambda e: e.tensor_mul(ltf[:, :, 0:CH], sx2[:, :, 0:CH],
                                             par_bc(l, "crm", CH)),
                      r=(SX2.r, "PPAR"), w=(LTF.r,), tag="crx1")
                kc.ve(lambda e: e.tensor_add(rx[:, :, 0:CH], ltf[:, :, 0:CH],
                                             u2[:, :, 0:CH]),
                      r=(LTF.r, U2.r), w=(RX.r,), tag="crx2")
                pipe = EvacPipe([2, 3, 4, 5])
                for j in range(24):
                    def mmf(bk, j=j):
                        kc.mmgroup(
                            [(CWK[:, t * 3 * D + j * P: t * 3 * D + (j + 1) * P],
                              kx[:, t, 0:CH]) for t in range(NT)],
                            pb[bk][:, 0:CH], PB[bk], r=("CWK", KX.r),
                            tag="ck_mm")

                    def evf(bk, j=j):
                        kc.ve(lambda e, bk=bk: e.tensor_relu(TR[:, 0:CH],
                                                             pb[bk][:, 0:CH]),
                              r=(PB[bk],), w=(TR.r,), tag="ck_rl")
                        kc.act(kkv[:, j, 0:CH], TR[:, 0:CH], AF.Square,
                               r=(TR.r,), w=(KK.r,), tag="ck_sq")
                    pipe.step(mmf, evf)
                for j in range(NT):
                    def mmf(bk, j=j):
                        kc.mmgroup(
                            [(CWR[:, t * D + j * P: t * D + (j + 1) * P],
                              rx[:, t, 0:CH]) for t in range(NT)],
                            pb[bk][:, 0:CH], PB[bk], r=("CWR", RX.r),
                            tag="cr_mm")

                    def evf(bk, j=j):
                        kc.act(rgv[:, j, 0:CH], pb[bk][:, 0:CH], AF.Sigmoid,
                               r=(PB[bk],), w=(RG.r,), tag="cr_ev")
                    pipe.step(mmf, evf)
                pipe.flush()
                for j in range(NT):
                    def mmf(bk, j=j):
                        kc.mmgroup(
                            [(CWV[:, f * D + j * P: f * D + (j + 1) * P],
                              kkv[:, f, 0:CH]) for f in range(24)],
                            pb[bk][:, 0:CH], PB[bk], r=("CWV", KK.r),
                            tag="cv_mm")

                    def evf(bk, j=j):
                        kc.ve(lambda e, j=j, bk=bk: e.tensor_mul(
                            ltf[:, j, 0:CH], pb[bk][:, 0:CH], rgv[:, j, 0:CH]),
                              r=(PB[bk], RG.r), w=(LTF.r,), tag="cv_gate")
                        kc.ve(lambda e, j=j: e.tensor_add(
                            fxh[:, j, 0:CH], fxh[:, j, 0:CH], ltf[:, j, 0:CH]),
                              r=(LTF.r, FXH.r), w=(FXH.r,), tag="chup")
                    pipe.step(mmf, evf)
                pipe.flush()
                if c == NCH - 1:
                    kc.ve(lambda e: e.tensor_copy(
                        HLAST[:, 0:8], fxh[:, :, CH - 1:CH].squeeze(2)),
                          r=(FXH.r,), w=("HLAST",), tag="hlast")
                st_dmaj(SYNC, hDv, f"hD:{c}", FXH, CH, c0, "cm_sth")
        sch.barrier(f"cm{l}")

    def phase_x1():
        with contextlib.ExitStack() as pes:
            FXH = S("xFXH", [P, NT * CH], F32, pes)
            FX2 = S("xFX2", [P, NT * CH], F32, pes)
            LXB = S("xLXB", [P, NT * CH], BF16, pes)
            LSQ = S("xLSQ", [P, NT * CH], BF16, pes)
            LTF = S("xLTF", [P, NT * CH], F32, pes)
            XFB = S("xXFB", [P, NT * CH], BF16, pes)
            SROW = S("SROW", [P, 4 * CH], F32, pes)
            ln = LN(kc, CONE[:, :], LXB, LSQ, SROW, LTF,
                    (PB[0], pb[0]), (PB[1], pb[1]), CH, epsc)
            fxh = v3(FXH, CH)
            fx2 = v3(FX2, CH)
            xfb = v3(XFB, CH)
            for c in range(NCH):
                c0 = c * CH
                ld_dmaj(SYNC, FXH, CH, hDv, f"hD:{c}", c0, "x1_ldh")
                ld_dmaj(SYNC, FX2, CH, x0v, "x0", c0, "x1_ldx0")
                ln.run(fxh[:, :, 0:CH], FXH.r, fxh[:, :, 0:CH], FXH.r,
                       wcol=LNOUTW.unsqueeze(2))
                kc.ve(lambda e: e.tensor_add(fxh[:, :, 0:CH], fxh[:, :, 0:CH],
                                             fx2[:, :, 0:CH]),
                      r=(FXH.r, FX2.r), w=(FXH.r,), tag="x1add")
                st_dmaj(SYNC, x1v, f"x1D:{c}", FXH, CH, c0, "x1_st")
                ln.run(fxh[:, :, 0:CH], FXH.r, xfb[:, :, 0:CH], XFB.r)
                st_dmaj(SYNC, xfv, f"xfD:{c}", XFB, CH, c0, "xf_st")
        sch.barrier("x1")

    def phase_moe():
        with contextlib.ExitStack() as pes:
            WP1 = S("mWP1", [P, NT * DFF], BF16, pes)
            WP2 = S("mWP2", [P, NFT * D], BF16, pes)
            HT = S("mHT", [P, NFT * MCH], BF16, pes)
            XFSs = [S("mXFSa", [P, NT * MCH], BF16, pes),
                    S("mXFSb", [P, NT * MCH], BF16, pes)]
            XFS = XFSs[0]
            ACC = S("mACC", [P, (MCH // P) * D], F32, pes)
            WRM = S("mWRM", [P, NT * NEXP], BF16, pes)
            B1Cs = [S("mB1Ca", [P, NFT], F32, pes),
                    S("mB1Cb", [P, NFT], F32, pes)]
            B2S = S("mB2S", [NEXP, D], BF16, pes)
            RL = S("mRL", [P, 3 * NEXP], F32, pes)
            RWB = S("mRWB", [P, NEXP], BF16, pes)
            BRB = S("mBRB", [P, NEXP], F32, pes)

            ht = HT.t[:, :].rearrange("p (f c) -> p f c", c=MCH)
            xfs = v3(XFS, MCH)
            accv = ACC.t[:, :].rearrange("p (a d) -> p a d", d=D)
            NTT = NTOK // P
            x1sb = HT.t[:, 0:2 * NT * MCH].bitcast(F32)\
                .rearrange("p (t c) -> p t c", c=MCH)
            WTTa = HT.t[0:NEXP, (NFT - NTT // 4) * MCH:NFT * MCH]

            onesrow = PROW[0:1, NEXP:NEXP + P]
            kc.mm(onesrow, PROW[0:1, 0:NEXP], pb[6][:, 0:NEXP],
                  PB[6], r=("PROW",), tag="brb_mm")
            kc.ve(lambda e: e.tensor_copy(BRB[:, :], pb[6][:, 0:NEXP]),
                  r=(PB[6],), w=("BRB",), tag="brb_ev")
            for t in range(NT):
                kc.dma(GPSIMD, WRM[:, t * NEXP:(t + 1) * NEXP],
                       wmWr.ap()[t * P:(t + 1) * P, :], w=("WRM",), tag="ldwr")
            kc.dma(GPSIMD, B2S[:, :], wb2.ap()[:, :], w=("B2S",), tag="ldb2")

            for c in range(NMCH):
                c0 = c * MCH
                for s2 in range(2):
                    t0, t1 = s2 * 4, (s2 + 1) * 4
                    kc.dma(SYNC, xfs[:, t0:t1, 0:MCH], xfv[:, t0:t1, c0:c0 + MCH],
                           r=(f"xfD:{c0 // CH}", f"xfD:{(c0 + MCH - 1) // CH}"),
                           w=(XFS.r,), tag="rxf_ld")
                for ti in range(MCH // P):
                    tt = c * (MCH // P) + ti
                    kc.mmgroup([(xfs[:, t, ti * P:(ti + 1) * P],
                                 WRM[:, t * NEXP:(t + 1) * NEXP])
                                for t in range(NT)],
                               pb[6][:, 0:NEXP], PB[6], r=(XFS.r, "WRM"),
                               tag="rt_mm")
                    kc.ve(lambda e: e.tensor_add(RL[:, 0:NEXP], pb[6][:, 0:NEXP],
                                                 BRB[:, :]),
                          r=(PB[6], "BRB"), w=("mRL",), tag="rt_br")
                    kc.ve(lambda e: e.reduce_max(RTMP[:, 0:1], RL[:, 0:NEXP],
                                                 AX.X),
                          r=("mRL",), w=("RTMP",), tag="rt_m1")
                    kc.ve(lambda e: e.tensor_single_scalar(
                        RL[:, NEXP:2 * NEXP], RL[:, 0:NEXP], RTMP[:, 0:1],
                        ALU.is_ge),
                          r=("mRL", "RTMP"), w=("mRL",), tag="rt_t1")
                    kc.ve(lambda e: e.scalar_tensor_tensor(
                        RL[:, NEXP:2 * NEXP], RL[:, NEXP:2 * NEXP], -1e30,
                        RL[:, 0:NEXP], ALU.mult, ALU.add),
                          r=("mRL",), w=("mRL",), tag="rt_pen")
                    kc.ve(lambda e: e.reduce_max(RTMP[:, 1:2],
                                                 RL[:, NEXP:2 * NEXP], AX.X),
                          r=("mRL",), w=("RTMP",), tag="rt_m2")
                    kc.ve(lambda e: e.tensor_single_scalar(
                        RL[:, NEXP:2 * NEXP], RL[:, 0:NEXP], RTMP[:, 1:2],
                        ALU.is_ge),
                          r=("mRL", "RTMP"), w=("mRL",), tag="rt_keep")
                    kc.ve(lambda e: e.tensor_scalar_mul(RTMP[:, 2:3],
                                                        RTMP[:, 0:1], -1.0),
                          r=("RTMP",), w=("RTMP",), tag="rt_nm1")
                    kc.act(RL[:, 2 * NEXP:3 * NEXP], RL[:, 0:NEXP], AF.Exp,
                           bias=RTMP[:, 2:3], r=("mRL", "RTMP"), w=("mRL",),
                           tag="rt_exp")
                    kc.ve(lambda e: e.tensor_mul(RL[:, 2 * NEXP:3 * NEXP],
                                                 RL[:, 2 * NEXP:3 * NEXP],
                                                 RL[:, NEXP:2 * NEXP]),
                          r=("mRL",), w=("mRL",), tag="rt_w")
                    kc.ve(lambda e: e.reduce_sum(RTMP[:, 3:4],
                                                 RL[:, 2 * NEXP:3 * NEXP], AX.X),
                          r=("mRL",), w=("RTMP",), tag="rt_den")
                    kc.ve(lambda e: e.reciprocal(RTMP[:, 4:5], RTMP[:, 3:4]),
                          r=("RTMP",), w=("RTMP",), tag="rt_rd")
                    kc.ve(lambda e, tt=tt: e.tensor_scalar_mul(
                        WTE[:, tt * NEXP:(tt + 1) * NEXP],
                        RL[:, 2 * NEXP:3 * NEXP], RTMP[:, 4:5]),
                          r=("mRL", "RTMP"), w=("WTE",), tag="rt_fin")
                    kc.ve(lambda e, tt=tt: e.tensor_copy(
                        RWB[:, :], WTE[:, tt * NEXP:(tt + 1) * NEXP]),
                          r=("WTE",), w=("RWB",), tag="rt_cw")
                    kc.transpose(pbT[0:NEXP, 0:P], RWB[:, :], CID[:, :],
                                 "pbT", r=("RWB", "CID"), tag="rt_tr")
                    kc.ve(lambda e, tt=tt: e.tensor_copy(
                        WTTa[0:NEXP, tt * P:(tt + 1) * P], pbT[0:NEXP, 0:P]),
                          r=("pbT",), w=("mHT",), tag="rt_trev")

            for c in range(NMCH):
                c0 = c * MCH
                for s2 in range(2):
                    t0, t1 = s2 * 4, (s2 + 1) * 4
                    kc.dma(SYNC, x1sb[:, t0:t1, 0:MCH],
                           x1v[:, t0:t1, c0:c0 + MCH],
                           r=(f"x1D:{c0 // CH}", f"x1D:{(c0 + MCH - 1) // CH}"),
                           w=("mHT",), tag="mi_ldx1")
                pipe = EvacPipe([0, 1, 2, 3, 4, 5])
                for ti in range(MCH // P):
                    tt = c * (MCH // P) + ti
                    for t in range(NT):
                        def mmf(bk, ti=ti, t=t):
                            kc.transpose(pb[bk][:, 0:P],
                                         x1sb[:, t, ti * P:(ti + 1) * P],
                                         CIDF[:, :], PB[bk], r=("mHT", "CIDF"),
                                         tag="mi_tr")

                        def evf(bk, ti=ti, t=t):
                            kc.ve(lambda e, ti=ti, t=t, bk=bk: e.tensor_copy(
                                accv[:, ti, t * P:(t + 1) * P], pb[bk][:, 0:P]),
                                  r=(PB[bk],), w=("mACC",), tag="mi_ev")
                        pipe.step(mmf, evf)
                    for dh in range(2):
                        def mmf(bk, tt=tt, dh=dh):
                            kc.mm(WTTa[0:NEXP, tt * P:(tt + 1) * P],
                                  B2S[:, dh * 512:(dh + 1) * 512],
                                  pb[bk][:, 0:512], PB[bk], r=("mHT", "B2S"),
                                  tag="mi_b2")

                        def evf(bk, ti=ti, dh=dh):
                            kc.ve(lambda e, ti=ti, dh=dh, bk=bk: e.tensor_add(
                                accv[:, ti, dh * 512:(dh + 1) * 512],
                                accv[:, ti, dh * 512:(dh + 1) * 512],
                                pb[bk][:, 0:512]),
                                  r=(PB[bk],), w=("mACC",), tag="mi_b2ev")
                        pipe.step(mmf, evf)
                pipe.flush()
                kc.dma(SYNC, outv[:, c * (MCH // P):(c + 1) * (MCH // P), :],
                       accv[:, 0:MCH // P, :], r=("mACC",), w=(f"out:{c}",),
                       tag="mi_st")

            pipe = EvacPipe([0, 1, 2, 3, 4, 5])
            for e_i in range(NEXP):
                for fg in range(4):
                    for t in range(NT):
                        kc.dma(GPSIMD,
                               WP1[:, t * DFF + fg * 8 * P:
                                   t * DFF + (fg + 1) * 8 * P],
                               wm1.ap()[e_i, t * P:(t + 1) * P,
                                        fg * 8 * P:(fg + 1) * 8 * P],
                               w=(f"mW1c:{fg}",), tag="ldw1")
                for f in range(NFT):
                    kc.dma(GPSIMD, WP2[:, f * D:(f + 1) * D],
                           wm2.ap()[e_i, f * P:(f + 1) * P, :],
                           w=(f"mW2:{f}",), tag="ldw2")
                B1C = B1Cs[e_i % 2]
                kc.dma(SYNC, B1C[:, :], wb1.ap()[e_i, :, :], w=(B1C.r,),
                       tag="ldb1")
                w2regs = tuple(f"mW2:{f}" for f in range(NFT))
                for c in range(NMCH):
                    c0 = c * MCH
                    XFS = XFSs[(e_i * NMCH + c) % 2]
                    xfs = v3(XFS, MCH)
                    for s2 in range(2):
                        t0, t1 = s2 * 4, (s2 + 1) * 4
                        kc.dma(SYNC, xfs[:, t0:t1, 0:MCH],
                               xfv[:, t0:t1, c0:c0 + MCH],
                               r=(f"xfD:{c0 // CH}",
                                  f"xfD:{(c0 + MCH - 1) // CH}"),
                               w=(XFS.r,), tag="me_xf")
                    kc.dma(SYNC, accv[:, 0:MCH // P, :],
                           outv[:, c * (MCH // P):(c + 1) * (MCH // P), :],
                           r=(f"out:{c}",), w=("mACC",), tag="me_lda")
                    for f in range(NFT):
                        def mmf(bk, f=f, xfs=xfs, XFS=XFS):
                            kc.mmgroup(
                                [(WP1[:, t * DFF + f * P: t * DFF + (f + 1) * P],
                                  xfs[:, t, 0:MCH]) for t in range(NT)],
                                pb[bk][:, 0:MCH], PB[bk],
                                r=(f"mW1c:{f // 8}", XFS.r), tag="me_h")

                        def evf(bk, f=f, B1C=B1C):
                            kc.act(ht[:, f, 0:MCH], pb[bk][:, 0:MCH], AF.Gelu,
                                   bias=B1C[:, f:f + 1], r=(PB[bk], B1C.r),
                                   w=("mHT",), tag="me_gelu")
                        pipe.step(mmf, evf)
                    pipe.flush()
                    for ti in range(MCH // P):
                        tt = c * (MCH // P) + ti
                        for dh in range(2):
                            def mmf(bk, ti=ti, dh=dh):
                                kc.mmgroup(
                                    [(ht[:, f, ti * P:(ti + 1) * P],
                                      WP2[:, f * D + dh * 512:
                                          f * D + (dh + 1) * 512])
                                     for f in range(NFT)],
                                    pb[bk][:, 0:512], PB[bk],
                                    r=w2regs + ("mHT",), tag="me_y")

                            def evf(bk, ti=ti, dh=dh, tt=tt, e_i=e_i):
                                kc.ve(lambda e, ti=ti, dh=dh, bk=bk, tt=tt,
                                      e_i=e_i: e.scalar_tensor_tensor(
                                          accv[:, ti, dh * 512:(dh + 1) * 512],
                                          pb[bk][:, 0:512],
                                          WTE[:, tt * NEXP + e_i:
                                              tt * NEXP + e_i + 1],
                                          accv[:, ti, dh * 512:(dh + 1) * 512],
                                          ALU.mult, ALU.add),
                                      r=(PB[bk], "WTE", "mACC"), w=("mACC",),
                                      tag="me_acc")
                            pipe.step(mmf, evf)
                    pipe.flush()
                    kc.dma(SYNC, outv[:, c * (MCH // P):(c + 1) * (MCH // P), :],
                           accv[:, 0:MCH // P, :], r=("mACC",),
                           w=(f"out:{c}",), tag="me_st")
        sch.barrier("moe")

    for l in range(LAYERS):
        phase_tmA(l)
        phase_tmB(l)
        phase_cm(l)
    phase_x1()
    phase_moe()
    sch.barrier("fin")

    sch.plan()
    if cfg.get("verbose"):
        print("SCHED:", sch.stats(), flush=True)
        print("sbuf bytes remaining:", nc.sbuf_bytes_remaining, flush=True)

    semnames = (["eng_" + e for e in ENGINES]
                + [f"dma_{e}{i}" for e in ENGINES for i in range(Sched.NLANES)]
                + ["cc"])
    sems = {}
    with contextlib.ExitStack() as ses:
        for sn in semnames:
            sems[sn] = ses.enter_context(nc.semaphore(sn))
        with nc.Block() as block:
            sch.emit(block, sems)
    kc.es.close()
    return nc



def _col8(vec):
    return np.ascontiguousarray(vec.reshape(NT, P).T)


def _pack_shared(i):
    f = np.float32
    L = 2
    ppar = np.zeros((P, 200), f)
    names = {"ln1": i["L0_attn_ln1_w"], "ln2": i["L0_attn_ln2_w"],
             "xm": i["L0_attn_x_maa"], "wm": i["L0_attn_w_maa"],
             "km": i["L0_attn_k_maa"], "vm": i["L0_attn_v_maa"],
             "rm": i["L0_attn_r_maa"], "gm": i["L0_attn_g_maa"],
             "ckm": i["L0_attn_ck_maa"], "crm": i["L0_attn_cr_maa"],
             "td": np.asarray(i["L0_attn_time_decay"], f).reshape(L, D),
             "tf": np.asarray(i["L0_attn_time_first"], f).reshape(L, D)}
    order = ["ln1", "ln2", "xm", "wm", "km", "vm", "rm", "gm", "ckm", "crm",
             "td", "tf"]
    for l in range(L):
        for pi, nm in enumerate(order):
            ppar[:, l * 96 + pi * 8:l * 96 + pi * 8 + 8] = \
                _col8(np.asarray(names[nm], f)[l])
    ppar[:, 192:200] = _col8(np.asarray(i["L0_attn_ln_out_w"], f))

    prows = np.zeros((1, 8 + P), f)
    prows[0, 0:8] = np.asarray(i["L0_ffn_br"], f)
    prows[0, 8:8 + P] = 1.0
    pgn = np.zeros((1, 4 * D), f)
    gnw = np.asarray(i["L0_attn_gn_w"], f)
    gnb = np.asarray(i["L0_attn_gn_b"], f)
    for l in range(L):
        pgn[0, 2 * l * D:(2 * l) * D + D] = gnw[l]
        pgn[0, (2 * l + 1) * D:(2 * l + 1) * D + D] = gnb[l]

    tf2 = np.asarray(i["L0_attn_time_first"], f).reshape(L, D)
    u2p = np.zeros((P, 32), f)
    for l in range(L):
        for tp in range(NT):
            blk = tf2[l, tp * P:(tp + 1) * P]
            u2p[0:64, l * 16 + tp * 2 + 0] = blk[0:64]
            u2p[64:128, l * 16 + tp * 2 + 1] = blk[64:128]

    tm1 = np.asarray(i["L0_attn_tm_w1"], f).reshape(L, D, 5, 32)
    wtm1 = np.zeros((L, D, 384), f)
    for j in range(5):
        jp, jh = j // 2, j % 2
        wtm1[:, :, jp * 128 + jh * 64:jp * 128 + jh * 64 + 32] = tm1[:, :, j]
    tm2 = np.asarray(i["L0_attn_tm_w2"], f)
    wtm2 = np.zeros((L, P, 5 * D), f)
    for j in range(5):
        jh = j % 2
        wtm2[:, jh * 64:jh * 64 + 32, j * D:(j + 1) * D] = tm2[:, j]
    td1 = np.asarray(i["L0_attn_td_w1"], f)
    wtd1 = np.zeros((L, D, P), f)
    wtd1[:, :, 0:64] = td1
    td2 = np.asarray(i["L0_attn_td_w2"], f)
    wtd2 = np.zeros((L, P, D), f)
    wtd2[:, 0:64, :] = td2

    b1 = np.asarray(i["L0_ffn_b1"], f)
    wb1 = np.ascontiguousarray(b1.reshape(8, 32, 128).transpose(0, 2, 1))

    shared = {
        "ppar": ppar, "prows": prows, "pgn": pgn, "u2p": u2p,
        "wWr": np.asarray(i["L0_attn_Wr"], f),
        "wWk": np.asarray(i["L0_attn_Wk"], f),
        "wWv": np.asarray(i["L0_attn_Wv"], f),
        "wWo": np.asarray(i["L0_attn_Wo"], f),
        "wWg": np.asarray(i["L0_attn_Wg"], f),
        "wtm1": wtm1, "wtm2": wtm2, "wtd1": wtd1, "wtd2": wtd2,
        "wcWk": np.asarray(i["L0_attn_cWk"], f),
        "wcWv": np.asarray(i["L0_attn_cWv"], f),
        "wcWr": np.asarray(i["L0_attn_cWr"], f),
        "wmWr": np.asarray(i["L0_ffn_Wr"], f),
        "wm1": np.asarray(i["L0_ffn_W1"], f),
        "wb1": wb1,
        "wm2": np.asarray(i["L0_ffn_W2"], f),
        "wb2": np.asarray(i["L0_ffn_b2"], f),
    }
    return shared


def _consts(is_second_half):
    f = np.float32
    c = np.zeros((P, 264), f)
    c[:, 0:P] = np.eye(P, dtype=f)
    ts = np.arange(P)
    c[:, P:2 * P] = (ts[:, None] < ts[None, :]).astype(f)
    c[0:64, 2 * P + 0] = 1.0
    c[64:128, 2 * P + 1] = 1.0
    c[:, 2 * P + 2] = 1.0 if is_second_half else 0.0
    c[:, 2 * P + 3] = 0.0
    c[:, 2 * P + 4] = 1.0
    c[:, 2 * P + 5] = EPS
    c[:, 2 * P + 6] = -1.0
    return c


_NC_CACHE = {}


def _get_nc(cfg_key):
    if cfg_key not in _NC_CACHE:
        cfg = dict(NTOK=4096, CH=256, LAYERS=2, MCH=512, NEXP=8, DFF=4096,
                   verbose=False)
        _NC_CACHE[cfg_key] = build_program(cfg)
    return _NC_CACHE[cfg_key]


def kernel(**inputs):
    x = np.asarray(inputs["L0_attn_x"], np.float32)
    B, T, _ = x.shape
    half = T // 2
    shared = _pack_shared(inputs)
    nc = _get_nc("full")
    in_maps = []
    cA = _consts(False)
    cB = _consts(True)
    for c in range(8):
        b, h = c // 2, c % 2
        m = dict(shared)
        m["x0"] = np.ascontiguousarray(x[b, h * half:(h + 1) * half, :].T)
        m["consts"] = cB if h else cA
        in_maps.append(m)
    res = run_bass_kernel_spmd(nc, in_maps, core_ids=list(range(8)))
    outf = np.empty((B, T, D), np.float32)
    for c in range(8):
        b, h = c // 2, c % 2
        outf[b, h * half:(h + 1) * half, :] = res.results[c]["out"]
    return outf


# version 04
# speedup vs baseline: 1.2370x; 1.0248x over previous
import contextlib
import numpy as np
import ml_dtypes

import concourse.bass as bass
import concourse.mybir as mybir
from concourse.bass_utils import run_bass_kernel_spmd

F32 = mybir.dt.float32
BF16 = mybir.dt.bfloat16
AF = mybir.ActivationFunctionType
ALU = mybir.AluOpType
AX = mybir.AxisListType

TENSOR, VECTOR, SCALAR, GPSIMD, SYNC = "tensor", "vector", "scalar", "gpsimd", "sync"
ENGINES = (TENSOR, VECTOR, SCALAR, GPSIMD, SYNC)

EPS = 1e-5
P = 128
D = 1024
NT = D // P
HS = 64
NH = 16

BF = ml_dtypes.bfloat16



class Op:
    __slots__ = ("idx", "eng", "fn", "reads", "writes", "dma", "mm_out",
                 "marked", "waits", "tag", "coll", "inc_amt", "epos",
                 "semname", "semval", "waitlist", "markable", "lane")

    def __init__(self, idx, eng, fn, reads, writes, dma, mm_out, tag, coll,
                 markable=True):
        self.idx = idx
        self.eng = eng
        self.fn = fn
        self.reads = tuple(reads)
        self.writes = tuple(writes)
        self.dma = dma
        self.mm_out = mm_out
        self.marked = False
        self.waits = []
        self.tag = tag
        self.coll = coll
        self.inc_amt = 0
        self.epos = 0
        self.semname = None
        self.semval = 0
        self.waitlist = []
        self.markable = markable
        self.lane = -1

    def kind(self):
        return (self.dma, self.coll)


class Sched:
    def __init__(self):
        self.ops = []
        self.last_write = {}
        self.readers = {}
        self.pe_mm = {}
        self.drains = []
        self.estreams = {r: [] for r in ENGINES}
        self.floor = {}
        self.dma_counts = {}

    def _new(self, eng, fn, reads, writes, dma, mm_out, tag, coll, markable=True):
        o = Op(len(self.ops), eng, fn, reads, writes, dma, mm_out, tag, coll,
               markable)
        o.epos = len(self.estreams[eng])
        self.estreams[eng].append(o)
        self.ops.append(o)
        return o

    def op(self, eng, fn, r=(), w=(), dma=False, mm_out=None, tag="", coll=False):
        o = self._new(eng, fn, r, w, dma, mm_out, tag, coll)
        if dma or coll:
            o.marked = True
        if dma:
            c = self.dma_counts.get(eng, 0)
            self.dma_counts[eng] = c + 1
            o.lane = c % self.NLANES
        deps = []
        for reg in o.reads:
            p = self.last_write.get(reg)
            if p is not None:
                deps.append(p)
            self.readers.setdefault(reg, []).append(o)
        for reg in o.writes:
            p = self.last_write.get(reg)
            if p is not None:
                deps.append(p)
            for rd in self.readers.get(reg, ()):
                deps.append(rd)
            self.readers[reg] = []
            self.last_write[reg] = o
        if o.eng != TENSOR:
            for reg in o.reads:
                mmop = self.pe_mm.get(reg)
                if mmop is not None:
                    deps.append(self._drain_after(mmop))
        if mm_out is not None:
            self.pe_mm[mm_out] = o
        fl = self.floor.get(eng)
        if fl is not None:
            deps.append(fl)
        self._edges(o, deps)
        return o

    def _drain_after(self, mmop):
        for dop in self.drains:
            if dop.epos > mmop.epos:
                return dop
        dop = self._new(TENSOR, lambda e: e.drain(), (), (), False, None,
                        "drain", False)
        mmop.marked = True
        dop.marked = True
        self.drains.append(dop)
        return dop

    def _edges(self, o, deps):
        best = {}
        for p in deps:
            if p is o or p.idx >= o.idx:
                continue
            key = (p.eng, p.kind(), p.lane)
            cur = best.get(key)
            if cur is None or p.epos > cur.epos:
                best[key] = p
        for (engname, kindk, _lane), p in best.items():
            if kindk != (False, False) or (p.marked and not p.markable):
                o.waits.append(p)
                continue
            stream = self.estreams[engname]
            mop = None
            for cand in stream[p.epos:]:
                if cand.idx >= o.idx:
                    break
                if cand.marked and cand.markable and cand.kind() == kindk:
                    mop = cand
                    break
            if mop is None:
                assert p.markable, f"unmarkable producer {p.tag} for {o.tag}"
                p.marked = True
                mop = p
            o.waits.append(mop)

    def barrier(self, tag="bar"):
        lasts = []
        for e in ENGINES:
            ncomp = 0
            ndma = 0
            ncoll = 0
            for o in reversed(self.estreams[e]):
                if not o.markable:
                    continue
                if o.dma:
                    if ndma < self.NLANES:
                        lasts.append(o)
                        ndma += 1
                elif o.coll:
                    if ncoll < 1:
                        lasts.append(o)
                        ncoll += 1
                else:
                    if ncomp < 1:
                        lasts.append(o)
                        ncomp += 1
                if ncomp and ncoll and ndma >= self.NLANES:
                    break
        outs = []
        for engname in ENGINES:
            o = self._new(engname, lambda e: e.nop(), (), (), False, None,
                          tag, False, markable=False)
            fl = self.floor.get(engname)
            deps = [x for x in lasts if x.eng != engname]
            if fl is not None:
                deps.append(fl)
            self._edges(o, deps)
            o.marked = True
            self.floor[engname] = o
            outs.append(o)
        return outs

    NLANES = 8

    def plan(self):
        counts = {}
        lane_idx = {}
        gate = {}
        for engname in ENGINES:
            for o in self.estreams[engname]:
                if not o.marked:
                    continue
                if o.coll:
                    o.semname, o.inc_amt = "cc", 1
                elif o.dma:
                    o.semname = f"dma_{engname}{o.lane}"
                    o.inc_amt = 16
                else:
                    o.semname, o.inc_amt = "eng_" + engname, 1
                c = counts.get(o.semname, 0) + o.inc_amt
                counts[o.semname] = c
                o.semval = c
                if o.semval > o.inc_amt:
                    gate[o.idx] = (o.semname, o.semval - o.inc_amt)
        for o in self.ops:
            wl = {}
            g = gate.get(o.idx)
            if g is not None:
                wl[g[0]] = g[1]
            for mop in o.waits:
                assert mop.marked, (o.tag, mop.tag)
                k = mop.semname
                if wl.get(k, 0) < mop.semval:
                    wl[k] = mop.semval
            o.waitlist = sorted(wl.items())
        self.sem_totals = counts

    def emit(self, block, sems):
        def mk(engname):
            def run(eng_obj):
                for o in self.estreams[engname]:
                    for semname, val in o.waitlist:
                        eng_obj.wait_ge(sems[semname], val)
                    inst = o.fn(eng_obj)
                    if o.marked:
                        inst.then_inc(sems[o.semname], o.inc_amt)
            return run

        block.tensor(mk(TENSOR))
        block.vector(mk(VECTOR))
        block.scalar(mk(SCALAR))
        block.gpsimd(mk(GPSIMD))
        block.sync(mk(SYNC))

    def stats(self):
        per = {e: len(self.estreams[e]) for e in ENGINES}
        nmark = sum(1 for o in self.ops if o.marked)
        nwait = sum(len(o.waitlist) for o in self.ops)
        return (f"ops={len(self.ops)} per-engine={per} marked={nmark} "
                f"waits={nwait} sems={getattr(self, 'sem_totals', None)}")



class KCtx:

    def __init__(self, nc, sch, cfg):
        self.nc = nc
        self.sch = sch
        self.cfg = cfg
        self.es = contextlib.ExitStack()
        self.dram = {}
        self.pb = []
        self.pbT = None

    def dma(self, q, out_ap, in_ap, r=(), w=(), tag="dma", slow=False):
        return self.sch.op(q, lambda e, o=out_ap, i=in_ap, s=slow:
                           e.dma_start(out=o, in_=i,
                                       allow_slow_non_contiguous=s),
                           r=r, w=w, dma=True, tag=tag)

    def mm(self, lhsT, rhs, out_ap, bank, r=(), start=True, stop=True, tag="mm"):
        return self.sch.op(TENSOR,
                           lambda e, a=lhsT, b=rhs, o=out_ap, s=start, t=stop:
                           e.matmul(o, a, b, start=s, stop=t),
                           r=r, w=(bank,), mm_out=bank, tag=tag)

    def mmgroup(self, pairs, out_ap, bank, r=(), tag="mmg"):
        def fn(e, ps=tuple(pairs), o=out_ap):
            inst = None
            n = len(ps)
            for i, (a, b) in enumerate(ps):
                inst = e.matmul(o, a, b, start=(i == 0), stop=(i == n - 1))
            return inst
        return self.sch.op(TENSOR, fn, r=r, w=(bank,), mm_out=bank, tag=tag)

    def transpose(self, out_ap, in_ap, ident, bank, r=(), tag="tr"):
        return self.sch.op(TENSOR,
                           lambda e, o=out_ap, i=in_ap, d=ident: e.transpose(o, i, d),
                           r=r, w=(bank,), mm_out=bank, tag=tag)

    def act(self, out_ap, in_ap, func, r=(), w=(), bias=0.0, scale=1.0, tag="act"):
        return self.sch.op(SCALAR,
                           lambda e, o=out_ap, i=in_ap, f=func, b=bias, s=scale:
                           e.activation(o, i, f, bias=b, scale=s),
                           r=r, w=w, tag=tag)

    def ve(self, fn, r=(), w=(), tag="ve"):
        return self.sch.op(VECTOR, fn, r=r, w=w, tag=tag)

    def gp(self, fn, r=(), w=(), tag="gp"):
        return self.sch.op(GPSIMD, fn, r=r, w=w, tag=tag)

    def sbuf(self, es, name, shape, dtype):
        return es.enter_context(self.nc.sbuf_tensor(name, shape, dtype))


class EvacPipe:

    def __init__(self, banks, lag=None):
        self.banks = banks
        self.lag = (len(banks) - 1) if lag is None else lag
        self.i = 0
        self.pend = []

    def step(self, mmfn, evfn):
        bk = self.banks[self.i % len(self.banks)]
        self.i += 1
        mmfn(bk)
        self.pend.append((evfn, bk))
        if len(self.pend) > self.lag:
            f, b = self.pend.pop(0)
            f(b)

    def flush(self):
        for f, b in self.pend:
            f(b)
        self.pend = []


def dmajor_view(dram_handle, p=P):
    return dram_handle.ap().rearrange("(t p) n -> p t n", p=p)


def tokmajor_view(dram_handle, p=P):
    return dram_handle.ap().rearrange("(a p) d -> p a d", p=p)



def load_dmajor_chunk(kc, q, sbuf_t, dram_handle, dname, c0, c1, w, tag,
                      nt=NT, split=2):
    v = dmajor_view(dram_handle)
    ops = []
    step = nt // split
    for s in range(split):
        t0, t1 = s * step, (s + 1) * step
        ops.append(kc.dma(q, sbuf_t[:, t0:t1, c0:c1], v[:, t0:t1, c0:c1],
                          r=(dname,) if isinstance(dname, str) else dname,
                          w=(w + f"/{s}",) if split > 1 else (w,), tag=tag))
    return ops


def store_dmajor_chunk(kc, q, dram_handle, dname, sbuf_t, rname, c0, c1,
                       nt=NT, split=2):
    v = dmajor_view(dram_handle)
    step = nt // split
    for s in range(split):
        t0, t1 = s * step, (s + 1) * step
        kc.dma(q, v[:, t0:t1, c0:c1], sbuf_t[:, t0:t1, c0:c1],
               r=(rname,), w=(dname,), tag="st_" + dname)





def dmajor_view(dram_handle, p=P):
    return dram_handle.ap().rearrange("(t p) n -> p t n", p=p)


def tokmajor_view(dram_handle, p=P):
    return dram_handle.ap().rearrange("(a p) d -> p a d", p=p)


class Slot:

    def __init__(self, t, name):
        self.t = t
        self.name = name

    def __getitem__(self, k):
        return self.t[k]

    @property
    def r(self):
        return self.name


def bc_nt(col_ap, C):
    return col_ap.broadcast_to([P, NT, C])


class LN:

    def __init__(self, kc, ones_bf, xb, sq, srow, tf, bank_a, bank_b, C,
                 epsc=None):
        self.kc, self.ones = kc, ones_bf
        self.xb, self.sq, self.srow, self.tf = xb, sq, srow, tf
        self.ba, self.bb = bank_a, bank_b
        self.C = C
        self.epsc = epsc

    def run(self, x_ap, xreg, out_ap, outreg, wcol=None, out2=None, out2reg=None):
        kc, C = self.kc, self.C
        xbs, sqs, srow, tfs = self.xb, self.sq, self.srow, self.tf
        xb = v3(xbs, C)
        sq = v3(sqs, C)
        tf = v3(tfs, C)
        (ban, ba), (bbn, bb) = self.ba, self.bb
        mb = srow[:, 0:C]
        vb = srow[:, C:2 * C]
        tp = srow[:, 2 * C:3 * C]
        rb = srow[:, 3 * C:4 * C]
        kc.ve(lambda e: e.tensor_copy(xb[:, :, 0:C], x_ap), r=(xreg,),
              w=(xbs.r,), tag="ln_cast")
        kc.act(sq[:, :, 0:C], xb[:, :, 0:C], AF.Square, r=(xbs.r,), w=(sqs.r,),
               tag="ln_sq")
        kc.mmgroup([(self.ones, xb[:, t, 0:C]) for t in range(NT)],
                   ba[:, 0:C], ban, r=(xbs.r,), tag="ln_sum")
        kc.mmgroup([(self.ones, sq[:, t, 0:C]) for t in range(NT)],
                   bb[:, 0:C], bbn, r=(sqs.r,), tag="ln_sumsq")
        kc.ve(lambda e: e.tensor_scalar_mul(mb, ba[:, 0:C], 1.0 / D),
              r=(ban,), w=(srow.r,), tag="ln_mean")
        kc.ve(lambda e: e.scalar_tensor_tensor(tp, mb, -1.0, mb, ALU.mult,
                                               ALU.mult),
              r=(srow.r,), w=(srow.r,), tag="ln_msq")
        kc.ve(lambda e: e.scalar_tensor_tensor(vb, bb[:, 0:C], 1.0 / D, tp,
                                               ALU.mult, ALU.add),
              r=(bbn, srow.r), w=(srow.r,), tag="ln_var")
        kc.act(tp, vb, AF.Sqrt, bias=self.epsc, r=(srow.r, "CCOL"),
               w=(srow.r,), tag="ln_sdv")
        kc.ve(lambda e: e.reciprocal(rb, tp), r=(srow.r,), w=(srow.r,),
              tag="ln_rstd")
        mb_bc = mb.unsqueeze(1).broadcast_to([P, NT, C])
        rb_bc = rb.unsqueeze(1).broadcast_to([P, NT, C])
        kc.ve(lambda e: e.tensor_sub(tf[:, :, 0:C], x_ap, mb_bc),
              r=(xreg, srow.r), w=(tfs.r,), tag="ln_sub")
        if wcol is None:
            kc.ve(lambda e: e.tensor_mul(out_ap, tf[:, :, 0:C], rb_bc),
                  r=(tfs.r, srow.r), w=(outreg,), tag="ln_out")
        else:
            kc.ve(lambda e: e.tensor_mul(tf[:, :, 0:C], tf[:, :, 0:C], rb_bc),
                  r=(tfs.r, srow.r), w=(tfs.r,), tag="ln_nrm")
            kc.ve(lambda e: e.tensor_mul(out_ap, tf[:, :, 0:C],
                                         bc_nt(wcol, C)),
                  r=(tfs.r, "PPAR"), w=(outreg,), tag="ln_scale")


def v3(slot, C=256):
    return slot.t[:, :].rearrange("p (t c) -> p t c", c=C)



def build_program(cfg):
    NTOK = cfg["NTOK"]
    CH = cfg["CH"]
    NCH = NTOK // CH
    NSUB = NTOK // P
    LAYERS = cfg["LAYERS"]
    MCH = cfg["MCH"]
    NMCH = NTOK // MCH
    NEXP = cfg["NEXP"]
    DFF = cfg["DFF"]
    NFT = DFF // P
    dbg = "ExternalOutput" if cfg.get("debug") else "Internal"

    nc = bass.Bass("TRN2", target_bir_lowering=False)
    sch = Sched()
    kc = KCtx(nc, sch, cfg)

    ext_in = {}

    def din(name, shape, dt=F32):
        t = nc.dram_tensor(name, list(shape), dt, kind="ExternalInput")
        ext_in[name] = t
        return t

    x0 = din("x0", [D, NTOK])
    consts = din("consts", [P, 264])
    ppar = din("ppar", [P, 2 * 96 + 8])
    prows = din("prows", [1, 8 + P])
    pgn = din("pgn", [1, 4 * D])
    u2p = din("u2p", [P, 32])
    wWr = din("wWr", [2, D, D]); wWk = din("wWk", [2, D, D])
    wWv = din("wWv", [2, D, D]); wWo = din("wWo", [2, D, D])
    wWg = din("wWg", [2, D, D])
    wtm1 = din("wtm1", [2, D, 384])
    wtm2 = din("wtm2", [2, P, 5 * D])
    wtd1 = din("wtd1", [2, D, P])
    wtd2 = din("wtd2", [2, P, D])
    wcWk = din("wcWk", [2, D, 3 * D])
    wcWv = din("wcWv", [2, 3 * D, D])
    wcWr = din("wcWr", [2, D, D])
    wmWr = din("wmWr", [D, NEXP])
    wm1 = din("wm1", [NEXP, D, DFF])
    wb1 = din("wb1", [NEXP, P, NFT])
    wm2 = din("wm2", [NEXP, DFF, D])
    wb2 = din("wb2", [NEXP, D])

    out = nc.dram_tensor("out", [NTOK, D], F32, kind="ExternalOutput")

    hD = nc.dram_tensor("hD", [D, NTOK], F32, kind=dbg)
    x1D = nc.dram_tensor("x1D", [D, NTOK], F32, kind=dbg)
    xfD = nc.dram_tensor("xfD", [D, NTOK], BF16, kind=dbg)
    olocD = nc.dram_tensor("olocD", [NTOK, D], F32, kind="Internal")
    ggD = nc.dram_tensor("ggD", [NTOK, D], BF16, kind="Internal")
    qaD = nc.dram_tensor("qaD", [D, NTOK], BF16, kind="Internal")
    qbD = nc.dram_tensor("qbD", [D, NTOK], BF16, kind="Internal")
    cinU = nc.dram_tensor("cinU", [P, P], F32, kind="Internal")
    coutU = nc.dram_tensor("coutU", [2 * P, P], F32, kind="Internal")
    cinS = nc.dram_tensor("cinS", [P, 512], F32, kind="Internal")
    coutS = nc.dram_tensor("coutS", [2 * P, 512], F32, kind="Internal")

    es0 = kc.es
    pb = [kc.sbuf(es0, f"pb{i}", [P, 512], F32) if False else
          es0.enter_context(nc.psum_tensor(f"pb{i}", [P, 512], F32))
          for i in range(7)]
    pbT = es0.enter_context(nc.psum_tensor("pbT", [P, 1024], BF16))
    PB = [f"pb{i}" for i in range(7)]

    _uniq = [0]

    def S(name, shape, dt, es=es0):
        _uniq[0] += 1
        uname = f"{name}_{_uniq[0]}"
        t = es.enter_context(nc.sbuf_tensor(uname, shape, dt))
        return Slot(t, name)

    CID = S("CID", [P, P], BF16)
    CIDF = S("CIDF", [P, P], F32)
    CTRI = S("CTRI", [P, P], F32)
    CTRI4 = S("CTRI4", [P, 512], F32)
    CONE = S("CONE", [P, P], BF16)
    CCOL = S("CCOL", [P, 8], F32)
    PPAR = S("PPARs", [P, 200], F32)
    PROW = S("PROWs", [1, 8 + P], F32)
    U2S = S("U2S", [P, 32], BF16)
    SST = S("SST", [P, 512], F32)
    SBFS = S("SBFS", [P, 512], BF16)
    ACUM = S("ACUM", [P, 8], F32)
    ACUMC = S("ACUMC", [P, 16], F32)
    ACTAB = S("ACTAB", [P, 8 * NSUB], F32)
    BPREV = S("BPREV", [P, 8], F32)
    CCU = S("CCU", [P, 8], F32)
    CCG2 = S("CCG2", [P, 512], F32)
    GST = S("GST", [P, 80], F32)
    RTMP = S("RTMP", [P, 16], F32)
    WTE = S("WTE", [P, (NTOK // P) * 8], F32)
    UB2 = S("UB2", [P, 16], BF16)
    HLAST = S("HLAST", [P, 8], F32)

    PAR_IDX = ["ln1", "ln2", "xm", "wm", "km", "vm", "rm", "gm", "ckm", "crm",
               "td", "tf"]

    def par(l, name):
        i = PAR_IDX.index(name)
        base = l * 96 + i * 8
        return PPAR[:, base:base + 8]

    def par_bc(l, name, C):
        return par(l, name).unsqueeze(2).broadcast_to([P, NT, C])

    def pcol(l, name, t):
        i = PAR_IDX.index(name)
        return PPAR[:, l * 96 + i * 8 + t: l * 96 + i * 8 + t + 1]

    LNOUTW = PPAR[:, 192:200]
    maskA = CCOL[:, 0:1]
    maskB = CCOL[:, 1:2]
    halfc = CCOL[:, 2:3]
    zeroc = CCOL[:, 3:4]
    epsc = CCOL[:, 5:6]
    neg1c = CCOL[:, 6:7]

    kc.dma(GPSIMD, CID[:, :], consts[:, 0:P], w=("CID",), tag="ld_cid")
    kc.dma(SYNC, CIDF[:, :], consts[:, 0:P], w=("CIDF",), tag="ld_cidf")
    kc.dma(SYNC, CTRI[:, :], consts[:, P:2 * P], w=("CTRI",), tag="ld_tri")
    kc.ve(lambda e: e.tensor_copy(
        CTRI4.t[:, :].rearrange("p (q c) -> p q c", c=P),
        CTRI[:, :].unsqueeze(1).broadcast_to([P, 4, P])),
          r=("CTRI",), w=("CTRI4",), tag="ctri4")
    kc.ve(lambda e: e.memset(CONE[:, :], 1.0), w=("CONE",), tag="ones")
    kc.dma(SYNC, CCOL[:, :], consts[:, 2 * P:2 * P + 8], w=("CCOL",), tag="ld_ccol")
    kc.dma(SYNC, PPAR[:, :], ppar[:, :], w=("PPAR",), tag="ld_par")
    kc.dma(SYNC, PROW[:, :], prows[:, :], w=("PROW",), tag="ld_prow")
    kc.dma(GPSIMD, U2S[:, :], u2p.ap(), w=("U2S",), tag="ld_u2")
    kc.ve(lambda e: e.memset(SST[:, :], 0.0), w=("SST",), tag="s0")
    kc.ve(lambda e: e.memset(SBFS[:, :], 0.0), w=("SBFS",), tag="sb0")
    kc.ve(lambda e: e.memset(ACUM[:, :], 1.0), w=("ACUM",), tag="ac0")
    kc.ve(lambda e: e.memset(CCG2[:, 0:P], 0.0), w=("CCG2",), tag="zsrow")
    kc.dma(GPSIMD, cinU.ap()[:, 8:P], CCG2[:, 8:P], r=("CCG2",), w=("cinU",),
           tag="zcinu")

    hDv = dmajor_view(hD)
    x0v = dmajor_view(x0)
    x1v = dmajor_view(x1D)
    xfv = dmajor_view(xfD)
    qav = dmajor_view(qaD)
    qbv = dmajor_view(qbD)
    olv = tokmajor_view(olocD)
    ggv = tokmajor_view(ggD)
    outv = tokmajor_view(out)

    def ld_dmaj(q, slot, C, view, dreg, c0, tag, f32=True):
        vv = v3(slot, C)
        for s2 in range(2):
            t0, t1 = s2 * 4, (s2 + 1) * 4
            kc.dma(q, vv[:, t0:t1, 0:C], view[:, t0:t1, c0:c0 + C],
                   r=(dreg,), w=(slot.r,), tag=tag)

    def st_dmaj(q, view, dreg, slot, C, c0, tag):
        vv = v3(slot, C)
        for s2 in range(2):
            t0, t1 = s2 * 4, (s2 + 1) * 4
            kc.dma(q, view[:, t0:t1, c0:c0 + C], vv[:, t0:t1, 0:C],
                   r=(slot.r,), w=(dreg,), tag=tag)

    def boundary_u(l, lnw_name, ph, UBB, tag):
        kc.ve(lambda e: e.tensor_copy(UBB[:, 0:8], HLAST[:, 0:8]),
              r=("HLAST",), w=(UBB.r,), tag=tag + "_ld")
        ubb = UBB[:, 0:8]
        ubbf = UBB[:, 8:16]
        kc.ve(lambda e: e.tensor_copy(UB2[:, 0:8], ubb), r=(UBB.r,),
              w=(UB2.r,), tag=tag + "_cast")
        kc.act(UB2[:, 8:16], UB2[:, 0:8], AF.Square, r=(UB2.r,), w=(UB2.r,),
               tag=tag + "_sq")
        kc.mmgroup([(CONE[:, :], UB2[:, t:t + 1]) for t in range(NT)],
                   pb[0][:, 0:1], PB[0], r=(UB2.r, "CONE"), tag=tag + "_sum")
        kc.mmgroup([(CONE[:, :], UB2[:, 8 + t:9 + t]) for t in range(NT)],
                   pb[1][:, 0:1], PB[1], r=(UB2.r, "CONE"), tag=tag + "_sumsq")
        kc.ve(lambda e: e.tensor_scalar_mul(RTMP[:, 0:1], pb[0][:, 0:1], 1.0 / D),
              r=(PB[0],), w=(RTMP.r,), tag=tag + "_m")
        kc.ve(lambda e: e.scalar_tensor_tensor(RTMP[:, 1:2], RTMP[:, 0:1], -1.0,
                                               RTMP[:, 0:1], ALU.mult, ALU.mult),
              r=(RTMP.r,), w=(RTMP.r,), tag=tag + "_msq")
        kc.ve(lambda e: e.scalar_tensor_tensor(RTMP[:, 2:3], pb[1][:, 0:1],
                                               1.0 / D, RTMP[:, 1:2], ALU.mult,
                                               ALU.add),
              r=(PB[1], RTMP.r), w=(RTMP.r,), tag=tag + "_var")
        kc.act(RTMP[:, 3:4], RTMP[:, 2:3], AF.Sqrt, bias=epsc,
               r=(RTMP.r, "CCOL"), w=(RTMP.r,), tag=tag + "_sdv")
        kc.ve(lambda e: e.reciprocal(RTMP[:, 4:5], RTMP[:, 3:4]),
              r=(RTMP.r,), w=(RTMP.r,), tag=tag + "_rstd")
        kc.ve(lambda e: e.tensor_scalar_sub(UBB[:, 0:8], ubb, RTMP[:, 0:1]),
              r=(UBB.r, RTMP.r), w=(UBB.r,), tag=tag + "_sub")
        kc.ve(lambda e: e.tensor_scalar_mul(UBB[:, 0:8], ubb, RTMP[:, 4:5]),
              r=(UBB.r, RTMP.r), w=(UBB.r,), tag=tag + "_nrm")
        if lnw_name is not None:
            kc.ve(lambda e: e.tensor_mul(UBB[:, 0:8], ubb, par(l, lnw_name)),
                  r=(UBB.r, "PPAR"), w=(UBB.r,), tag=tag + "_w")
        kc.dma(SYNC, cinU.ap()[:, 0:8], UBB[:, 0:8], r=(UBB.r,), w=("cinU",),
               tag=tag + "_stg")
        kc.sch.op(GPSIMD,
                  lambda e: e.collective_compute(
                      "AllGather", ALU.bypass,
                      replica_groups=[[0, 1], [2, 3], [4, 5], [6, 7]],
                      ins=[cinU.ap()], outs=[coutU.ap()]),
                  r=("cinU",), w=("coutU",), coll=True, tag=tag + "_cc")
        kc.dma(SYNC, CCU[:, 0:8], coutU.ap()[0:P, 0:8], r=("coutU",),
               w=("CCU",), tag=tag + "_g")
        kc.ve(lambda e: e.tensor_scalar_mul(BPREV[:, 0:8], CCU[:, 0:8], halfc),
              r=("CCU", "CCOL"), w=("BPREV",), tag=tag + "_bp")

    with contextlib.ExitStack() as pes:
        FXH = S("p0FXH", [P, NT * CH], F32, pes)
        LXB = S("p0LXB", [P, NT * CH], BF16, pes)
        LSQ = S("p0LSQ", [P, NT * CH], BF16, pes)
        LTF = S("p0LTF", [P, NT * CH], F32, pes)
        SROW = S("SROW", [P, 4 * CH], F32, pes)
        ln = LN(kc, CONE[:, :], LXB, LSQ, SROW, LTF,
                (PB[0], pb[0]), (PB[1], pb[1]), CH, epsc)
        for c in range(NCH):
            c0 = c * CH
            ld_dmaj(SYNC, FXH, CH, x0v, "x0", c0, "p0_ld")
            ln.run(v3(FXH, CH)[:, :, 0:CH], FXH.r,
                   v3(FXH, CH)[:, :, 0:CH], FXH.r)
            if c == NCH - 1:
                kc.ve(lambda e: e.tensor_copy(
                    HLAST[:, 0:8], v3(FXH, CH)[:, :, CH - 1:CH].squeeze(2)),
                      r=(FXH.r,), w=("HLAST",), tag="hlast")
            st_dmaj(SYNC, hDv, f"hD:{c}", FXH, CH, c0, "p0_st")
    sch.barrier("p0")

    def phase_tmA(l):
        with contextlib.ExitStack() as pes:
            WPA = S("WPA", [P, 4 * NT * D], BF16, pes)
            TM1 = S("TM1", [P, NT * 384], BF16, pes)
            TM2 = S("TM2", [P, 5 * D], BF16, pes)
            TD1 = S("TD1", [P, NT * P], BF16, pes)
            TD2 = S("TD2", [P, D], BF16, pes)
            FXH = S("FXH", [P, NT * CH], F32, pes)
            LOGW = S("LOGW", [P, NT * CH], F32, pes)
            PLAX = S("PLAX", [P, NT * CH], F32, pes)
            LTF = S("LTF", [P, NT * CH], F32, pes)
            LXB = S("LXB", [P, NT * CH], BF16, pes)
            LSQ = S("LSQ", [P, NT * CH], BF16, pes)
            U = S("U", [P, NT * CH], BF16, pes)
            SX = S("SX", [P, NT * CH], BF16, pes)
            EEXC = S("EEXC", [P, NT * CH], BF16, pes)
            EINC = S("EINC", [P, NT * CH], BF16, pes)
            KT = S("KT", [P, NT * CH], BF16, pes)
            RT = S("RT", [P, NT * CH], BF16, pes)
            QA = S("QA", [P, NT * CH], BF16, pes)
            QB = S("QB", [P, NT * CH], BF16, pes)
            RK = S("RK", [P, NT * CH], BF16, pes)
            KH = S("KH", [P, NT * CH], BF16, pes)
            PX = S("PXs", [P, NT * CH], BF16, pes)
            TA = S("TAs", [P, NT * CH], BF16, pes)
            TMPB = S("TMPB", [P, 3 * CH], BF16, pes)
            VT = S("VTs", [P, 2 * D], BF16, pes)
            GT = S("GTs", [P, 2 * D], BF16, pes)
            UBB = S("UBB", [P, 16], F32, pes)

            def wslab(i, t):
                return WPA[:, (i * NT + t) * D:(i * NT + t) * D + D]

            for i, wt in enumerate((wWr, wWk, wWv, wWg)):
                for t in range(NT):
                    kc.dma(GPSIMD, wslab(i, t), wt.ap()[l, t * P:(t + 1) * P, :],
                           w=(f"WPA{i}",), tag=f"ldw{i}")
            for t in range(NT):
                kc.dma(GPSIMD, TM1[:, t * 384:(t + 1) * 384],
                       wtm1.ap()[l, t * P:(t + 1) * P, :], w=("TM1",), tag="ldtm1")
                kc.dma(GPSIMD, TD1[:, t * P:(t + 1) * P],
                       wtd1.ap()[l, t * P:(t + 1) * P, :], w=("TD1",), tag="ldtd1")
            kc.dma(GPSIMD, TM2[:, :], wtm2.ap()[l, :, :], w=("TM2",), tag="ldtm2")
            kc.dma(GPSIMD, TD2[:, :], wtd2.ap()[l, :, :], w=("TD2",), tag="ldtd2")

            boundary_u(l, "ln1", None, UBB, f"ubA{l}")
            kc.ve(lambda e: e.memset(SST[:, :], 0.0), w=("SST",), tag="s0")
            kc.ve(lambda e: e.memset(SBFS[:, :], 0.0), w=("SBFS",), tag="sb0")
            kc.ve(lambda e: e.memset(ACUM[:, :], 1.0), w=("ACUM",), tag="ac0")

            SROW = S("SROW", [P, 4 * CH], F32, pes)
            ln = LN(kc, CONE[:, :], LXB, LSQ, SROW, LTF,
                    (PB[0], pb[0]), (PB[1], pb[1]), CH, epsc)
            fxh = v3(FXH, CH)
            uu = v3(U, CH)
            sx = v3(SX, CH)
            logw = v3(LOGW, CH)
            plax = v3(PLAX, CH)
            eexc = v3(EEXC, CH)
            einc = v3(EINC, CH)
            ktv = v3(KT, CH)
            rtv = v3(RT, CH)
            qa = v3(QA, CH)
            qb = v3(QB, CH)
            rk = v3(RK, CH)
            kh = v3(KH, CH)
            pxv = v3(PX, CH)
            tav = v3(TA, CH)
            ltf = v3(LTF, CH)
            vtv = VT.t[:, :].rearrange("p (s d) -> p s d", d=D)
            gtv = GT.t[:, :].rearrange("p (s d) -> p s d", d=D)

            for c in range(NCH):
                c0 = c * CH
                ld_dmaj(SYNC, FXH, CH, hDv, f"hD:{c}", c0, "tmA_ldh")
                ln.run(fxh[:, :, 0:CH], FXH.r, uu[:, :, 0:CH], U.r,
                       wcol=par(l, "ln1").unsqueeze(2))
                kc.ve(lambda e: e.tensor_sub(sx[:, :, 1:CH], uu[:, :, 0:CH - 1],
                                             uu[:, :, 1:CH]),
                      r=(U.r,), w=(SX.r,), tag="sx")
                kc.ve(lambda e: e.tensor_sub(sx[:, :, 0:1],
                                             BPREV[:, 0:8].unsqueeze(2),
                                             uu[:, :, 0:1]),
                      r=(U.r, "BPREV"), w=(SX.r,), tag="sx0")
                kc.ve(lambda e: e.tensor_copy(BPREV[:, 0:8],
                                              uu[:, :, CH - 1:CH].squeeze(2)),
                      r=(U.r,), w=("BPREV",), tag="bpup")
                kc.ve(lambda e: e.tensor_mul(pxv[:, :, 0:CH], sx[:, :, 0:CH],
                                             par_bc(l, "xm", CH)),
                      r=(SX.r, "PPAR"), w=(PX.r,), tag="xxx1")
                kc.ve(lambda e: e.tensor_add(pxv[:, :, 0:CH], pxv[:, :, 0:CH],
                                             uu[:, :, 0:CH]),
                      r=(PX.r, U.r), w=(PX.r,), tag="xxx2")
                pipe = EvacPipe([2, 3, 4, 5])
                for jp in range(3):
                    def mmf(bk, jp=jp):
                        kc.mmgroup(
                            [(TM1[:, t * 384 + jp * P: t * 384 + (jp + 1) * P],
                              pxv[:, t, 0:CH]) for t in range(NT)],
                            pb[bk][:, 0:CH], PB[bk], r=("TM1", PX.r),
                            tag="tmp_mm")

                    def evf(bk, jp=jp):
                        kc.act(TMPB[:, jp * CH:(jp + 1) * CH], pb[bk][:, 0:CH],
                               AF.Tanh, r=(PB[bk],), w=("TMPB",), tag="tmp_tanh")
                    pipe.step(mmf, evf)
                pipe.flush()
                MAAN = ["wm", "km", "vm", "rm", "gm"]
                for Pi in range(5):
                    for j in range(NT):
                        def mmf(bk, j=j, Pi=Pi):
                            kc.mm(TM2[:, Pi * D + j * P: Pi * D + (j + 1) * P],
                                  TMPB[:, (Pi // 2) * CH:(Pi // 2 + 1) * CH],
                                  pb[bk][:, 0:CH], PB[bk], r=("TM2", "TMPB"),
                                  tag="mix_mm")

                        def evf(bk, j=j, Pi=Pi):
                            kc.ve(lambda e, j=j, bk=bk, Pi=Pi:
                                  e.scalar_tensor_tensor(
                                      ltf[:, j, 0:CH], pb[bk][:, 0:CH],
                                      pcol(l, MAAN[Pi], j), sx[:, j, 0:CH],
                                      ALU.add, ALU.mult),
                                  r=(PB[bk], "PPAR", SX.r), w=(LTF.r,),
                                  tag="pxa")
                            kc.ve(lambda e, j=j: e.tensor_add(
                                pxv[:, j, 0:CH], ltf[:, j, 0:CH],
                                uu[:, j, 0:CH]),
                                  r=(LTF.r, U.r), w=(PX.r,), tag="pxb")
                        pipe.step(mmf, evf)
                    pipe.flush()
                    if Pi == 0:
                        def mmf(bk):
                            kc.mmgroup([(TD1[:, t * P:(t + 1) * P],
                                         pxv[:, t, 0:CH]) for t in range(NT)],
                                       pb[bk][:, 0:CH], PB[bk],
                                       r=("TD1", PX.r), tag="td1_mm")

                        def evf(bk):
                            kc.act(TA[:, 0:CH], pb[bk][:, 0:CH], AF.Tanh,
                                   r=(PB[bk],), w=(TA.r,), tag="td1_tanh")
                        pipe.step(mmf, evf)
                        pipe.flush()
                        for j in range(NT):
                            def mmf(bk, j=j):
                                kc.mm(TD2[:, j * P:(j + 1) * P], TA[:, 0:CH],
                                      pb[bk][:, 0:CH], PB[bk],
                                      r=("TD2", TA.r), tag="td2_mm")

                            def evf(bk, j=j):
                                kc.act(logw[:, j, 0:CH], pb[bk][:, 0:CH],
                                       AF.Exp, bias=pcol(l, "td", j),
                                       r=(PB[bk], "PPAR"), w=(LOGW.r,),
                                       tag="logw")
                            pipe.step(mmf, evf)
                    elif Pi == 1:
                        for j in range(NT):
                            def mmf(bk, j=j):
                                kc.mmgroup(
                                    [(wslab(1, t)[:, j * P:(j + 1) * P],
                                      pxv[:, t, 0:CH]) for t in range(NT)],
                                    pb[bk][:, 0:CH], PB[bk],
                                    r=("WPA1", PX.r), tag="k_mm")

                            def evf(bk, j=j):
                                kc.sch.op(SCALAR,
                                          lambda e, j=j, bk=bk: e.copy(
                                              ktv[:, j, 0:CH], pb[bk][:, 0:CH]),
                                          r=(PB[bk],), w=(KT.r,), tag="k_ev")
                            pipe.step(mmf, evf)
                    elif Pi == 2:
                        for st in range(2):
                            for nh in range(2):
                                def mmf(bk, st=st, nh=nh):
                                    kc.mmgroup(
                                        [(pxv[:, t, st * P:(st + 1) * P],
                                          wslab(2, t)[:, nh * 512:(nh + 1) * 512])
                                         for t in range(NT)],
                                        pb[bk][:, 0:512], PB[bk],
                                        r=("WPA2", PX.r), tag="v_mm")

                                def evf(bk, st=st, nh=nh):
                                    kc.sch.op(
                                        SCALAR,
                                        lambda e, st=st, nh=nh, bk=bk: e.copy(
                                            vtv[:, st, nh * 512:(nh + 1) * 512],
                                            pb[bk][:, 0:512]),
                                        r=(PB[bk],), w=(VT.r,), tag="v_ev")
                                pipe.step(mmf, evf)
                    elif Pi == 3:
                        for j in range(NT):
                            def mmf(bk, j=j):
                                kc.mmgroup(
                                    [(wslab(0, t)[:, j * P:(j + 1) * P],
                                      pxv[:, t, 0:CH]) for t in range(NT)],
                                    pb[bk][:, 0:CH], PB[bk],
                                    r=("WPA0", PX.r), tag="r_mm")

                            def evf(bk, j=j):
                                kc.act(rtv[:, j, 0:CH], pb[bk][:, 0:CH],
                                       AF.Sigmoid, r=(PB[bk],), w=(RT.r,),
                                       tag="r_ev")
                            pipe.step(mmf, evf)
                    elif Pi == 4:
                        for st in range(2):
                            for nh in range(2):
                                def mmf(bk, st=st, nh=nh):
                                    kc.mmgroup(
                                        [(pxv[:, t, st * P:(st + 1) * P],
                                          wslab(3, t)[:, nh * 512:(nh + 1) * 512])
                                         for t in range(NT)],
                                        pb[bk][:, 0:512], PB[bk],
                                        r=("WPA3", PX.r), tag="g_mm")

                                def evf(bk, st=st, nh=nh):
                                    kc.act(gtv[:, st, nh * 512:(nh + 1) * 512],
                                           pb[bk][:, 0:512], AF.Silu,
                                           r=(PB[bk],), w=(GT.r,), tag="g_ev")
                                pipe.step(mmf, evf)
                pipe.flush()
                for st in range(2):
                    kc.dma(SYNC, ggv[:, c * 2 + st, :], gtv[:, st, :],
                           r=(GT.r,), w=(f"ggD:{c}",), tag="g_sp")
                for s in range(2):
                    w0 = s * P
                    kc.ve(lambda e, w0=w0: e.memset(plax[:, :, w0:w0 + 1], 0.0),
                          w=(PLAX.r,), tag="plx0")
                    for tp in range(NT):
                        kc.ve(lambda e, w0=w0, tp=tp: e.tensor_tensor_scan(
                            plax[:, tp, w0 + 1:w0 + P],
                            logw[:, tp, w0:w0 + P - 1],
                            zeroc.broadcast_to([P, P - 1]),
                            0.0, ALU.add, ALU.add),
                              r=(LOGW.r, "CCOL"), w=(PLAX.r,), tag="scan")
                kc.ve(lambda e: e.tensor_add(logw[:, :, 0:CH], plax[:, :, 0:CH],
                                             logw[:, :, 0:CH]),
                      r=(PLAX.r, LOGW.r), w=(LOGW.r,), tag="plainc")
                for s in range(2):
                    w1 = s * P + P - 1
                    kc.act(ACUMC[:, s * 8:(s + 1) * 8].unsqueeze(2),
                           logw[:, :, w1:w1 + 1], AF.Exp, scale=neg1c,
                           r=(LOGW.r, "CCOL"), w=("ACUMC",), tag="aC")
                kc.act(eexc[:, :, 0:CH], plax[:, :, 0:CH], AF.Exp, scale=neg1c,
                       r=(PLAX.r, "CCOL"), w=(EEXC.r,), tag="eexc")
                kc.act(einc[:, :, 0:CH], logw[:, :, 0:CH], AF.Exp,
                       r=(LOGW.r,), w=(EINC.r,), tag="einc")
                kc.ve(lambda e: e.scalar_tensor_tensor(
                    v3(LXB, CH)[:, :, 0:CH], einc[:, :, 0:CH], maskA,
                    ktv[:, :, 0:CH], ALU.mult, ALU.mult),
                      r=(EINC.r, "CCOL", KT.r), w=(LXB.r,), tag="ka")
                kc.ve(lambda e: e.scalar_tensor_tensor(
                    v3(LSQ, CH)[:, :, 0:CH], einc[:, :, 0:CH], maskB,
                    ktv[:, :, 0:CH], ALU.mult, ALU.mult),
                      r=(EINC.r, "CCOL", KT.r), w=(LSQ.r,), tag="kb")
                kc.ve(lambda e: e.scalar_tensor_tensor(
                    qa[:, :, 0:CH], eexc[:, :, 0:CH], maskA,
                    rtv[:, :, 0:CH], ALU.mult, ALU.mult),
                      r=(EEXC.r, "CCOL", RT.r), w=(QA.r,), tag="qa")
                kc.ve(lambda e: e.scalar_tensor_tensor(
                    qb[:, :, 0:CH], eexc[:, :, 0:CH], maskB,
                    rtv[:, :, 0:CH], ALU.mult, ALU.mult),
                      r=(EEXC.r, "CCOL", RT.r), w=(QB.r,), tag="qb")
                kc.ve(lambda e: e.tensor_mul(rk[:, :, 0:CH], rtv[:, :, 0:CH],
                                             ktv[:, :, 0:CH]),
                      r=(RT.r, KT.r), w=(RK.r,), tag="rk")
                for s in range(2):
                    w0 = s * P
                    kc.ve(lambda e, s=s, w0=w0: e.tensor_mul(
                        tav[:, :, w0:w0 + P], einc[:, :, w0:w0 + P],
                        ACUMC[:, s * 8:(s + 1) * 8].unsqueeze(2)
                        .broadcast_to([P, NT, P])),
                          r=(EINC.r, "ACUMC"), w=(TA.r,), tag="est")
                    kc.ve(lambda e, w0=w0: e.tensor_mul(
                        kh[:, :, w0:w0 + P], ktv[:, :, w0:w0 + P],
                        tav[:, :, w0:w0 + P]),
                          r=(KT.r, TA.r), w=(KH.r,), tag="khat")
                for s2 in range(2):
                    t0, t1 = s2 * 4, (s2 + 1) * 4
                    kc.dma(SYNC, qav[:, t0:t1, c0:c0 + CH], qa[:, t0:t1, 0:CH],
                           r=(QA.r,), w=(f"qaD:{c}",), tag="qa_sp")
                    kc.dma(SYNC, qbv[:, t0:t1, c0:c0 + CH], qb[:, t0:t1, 0:CH],
                           r=(QB.r,), w=(f"qbD:{c}",), tag="qb_sp")
                for s in range(2):
                    sub = c * 2 + s
                    kc.ve(lambda e, sub=sub: e.tensor_copy(
                        ACTAB[:, sub * 8:(sub + 1) * 8], ACUM[:, 0:8]),
                          r=("ACUM",), w=("ACTAB",), tag="acst")
                    kc.ve(lambda e, s=s: e.tensor_mul(
                        ACUM[:, 0:8], ACUM[:, 0:8], ACUMC[:, s * 8:(s + 1) * 8]),
                          r=("ACUM", "ACUMC"), w=("ACUM",), tag="acup")
                sv = SST.t[:, :].rearrange("p (t v) -> p t v", v=64)
                pxc = PX.t[:, 0:1024].rearrange("p (c v) -> p c v", v=64)
                vt4 = VT.t[:, :].rearrange("p (s c v) -> p s c v", s=2, v=64)
                for s in range(2):
                    w0 = s * P
                    for tp in range(NT):
                        b, q = tp // 4, tp % 4
                        kc.mm(v3(LXB, CH)[:, tp, w0:w0 + P],
                              qa[:, tp, w0:w0 + P],
                              pb[b][:, q * P:(q + 1) * P], PB[b],
                              r=(LXB.r, QA.r), tag="ata")
                        kc.mm(v3(LSQ, CH)[:, tp, w0:w0 + P],
                              qb[:, tp, w0:w0 + P],
                              pb[2 + b][:, q * P:(q + 1) * P], PB[2 + b],
                              r=(LSQ.r, QB.r), tag="atb")
                        kc.mm(rk[:, tp, w0:w0 + P],
                              U2S[:, l * 16 + tp * 2: l * 16 + tp * 2 + 2],
                              pb[6][:, s * 64 + tp * 2: s * 64 + tp * 2 + 2],
                              PB[6], r=(RK.r, "U2S"), tag="c2")
                    for mb in range(4):
                        kc.ve(lambda e, mb=mb: e.tensor_mul(
                            TA[:, mb * 512:(mb + 1) * 512],
                            pb[mb][:, 0:512], CTRI4[:, :]),
                              r=(PB[mb], "CTRI4"), w=(TA.r,), tag="mstr")
                    kc.ve(lambda e, s=s: e.tensor_mul(
                        pxc, vt4[:, s, :, :],
                        pb[6][:, s * 64:s * 64 + 16].unsqueeze(2)
                        .broadcast_to([P, 16, 64])),
                          r=(VT.r, PB[6]), w=(PX.r,), tag="vc")
                    for tp in range(NT):
                        b, q = tp // 4, tp % 4
                        for h, qv in enumerate((qa, qb)):
                            kc.mmgroup(
                                [(TA[:, h * 1024 + tp * P:
                                     h * 1024 + (tp + 1) * P],
                                  vtv[:, s, tp * P + h * 64:
                                      tp * P + (h + 1) * 64]),
                                 (qv[:, tp, w0:w0 + P],
                                  SBFS[:, tp * 64:(tp + 1) * 64]),
                                 (CID[:, :], PX[:, tp * P + h * 64:
                                                tp * P + (h + 1) * 64])],
                                pb[4 + b][:, q * P + h * 64:
                                          q * P + (h + 1) * 64], PB[4 + b],
                                r=(TA.r, VT.r, QA.r if h == 0 else QB.r,
                                   "SBFS", "CID", PX.r),
                                tag="o_mm")
                        kc.transpose(pbT[:, tp * P:(tp + 1) * P],
                                     kh[:, tp, w0:w0 + P],
                                     CID[:, :], "pbT", r=(KH.r, "CID"),
                                     tag="khT")
                    kc.ve(lambda e: e.tensor_copy(PX.t[:, 1024:2048],
                                                  pbT[:, 0:1024]),
                          r=("pbT",), w=(PX.r,), tag="khT_ev")
                    for b in range(2):
                        kc.ve(lambda e, b=b: e.tensor_copy(
                            LTF[:, b * 512:(b + 1) * 512], pb[4 + b][:, 0:512]),
                              r=(PB[4 + b],), w=(LTF.r,), tag="o_ev")
                    kc.dma(SYNC, olv[:, c * 2 + s, :], LTF[:, 0:D],
                           r=(LTF.r,), w=(f"olocD:{c}",), tag="o_sp")
                    for tp in range(NT):
                        b, q = tp // 4, tp % 4
                        kc.mm(PX[:, 1024 + tp * P: 1024 + (tp + 1) * P],
                              vtv[:, s, tp * P:(tp + 1) * P],
                              pb[b][:, q * P:(q + 1) * P], PB[b],
                              r=(PX.r, VT.r), tag="ds")
                    for b in range(2):
                        dsv = pb[b][:, 0:512].rearrange("p (q c) -> p q c", c=P)
                        for h in range(2):
                            h0 = h * 64
                            kc.ve(lambda e, b=b, h0=h0, s=s: e.tensor_mul(
                                sv[h0:h0 + 64, b * 4:(b + 1) * 4, :],
                                sv[h0:h0 + 64, b * 4:(b + 1) * 4, :],
                                ACUMC[h0:h0 + 64, s * 8 + b * 4:
                                      s * 8 + b * 4 + 4].unsqueeze(2)
                                .broadcast_to([64, 4, 64])),
                                  r=("SST", "ACUMC"), w=("SST",), tag="sup1")
                            kc.ve(lambda e, b=b, h0=h0, dsv=dsv: e.tensor_add(
                                sv[h0:h0 + 64, b * 4:(b + 1) * 4, :],
                                sv[h0:h0 + 64, b * 4:(b + 1) * 4, :],
                                dsv[h0:h0 + 64, :, h0:h0 + 64]),
                                  r=("SST", PB[b]), w=("SST",), tag="sup2")
                    kc.ve(lambda e: e.tensor_copy(SBFS[:, 0:512],
                                                  SST[:, 0:512]),
                          r=("SST",), w=("SBFS",), tag="sbf")
            kc.dma(SYNC, cinS.ap()[:, :], SST[:, :], r=("SST",), w=("cinS",),
                   tag="st_stg")
            kc.sch.op(GPSIMD,
                      lambda e: e.collective_compute(
                          "AllGather", ALU.bypass,
                          replica_groups=[[0, 1], [2, 3], [4, 5], [6, 7]],
                          ins=[cinS.ap()], outs=[coutS.ap()]),
                      r=("cinS",), w=("coutS",), coll=True, tag="st_cc")
        sch.barrier(f"tmA{l}")

    def phase_tmB(l):
        with contextlib.ExitStack() as pes:
            WOS = S("WOS", [P, NT * D], BF16, pes)
            FXH = S("bFXH", [P, NT * CH], F32, pes)
            OLS = S("OLS", [P, 2 * D], F32, pes)
            OF = S("OFs", [P, D], F32, pes)
            OSQ = S("OSQ", [P, D], F32, pes)
            QAS = S("QAS", [P, NT * CH], BF16, pes)
            QBS = S("QBS", [P, NT * CH], BF16, pes)
            GS = S("GSs", [P, 2 * D], BF16, pes)
            SCB = S("SCB", [P, 2 * 512], BF16, pes)
            GTB = S("GTB", [P, D], BF16, pes)
            GTT = S("GTTs", [P, NT * CH], BF16, pes)
            GNREP = S("GNREP", [P, 2 * D], F32, pes)
            PGN = S("PGNs", [1, 4 * D], F32, pes)

            for t in range(NT):
                kc.dma(GPSIMD, WOS[:, t * D:(t + 1) * D],
                       wWo.ap()[l, t * P:(t + 1) * P, :], w=("WOS",), tag="ldwo")
            kc.dma(SYNC, PGN[:, :], pgn.ap()[:, :], w=("PGN",), tag="ldpgn")
            for i, base in enumerate((2 * l * D, (2 * l + 1) * D)):
                for nh in range(2):
                    kc.mm(PROW[0:1, NEXP:NEXP + P],
                          PGN[0:1, base + nh * 512: base + (nh + 1) * 512],
                          pb[i][:, 0:512], PB[i], r=("PROW", "PGN"),
                          tag="gnrep_mm")
                    kc.ve(lambda e, i=i, nh=nh: e.tensor_copy(
                        GNREP[:, i * D + nh * 512: i * D + (nh + 1) * 512],
                        pb[i][:, 0:512]),
                          r=(PB[i],), w=("GNREP",), tag="gnrep_ev")
            kc.dma(SYNC, CCG2[:, :], coutS.ap()[0:P, :], r=("coutS",),
                   w=("CCG2",), tag="g0s")
            qas = v3(QAS, CH)
            qbs = v3(QBS, CH)
            fxh = v3(FXH, CH)
            gtt = v3(GTT, CH)
            ccg2v = CCG2.t[:, :].rearrange("p (t v) -> p t v", v=64)
            scbv = SCB.t[:, :].rearrange("p (s x) -> p s x", x=512)
            for c in range(NCH):
                c0 = c * CH
                ld_dmaj(SYNC, FXH, CH, hDv, f"hD:{c}", c0, "tmB_ldh")
                for s2 in range(2):
                    t0, t1 = s2 * 4, (s2 + 1) * 4
                    kc.dma(SYNC, qas[:, t0:t1, 0:CH], qav[:, t0:t1, c0:c0 + CH],
                           r=(f"qaD:{c}",), w=(QAS.r,), tag="qa_ld")
                    kc.dma(SYNC, qbs[:, t0:t1, 0:CH], qbv[:, t0:t1, c0:c0 + CH],
                           r=(f"qbD:{c}",), w=(QBS.r,), tag="qb_ld")
                for s in range(2):
                    kc.dma(SYNC, OLS[:, s * D:(s + 1) * D], olv[:, c * 2 + s, :],
                           r=(f"olocD:{c}",), w=(OLS.r,), tag="ol_ld")
                    kc.dma(GPSIMD, GS[:, s * D:(s + 1) * D], ggv[:, c * 2 + s, :],
                           r=(f"ggD:{c}",), w=(GS.r,), tag="g_ld")
                    sub = c * 2 + s
                    kc.ve(lambda e, s=s, sub=sub: e.tensor_mul(
                        scbv[:, s, :].rearrange("p (t v) -> p t v", v=64),
                        ccg2v[:, :, :],
                        ACTAB[:, sub * 8:(sub + 1) * 8].unsqueeze(2)
                        .broadcast_to([P, NT, 64])),
                          r=("CCG2", "ACTAB"), w=(SCB.r,), tag="scb1")
                    kc.ve(lambda e, s=s: e.tensor_scalar_mul(
                        scbv[:, s, :], scbv[:, s, :], halfc),
                          r=(SCB.r, "CCOL"), w=(SCB.r,), tag="scb2")
                pipe = EvacPipe([2, 3, 4, 5])
                for s in range(2):
                    w0 = s * P
                    for tp in range(NT):
                        def mmf(bk, s=s, tp=tp, w0=w0):
                            for h, qv in enumerate((qas, qbs)):
                                kc.mm(qv[:, tp, w0:w0 + P],
                                      scbv[:, s, tp * 64:(tp + 1) * 64],
                                      pb[bk][:, h * 64:(h + 1) * 64], PB[bk],
                                      r=(QAS.r if h == 0 else QBS.r, SCB.r),
                                      tag="corr")

                        def evf(bk, s=s, tp=tp):
                            kc.ve(lambda e, s=s, tp=tp, bk=bk: e.tensor_add(
                                OF[:, tp * P:(tp + 1) * P],
                                OLS[:, s * D + tp * P: s * D + (tp + 1) * P],
                                pb[bk][:, 0:P]),
                                  r=(OLS.r, PB[bk]), w=(OF.r,), tag="ofull")
                        pipe.step(mmf, evf)
                    pipe.flush()
                    ofv = OF.t[:, :].rearrange("p (g v) -> p g v", v=64)
                    kc.ve(lambda e: e.reduce_sum(GST[:, 0:16].unsqueeze(2),
                                                 ofv[:, :, :], AX.X),
                          r=(OF.r,), w=("GST",), tag="gn_sum")
                    kc.act(OSQ[:, 0:D], OF[:, 0:D], AF.Square, r=(OF.r,),
                           w=(OSQ.r,), tag="gn_sq")
                    kc.ve(lambda e: e.reduce_sum(GST[:, 16:32].unsqueeze(2),
                                                 OSQ.t[:, :].rearrange(
                                                     "p (g v) -> p g v", v=64),
                                                 AX.X),
                          r=(OSQ.r,), w=("GST",), tag="gn_sumsq")
                    kc.ve(lambda e: e.tensor_scalar_mul(GST[:, 0:16],
                                                        GST[:, 0:16], 1.0 / 64),
                          r=("GST",), w=("GST",), tag="gn_mean")
                    kc.ve(lambda e: e.scalar_tensor_tensor(
                        GST[:, 32:48], GST[:, 0:16], -1.0, GST[:, 0:16],
                        ALU.mult, ALU.mult),
                          r=("GST",), w=("GST",), tag="gn_msq")
                    kc.ve(lambda e: e.scalar_tensor_tensor(
                        GST[:, 48:64], GST[:, 16:32], 1.0 / 64, GST[:, 32:48],
                        ALU.mult, ALU.add),
                          r=("GST",), w=("GST",), tag="gn_var")
                    kc.act(GST[:, 32:48], GST[:, 48:64], AF.Sqrt,
                           bias=epsc, r=("GST", "CCOL"), w=("GST",),
                           tag="gn_sdv")
                    kc.ve(lambda e: e.reciprocal(GST[:, 48:64], GST[:, 32:48]),
                          r=("GST",), w=("GST",), tag="gn_rstd")
                    kc.ve(lambda e: e.tensor_sub(
                        ofv[:, :, :], ofv[:, :, :],
                        GST[:, 0:16].unsqueeze(2).broadcast_to([P, 16, 64])),
                          r=(OF.r, "GST"), w=(OF.r,), tag="gn_sub")
                    kc.ve(lambda e: e.tensor_mul(
                        ofv[:, :, :], ofv[:, :, :],
                        GST[:, 48:64].unsqueeze(2).broadcast_to([P, 16, 64])),
                          r=(OF.r, "GST"), w=(OF.r,), tag="gn_nrm")
                    kc.ve(lambda e: e.tensor_mul(OF[:, 0:D], OF[:, 0:D],
                                                 GNREP[:, 0:D]),
                          r=(OF.r, "GNREP"), w=(OF.r,), tag="gn_w")
                    kc.ve(lambda e: e.tensor_add(OF[:, 0:D], OF[:, 0:D],
                                                 GNREP[:, D:2 * D]),
                          r=(OF.r, "GNREP"), w=(OF.r,), tag="gn_b")
                    kc.ve(lambda e, s=s: e.tensor_mul(
                        GTB[:, 0:D], OF[:, 0:D], GS[:, s * D:(s + 1) * D]),
                          r=(OF.r, GS.r), w=(GTB.r,), tag="gate")
                    for t in range(NT):
                        kc.transpose(pbT[:, t * P:(t + 1) * P],
                                     GTB[:, t * P:(t + 1) * P],
                                     CID[:, :], "pbT", r=(GTB.r, "CID"),
                                     tag="gT")
                    kc.ve(lambda e, s=s: e.tensor_copy(
                        gtt[:, :, s * P:(s + 1) * P],
                        pbT[:, 0:D].rearrange("p (t c) -> p t c", c=P)),
                          r=("pbT",), w=(GTT.r,), tag="gT_ev")
                for j in range(NT):
                    def mmf(bk, j=j):
                        kc.mmgroup([(WOS[:, t * D + j * P: t * D + (j + 1) * P],
                                     gtt[:, t, 0:CH]) for t in range(NT)],
                                   pb[bk][:, 0:CH], PB[bk], r=("WOS", GTT.r),
                                   tag="wo_mm")

                    def evf(bk, j=j):
                        kc.ve(lambda e, j=j, bk=bk: e.tensor_add(
                            fxh[:, j, 0:CH], fxh[:, j, 0:CH], pb[bk][:, 0:CH]),
                              r=(FXH.r, PB[bk]), w=(FXH.r,), tag="hup")
                    pipe.step(mmf, evf)
                pipe.flush()
                if c == NCH - 1:
                    kc.ve(lambda e: e.tensor_copy(
                        HLAST[:, 0:8], fxh[:, :, CH - 1:CH].squeeze(2)),
                          r=(FXH.r,), w=("HLAST",), tag="hlast")
                st_dmaj(SYNC, hDv, f"hD:{c}", FXH, CH, c0, "tmB_sth")
        sch.barrier(f"tmB{l}")

    def phase_cm(l):
        with contextlib.ExitStack() as pes:
            CWK = S("CWK", [P, NT * 3 * D], BF16, pes)
            CWV = S("CWV", [P, 24 * D], BF16, pes)
            CWR = S("CWR", [P, NT * D], BF16, pes)
            FXH = S("cFXH", [P, NT * CH], F32, pes)
            LXB = S("cLXB", [P, NT * CH], BF16, pes)
            LSQ = S("cLSQ", [P, NT * CH], BF16, pes)
            LTF = S("cLTF", [P, NT * CH], F32, pes)
            U2 = S("cU2", [P, NT * CH], BF16, pes)
            SX2 = S("cSX2", [P, NT * CH], BF16, pes)
            KX = S("cKX", [P, NT * CH], BF16, pes)
            RX = S("cRX", [P, NT * CH], BF16, pes)
            KK = S("cKK", [P, 24 * CH], BF16, pes)
            RG = S("cRG", [P, NT * CH], BF16, pes)
            TR = S("cTR", [P, CH], F32, pes)
            UBB = S("cUBB", [P, 16], F32, pes)

            for t in range(NT):
                kc.dma(GPSIMD, CWK[:, t * 3 * D:(t + 1) * 3 * D],
                       wcWk.ap()[l, t * P:(t + 1) * P, :], w=("CWK",), tag="ldck")
                kc.dma(GPSIMD, CWR[:, t * D:(t + 1) * D],
                       wcWr.ap()[l, t * P:(t + 1) * P, :], w=("CWR",), tag="ldcr")
            for f in range(24):
                kc.dma(GPSIMD, CWV[:, f * D:(f + 1) * D],
                       wcWv.ap()[l, f * P:(f + 1) * P, :], w=("CWV",), tag="ldcv")
            boundary_u(l, "ln2", None, UBB, f"ubC{l}")

            SROW = S("SROW", [P, 4 * CH], F32, pes)
            ln = LN(kc, CONE[:, :], LXB, LSQ, SROW, LTF,
                    (PB[0], pb[0]), (PB[1], pb[1]), CH, epsc)
            fxh = v3(FXH, CH)
            u2 = v3(U2, CH)
            sx2 = v3(SX2, CH)
            kx = v3(KX, CH)
            rx = v3(RX, CH)
            kkv = v3(KK, CH)
            rgv = v3(RG, CH)
            ltf = v3(LTF, CH)
            for c in range(NCH):
                c0 = c * CH
                ld_dmaj(SYNC, FXH, CH, hDv, f"hD:{c}", c0, "cm_ldh")
                ln.run(fxh[:, :, 0:CH], FXH.r, u2[:, :, 0:CH], U2.r,
                       wcol=par(l, "ln2").unsqueeze(2))
                kc.ve(lambda e: e.tensor_sub(sx2[:, :, 1:CH], u2[:, :, 0:CH - 1],
                                             u2[:, :, 1:CH]),
                      r=(U2.r,), w=(SX2.r,), tag="csx")
                kc.ve(lambda e: e.tensor_sub(sx2[:, :, 0:1],
                                             BPREV[:, 0:8].unsqueeze(2),
                                             u2[:, :, 0:1]),
                      r=(U2.r, "BPREV"), w=(SX2.r,), tag="csx0")
                kc.ve(lambda e: e.tensor_copy(BPREV[:, 0:8],
                                              u2[:, :, CH - 1:CH].squeeze(2)),
                      r=(U2.r,), w=("BPREV",), tag="cbpup")
                kc.ve(lambda e: e.tensor_mul(ltf[:, :, 0:CH], sx2[:, :, 0:CH],
                                             par_bc(l, "ckm", CH)),
                      r=(SX2.r, "PPAR"), w=(LTF.r,), tag="ckx1")
                kc.ve(lambda e: e.tensor_add(kx[:, :, 0:CH], ltf[:, :, 0:CH],
                                             u2[:, :, 0:CH]),
                      r=(LTF.r, U2.r), w=(KX.r,), tag="ckx2")
                kc.ve(lambda e: e.tensor_mul(ltf[:, :, 0:CH], sx2[:, :, 0:CH],
                                             par_bc(l, "crm", CH)),
                      r=(SX2.r, "PPAR"), w=(LTF.r,), tag="crx1")
                kc.ve(lambda e: e.tensor_add(rx[:, :, 0:CH], ltf[:, :, 0:CH],
                                             u2[:, :, 0:CH]),
                      r=(LTF.r, U2.r), w=(RX.r,), tag="crx2")
                pipe = EvacPipe([2, 3, 4, 5])
                for j in range(24):
                    def mmf(bk, j=j):
                        kc.mmgroup(
                            [(CWK[:, t * 3 * D + j * P: t * 3 * D + (j + 1) * P],
                              kx[:, t, 0:CH]) for t in range(NT)],
                            pb[bk][:, 0:CH], PB[bk], r=("CWK", KX.r),
                            tag="ck_mm")

                    def evf(bk, j=j):
                        kc.ve(lambda e, bk=bk: e.tensor_relu(TR[:, 0:CH],
                                                             pb[bk][:, 0:CH]),
                              r=(PB[bk],), w=(TR.r,), tag="ck_rl")
                        kc.act(kkv[:, j, 0:CH], TR[:, 0:CH], AF.Square,
                               r=(TR.r,), w=(KK.r,), tag="ck_sq")
                    pipe.step(mmf, evf)
                for j in range(NT):
                    def mmf(bk, j=j):
                        kc.mmgroup(
                            [(CWR[:, t * D + j * P: t * D + (j + 1) * P],
                              rx[:, t, 0:CH]) for t in range(NT)],
                            pb[bk][:, 0:CH], PB[bk], r=("CWR", RX.r),
                            tag="cr_mm")

                    def evf(bk, j=j):
                        kc.act(rgv[:, j, 0:CH], pb[bk][:, 0:CH], AF.Sigmoid,
                               r=(PB[bk],), w=(RG.r,), tag="cr_ev")
                    pipe.step(mmf, evf)
                pipe.flush()
                for j in range(NT):
                    def mmf(bk, j=j):
                        kc.mmgroup(
                            [(CWV[:, f * D + j * P: f * D + (j + 1) * P],
                              kkv[:, f, 0:CH]) for f in range(24)],
                            pb[bk][:, 0:CH], PB[bk], r=("CWV", KK.r),
                            tag="cv_mm")

                    def evf(bk, j=j):
                        kc.ve(lambda e, j=j, bk=bk: e.tensor_mul(
                            ltf[:, j, 0:CH], pb[bk][:, 0:CH], rgv[:, j, 0:CH]),
                              r=(PB[bk], RG.r), w=(LTF.r,), tag="cv_gate")
                        kc.ve(lambda e, j=j: e.tensor_add(
                            fxh[:, j, 0:CH], fxh[:, j, 0:CH], ltf[:, j, 0:CH]),
                              r=(LTF.r, FXH.r), w=(FXH.r,), tag="chup")
                    pipe.step(mmf, evf)
                pipe.flush()
                if c == NCH - 1:
                    kc.ve(lambda e: e.tensor_copy(
                        HLAST[:, 0:8], fxh[:, :, CH - 1:CH].squeeze(2)),
                          r=(FXH.r,), w=("HLAST",), tag="hlast")
                st_dmaj(SYNC, hDv, f"hD:{c}", FXH, CH, c0, "cm_sth")
        sch.barrier(f"cm{l}")

    def phase_x1():
        with contextlib.ExitStack() as pes:
            FXH = S("xFXH", [P, NT * CH], F32, pes)
            FX2 = S("xFX2", [P, NT * CH], F32, pes)
            LXB = S("xLXB", [P, NT * CH], BF16, pes)
            LSQ = S("xLSQ", [P, NT * CH], BF16, pes)
            LTF = S("xLTF", [P, NT * CH], F32, pes)
            XFB = S("xXFB", [P, NT * CH], BF16, pes)
            SROW = S("SROW", [P, 4 * CH], F32, pes)
            ln = LN(kc, CONE[:, :], LXB, LSQ, SROW, LTF,
                    (PB[0], pb[0]), (PB[1], pb[1]), CH, epsc)
            fxh = v3(FXH, CH)
            fx2 = v3(FX2, CH)
            xfb = v3(XFB, CH)
            for c in range(NCH):
                c0 = c * CH
                ld_dmaj(SYNC, FXH, CH, hDv, f"hD:{c}", c0, "x1_ldh")
                ld_dmaj(SYNC, FX2, CH, x0v, "x0", c0, "x1_ldx0")
                ln.run(fxh[:, :, 0:CH], FXH.r, fxh[:, :, 0:CH], FXH.r,
                       wcol=LNOUTW.unsqueeze(2))
                kc.ve(lambda e: e.tensor_add(fxh[:, :, 0:CH], fxh[:, :, 0:CH],
                                             fx2[:, :, 0:CH]),
                      r=(FXH.r, FX2.r), w=(FXH.r,), tag="x1add")
                st_dmaj(SYNC, x1v, f"x1D:{c}", FXH, CH, c0, "x1_st")
                ln.run(fxh[:, :, 0:CH], FXH.r, xfb[:, :, 0:CH], XFB.r)
                st_dmaj(SYNC, xfv, f"xfD:{c}", XFB, CH, c0, "xf_st")
        sch.barrier("x1")

    def phase_moe():
        with contextlib.ExitStack() as pes:
            WP1 = S("mWP1", [P, NT * DFF], BF16, pes)
            WP2 = S("mWP2", [P, NFT * D], BF16, pes)
            HT = S("mHT", [P, NFT * MCH], BF16, pes)
            XFSs = [S("mXFSa", [P, NT * MCH], BF16, pes),
                    S("mXFSb", [P, NT * MCH], BF16, pes)]
            XFS = XFSs[0]
            ACC = S("mACC", [P, (MCH // P) * D], F32, pes)
            WRM = S("mWRM", [P, NT * NEXP], BF16, pes)
            B1Cs = [S("mB1Ca", [P, NFT], F32, pes),
                    S("mB1Cb", [P, NFT], F32, pes)]
            B2S = S("mB2S", [NEXP, D], BF16, pes)
            RL = S("mRL", [P, 3 * NEXP], F32, pes)
            RWB = S("mRWB", [P, NEXP], BF16, pes)
            BRB = S("mBRB", [P, NEXP], F32, pes)

            ht = HT.t[:, :].rearrange("p (f c) -> p f c", c=MCH)
            xfs = v3(XFS, MCH)
            accv = ACC.t[:, :].rearrange("p (a d) -> p a d", d=D)
            NTT = NTOK // P
            x1sb = HT.t[:, 0:2 * NT * MCH].bitcast(F32)\
                .rearrange("p (t c) -> p t c", c=MCH)
            WTTa = HT.t[0:NEXP, (NFT - NTT // 4) * MCH:NFT * MCH]

            onesrow = PROW[0:1, NEXP:NEXP + P]
            kc.mm(onesrow, PROW[0:1, 0:NEXP], pb[6][:, 0:NEXP],
                  PB[6], r=("PROW",), tag="brb_mm")
            kc.ve(lambda e: e.tensor_copy(BRB[:, :], pb[6][:, 0:NEXP]),
                  r=(PB[6],), w=("BRB",), tag="brb_ev")
            for t in range(NT):
                kc.dma(GPSIMD, WRM[:, t * NEXP:(t + 1) * NEXP],
                       wmWr.ap()[t * P:(t + 1) * P, :], w=("WRM",), tag="ldwr")
            kc.dma(GPSIMD, B2S[:, :], wb2.ap()[:, :], w=("B2S",), tag="ldb2")

            for c in range(NMCH):
                c0 = c * MCH
                for s2 in range(2):
                    t0, t1 = s2 * 4, (s2 + 1) * 4
                    kc.dma(SYNC, xfs[:, t0:t1, 0:MCH], xfv[:, t0:t1, c0:c0 + MCH],
                           r=(f"xfD:{c0 // CH}", f"xfD:{(c0 + MCH - 1) // CH}"),
                           w=(XFS.r,), tag="rxf_ld")
                for ti in range(MCH // P):
                    tt = c * (MCH // P) + ti
                    kc.mmgroup([(xfs[:, t, ti * P:(ti + 1) * P],
                                 WRM[:, t * NEXP:(t + 1) * NEXP])
                                for t in range(NT)],
                               pb[6][:, 0:NEXP], PB[6], r=(XFS.r, "WRM"),
                               tag="rt_mm")
                    kc.ve(lambda e: e.tensor_add(RL[:, 0:NEXP], pb[6][:, 0:NEXP],
                                                 BRB[:, :]),
                          r=(PB[6], "BRB"), w=("mRL",), tag="rt_br")
                    kc.ve(lambda e: e.reduce_max(RTMP[:, 0:1], RL[:, 0:NEXP],
                                                 AX.X),
                          r=("mRL",), w=("RTMP",), tag="rt_m1")
                    kc.ve(lambda e: e.tensor_single_scalar(
                        RL[:, NEXP:2 * NEXP], RL[:, 0:NEXP], RTMP[:, 0:1],
                        ALU.is_ge),
                          r=("mRL", "RTMP"), w=("mRL",), tag="rt_t1")
                    kc.ve(lambda e: e.scalar_tensor_tensor(
                        RL[:, NEXP:2 * NEXP], RL[:, NEXP:2 * NEXP], -1e30,
                        RL[:, 0:NEXP], ALU.mult, ALU.add),
                          r=("mRL",), w=("mRL",), tag="rt_pen")
                    kc.ve(lambda e: e.reduce_max(RTMP[:, 1:2],
                                                 RL[:, NEXP:2 * NEXP], AX.X),
                          r=("mRL",), w=("RTMP",), tag="rt_m2")
                    kc.ve(lambda e: e.tensor_single_scalar(
                        RL[:, NEXP:2 * NEXP], RL[:, 0:NEXP], RTMP[:, 1:2],
                        ALU.is_ge),
                          r=("mRL", "RTMP"), w=("mRL",), tag="rt_keep")
                    kc.ve(lambda e: e.tensor_scalar_mul(RTMP[:, 2:3],
                                                        RTMP[:, 0:1], -1.0),
                          r=("RTMP",), w=("RTMP",), tag="rt_nm1")
                    kc.act(RL[:, 2 * NEXP:3 * NEXP], RL[:, 0:NEXP], AF.Exp,
                           bias=RTMP[:, 2:3], r=("mRL", "RTMP"), w=("mRL",),
                           tag="rt_exp")
                    kc.ve(lambda e: e.tensor_mul(RL[:, 2 * NEXP:3 * NEXP],
                                                 RL[:, 2 * NEXP:3 * NEXP],
                                                 RL[:, NEXP:2 * NEXP]),
                          r=("mRL",), w=("mRL",), tag="rt_w")
                    kc.ve(lambda e: e.reduce_sum(RTMP[:, 3:4],
                                                 RL[:, 2 * NEXP:3 * NEXP], AX.X),
                          r=("mRL",), w=("RTMP",), tag="rt_den")
                    kc.ve(lambda e: e.reciprocal(RTMP[:, 4:5], RTMP[:, 3:4]),
                          r=("RTMP",), w=("RTMP",), tag="rt_rd")
                    kc.ve(lambda e, tt=tt: e.tensor_scalar_mul(
                        WTE[:, tt * NEXP:(tt + 1) * NEXP],
                        RL[:, 2 * NEXP:3 * NEXP], RTMP[:, 4:5]),
                          r=("mRL", "RTMP"), w=("WTE",), tag="rt_fin")
                    kc.ve(lambda e, tt=tt: e.tensor_copy(
                        RWB[:, :], WTE[:, tt * NEXP:(tt + 1) * NEXP]),
                          r=("WTE",), w=("RWB",), tag="rt_cw")
                    kc.transpose(pbT[0:NEXP, 0:P], RWB[:, :], CID[:, :],
                                 "pbT", r=("RWB", "CID"), tag="rt_tr")
                    kc.ve(lambda e, tt=tt: e.tensor_copy(
                        WTTa[0:NEXP, tt * P:(tt + 1) * P], pbT[0:NEXP, 0:P]),
                          r=("pbT",), w=("mHT",), tag="rt_trev")

            for c in range(NMCH):
                c0 = c * MCH
                for s2 in range(2):
                    t0, t1 = s2 * 4, (s2 + 1) * 4
                    kc.dma(SYNC, x1sb[:, t0:t1, 0:MCH],
                           x1v[:, t0:t1, c0:c0 + MCH],
                           r=(f"x1D:{c0 // CH}", f"x1D:{(c0 + MCH - 1) // CH}"),
                           w=("mHT",), tag="mi_ldx1")
                pipe = EvacPipe([0, 1, 2, 3, 4, 5])
                for ti in range(MCH // P):
                    tt = c * (MCH // P) + ti
                    for t in range(NT):
                        def mmf(bk, ti=ti, t=t):
                            kc.transpose(pb[bk][:, 0:P],
                                         x1sb[:, t, ti * P:(ti + 1) * P],
                                         CIDF[:, :], PB[bk], r=("mHT", "CIDF"),
                                         tag="mi_tr")

                        def evf(bk, ti=ti, t=t):
                            kc.ve(lambda e, ti=ti, t=t, bk=bk: e.tensor_copy(
                                accv[:, ti, t * P:(t + 1) * P], pb[bk][:, 0:P]),
                                  r=(PB[bk],), w=("mACC",), tag="mi_ev")
                        pipe.step(mmf, evf)
                    for dh in range(2):
                        def mmf(bk, tt=tt, dh=dh):
                            kc.mm(WTTa[0:NEXP, tt * P:(tt + 1) * P],
                                  B2S[:, dh * 512:(dh + 1) * 512],
                                  pb[bk][:, 0:512], PB[bk], r=("mHT", "B2S"),
                                  tag="mi_b2")

                        def evf(bk, ti=ti, dh=dh):
                            kc.ve(lambda e, ti=ti, dh=dh, bk=bk: e.tensor_add(
                                accv[:, ti, dh * 512:(dh + 1) * 512],
                                accv[:, ti, dh * 512:(dh + 1) * 512],
                                pb[bk][:, 0:512]),
                                  r=(PB[bk],), w=("mACC",), tag="mi_b2ev")
                        pipe.step(mmf, evf)
                pipe.flush()
                kc.dma(SYNC, outv[:, c * (MCH // P):(c + 1) * (MCH // P), :],
                       accv[:, 0:MCH // P, :], r=("mACC",), w=(f"out:{c}",),
                       tag="mi_st")

            pipe = EvacPipe([0, 1, 2, 3, 4, 5])
            for e_i in range(NEXP):
                for fg in range(4):
                    for t in range(NT):
                        kc.dma(GPSIMD,
                               WP1[:, t * DFF + fg * 8 * P:
                                   t * DFF + (fg + 1) * 8 * P],
                               wm1.ap()[e_i, t * P:(t + 1) * P,
                                        fg * 8 * P:(fg + 1) * 8 * P],
                               w=(f"mW1c:{fg}",), tag="ldw1")
                for f in range(NFT):
                    kc.dma(GPSIMD, WP2[:, f * D:(f + 1) * D],
                           wm2.ap()[e_i, f * P:(f + 1) * P, :],
                           w=(f"mW2:{f}",), tag="ldw2")
                B1C = B1Cs[e_i % 2]
                kc.dma(SYNC, B1C[:, :], wb1.ap()[e_i, :, :], w=(B1C.r,),
                       tag="ldb1")
                w2regs = tuple(f"mW2:{f}" for f in range(NFT))
                for c in range(NMCH):
                    c0 = c * MCH
                    XFS = XFSs[(e_i * NMCH + c) % 2]
                    xfs = v3(XFS, MCH)
                    for s2 in range(2):
                        t0, t1 = s2 * 4, (s2 + 1) * 4
                        kc.dma(SYNC, xfs[:, t0:t1, 0:MCH],
                               xfv[:, t0:t1, c0:c0 + MCH],
                               r=(f"xfD:{c0 // CH}",
                                  f"xfD:{(c0 + MCH - 1) // CH}"),
                               w=(XFS.r,), tag="me_xf")
                    kc.dma(SYNC, accv[:, 0:MCH // P, :],
                           outv[:, c * (MCH // P):(c + 1) * (MCH // P), :],
                           r=(f"out:{c}",), w=("mACC",), tag="me_lda")
                    for f in range(NFT):
                        def mmf(bk, f=f, xfs=xfs, XFS=XFS):
                            kc.mmgroup(
                                [(WP1[:, t * DFF + f * P: t * DFF + (f + 1) * P],
                                  xfs[:, t, 0:MCH]) for t in range(NT)],
                                pb[bk][:, 0:MCH], PB[bk],
                                r=(f"mW1c:{f // 8}", XFS.r), tag="me_h")

                        def evf(bk, f=f, B1C=B1C):
                            kc.act(ht[:, f, 0:MCH], pb[bk][:, 0:MCH], AF.Gelu,
                                   bias=B1C[:, f:f + 1], r=(PB[bk], B1C.r),
                                   w=("mHT",), tag="me_gelu")
                        pipe.step(mmf, evf)
                    pipe.flush()
                    for ti in range(MCH // P):
                        tt = c * (MCH // P) + ti
                        for dh in range(2):
                            def mmf(bk, ti=ti, dh=dh):
                                kc.mmgroup(
                                    [(ht[:, f, ti * P:(ti + 1) * P],
                                      WP2[:, f * D + dh * 512:
                                          f * D + (dh + 1) * 512])
                                     for f in range(NFT)],
                                    pb[bk][:, 0:512], PB[bk],
                                    r=w2regs + ("mHT",), tag="me_y")

                            def evf(bk, ti=ti, dh=dh, tt=tt, e_i=e_i):
                                kc.ve(lambda e, ti=ti, dh=dh, bk=bk, tt=tt,
                                      e_i=e_i: e.scalar_tensor_tensor(
                                          accv[:, ti, dh * 512:(dh + 1) * 512],
                                          pb[bk][:, 0:512],
                                          WTE[:, tt * NEXP + e_i:
                                              tt * NEXP + e_i + 1],
                                          accv[:, ti, dh * 512:(dh + 1) * 512],
                                          ALU.mult, ALU.add),
                                      r=(PB[bk], "WTE", "mACC"), w=("mACC",),
                                      tag="me_acc")
                            pipe.step(mmf, evf)
                    pipe.flush()
                    kc.dma(SYNC, outv[:, c * (MCH // P):(c + 1) * (MCH // P), :],
                           accv[:, 0:MCH // P, :], r=("mACC",),
                           w=(f"out:{c}",), tag="me_st")
        sch.barrier("moe")

    for l in range(LAYERS):
        phase_tmA(l)
        phase_tmB(l)
        phase_cm(l)
    phase_x1()
    phase_moe()
    sch.barrier("fin")

    sch.plan()
    if cfg.get("verbose"):
        print("SCHED:", sch.stats(), flush=True)
        print("sbuf bytes remaining:", nc.sbuf_bytes_remaining, flush=True)

    semnames = (["eng_" + e for e in ENGINES]
                + [f"dma_{e}{i}" for e in ENGINES for i in range(Sched.NLANES)]
                + ["cc"])
    sems = {}
    with contextlib.ExitStack() as ses:
        for sn in semnames:
            sems[sn] = ses.enter_context(nc.semaphore(sn))
        with nc.Block() as block:
            sch.emit(block, sems)
    kc.es.close()
    return nc



def _col8(vec):
    return np.ascontiguousarray(vec.reshape(NT, P).T)


def _pack_shared(i):
    f = np.float32
    L = 2
    ppar = np.zeros((P, 200), f)
    names = {"ln1": i["L0_attn_ln1_w"], "ln2": i["L0_attn_ln2_w"],
             "xm": i["L0_attn_x_maa"], "wm": i["L0_attn_w_maa"],
             "km": i["L0_attn_k_maa"], "vm": i["L0_attn_v_maa"],
             "rm": i["L0_attn_r_maa"], "gm": i["L0_attn_g_maa"],
             "ckm": i["L0_attn_ck_maa"], "crm": i["L0_attn_cr_maa"],
             "td": np.asarray(i["L0_attn_time_decay"], f).reshape(L, D),
             "tf": np.asarray(i["L0_attn_time_first"], f).reshape(L, D)}
    order = ["ln1", "ln2", "xm", "wm", "km", "vm", "rm", "gm", "ckm", "crm",
             "td", "tf"]
    for l in range(L):
        for pi, nm in enumerate(order):
            ppar[:, l * 96 + pi * 8:l * 96 + pi * 8 + 8] = \
                _col8(np.asarray(names[nm], f)[l])
    ppar[:, 192:200] = _col8(np.asarray(i["L0_attn_ln_out_w"], f))

    prows = np.zeros((1, 8 + P), f)
    prows[0, 0:8] = np.asarray(i["L0_ffn_br"], f)
    prows[0, 8:8 + P] = 1.0
    pgn = np.zeros((1, 4 * D), f)
    gnw = np.asarray(i["L0_attn_gn_w"], f)
    gnb = np.asarray(i["L0_attn_gn_b"], f)
    for l in range(L):
        pgn[0, 2 * l * D:(2 * l) * D + D] = gnw[l]
        pgn[0, (2 * l + 1) * D:(2 * l + 1) * D + D] = gnb[l]

    tf2 = np.asarray(i["L0_attn_time_first"], f).reshape(L, D)
    u2p = np.zeros((P, 32), f)
    for l in range(L):
        for tp in range(NT):
            blk = tf2[l, tp * P:(tp + 1) * P]
            u2p[0:64, l * 16 + tp * 2 + 0] = blk[0:64]
            u2p[64:128, l * 16 + tp * 2 + 1] = blk[64:128]

    tm1 = np.asarray(i["L0_attn_tm_w1"], f).reshape(L, D, 5, 32)
    wtm1 = np.zeros((L, D, 384), f)
    for j in range(5):
        jp, jh = j // 2, j % 2
        wtm1[:, :, jp * 128 + jh * 64:jp * 128 + jh * 64 + 32] = tm1[:, :, j]
    tm2 = np.asarray(i["L0_attn_tm_w2"], f)
    wtm2 = np.zeros((L, P, 5 * D), f)
    for j in range(5):
        jh = j % 2
        wtm2[:, jh * 64:jh * 64 + 32, j * D:(j + 1) * D] = tm2[:, j]
    td1 = np.asarray(i["L0_attn_td_w1"], f)
    wtd1 = np.zeros((L, D, P), f)
    wtd1[:, :, 0:64] = td1
    td2 = np.asarray(i["L0_attn_td_w2"], f)
    wtd2 = np.zeros((L, P, D), f)
    wtd2[:, 0:64, :] = td2

    b1 = np.asarray(i["L0_ffn_b1"], f)
    wb1 = np.ascontiguousarray(b1.reshape(8, 32, 128).transpose(0, 2, 1))

    shared = {
        "ppar": ppar, "prows": prows, "pgn": pgn, "u2p": u2p,
        "wWr": np.asarray(i["L0_attn_Wr"], f),
        "wWk": np.asarray(i["L0_attn_Wk"], f),
        "wWv": np.asarray(i["L0_attn_Wv"], f),
        "wWo": np.asarray(i["L0_attn_Wo"], f),
        "wWg": np.asarray(i["L0_attn_Wg"], f),
        "wtm1": wtm1, "wtm2": wtm2, "wtd1": wtd1, "wtd2": wtd2,
        "wcWk": np.asarray(i["L0_attn_cWk"], f),
        "wcWv": np.asarray(i["L0_attn_cWv"], f),
        "wcWr": np.asarray(i["L0_attn_cWr"], f),
        "wmWr": np.asarray(i["L0_ffn_Wr"], f),
        "wm1": np.asarray(i["L0_ffn_W1"], f),
        "wb1": wb1,
        "wm2": np.asarray(i["L0_ffn_W2"], f),
        "wb2": np.asarray(i["L0_ffn_b2"], f),
    }
    return shared


def _consts(is_second_half):
    f = np.float32
    c = np.zeros((P, 264), f)
    c[:, 0:P] = np.eye(P, dtype=f)
    ts = np.arange(P)
    c[:, P:2 * P] = (ts[:, None] < ts[None, :]).astype(f)
    c[0:64, 2 * P + 0] = 1.0
    c[64:128, 2 * P + 1] = 1.0
    c[:, 2 * P + 2] = 1.0 if is_second_half else 0.0
    c[:, 2 * P + 3] = 0.0
    c[:, 2 * P + 4] = 1.0
    c[:, 2 * P + 5] = EPS
    c[:, 2 * P + 6] = -1.0
    return c


_NC_CACHE = {}


def _get_nc(cfg_key):
    if cfg_key not in _NC_CACHE:
        cfg = dict(NTOK=4096, CH=256, LAYERS=2, MCH=512, NEXP=8, DFF=4096,
                   verbose=False)
        _NC_CACHE[cfg_key] = build_program(cfg)
    return _NC_CACHE[cfg_key]


def kernel(**inputs):
    x = np.asarray(inputs["L0_attn_x"], np.float32)
    B, T, _ = x.shape
    half = T // 2
    shared = _pack_shared(inputs)
    nc = _get_nc("full")
    in_maps = []
    cA = _consts(False)
    cB = _consts(True)
    for c in range(8):
        b, h = c // 2, c % 2
        m = dict(shared)
        m["x0"] = np.ascontiguousarray(x[b, h * half:(h + 1) * half, :].T)
        m["consts"] = cB if h else cA
        in_maps.append(m)
    res = run_bass_kernel_spmd(nc, in_maps, core_ids=list(range(8)))
    outf = np.empty((B, T, D), np.float32)
    for c in range(8):
        b, h = c // 2, c % 2
        outf[b, h * half:(h + 1) * half, :] = res.results[c]["out"]
    return outf


# version 05
# speedup vs baseline: 1.2431x; 1.0049x over previous
import contextlib
import numpy as np
import ml_dtypes

import concourse.bass as bass
import concourse.mybir as mybir
from concourse.bass_utils import run_bass_kernel_spmd

F32 = mybir.dt.float32
BF16 = mybir.dt.bfloat16
AF = mybir.ActivationFunctionType
ALU = mybir.AluOpType
AX = mybir.AxisListType

TENSOR, VECTOR, SCALAR, GPSIMD, SYNC = "tensor", "vector", "scalar", "gpsimd", "sync"
ENGINES = (TENSOR, VECTOR, SCALAR, GPSIMD, SYNC)

EPS = 1e-5
P = 128
D = 1024
NT = D // P
HS = 64
NH = 16

BF = ml_dtypes.bfloat16



class Op:
    __slots__ = ("idx", "eng", "fn", "reads", "writes", "dma", "mm_out",
                 "marked", "waits", "tag", "coll", "inc_amt", "epos",
                 "semname", "semval", "waitlist", "markable", "lane")

    def __init__(self, idx, eng, fn, reads, writes, dma, mm_out, tag, coll,
                 markable=True):
        self.idx = idx
        self.eng = eng
        self.fn = fn
        self.reads = tuple(reads)
        self.writes = tuple(writes)
        self.dma = dma
        self.mm_out = mm_out
        self.marked = False
        self.waits = []
        self.tag = tag
        self.coll = coll
        self.inc_amt = 0
        self.epos = 0
        self.semname = None
        self.semval = 0
        self.waitlist = []
        self.markable = markable
        self.lane = -1

    def kind(self):
        return (self.dma, self.coll)


class Sched:
    def __init__(self):
        self.ops = []
        self.last_write = {}
        self.readers = {}
        self.pe_mm = {}
        self.drains = []
        self.estreams = {r: [] for r in ENGINES}
        self.floor = {}
        self.dma_counts = {}

    def _new(self, eng, fn, reads, writes, dma, mm_out, tag, coll, markable=True):
        o = Op(len(self.ops), eng, fn, reads, writes, dma, mm_out, tag, coll,
               markable)
        o.epos = len(self.estreams[eng])
        self.estreams[eng].append(o)
        self.ops.append(o)
        return o

    def op(self, eng, fn, r=(), w=(), dma=False, mm_out=None, tag="", coll=False):
        o = self._new(eng, fn, r, w, dma, mm_out, tag, coll)
        if dma or coll:
            o.marked = True
        if dma:
            c = self.dma_counts.get(eng, 0)
            self.dma_counts[eng] = c + 1
            o.lane = c % self.NLANES
        deps = []
        for reg in o.reads:
            p = self.last_write.get(reg)
            if p is not None:
                deps.append(p)
            self.readers.setdefault(reg, []).append(o)
        for reg in o.writes:
            p = self.last_write.get(reg)
            if p is not None:
                deps.append(p)
            for rd in self.readers.get(reg, ()):
                deps.append(rd)
            self.readers[reg] = []
            self.last_write[reg] = o
        if o.eng != TENSOR:
            for reg in o.reads:
                mmop = self.pe_mm.get(reg)
                if mmop is not None:
                    deps.append(self._drain_after(mmop))
        if mm_out is not None:
            self.pe_mm[mm_out] = o
        fl = self.floor.get(eng)
        if fl is not None:
            deps.append(fl)
        self._edges(o, deps)
        return o

    def _drain_after(self, mmop):
        for dop in self.drains:
            if dop.epos > mmop.epos:
                return dop
        dop = self._new(TENSOR, lambda e: e.drain(), (), (), False, None,
                        "drain", False)
        mmop.marked = True
        dop.marked = True
        self.drains.append(dop)
        return dop

    def _edges(self, o, deps):
        best = {}
        for p in deps:
            if p is o or p.idx >= o.idx:
                continue
            key = (p.eng, p.kind(), p.lane)
            cur = best.get(key)
            if cur is None or p.epos > cur.epos:
                best[key] = p
        for (engname, kindk, _lane), p in best.items():
            if kindk != (False, False) or (p.marked and not p.markable):
                o.waits.append(p)
                continue
            stream = self.estreams[engname]
            mop = None
            for cand in stream[p.epos:]:
                if cand.idx >= o.idx:
                    break
                if cand.marked and cand.markable and cand.kind() == kindk:
                    mop = cand
                    break
            if mop is None:
                assert p.markable, f"unmarkable producer {p.tag} for {o.tag}"
                p.marked = True
                mop = p
            o.waits.append(mop)

    def barrier(self, tag="bar"):
        lasts = []
        for e in ENGINES:
            ncomp = 0
            ndma = 0
            ncoll = 0
            for o in reversed(self.estreams[e]):
                if not o.markable:
                    continue
                if o.dma:
                    if ndma < self.NLANES:
                        lasts.append(o)
                        ndma += 1
                elif o.coll:
                    if ncoll < 1:
                        lasts.append(o)
                        ncoll += 1
                else:
                    if ncomp < 1:
                        lasts.append(o)
                        ncomp += 1
                if ncomp and ncoll and ndma >= self.NLANES:
                    break
        outs = []
        for engname in ENGINES:
            o = self._new(engname, lambda e: e.nop(), (), (), False, None,
                          tag, False, markable=False)
            fl = self.floor.get(engname)
            deps = [x for x in lasts if x.eng != engname]
            if fl is not None:
                deps.append(fl)
            self._edges(o, deps)
            o.marked = True
            self.floor[engname] = o
            outs.append(o)
        return outs

    NLANES = 8

    def plan(self):
        counts = {}
        lane_idx = {}
        gate = {}
        for engname in ENGINES:
            for o in self.estreams[engname]:
                if not o.marked:
                    continue
                if o.coll:
                    o.semname, o.inc_amt = "cc", 1
                elif o.dma:
                    o.semname = f"dma_{engname}{o.lane}"
                    o.inc_amt = 16
                else:
                    o.semname, o.inc_amt = "eng_" + engname, 1
                c = counts.get(o.semname, 0) + o.inc_amt
                counts[o.semname] = c
                o.semval = c
                if o.semval > o.inc_amt:
                    gate[o.idx] = (o.semname, o.semval - o.inc_amt)
        for o in self.ops:
            wl = {}
            g = gate.get(o.idx)
            if g is not None:
                wl[g[0]] = g[1]
            for mop in o.waits:
                assert mop.marked, (o.tag, mop.tag)
                k = mop.semname
                if wl.get(k, 0) < mop.semval:
                    wl[k] = mop.semval
            o.waitlist = sorted(wl.items())
        self.sem_totals = counts

    def emit(self, block, sems):
        def mk(engname):
            def run(eng_obj):
                for o in self.estreams[engname]:
                    for semname, val in o.waitlist:
                        eng_obj.wait_ge(sems[semname], val)
                    inst = o.fn(eng_obj)
                    if o.marked:
                        inst.then_inc(sems[o.semname], o.inc_amt)
            return run

        block.tensor(mk(TENSOR))
        block.vector(mk(VECTOR))
        block.scalar(mk(SCALAR))
        block.gpsimd(mk(GPSIMD))
        block.sync(mk(SYNC))

    def stats(self):
        per = {e: len(self.estreams[e]) for e in ENGINES}
        nmark = sum(1 for o in self.ops if o.marked)
        nwait = sum(len(o.waitlist) for o in self.ops)
        return (f"ops={len(self.ops)} per-engine={per} marked={nmark} "
                f"waits={nwait} sems={getattr(self, 'sem_totals', None)}")



class KCtx:

    def __init__(self, nc, sch, cfg):
        self.nc = nc
        self.sch = sch
        self.cfg = cfg
        self.es = contextlib.ExitStack()
        self.dram = {}
        self.pb = []
        self.pbT = None

    def dma(self, q, out_ap, in_ap, r=(), w=(), tag="dma", slow=False):
        return self.sch.op(q, lambda e, o=out_ap, i=in_ap, s=slow:
                           e.dma_start(out=o, in_=i,
                                       allow_slow_non_contiguous=s),
                           r=r, w=w, dma=True, tag=tag)

    def mm(self, lhsT, rhs, out_ap, bank, r=(), start=True, stop=True, tag="mm"):
        return self.sch.op(TENSOR,
                           lambda e, a=lhsT, b=rhs, o=out_ap, s=start, t=stop:
                           e.matmul(o, a, b, start=s, stop=t),
                           r=r, w=(bank,), mm_out=bank, tag=tag)

    def mmgroup(self, pairs, out_ap, bank, r=(), tag="mmg"):
        def fn(e, ps=tuple(pairs), o=out_ap):
            inst = None
            n = len(ps)
            for i, (a, b) in enumerate(ps):
                inst = e.matmul(o, a, b, start=(i == 0), stop=(i == n - 1))
            return inst
        return self.sch.op(TENSOR, fn, r=r, w=(bank,), mm_out=bank, tag=tag)

    def transpose(self, out_ap, in_ap, ident, bank, r=(), tag="tr"):
        return self.sch.op(TENSOR,
                           lambda e, o=out_ap, i=in_ap, d=ident: e.transpose(o, i, d),
                           r=r, w=(bank,), mm_out=bank, tag=tag)

    def act(self, out_ap, in_ap, func, r=(), w=(), bias=0.0, scale=1.0, tag="act"):
        return self.sch.op(SCALAR,
                           lambda e, o=out_ap, i=in_ap, f=func, b=bias, s=scale:
                           e.activation(o, i, f, bias=b, scale=s),
                           r=r, w=w, tag=tag)

    def ve(self, fn, r=(), w=(), tag="ve"):
        return self.sch.op(VECTOR, fn, r=r, w=w, tag=tag)

    def gp(self, fn, r=(), w=(), tag="gp"):
        return self.sch.op(GPSIMD, fn, r=r, w=w, tag=tag)

    def sbuf(self, es, name, shape, dtype):
        return es.enter_context(self.nc.sbuf_tensor(name, shape, dtype))


class EvacPipe:

    def __init__(self, banks, lag=None):
        self.banks = banks
        self.lag = (len(banks) - 1) if lag is None else lag
        self.i = 0
        self.pend = []

    def step(self, mmfn, evfn):
        bk = self.banks[self.i % len(self.banks)]
        self.i += 1
        mmfn(bk)
        self.pend.append((evfn, bk))
        if len(self.pend) > self.lag:
            f, b = self.pend.pop(0)
            f(b)

    def flush(self):
        for f, b in self.pend:
            f(b)
        self.pend = []


def dmajor_view(dram_handle, p=P):
    return dram_handle.ap().rearrange("(t p) n -> p t n", p=p)


def tokmajor_view(dram_handle, p=P):
    return dram_handle.ap().rearrange("(a p) d -> p a d", p=p)



def load_dmajor_chunk(kc, q, sbuf_t, dram_handle, dname, c0, c1, w, tag,
                      nt=NT, split=2):
    v = dmajor_view(dram_handle)
    ops = []
    step = nt // split
    for s in range(split):
        t0, t1 = s * step, (s + 1) * step
        ops.append(kc.dma(q, sbuf_t[:, t0:t1, c0:c1], v[:, t0:t1, c0:c1],
                          r=(dname,) if isinstance(dname, str) else dname,
                          w=(w + f"/{s}",) if split > 1 else (w,), tag=tag))
    return ops


def store_dmajor_chunk(kc, q, dram_handle, dname, sbuf_t, rname, c0, c1,
                       nt=NT, split=2):
    v = dmajor_view(dram_handle)
    step = nt // split
    for s in range(split):
        t0, t1 = s * step, (s + 1) * step
        kc.dma(q, v[:, t0:t1, c0:c1], sbuf_t[:, t0:t1, c0:c1],
               r=(rname,), w=(dname,), tag="st_" + dname)





def dmajor_view(dram_handle, p=P):
    return dram_handle.ap().rearrange("(t p) n -> p t n", p=p)


def tokmajor_view(dram_handle, p=P):
    return dram_handle.ap().rearrange("(a p) d -> p a d", p=p)


class Slot:

    def __init__(self, t, name):
        self.t = t
        self.name = name

    def __getitem__(self, k):
        return self.t[k]

    @property
    def r(self):
        return self.name


def bc_nt(col_ap, C):
    return col_ap.broadcast_to([P, NT, C])


class LN:

    def __init__(self, kc, ones_bf, xb, sq, srow, tf, bank_a, bank_b, C,
                 epsc=None):
        self.kc, self.ones = kc, ones_bf
        self.xb, self.sq, self.srow, self.tf = xb, sq, srow, tf
        self.ba, self.bb = bank_a, bank_b
        self.C = C
        self.epsc = epsc

    def run(self, x_ap, xreg, out_ap, outreg, wcol=None, out2=None, out2reg=None):
        kc, C = self.kc, self.C
        xbs, sqs, srow, tfs = self.xb, self.sq, self.srow, self.tf
        xb = v3(xbs, C)
        sq = v3(sqs, C)
        tf = v3(tfs, C)
        (ban, ba), (bbn, bb) = self.ba, self.bb
        mb = srow[:, 0:C]
        vb = srow[:, C:2 * C]
        tp = srow[:, 2 * C:3 * C]
        rb = srow[:, 3 * C:4 * C]
        kc.ve(lambda e: e.tensor_copy(xb[:, :, 0:C], x_ap), r=(xreg,),
              w=(xbs.r,), tag="ln_cast")
        kc.act(sq[:, :, 0:C], xb[:, :, 0:C], AF.Square, r=(xbs.r,), w=(sqs.r,),
               tag="ln_sq")
        kc.mmgroup([(self.ones, xb[:, t, 0:C]) for t in range(NT)],
                   ba[:, 0:C], ban, r=(xbs.r,), tag="ln_sum")
        kc.mmgroup([(self.ones, sq[:, t, 0:C]) for t in range(NT)],
                   bb[:, 0:C], bbn, r=(sqs.r,), tag="ln_sumsq")
        kc.ve(lambda e: e.tensor_scalar_mul(mb, ba[:, 0:C], 1.0 / D),
              r=(ban,), w=(srow.r,), tag="ln_mean")
        kc.ve(lambda e: e.scalar_tensor_tensor(tp, mb, -1.0, mb, ALU.mult,
                                               ALU.mult),
              r=(srow.r,), w=(srow.r,), tag="ln_msq")
        kc.ve(lambda e: e.scalar_tensor_tensor(vb, bb[:, 0:C], 1.0 / D, tp,
                                               ALU.mult, ALU.add),
              r=(bbn, srow.r), w=(srow.r,), tag="ln_var")
        kc.act(tp, vb, AF.Sqrt, bias=self.epsc, r=(srow.r, "CCOL"),
               w=(srow.r,), tag="ln_sdv")
        kc.ve(lambda e: e.reciprocal(rb, tp), r=(srow.r,), w=(srow.r,),
              tag="ln_rstd")
        mb_bc = mb.unsqueeze(1).broadcast_to([P, NT, C])
        rb_bc = rb.unsqueeze(1).broadcast_to([P, NT, C])
        kc.ve(lambda e: e.tensor_sub(tf[:, :, 0:C], x_ap, mb_bc),
              r=(xreg, srow.r), w=(tfs.r,), tag="ln_sub")
        if wcol is None:
            kc.ve(lambda e: e.tensor_mul(out_ap, tf[:, :, 0:C], rb_bc),
                  r=(tfs.r, srow.r), w=(outreg,), tag="ln_out")
        else:
            kc.ve(lambda e: e.tensor_mul(tf[:, :, 0:C], tf[:, :, 0:C], rb_bc),
                  r=(tfs.r, srow.r), w=(tfs.r,), tag="ln_nrm")
            kc.ve(lambda e: e.tensor_mul(out_ap, tf[:, :, 0:C],
                                         bc_nt(wcol, C)),
                  r=(tfs.r, "PPAR"), w=(outreg,), tag="ln_scale")


def v3(slot, C=256):
    return slot.t[:, :].rearrange("p (t c) -> p t c", c=C)



def build_program(cfg):
    NTOK = cfg["NTOK"]
    CH = cfg["CH"]
    NCH = NTOK // CH
    NSUB = NTOK // P
    LAYERS = cfg["LAYERS"]
    MCH = cfg["MCH"]
    NMCH = NTOK // MCH
    NEXP = cfg["NEXP"]
    DFF = cfg["DFF"]
    NFT = DFF // P
    dbg = "ExternalOutput" if cfg.get("debug") else "Internal"

    nc = bass.Bass("TRN2", target_bir_lowering=False)
    sch = Sched()
    kc = KCtx(nc, sch, cfg)

    ext_in = {}

    def din(name, shape, dt=F32):
        t = nc.dram_tensor(name, list(shape), dt, kind="ExternalInput")
        ext_in[name] = t
        return t

    x0 = din("x0", [D, NTOK])
    consts = din("consts", [P, 264])
    ppar = din("ppar", [P, 2 * 96 + 8])
    prows = din("prows", [1, 8 + P])
    pgn = din("pgn", [1, 4 * D])
    u2p = din("u2p", [P, 32])
    wWr = din("wWr", [2, D, D]); wWk = din("wWk", [2, D, D])
    wWv = din("wWv", [2, D, D]); wWo = din("wWo", [2, D, D])
    wWg = din("wWg", [2, D, D])
    wtm1 = din("wtm1", [2, D, 384])
    wtm2 = din("wtm2", [2, P, 5 * D])
    wtd1 = din("wtd1", [2, D, P])
    wtd2 = din("wtd2", [2, P, D])
    wcWk = din("wcWk", [2, D, 3 * D])
    wcWv = din("wcWv", [2, 3 * D, D])
    wcWr = din("wcWr", [2, D, D])
    wmWr = din("wmWr", [D, NEXP])
    wm1 = din("wm1", [NEXP, D, DFF])
    wb1 = din("wb1", [NEXP, P, NFT])
    wm2 = din("wm2", [NEXP, DFF, D])
    wb2 = din("wb2", [NEXP, D])

    out = nc.dram_tensor("out", [NTOK, D], F32, kind="ExternalOutput")

    hD = nc.dram_tensor("hD", [D, NTOK], F32, kind=dbg)
    x1D = nc.dram_tensor("x1D", [D, NTOK], F32, kind=dbg)
    xfD = nc.dram_tensor("xfD", [D, NTOK], BF16, kind=dbg)
    olocD = nc.dram_tensor("olocD", [NTOK, D], F32, kind="Internal")
    ggD = nc.dram_tensor("ggD", [NTOK, D], BF16, kind="Internal")
    qaD = nc.dram_tensor("qaD", [D, NTOK], BF16, kind="Internal")
    qbD = nc.dram_tensor("qbD", [D, NTOK], BF16, kind="Internal")
    cinU = nc.dram_tensor("cinU", [P, P], F32, kind="Internal")
    coutU = nc.dram_tensor("coutU", [2 * P, P], F32, kind="Internal")
    cinS = nc.dram_tensor("cinS", [P, 512], F32, kind="Internal")
    coutS = nc.dram_tensor("coutS", [2 * P, 512], F32, kind="Internal")

    es0 = kc.es
    pb = [kc.sbuf(es0, f"pb{i}", [P, 512], F32) if False else
          es0.enter_context(nc.psum_tensor(f"pb{i}", [P, 512], F32))
          for i in range(7)]
    pbT = es0.enter_context(nc.psum_tensor("pbT", [P, 1024], BF16))
    PB = [f"pb{i}" for i in range(7)]

    _uniq = [0]

    def S(name, shape, dt, es=es0):
        _uniq[0] += 1
        uname = f"{name}_{_uniq[0]}"
        t = es.enter_context(nc.sbuf_tensor(uname, shape, dt))
        return Slot(t, name)

    CID = S("CID", [P, P], BF16)
    CIDF = S("CIDF", [P, P], F32)
    CTRI = S("CTRI", [P, P], F32)
    CTRI4 = S("CTRI4", [P, 512], F32)
    CONE = S("CONE", [P, P], BF16)
    CCOL = S("CCOL", [P, 8], F32)
    PPAR = S("PPARs", [P, 200], F32)
    PROW = S("PROWs", [1, 8 + P], F32)
    U2S = S("U2S", [P, 32], BF16)
    SST = S("SST", [P, 512], F32)
    SBFS = S("SBFS", [P, 512], BF16)
    ACUM = S("ACUM", [P, 8], F32)
    ACUMC = S("ACUMC", [P, 16], F32)
    ACTAB = S("ACTAB", [P, 8 * NSUB], F32)
    BPREV = S("BPREV", [P, 8], F32)
    CCU = S("CCU", [P, 8], F32)
    CCG2 = S("CCG2", [P, 512], F32)
    GST = S("GST", [P, 80], F32)
    RTMP = S("RTMP", [P, 16], F32)
    WTE = S("WTE", [P, (NTOK // P) * 8], F32)
    UB2 = S("UB2", [P, 16], BF16)
    HLAST = S("HLAST", [P, 8], F32)

    PAR_IDX = ["ln1", "ln2", "xm", "wm", "km", "vm", "rm", "gm", "ckm", "crm",
               "td", "tf"]

    def par(l, name):
        i = PAR_IDX.index(name)
        base = l * 96 + i * 8
        return PPAR[:, base:base + 8]

    def par_bc(l, name, C):
        return par(l, name).unsqueeze(2).broadcast_to([P, NT, C])

    def pcol(l, name, t):
        i = PAR_IDX.index(name)
        return PPAR[:, l * 96 + i * 8 + t: l * 96 + i * 8 + t + 1]

    LNOUTW = PPAR[:, 192:200]
    maskA = CCOL[:, 0:1]
    maskB = CCOL[:, 1:2]
    halfc = CCOL[:, 2:3]
    zeroc = CCOL[:, 3:4]
    epsc = CCOL[:, 5:6]
    neg1c = CCOL[:, 6:7]

    kc.dma(GPSIMD, CID[:, :], consts[:, 0:P], w=("CID",), tag="ld_cid")
    kc.dma(SYNC, CIDF[:, :], consts[:, 0:P], w=("CIDF",), tag="ld_cidf")
    kc.dma(SYNC, CTRI[:, :], consts[:, P:2 * P], w=("CTRI",), tag="ld_tri")
    kc.ve(lambda e: e.tensor_copy(
        CTRI4.t[:, :].rearrange("p (q c) -> p q c", c=P),
        CTRI[:, :].unsqueeze(1).broadcast_to([P, 4, P])),
          r=("CTRI",), w=("CTRI4",), tag="ctri4")
    kc.ve(lambda e: e.memset(CONE[:, :], 1.0), w=("CONE",), tag="ones")
    kc.dma(SYNC, CCOL[:, :], consts[:, 2 * P:2 * P + 8], w=("CCOL",), tag="ld_ccol")
    kc.dma(SYNC, PPAR[:, :], ppar[:, :], w=("PPAR",), tag="ld_par")
    kc.dma(SYNC, PROW[:, :], prows[:, :], w=("PROW",), tag="ld_prow")
    kc.dma(GPSIMD, U2S[:, :], u2p.ap(), w=("U2S",), tag="ld_u2")
    kc.ve(lambda e: e.memset(SST[:, :], 0.0), w=("SST",), tag="s0")
    kc.ve(lambda e: e.memset(SBFS[:, :], 0.0), w=("SBFS",), tag="sb0")
    kc.ve(lambda e: e.memset(ACUM[:, :], 1.0), w=("ACUM",), tag="ac0")
    kc.ve(lambda e: e.memset(CCG2[:, 0:P], 0.0), w=("CCG2",), tag="zsrow")
    kc.dma(GPSIMD, cinU.ap()[:, 8:P], CCG2[:, 8:P], r=("CCG2",), w=("cinU",),
           tag="zcinu")

    hDv = dmajor_view(hD)
    x0v = dmajor_view(x0)
    x1v = dmajor_view(x1D)
    xfv = dmajor_view(xfD)
    qav = dmajor_view(qaD)
    qbv = dmajor_view(qbD)
    olv = tokmajor_view(olocD)
    ggv = tokmajor_view(ggD)
    outv = tokmajor_view(out)

    def ld_dmaj(q, slot, C, view, dreg, c0, tag, f32=True):
        vv = v3(slot, C)
        for s2 in range(2):
            t0, t1 = s2 * 4, (s2 + 1) * 4
            kc.dma(q, vv[:, t0:t1, 0:C], view[:, t0:t1, c0:c0 + C],
                   r=(dreg,), w=(slot.r,), tag=tag)

    def st_dmaj(q, view, dreg, slot, C, c0, tag):
        vv = v3(slot, C)
        for s2 in range(2):
            t0, t1 = s2 * 4, (s2 + 1) * 4
            kc.dma(q, view[:, t0:t1, c0:c0 + C], vv[:, t0:t1, 0:C],
                   r=(slot.r,), w=(dreg,), tag=tag)

    def boundary_u(l, lnw_name, ph, UBB, tag):
        kc.ve(lambda e: e.tensor_copy(UBB[:, 0:8], HLAST[:, 0:8]),
              r=("HLAST",), w=(UBB.r,), tag=tag + "_ld")
        ubb = UBB[:, 0:8]
        ubbf = UBB[:, 8:16]
        kc.ve(lambda e: e.tensor_copy(UB2[:, 0:8], ubb), r=(UBB.r,),
              w=(UB2.r,), tag=tag + "_cast")
        kc.act(UB2[:, 8:16], UB2[:, 0:8], AF.Square, r=(UB2.r,), w=(UB2.r,),
               tag=tag + "_sq")
        kc.mmgroup([(CONE[:, :], UB2[:, t:t + 1]) for t in range(NT)],
                   pb[0][:, 0:1], PB[0], r=(UB2.r, "CONE"), tag=tag + "_sum")
        kc.mmgroup([(CONE[:, :], UB2[:, 8 + t:9 + t]) for t in range(NT)],
                   pb[1][:, 0:1], PB[1], r=(UB2.r, "CONE"), tag=tag + "_sumsq")
        kc.ve(lambda e: e.tensor_scalar_mul(RTMP[:, 0:1], pb[0][:, 0:1], 1.0 / D),
              r=(PB[0],), w=(RTMP.r,), tag=tag + "_m")
        kc.ve(lambda e: e.scalar_tensor_tensor(RTMP[:, 1:2], RTMP[:, 0:1], -1.0,
                                               RTMP[:, 0:1], ALU.mult, ALU.mult),
              r=(RTMP.r,), w=(RTMP.r,), tag=tag + "_msq")
        kc.ve(lambda e: e.scalar_tensor_tensor(RTMP[:, 2:3], pb[1][:, 0:1],
                                               1.0 / D, RTMP[:, 1:2], ALU.mult,
                                               ALU.add),
              r=(PB[1], RTMP.r), w=(RTMP.r,), tag=tag + "_var")
        kc.act(RTMP[:, 3:4], RTMP[:, 2:3], AF.Sqrt, bias=epsc,
               r=(RTMP.r, "CCOL"), w=(RTMP.r,), tag=tag + "_sdv")
        kc.ve(lambda e: e.reciprocal(RTMP[:, 4:5], RTMP[:, 3:4]),
              r=(RTMP.r,), w=(RTMP.r,), tag=tag + "_rstd")
        kc.ve(lambda e: e.tensor_scalar_sub(UBB[:, 0:8], ubb, RTMP[:, 0:1]),
              r=(UBB.r, RTMP.r), w=(UBB.r,), tag=tag + "_sub")
        kc.ve(lambda e: e.tensor_scalar_mul(UBB[:, 0:8], ubb, RTMP[:, 4:5]),
              r=(UBB.r, RTMP.r), w=(UBB.r,), tag=tag + "_nrm")
        if lnw_name is not None:
            kc.ve(lambda e: e.tensor_mul(UBB[:, 0:8], ubb, par(l, lnw_name)),
                  r=(UBB.r, "PPAR"), w=(UBB.r,), tag=tag + "_w")
        kc.dma(SYNC, cinU.ap()[:, 0:8], UBB[:, 0:8], r=(UBB.r,), w=("cinU",),
               tag=tag + "_stg")
        kc.sch.op(GPSIMD,
                  lambda e: e.collective_compute(
                      "AllGather", ALU.bypass,
                      replica_groups=[[0, 1], [2, 3], [4, 5], [6, 7]],
                      ins=[cinU.ap()], outs=[coutU.ap()]),
                  r=("cinU",), w=("coutU",), coll=True, tag=tag + "_cc")
        kc.dma(SYNC, CCU[:, 0:8], coutU.ap()[0:P, 0:8], r=("coutU",),
               w=("CCU",), tag=tag + "_g")
        kc.ve(lambda e: e.tensor_scalar_mul(BPREV[:, 0:8], CCU[:, 0:8], halfc),
              r=("CCU", "CCOL"), w=("BPREV",), tag=tag + "_bp")

    with contextlib.ExitStack() as pes:
        FXH = S("p0FXH", [P, NT * CH], F32, pes)
        LXB = S("p0LXB", [P, NT * CH], BF16, pes)
        LSQ = S("p0LSQ", [P, NT * CH], BF16, pes)
        LTF = S("p0LTF", [P, NT * CH], F32, pes)
        SROW = S("SROW", [P, 4 * CH], F32, pes)
        ln = LN(kc, CONE[:, :], LXB, LSQ, SROW, LTF,
                (PB[0], pb[0]), (PB[1], pb[1]), CH, epsc)
        for c in range(NCH):
            c0 = c * CH
            ld_dmaj(SYNC, FXH, CH, x0v, "x0", c0, "p0_ld")
            ln.run(v3(FXH, CH)[:, :, 0:CH], FXH.r,
                   v3(FXH, CH)[:, :, 0:CH], FXH.r)
            if c == NCH - 1:
                kc.ve(lambda e: e.tensor_copy(
                    HLAST[:, 0:8], v3(FXH, CH)[:, :, CH - 1:CH].squeeze(2)),
                      r=(FXH.r,), w=("HLAST",), tag="hlast")
            st_dmaj(SYNC, hDv, f"hD:{c}", FXH, CH, c0, "p0_st")
    sch.barrier("p0")

    def phase_tmA(l):
        with contextlib.ExitStack() as pes:
            WPA = S("WPA", [P, 4 * NT * D], BF16, pes)
            TM1 = S("TM1", [P, NT * 384], BF16, pes)
            TM2 = S("TM2", [P, 5 * D], BF16, pes)
            TD1 = S("TD1", [P, NT * P], BF16, pes)
            TD2 = S("TD2", [P, D], BF16, pes)
            FXH = S("FXH", [P, NT * CH], F32, pes)
            LOGW = S("LOGW", [P, NT * CH], F32, pes)
            PLAX = S("PLAX", [P, NT * CH], F32, pes)
            LTF = S("LTF", [P, NT * CH], F32, pes)
            LXB = S("LXB", [P, NT * CH], BF16, pes)
            LSQ = S("LSQ", [P, NT * CH], BF16, pes)
            U = S("U", [P, NT * CH], BF16, pes)
            SX = S("SX", [P, NT * CH], BF16, pes)
            EEXC = S("EEXC", [P, NT * CH], BF16, pes)
            EINC = S("EINC", [P, NT * CH], BF16, pes)
            KT = S("KT", [P, NT * CH], BF16, pes)
            RT = S("RT", [P, NT * CH], BF16, pes)
            QA = S("QA", [P, NT * CH], BF16, pes)
            QB = S("QB", [P, NT * CH], BF16, pes)
            RK = S("RK", [P, NT * CH], BF16, pes)
            KH = S("KH", [P, NT * CH], BF16, pes)
            PX = S("PXs", [P, NT * CH], BF16, pes)
            TA = S("TAs", [P, NT * CH], BF16, pes)
            TMPB = S("TMPB", [P, 3 * CH], BF16, pes)
            VT = S("VTs", [P, 2 * D], BF16, pes)
            GT = S("GTs", [P, 2 * D], BF16, pes)
            UBB = S("UBB", [P, 16], F32, pes)

            def wslab(i, t):
                return WPA[:, (i * NT + t) * D:(i * NT + t) * D + D]

            for i, wt in enumerate((wWr, wWk, wWv, wWg)):
                for t in range(NT):
                    kc.dma(GPSIMD, wslab(i, t), wt.ap()[l, t * P:(t + 1) * P, :],
                           w=(f"WPA{i}",), tag=f"ldw{i}")
            for t in range(NT):
                kc.dma(GPSIMD, TM1[:, t * 384:(t + 1) * 384],
                       wtm1.ap()[l, t * P:(t + 1) * P, :], w=("TM1",), tag="ldtm1")
                kc.dma(GPSIMD, TD1[:, t * P:(t + 1) * P],
                       wtd1.ap()[l, t * P:(t + 1) * P, :], w=("TD1",), tag="ldtd1")
            kc.dma(GPSIMD, TM2[:, :], wtm2.ap()[l, :, :], w=("TM2",), tag="ldtm2")
            kc.dma(GPSIMD, TD2[:, :], wtd2.ap()[l, :, :], w=("TD2",), tag="ldtd2")

            boundary_u(l, "ln1", None, UBB, f"ubA{l}")
            kc.ve(lambda e: e.memset(SST[:, :], 0.0), w=("SST",), tag="s0")
            kc.ve(lambda e: e.memset(SBFS[:, :], 0.0), w=("SBFS",), tag="sb0")
            kc.ve(lambda e: e.memset(ACUM[:, :], 1.0), w=("ACUM",), tag="ac0")

            SROW = S("SROW", [P, 4 * CH], F32, pes)
            ln = LN(kc, CONE[:, :], LXB, LSQ, SROW, LTF,
                    (PB[0], pb[0]), (PB[1], pb[1]), CH, epsc)
            fxh = v3(FXH, CH)
            uu = v3(U, CH)
            sx = v3(SX, CH)
            logw = v3(LOGW, CH)
            plax = v3(PLAX, CH)
            eexc = v3(EEXC, CH)
            einc = v3(EINC, CH)
            ktv = v3(KT, CH)
            rtv = v3(RT, CH)
            qa = v3(QA, CH)
            qb = v3(QB, CH)
            rk = v3(RK, CH)
            kh = v3(KH, CH)
            pxv = v3(PX, CH)
            tav = v3(TA, CH)
            ltf = v3(LTF, CH)
            vtv = VT.t[:, :].rearrange("p (s d) -> p s d", d=D)
            gtv = GT.t[:, :].rearrange("p (s d) -> p s d", d=D)

            for c in range(NCH):
                c0 = c * CH
                ld_dmaj(SYNC, FXH, CH, hDv, f"hD:{c}", c0, "tmA_ldh")
                ln.run(fxh[:, :, 0:CH], FXH.r, uu[:, :, 0:CH], U.r,
                       wcol=par(l, "ln1").unsqueeze(2))
                kc.ve(lambda e: e.tensor_sub(sx[:, :, 1:CH], uu[:, :, 0:CH - 1],
                                             uu[:, :, 1:CH]),
                      r=(U.r,), w=(SX.r,), tag="sx")
                kc.ve(lambda e: e.tensor_sub(sx[:, :, 0:1],
                                             BPREV[:, 0:8].unsqueeze(2),
                                             uu[:, :, 0:1]),
                      r=(U.r, "BPREV"), w=(SX.r,), tag="sx0")
                kc.ve(lambda e: e.tensor_copy(BPREV[:, 0:8],
                                              uu[:, :, CH - 1:CH].squeeze(2)),
                      r=(U.r,), w=("BPREV",), tag="bpup")
                kc.ve(lambda e: e.tensor_mul(pxv[:, :, 0:CH], sx[:, :, 0:CH],
                                             par_bc(l, "xm", CH)),
                      r=(SX.r, "PPAR"), w=(PX.r,), tag="xxx1")
                kc.ve(lambda e: e.tensor_add(pxv[:, :, 0:CH], pxv[:, :, 0:CH],
                                             uu[:, :, 0:CH]),
                      r=(PX.r, U.r), w=(PX.r,), tag="xxx2")
                pipe = EvacPipe([2, 3, 4, 5])
                for jp in range(3):
                    def mmf(bk, jp=jp):
                        kc.mmgroup(
                            [(TM1[:, t * 384 + jp * P: t * 384 + (jp + 1) * P],
                              pxv[:, t, 0:CH]) for t in range(NT)],
                            pb[bk][:, 0:CH], PB[bk], r=("TM1", PX.r),
                            tag="tmp_mm")

                    def evf(bk, jp=jp):
                        kc.act(TMPB[:, jp * CH:(jp + 1) * CH], pb[bk][:, 0:CH],
                               AF.Tanh, r=(PB[bk],), w=("TMPB",), tag="tmp_tanh")
                    pipe.step(mmf, evf)
                pipe.flush()
                MAAN = ["wm", "km", "vm", "rm", "gm"]
                for Pi in range(5):
                    for j in range(NT):
                        def mmf(bk, j=j, Pi=Pi):
                            kc.mm(TM2[:, Pi * D + j * P: Pi * D + (j + 1) * P],
                                  TMPB[:, (Pi // 2) * CH:(Pi // 2 + 1) * CH],
                                  pb[bk][:, 0:CH], PB[bk], r=("TM2", "TMPB"),
                                  tag="mix_mm")

                        def evf(bk, j=j, Pi=Pi):
                            kc.ve(lambda e, j=j, bk=bk, Pi=Pi:
                                  e.scalar_tensor_tensor(
                                      ltf[:, j, 0:CH], pb[bk][:, 0:CH],
                                      pcol(l, MAAN[Pi], j), sx[:, j, 0:CH],
                                      ALU.add, ALU.mult),
                                  r=(PB[bk], "PPAR", SX.r), w=(LTF.r,),
                                  tag="pxa")
                            kc.ve(lambda e, j=j: e.tensor_add(
                                pxv[:, j, 0:CH], ltf[:, j, 0:CH],
                                uu[:, j, 0:CH]),
                                  r=(LTF.r, U.r), w=(PX.r,), tag="pxb")
                        pipe.step(mmf, evf)
                    pipe.flush()
                    if Pi == 0:
                        def mmf(bk):
                            kc.mmgroup([(TD1[:, t * P:(t + 1) * P],
                                         pxv[:, t, 0:CH]) for t in range(NT)],
                                       pb[bk][:, 0:CH], PB[bk],
                                       r=("TD1", PX.r), tag="td1_mm")

                        def evf(bk):
                            kc.act(TA[:, 0:CH], pb[bk][:, 0:CH], AF.Tanh,
                                   r=(PB[bk],), w=(TA.r,), tag="td1_tanh")
                        pipe.step(mmf, evf)
                        pipe.flush()
                        for j in range(NT):
                            def mmf(bk, j=j):
                                kc.mm(TD2[:, j * P:(j + 1) * P], TA[:, 0:CH],
                                      pb[bk][:, 0:CH], PB[bk],
                                      r=("TD2", TA.r), tag="td2_mm")

                            def evf(bk, j=j):
                                kc.act(logw[:, j, 0:CH], pb[bk][:, 0:CH],
                                       AF.Exp, bias=pcol(l, "td", j),
                                       r=(PB[bk], "PPAR"), w=(LOGW.r,),
                                       tag="logw")
                            pipe.step(mmf, evf)
                    elif Pi == 1:
                        for j in range(NT):
                            def mmf(bk, j=j):
                                kc.mmgroup(
                                    [(wslab(1, t)[:, j * P:(j + 1) * P],
                                      pxv[:, t, 0:CH]) for t in range(NT)],
                                    pb[bk][:, 0:CH], PB[bk],
                                    r=("WPA1", PX.r), tag="k_mm")

                            def evf(bk, j=j):
                                kc.sch.op(SCALAR,
                                          lambda e, j=j, bk=bk: e.copy(
                                              ktv[:, j, 0:CH], pb[bk][:, 0:CH]),
                                          r=(PB[bk],), w=(KT.r,), tag="k_ev")
                            pipe.step(mmf, evf)
                    elif Pi == 2:
                        for st in range(2):
                            for nh in range(2):
                                def mmf(bk, st=st, nh=nh):
                                    kc.mmgroup(
                                        [(pxv[:, t, st * P:(st + 1) * P],
                                          wslab(2, t)[:, nh * 512:(nh + 1) * 512])
                                         for t in range(NT)],
                                        pb[bk][:, 0:512], PB[bk],
                                        r=("WPA2", PX.r), tag="v_mm")

                                def evf(bk, st=st, nh=nh):
                                    kc.sch.op(
                                        SCALAR,
                                        lambda e, st=st, nh=nh, bk=bk: e.copy(
                                            vtv[:, st, nh * 512:(nh + 1) * 512],
                                            pb[bk][:, 0:512]),
                                        r=(PB[bk],), w=(VT.r,), tag="v_ev")
                                pipe.step(mmf, evf)
                    elif Pi == 3:
                        for j in range(NT):
                            def mmf(bk, j=j):
                                kc.mmgroup(
                                    [(wslab(0, t)[:, j * P:(j + 1) * P],
                                      pxv[:, t, 0:CH]) for t in range(NT)],
                                    pb[bk][:, 0:CH], PB[bk],
                                    r=("WPA0", PX.r), tag="r_mm")

                            def evf(bk, j=j):
                                kc.act(rtv[:, j, 0:CH], pb[bk][:, 0:CH],
                                       AF.Sigmoid, r=(PB[bk],), w=(RT.r,),
                                       tag="r_ev")
                            pipe.step(mmf, evf)
                    elif Pi == 4:
                        for st in range(2):
                            for nh in range(2):
                                def mmf(bk, st=st, nh=nh):
                                    kc.mmgroup(
                                        [(pxv[:, t, st * P:(st + 1) * P],
                                          wslab(3, t)[:, nh * 512:(nh + 1) * 512])
                                         for t in range(NT)],
                                        pb[bk][:, 0:512], PB[bk],
                                        r=("WPA3", PX.r), tag="g_mm")

                                def evf(bk, st=st, nh=nh):
                                    kc.act(gtv[:, st, nh * 512:(nh + 1) * 512],
                                           pb[bk][:, 0:512], AF.Silu,
                                           r=(PB[bk],), w=(GT.r,), tag="g_ev")
                                pipe.step(mmf, evf)
                pipe.flush()
                for st in range(2):
                    kc.dma(SYNC, ggv[:, c * 2 + st, :], gtv[:, st, :],
                           r=(GT.r,), w=(f"ggD:{c}",), tag="g_sp")
                for s in range(2):
                    w0 = s * P
                    kc.ve(lambda e, w0=w0: e.memset(plax[:, :, w0:w0 + 1], 0.0),
                          w=(PLAX.r,), tag="plx0")
                    for tp in range(NT):
                        kc.ve(lambda e, w0=w0, tp=tp: e.tensor_tensor_scan(
                            plax[:, tp, w0 + 1:w0 + P],
                            logw[:, tp, w0:w0 + P - 1],
                            zeroc.broadcast_to([P, P - 1]),
                            0.0, ALU.add, ALU.add),
                              r=(LOGW.r, "CCOL"), w=(PLAX.r,), tag="scan")
                kc.ve(lambda e: e.tensor_add(logw[:, :, 0:CH], plax[:, :, 0:CH],
                                             logw[:, :, 0:CH]),
                      r=(PLAX.r, LOGW.r), w=(LOGW.r,), tag="plainc")
                for s in range(2):
                    w1 = s * P + P - 1
                    kc.act(ACUMC[:, s * 8:(s + 1) * 8].unsqueeze(2),
                           logw[:, :, w1:w1 + 1], AF.Exp, scale=neg1c,
                           r=(LOGW.r, "CCOL"), w=("ACUMC",), tag="aC")
                kc.act(eexc[:, :, 0:CH], plax[:, :, 0:CH], AF.Exp, scale=neg1c,
                       r=(PLAX.r, "CCOL"), w=(EEXC.r,), tag="eexc")
                kc.act(einc[:, :, 0:CH], logw[:, :, 0:CH], AF.Exp,
                       r=(LOGW.r,), w=(EINC.r,), tag="einc")
                kc.ve(lambda e: e.scalar_tensor_tensor(
                    v3(LXB, CH)[:, :, 0:CH], einc[:, :, 0:CH], maskA,
                    ktv[:, :, 0:CH], ALU.mult, ALU.mult),
                      r=(EINC.r, "CCOL", KT.r), w=(LXB.r,), tag="ka")
                kc.ve(lambda e: e.scalar_tensor_tensor(
                    v3(LSQ, CH)[:, :, 0:CH], einc[:, :, 0:CH], maskB,
                    ktv[:, :, 0:CH], ALU.mult, ALU.mult),
                      r=(EINC.r, "CCOL", KT.r), w=(LSQ.r,), tag="kb")
                kc.ve(lambda e: e.scalar_tensor_tensor(
                    qa[:, :, 0:CH], eexc[:, :, 0:CH], maskA,
                    rtv[:, :, 0:CH], ALU.mult, ALU.mult),
                      r=(EEXC.r, "CCOL", RT.r), w=(QA.r,), tag="qa")
                kc.ve(lambda e: e.scalar_tensor_tensor(
                    qb[:, :, 0:CH], eexc[:, :, 0:CH], maskB,
                    rtv[:, :, 0:CH], ALU.mult, ALU.mult),
                      r=(EEXC.r, "CCOL", RT.r), w=(QB.r,), tag="qb")
                kc.ve(lambda e: e.tensor_mul(rk[:, :, 0:CH], rtv[:, :, 0:CH],
                                             ktv[:, :, 0:CH]),
                      r=(RT.r, KT.r), w=(RK.r,), tag="rk")
                for s in range(2):
                    w0 = s * P
                    kc.ve(lambda e, s=s, w0=w0: e.tensor_mul(
                        tav[:, :, w0:w0 + P], einc[:, :, w0:w0 + P],
                        ACUMC[:, s * 8:(s + 1) * 8].unsqueeze(2)
                        .broadcast_to([P, NT, P])),
                          r=(EINC.r, "ACUMC"), w=(TA.r,), tag="est")
                    kc.ve(lambda e, w0=w0: e.tensor_mul(
                        kh[:, :, w0:w0 + P], ktv[:, :, w0:w0 + P],
                        tav[:, :, w0:w0 + P]),
                          r=(KT.r, TA.r), w=(KH.r,), tag="khat")
                for s2 in range(2):
                    t0, t1 = s2 * 4, (s2 + 1) * 4
                    kc.dma(SYNC, qav[:, t0:t1, c0:c0 + CH], qa[:, t0:t1, 0:CH],
                           r=(QA.r,), w=(f"qaD:{c}",), tag="qa_sp")
                    kc.dma(SYNC, qbv[:, t0:t1, c0:c0 + CH], qb[:, t0:t1, 0:CH],
                           r=(QB.r,), w=(f"qbD:{c}",), tag="qb_sp")
                for s in range(2):
                    sub = c * 2 + s
                    kc.ve(lambda e, sub=sub: e.tensor_copy(
                        ACTAB[:, sub * 8:(sub + 1) * 8], ACUM[:, 0:8]),
                          r=("ACUM",), w=("ACTAB",), tag="acst")
                    kc.ve(lambda e, s=s: e.tensor_mul(
                        ACUM[:, 0:8], ACUM[:, 0:8], ACUMC[:, s * 8:(s + 1) * 8]),
                          r=("ACUM", "ACUMC"), w=("ACUM",), tag="acup")
                sv = SST.t[:, :].rearrange("p (t v) -> p t v", v=64)
                pxc = PX.t[:, 0:1024].rearrange("p (c v) -> p c v", v=64)
                vt4 = VT.t[:, :].rearrange("p (s c v) -> p s c v", s=2, v=64)
                for s in range(2):
                    w0 = s * P
                    for tp in range(NT):
                        b, q = tp // 4, tp % 4
                        kc.mm(v3(LXB, CH)[:, tp, w0:w0 + P],
                              qa[:, tp, w0:w0 + P],
                              pb[b][:, q * P:(q + 1) * P], PB[b],
                              r=(LXB.r, QA.r), tag="ata")
                        kc.mm(v3(LSQ, CH)[:, tp, w0:w0 + P],
                              qb[:, tp, w0:w0 + P],
                              pb[2 + b][:, q * P:(q + 1) * P], PB[2 + b],
                              r=(LSQ.r, QB.r), tag="atb")
                        kc.mm(rk[:, tp, w0:w0 + P],
                              U2S[:, l * 16 + tp * 2: l * 16 + tp * 2 + 2],
                              pb[6][:, s * 64 + tp * 2: s * 64 + tp * 2 + 2],
                              PB[6], r=(RK.r, "U2S"), tag="c2")
                    for mb in range(4):
                        kc.ve(lambda e, mb=mb: e.tensor_mul(
                            TA[:, mb * 512:(mb + 1) * 512],
                            pb[mb][:, 0:512], CTRI4[:, :]),
                              r=(PB[mb], "CTRI4"), w=(TA.r,), tag="mstr")
                    kc.ve(lambda e, s=s: e.tensor_mul(
                        pxc, vt4[:, s, :, :],
                        pb[6][:, s * 64:s * 64 + 16].unsqueeze(2)
                        .broadcast_to([P, 16, 64])),
                          r=(VT.r, PB[6]), w=(PX.r,), tag="vc")
                    for tp in range(NT):
                        b, q = tp // 4, tp % 4
                        for h, qv in enumerate((qa, qb)):
                            kc.mmgroup(
                                [(TA[:, h * 1024 + tp * P:
                                     h * 1024 + (tp + 1) * P],
                                  vtv[:, s, tp * P + h * 64:
                                      tp * P + (h + 1) * 64]),
                                 (qv[:, tp, w0:w0 + P],
                                  SBFS[:, tp * 64:(tp + 1) * 64]),
                                 (CID[:, :], PX[:, tp * P + h * 64:
                                                tp * P + (h + 1) * 64])],
                                pb[4 + b][:, q * P + h * 64:
                                          q * P + (h + 1) * 64], PB[4 + b],
                                r=(TA.r, VT.r, QA.r if h == 0 else QB.r,
                                   "SBFS", "CID", PX.r),
                                tag="o_mm")
                        kc.transpose(pbT[:, tp * P:(tp + 1) * P],
                                     kh[:, tp, w0:w0 + P],
                                     CID[:, :], "pbT", r=(KH.r, "CID"),
                                     tag="khT")
                    kc.ve(lambda e: e.tensor_copy(PX.t[:, 1024:2048],
                                                  pbT[:, 0:1024]),
                          r=("pbT",), w=(PX.r,), tag="khT_ev")
                    for b in range(2):
                        kc.ve(lambda e, b=b: e.tensor_copy(
                            LTF[:, b * 512:(b + 1) * 512], pb[4 + b][:, 0:512]),
                              r=(PB[4 + b],), w=(LTF.r,), tag="o_ev")
                    kc.dma(SYNC, olv[:, c * 2 + s, :], LTF[:, 0:D],
                           r=(LTF.r,), w=(f"olocD:{c}",), tag="o_sp")
                    for tp in range(NT):
                        b, q = tp // 4, tp % 4
                        kc.mm(PX[:, 1024 + tp * P: 1024 + (tp + 1) * P],
                              vtv[:, s, tp * P:(tp + 1) * P],
                              pb[b][:, q * P:(q + 1) * P], PB[b],
                              r=(PX.r, VT.r), tag="ds")
                    for b in range(2):
                        dsv = pb[b][:, 0:512].rearrange("p (q c) -> p q c", c=P)
                        for h in range(2):
                            h0 = h * 64
                            kc.ve(lambda e, b=b, h0=h0, s=s: e.tensor_mul(
                                sv[h0:h0 + 64, b * 4:(b + 1) * 4, :],
                                sv[h0:h0 + 64, b * 4:(b + 1) * 4, :],
                                ACUMC[h0:h0 + 64, s * 8 + b * 4:
                                      s * 8 + b * 4 + 4].unsqueeze(2)
                                .broadcast_to([64, 4, 64])),
                                  r=("SST", "ACUMC"), w=("SST",), tag="sup1")
                            kc.ve(lambda e, b=b, h0=h0, dsv=dsv: e.tensor_add(
                                sv[h0:h0 + 64, b * 4:(b + 1) * 4, :],
                                sv[h0:h0 + 64, b * 4:(b + 1) * 4, :],
                                dsv[h0:h0 + 64, :, h0:h0 + 64]),
                                  r=("SST", PB[b]), w=("SST",), tag="sup2")
                    kc.ve(lambda e: e.tensor_copy(SBFS[:, 0:512],
                                                  SST[:, 0:512]),
                          r=("SST",), w=("SBFS",), tag="sbf")
            kc.dma(SYNC, cinS.ap()[:, :], SST[:, :], r=("SST",), w=("cinS",),
                   tag="st_stg")
            kc.sch.op(GPSIMD,
                      lambda e: e.collective_compute(
                          "AllGather", ALU.bypass,
                          replica_groups=[[0, 1], [2, 3], [4, 5], [6, 7]],
                          ins=[cinS.ap()], outs=[coutS.ap()]),
                      r=("cinS",), w=("coutS",), coll=True, tag="st_cc")
        sch.barrier(f"tmA{l}")

    def phase_tmB(l):
        with contextlib.ExitStack() as pes:
            WOS = S("WOS", [P, NT * D], BF16, pes)
            FXH = S("bFXH", [P, NT * CH], F32, pes)
            OLS = S("OLS", [P, 2 * D], F32, pes)
            OF = S("OFs", [P, D], F32, pes)
            OSQ = S("OSQ", [P, D], F32, pes)
            QAS = S("QAS", [P, NT * CH], BF16, pes)
            QBS = S("QBS", [P, NT * CH], BF16, pes)
            GS = S("GSs", [P, 2 * D], BF16, pes)
            SCB = S("SCB", [P, 2 * 512], BF16, pes)
            GTB = S("GTB", [P, D], BF16, pes)
            GTT = S("GTTs", [P, NT * CH], BF16, pes)
            GNREP = S("GNREP", [P, 2 * D], F32, pes)
            PGN = S("PGNs", [1, 4 * D], F32, pes)

            for t in range(NT):
                kc.dma(GPSIMD, WOS[:, t * D:(t + 1) * D],
                       wWo.ap()[l, t * P:(t + 1) * P, :], w=("WOS",), tag="ldwo")
            kc.dma(SYNC, PGN[:, :], pgn.ap()[:, :], w=("PGN",), tag="ldpgn")
            for i, base in enumerate((2 * l * D, (2 * l + 1) * D)):
                for nh in range(2):
                    kc.mm(PROW[0:1, NEXP:NEXP + P],
                          PGN[0:1, base + nh * 512: base + (nh + 1) * 512],
                          pb[i][:, 0:512], PB[i], r=("PROW", "PGN"),
                          tag="gnrep_mm")
                    kc.ve(lambda e, i=i, nh=nh: e.tensor_copy(
                        GNREP[:, i * D + nh * 512: i * D + (nh + 1) * 512],
                        pb[i][:, 0:512]),
                          r=(PB[i],), w=("GNREP",), tag="gnrep_ev")
            kc.dma(SYNC, CCG2[:, :], coutS.ap()[0:P, :], r=("coutS",),
                   w=("CCG2",), tag="g0s")
            qas = v3(QAS, CH)
            qbs = v3(QBS, CH)
            fxh = v3(FXH, CH)
            gtt = v3(GTT, CH)
            ccg2v = CCG2.t[:, :].rearrange("p (t v) -> p t v", v=64)
            scbv = SCB.t[:, :].rearrange("p (s x) -> p s x", x=512)
            for c in range(NCH):
                c0 = c * CH
                ld_dmaj(SYNC, FXH, CH, hDv, f"hD:{c}", c0, "tmB_ldh")
                for s2 in range(2):
                    t0, t1 = s2 * 4, (s2 + 1) * 4
                    kc.dma(SYNC, qas[:, t0:t1, 0:CH], qav[:, t0:t1, c0:c0 + CH],
                           r=(f"qaD:{c}",), w=(QAS.r,), tag="qa_ld")
                    kc.dma(SYNC, qbs[:, t0:t1, 0:CH], qbv[:, t0:t1, c0:c0 + CH],
                           r=(f"qbD:{c}",), w=(QBS.r,), tag="qb_ld")
                for s in range(2):
                    kc.dma(SYNC, OLS[:, s * D:(s + 1) * D], olv[:, c * 2 + s, :],
                           r=(f"olocD:{c}",), w=(OLS.r,), tag="ol_ld")
                    kc.dma(GPSIMD, GS[:, s * D:(s + 1) * D], ggv[:, c * 2 + s, :],
                           r=(f"ggD:{c}",), w=(GS.r,), tag="g_ld")
                    sub = c * 2 + s
                    kc.ve(lambda e, s=s, sub=sub: e.tensor_mul(
                        scbv[:, s, :].rearrange("p (t v) -> p t v", v=64),
                        ccg2v[:, :, :],
                        ACTAB[:, sub * 8:(sub + 1) * 8].unsqueeze(2)
                        .broadcast_to([P, NT, 64])),
                          r=("CCG2", "ACTAB"), w=(SCB.r,), tag="scb1")
                    kc.ve(lambda e, s=s: e.tensor_scalar_mul(
                        scbv[:, s, :], scbv[:, s, :], halfc),
                          r=(SCB.r, "CCOL"), w=(SCB.r,), tag="scb2")
                pipe = EvacPipe([2, 3, 4, 5])
                for s in range(2):
                    w0 = s * P
                    for tp in range(NT):
                        def mmf(bk, s=s, tp=tp, w0=w0):
                            for h, qv in enumerate((qas, qbs)):
                                kc.mm(qv[:, tp, w0:w0 + P],
                                      scbv[:, s, tp * 64:(tp + 1) * 64],
                                      pb[bk][:, h * 64:(h + 1) * 64], PB[bk],
                                      r=(QAS.r if h == 0 else QBS.r, SCB.r),
                                      tag="corr")

                        def evf(bk, s=s, tp=tp):
                            kc.ve(lambda e, s=s, tp=tp, bk=bk: e.tensor_add(
                                OF[:, tp * P:(tp + 1) * P],
                                OLS[:, s * D + tp * P: s * D + (tp + 1) * P],
                                pb[bk][:, 0:P]),
                                  r=(OLS.r, PB[bk]), w=(OF.r,), tag="ofull")
                        pipe.step(mmf, evf)
                    pipe.flush()
                    ofv = OF.t[:, :].rearrange("p (g v) -> p g v", v=64)
                    kc.ve(lambda e: e.reduce_sum(GST[:, 0:16].unsqueeze(2),
                                                 ofv[:, :, :], AX.X),
                          r=(OF.r,), w=("GST",), tag="gn_sum")
                    kc.act(OSQ[:, 0:D], OF[:, 0:D], AF.Square, r=(OF.r,),
                           w=(OSQ.r,), tag="gn_sq")
                    kc.ve(lambda e: e.reduce_sum(GST[:, 16:32].unsqueeze(2),
                                                 OSQ.t[:, :].rearrange(
                                                     "p (g v) -> p g v", v=64),
                                                 AX.X),
                          r=(OSQ.r,), w=("GST",), tag="gn_sumsq")
                    kc.ve(lambda e: e.tensor_scalar_mul(GST[:, 0:16],
                                                        GST[:, 0:16], 1.0 / 64),
                          r=("GST",), w=("GST",), tag="gn_mean")
                    kc.ve(lambda e: e.scalar_tensor_tensor(
                        GST[:, 32:48], GST[:, 0:16], -1.0, GST[:, 0:16],
                        ALU.mult, ALU.mult),
                          r=("GST",), w=("GST",), tag="gn_msq")
                    kc.ve(lambda e: e.scalar_tensor_tensor(
                        GST[:, 48:64], GST[:, 16:32], 1.0 / 64, GST[:, 32:48],
                        ALU.mult, ALU.add),
                          r=("GST",), w=("GST",), tag="gn_var")
                    kc.act(GST[:, 32:48], GST[:, 48:64], AF.Sqrt,
                           bias=epsc, r=("GST", "CCOL"), w=("GST",),
                           tag="gn_sdv")
                    kc.ve(lambda e: e.reciprocal(GST[:, 48:64], GST[:, 32:48]),
                          r=("GST",), w=("GST",), tag="gn_rstd")
                    kc.ve(lambda e: e.tensor_sub(
                        ofv[:, :, :], ofv[:, :, :],
                        GST[:, 0:16].unsqueeze(2).broadcast_to([P, 16, 64])),
                          r=(OF.r, "GST"), w=(OF.r,), tag="gn_sub")
                    kc.ve(lambda e: e.tensor_mul(
                        ofv[:, :, :], ofv[:, :, :],
                        GST[:, 48:64].unsqueeze(2).broadcast_to([P, 16, 64])),
                          r=(OF.r, "GST"), w=(OF.r,), tag="gn_nrm")
                    kc.ve(lambda e: e.tensor_mul(OF[:, 0:D], OF[:, 0:D],
                                                 GNREP[:, 0:D]),
                          r=(OF.r, "GNREP"), w=(OF.r,), tag="gn_w")
                    kc.ve(lambda e: e.tensor_add(OF[:, 0:D], OF[:, 0:D],
                                                 GNREP[:, D:2 * D]),
                          r=(OF.r, "GNREP"), w=(OF.r,), tag="gn_b")
                    kc.ve(lambda e, s=s: e.tensor_mul(
                        GTB[:, 0:D], OF[:, 0:D], GS[:, s * D:(s + 1) * D]),
                          r=(OF.r, GS.r), w=(GTB.r,), tag="gate")
                    for t in range(NT):
                        kc.transpose(pbT[:, t * P:(t + 1) * P],
                                     GTB[:, t * P:(t + 1) * P],
                                     CID[:, :], "pbT", r=(GTB.r, "CID"),
                                     tag="gT")
                    kc.ve(lambda e, s=s: e.tensor_copy(
                        gtt[:, :, s * P:(s + 1) * P],
                        pbT[:, 0:D].rearrange("p (t c) -> p t c", c=P)),
                          r=("pbT",), w=(GTT.r,), tag="gT_ev")
                for j in range(NT):
                    def mmf(bk, j=j):
                        kc.mmgroup([(WOS[:, t * D + j * P: t * D + (j + 1) * P],
                                     gtt[:, t, 0:CH]) for t in range(NT)],
                                   pb[bk][:, 0:CH], PB[bk], r=("WOS", GTT.r),
                                   tag="wo_mm")

                    def evf(bk, j=j):
                        kc.ve(lambda e, j=j, bk=bk: e.tensor_add(
                            fxh[:, j, 0:CH], fxh[:, j, 0:CH], pb[bk][:, 0:CH]),
                              r=(FXH.r, PB[bk]), w=(FXH.r,), tag="hup")
                    pipe.step(mmf, evf)
                pipe.flush()
                if c == NCH - 1:
                    kc.ve(lambda e: e.tensor_copy(
                        HLAST[:, 0:8], fxh[:, :, CH - 1:CH].squeeze(2)),
                          r=(FXH.r,), w=("HLAST",), tag="hlast")
                st_dmaj(SYNC, hDv, f"hD:{c}", FXH, CH, c0, "tmB_sth")
        sch.barrier(f"tmB{l}")

    def phase_cm(l):
        with contextlib.ExitStack() as pes:
            CWK = S("CWK", [P, NT * 3 * D], BF16, pes)
            CWV = S("CWV", [P, 24 * D], BF16, pes)
            CWR = S("CWR", [P, NT * D], BF16, pes)
            FXH = S("cFXH", [P, NT * CH], F32, pes)
            LXB = S("cLXB", [P, NT * CH], BF16, pes)
            LSQ = S("cLSQ", [P, NT * CH], BF16, pes)
            LTF = S("cLTF", [P, NT * CH], F32, pes)
            U2 = S("cU2", [P, NT * CH], BF16, pes)
            SX2 = S("cSX2", [P, NT * CH], BF16, pes)
            KX = S("cKX", [P, NT * CH], BF16, pes)
            RX = S("cRX", [P, NT * CH], BF16, pes)
            KK = S("cKK", [P, 24 * CH], BF16, pes)
            RG = S("cRG", [P, NT * CH], BF16, pes)
            TR = S("cTR", [P, CH], F32, pes)
            UBB = S("cUBB", [P, 16], F32, pes)

            for t in range(NT):
                kc.dma(GPSIMD, CWK[:, t * 3 * D:(t + 1) * 3 * D],
                       wcWk.ap()[l, t * P:(t + 1) * P, :], w=("CWK",), tag="ldck")
                kc.dma(GPSIMD, CWR[:, t * D:(t + 1) * D],
                       wcWr.ap()[l, t * P:(t + 1) * P, :], w=("CWR",), tag="ldcr")
            for f in range(24):
                kc.dma(GPSIMD, CWV[:, f * D:(f + 1) * D],
                       wcWv.ap()[l, f * P:(f + 1) * P, :], w=("CWV",), tag="ldcv")
            boundary_u(l, "ln2", None, UBB, f"ubC{l}")

            SROW = S("SROW", [P, 4 * CH], F32, pes)
            ln = LN(kc, CONE[:, :], LXB, LSQ, SROW, LTF,
                    (PB[0], pb[0]), (PB[1], pb[1]), CH, epsc)
            fxh = v3(FXH, CH)
            u2 = v3(U2, CH)
            sx2 = v3(SX2, CH)
            kx = v3(KX, CH)
            rx = v3(RX, CH)
            kkv = v3(KK, CH)
            rgv = v3(RG, CH)
            ltf = v3(LTF, CH)
            for c in range(NCH):
                c0 = c * CH
                ld_dmaj(SYNC, FXH, CH, hDv, f"hD:{c}", c0, "cm_ldh")
                ln.run(fxh[:, :, 0:CH], FXH.r, u2[:, :, 0:CH], U2.r,
                       wcol=par(l, "ln2").unsqueeze(2))
                kc.ve(lambda e: e.tensor_sub(sx2[:, :, 1:CH], u2[:, :, 0:CH - 1],
                                             u2[:, :, 1:CH]),
                      r=(U2.r,), w=(SX2.r,), tag="csx")
                kc.ve(lambda e: e.tensor_sub(sx2[:, :, 0:1],
                                             BPREV[:, 0:8].unsqueeze(2),
                                             u2[:, :, 0:1]),
                      r=(U2.r, "BPREV"), w=(SX2.r,), tag="csx0")
                kc.ve(lambda e: e.tensor_copy(BPREV[:, 0:8],
                                              u2[:, :, CH - 1:CH].squeeze(2)),
                      r=(U2.r,), w=("BPREV",), tag="cbpup")
                kc.ve(lambda e: e.tensor_mul(ltf[:, :, 0:CH], sx2[:, :, 0:CH],
                                             par_bc(l, "ckm", CH)),
                      r=(SX2.r, "PPAR"), w=(LTF.r,), tag="ckx1")
                kc.ve(lambda e: e.tensor_add(kx[:, :, 0:CH], ltf[:, :, 0:CH],
                                             u2[:, :, 0:CH]),
                      r=(LTF.r, U2.r), w=(KX.r,), tag="ckx2")
                kc.ve(lambda e: e.tensor_mul(ltf[:, :, 0:CH], sx2[:, :, 0:CH],
                                             par_bc(l, "crm", CH)),
                      r=(SX2.r, "PPAR"), w=(LTF.r,), tag="crx1")
                kc.ve(lambda e: e.tensor_add(rx[:, :, 0:CH], ltf[:, :, 0:CH],
                                             u2[:, :, 0:CH]),
                      r=(LTF.r, U2.r), w=(RX.r,), tag="crx2")
                pipe = EvacPipe([2, 3, 4, 5])
                for j in range(24):
                    def mmf(bk, j=j):
                        kc.mmgroup(
                            [(CWK[:, t * 3 * D + j * P: t * 3 * D + (j + 1) * P],
                              kx[:, t, 0:CH]) for t in range(NT)],
                            pb[bk][:, 0:CH], PB[bk], r=("CWK", KX.r),
                            tag="ck_mm")

                    def evf(bk, j=j):
                        kc.ve(lambda e, bk=bk: e.tensor_relu(TR[:, 0:CH],
                                                             pb[bk][:, 0:CH]),
                              r=(PB[bk],), w=(TR.r,), tag="ck_rl")
                        kc.act(kkv[:, j, 0:CH], TR[:, 0:CH], AF.Square,
                               r=(TR.r,), w=(KK.r,), tag="ck_sq")
                    pipe.step(mmf, evf)
                for j in range(NT):
                    def mmf(bk, j=j):
                        kc.mmgroup(
                            [(CWR[:, t * D + j * P: t * D + (j + 1) * P],
                              rx[:, t, 0:CH]) for t in range(NT)],
                            pb[bk][:, 0:CH], PB[bk], r=("CWR", RX.r),
                            tag="cr_mm")

                    def evf(bk, j=j):
                        kc.act(rgv[:, j, 0:CH], pb[bk][:, 0:CH], AF.Sigmoid,
                               r=(PB[bk],), w=(RG.r,), tag="cr_ev")
                    pipe.step(mmf, evf)
                pipe.flush()
                for j in range(NT):
                    def mmf(bk, j=j):
                        kc.mmgroup(
                            [(CWV[:, f * D + j * P: f * D + (j + 1) * P],
                              kkv[:, f, 0:CH]) for f in range(24)],
                            pb[bk][:, 0:CH], PB[bk], r=("CWV", KK.r),
                            tag="cv_mm")

                    def evf(bk, j=j):
                        kc.ve(lambda e, j=j, bk=bk: e.tensor_mul(
                            ltf[:, j, 0:CH], pb[bk][:, 0:CH], rgv[:, j, 0:CH]),
                              r=(PB[bk], RG.r), w=(LTF.r,), tag="cv_gate")
                        kc.ve(lambda e, j=j: e.tensor_add(
                            fxh[:, j, 0:CH], fxh[:, j, 0:CH], ltf[:, j, 0:CH]),
                              r=(LTF.r, FXH.r), w=(FXH.r,), tag="chup")
                    pipe.step(mmf, evf)
                pipe.flush()
                if c == NCH - 1:
                    kc.ve(lambda e: e.tensor_copy(
                        HLAST[:, 0:8], fxh[:, :, CH - 1:CH].squeeze(2)),
                          r=(FXH.r,), w=("HLAST",), tag="hlast")
                st_dmaj(SYNC, hDv, f"hD:{c}", FXH, CH, c0, "cm_sth")
        sch.barrier(f"cm{l}")

    def phase_x1():
        with contextlib.ExitStack() as pes:
            FXH = S("xFXH", [P, NT * CH], F32, pes)
            FX2 = S("xFX2", [P, NT * CH], F32, pes)
            LXB = S("xLXB", [P, NT * CH], BF16, pes)
            LSQ = S("xLSQ", [P, NT * CH], BF16, pes)
            LTF = S("xLTF", [P, NT * CH], F32, pes)
            XFB = S("xXFB", [P, NT * CH], BF16, pes)
            SROW = S("SROW", [P, 4 * CH], F32, pes)
            ln = LN(kc, CONE[:, :], LXB, LSQ, SROW, LTF,
                    (PB[0], pb[0]), (PB[1], pb[1]), CH, epsc)
            fxh = v3(FXH, CH)
            fx2 = v3(FX2, CH)
            xfb = v3(XFB, CH)
            for c in range(NCH):
                c0 = c * CH
                ld_dmaj(SYNC, FXH, CH, hDv, f"hD:{c}", c0, "x1_ldh")
                ld_dmaj(SYNC, FX2, CH, x0v, "x0", c0, "x1_ldx0")
                ln.run(fxh[:, :, 0:CH], FXH.r, fxh[:, :, 0:CH], FXH.r,
                       wcol=LNOUTW.unsqueeze(2))
                kc.ve(lambda e: e.tensor_add(fxh[:, :, 0:CH], fxh[:, :, 0:CH],
                                             fx2[:, :, 0:CH]),
                      r=(FXH.r, FX2.r), w=(FXH.r,), tag="x1add")
                st_dmaj(SYNC, x1v, f"x1D:{c}", FXH, CH, c0, "x1_st")
                ln.run(fxh[:, :, 0:CH], FXH.r, xfb[:, :, 0:CH], XFB.r)
                st_dmaj(SYNC, xfv, f"xfD:{c}", XFB, CH, c0, "xf_st")
        sch.barrier("x1")

    def phase_moe():
        with contextlib.ExitStack() as pes:
            WP1 = S("mWP1", [P, NT * DFF], BF16, pes)
            WP2 = S("mWP2", [P, NFT * D], BF16, pes)
            HT = S("mHT", [P, NFT * MCH], BF16, pes)
            XFSs = [S("mXFSa", [P, NT * MCH], BF16, pes),
                    S("mXFSb", [P, NT * MCH], BF16, pes)]
            XFS = XFSs[0]
            ACC = S("mACC", [P, (MCH // P) * D], F32, pes)
            WRM = S("mWRM", [P, NT * NEXP], BF16, pes)
            B1Cs = [S("mB1Ca", [P, NFT], F32, pes),
                    S("mB1Cb", [P, NFT], F32, pes)]
            B2S = S("mB2S", [NEXP, D], BF16, pes)
            RL = S("mRL", [P, 3 * NEXP], F32, pes)
            RWB = S("mRWB", [P, NEXP], BF16, pes)
            BRB = S("mBRB", [P, NEXP], F32, pes)

            ht = HT.t[:, :].rearrange("p (f c) -> p f c", c=MCH)
            xfs = v3(XFS, MCH)
            accv = ACC.t[:, :].rearrange("p (a d) -> p a d", d=D)
            NTT = NTOK // P
            x1sb = HT.t[:, 0:2 * NT * MCH].bitcast(F32)\
                .rearrange("p (t c) -> p t c", c=MCH)
            WTTa = HT.t[0:NEXP, (NFT - NTT // 4) * MCH:NFT * MCH]

            onesrow = PROW[0:1, NEXP:NEXP + P]
            kc.mm(onesrow, PROW[0:1, 0:NEXP], pb[6][:, 0:NEXP],
                  PB[6], r=("PROW",), tag="brb_mm")
            kc.ve(lambda e: e.tensor_copy(BRB[:, :], pb[6][:, 0:NEXP]),
                  r=(PB[6],), w=("BRB",), tag="brb_ev")
            for t in range(NT):
                kc.dma(GPSIMD, WRM[:, t * NEXP:(t + 1) * NEXP],
                       wmWr.ap()[t * P:(t + 1) * P, :], w=("WRM",), tag="ldwr")
            kc.dma(GPSIMD, B2S[:, :], wb2.ap()[:, :], w=("B2S",), tag="ldb2")

            for c in range(NMCH):
                c0 = c * MCH
                for s2 in range(2):
                    t0, t1 = s2 * 4, (s2 + 1) * 4
                    kc.dma(SYNC, xfs[:, t0:t1, 0:MCH], xfv[:, t0:t1, c0:c0 + MCH],
                           r=(f"xfD:{c0 // CH}", f"xfD:{(c0 + MCH - 1) // CH}"),
                           w=(XFS.r,), tag="rxf_ld")
                for ti in range(MCH // P):
                    tt = c * (MCH // P) + ti
                    kc.mmgroup([(xfs[:, t, ti * P:(ti + 1) * P],
                                 WRM[:, t * NEXP:(t + 1) * NEXP])
                                for t in range(NT)],
                               pb[6][:, 0:NEXP], PB[6], r=(XFS.r, "WRM"),
                               tag="rt_mm")
                    kc.ve(lambda e: e.tensor_add(RL[:, 0:NEXP], pb[6][:, 0:NEXP],
                                                 BRB[:, :]),
                          r=(PB[6], "BRB"), w=("mRL",), tag="rt_br")
                    kc.ve(lambda e: e.reduce_max(RTMP[:, 0:1], RL[:, 0:NEXP],
                                                 AX.X),
                          r=("mRL",), w=("RTMP",), tag="rt_m1")
                    kc.ve(lambda e: e.tensor_single_scalar(
                        RL[:, NEXP:2 * NEXP], RL[:, 0:NEXP], RTMP[:, 0:1],
                        ALU.is_ge),
                          r=("mRL", "RTMP"), w=("mRL",), tag="rt_t1")
                    kc.ve(lambda e: e.scalar_tensor_tensor(
                        RL[:, NEXP:2 * NEXP], RL[:, NEXP:2 * NEXP], -1e30,
                        RL[:, 0:NEXP], ALU.mult, ALU.add),
                          r=("mRL",), w=("mRL",), tag="rt_pen")
                    kc.ve(lambda e: e.reduce_max(RTMP[:, 1:2],
                                                 RL[:, NEXP:2 * NEXP], AX.X),
                          r=("mRL",), w=("RTMP",), tag="rt_m2")
                    kc.ve(lambda e: e.tensor_single_scalar(
                        RL[:, NEXP:2 * NEXP], RL[:, 0:NEXP], RTMP[:, 1:2],
                        ALU.is_ge),
                          r=("mRL", "RTMP"), w=("mRL",), tag="rt_keep")
                    kc.ve(lambda e: e.tensor_scalar_mul(RTMP[:, 2:3],
                                                        RTMP[:, 0:1], -1.0),
                          r=("RTMP",), w=("RTMP",), tag="rt_nm1")
                    kc.act(RL[:, 2 * NEXP:3 * NEXP], RL[:, 0:NEXP], AF.Exp,
                           bias=RTMP[:, 2:3], r=("mRL", "RTMP"), w=("mRL",),
                           tag="rt_exp")
                    kc.ve(lambda e: e.tensor_mul(RL[:, 2 * NEXP:3 * NEXP],
                                                 RL[:, 2 * NEXP:3 * NEXP],
                                                 RL[:, NEXP:2 * NEXP]),
                          r=("mRL",), w=("mRL",), tag="rt_w")
                    kc.ve(lambda e: e.reduce_sum(RTMP[:, 3:4],
                                                 RL[:, 2 * NEXP:3 * NEXP], AX.X),
                          r=("mRL",), w=("RTMP",), tag="rt_den")
                    kc.ve(lambda e: e.reciprocal(RTMP[:, 4:5], RTMP[:, 3:4]),
                          r=("RTMP",), w=("RTMP",), tag="rt_rd")
                    kc.ve(lambda e, tt=tt: e.tensor_scalar_mul(
                        WTE[:, tt * NEXP:(tt + 1) * NEXP],
                        RL[:, 2 * NEXP:3 * NEXP], RTMP[:, 4:5]),
                          r=("mRL", "RTMP"), w=("WTE",), tag="rt_fin")
                    kc.ve(lambda e, tt=tt: e.tensor_copy(
                        RWB[:, :], WTE[:, tt * NEXP:(tt + 1) * NEXP]),
                          r=("WTE",), w=("RWB",), tag="rt_cw")
                    kc.transpose(pbT[0:NEXP, 0:P], RWB[:, :], CID[:, :],
                                 "pbT", r=("RWB", "CID"), tag="rt_tr")
                    kc.ve(lambda e, tt=tt: e.tensor_copy(
                        WTTa[0:NEXP, tt * P:(tt + 1) * P], pbT[0:NEXP, 0:P]),
                          r=("pbT",), w=("mHT",), tag="rt_trev")

            for c in range(NMCH):
                c0 = c * MCH
                for s2 in range(2):
                    t0, t1 = s2 * 4, (s2 + 1) * 4
                    kc.dma(SYNC, x1sb[:, t0:t1, 0:MCH],
                           x1v[:, t0:t1, c0:c0 + MCH],
                           r=(f"x1D:{c0 // CH}", f"x1D:{(c0 + MCH - 1) // CH}"),
                           w=("mHT",), tag="mi_ldx1")
                pipe = EvacPipe([0, 1, 2, 3, 4, 5])
                for ti in range(MCH // P):
                    tt = c * (MCH // P) + ti
                    for t in range(NT):
                        def mmf(bk, ti=ti, t=t):
                            kc.transpose(pb[bk][:, 0:P],
                                         x1sb[:, t, ti * P:(ti + 1) * P],
                                         CIDF[:, :], PB[bk], r=("mHT", "CIDF"),
                                         tag="mi_tr")

                        def evf(bk, ti=ti, t=t):
                            kc.ve(lambda e, ti=ti, t=t, bk=bk: e.tensor_copy(
                                accv[:, ti, t * P:(t + 1) * P], pb[bk][:, 0:P]),
                                  r=(PB[bk],), w=(f"mACC:{ti}",), tag="mi_ev")
                        pipe.step(mmf, evf)
                    for dh in range(2):
                        def mmf(bk, tt=tt, dh=dh):
                            kc.mm(WTTa[0:NEXP, tt * P:(tt + 1) * P],
                                  B2S[:, dh * 512:(dh + 1) * 512],
                                  pb[bk][:, 0:512], PB[bk], r=("mHT", "B2S"),
                                  tag="mi_b2")

                        def evf(bk, ti=ti, dh=dh):
                            kc.ve(lambda e, ti=ti, dh=dh, bk=bk: e.tensor_add(
                                accv[:, ti, dh * 512:(dh + 1) * 512],
                                accv[:, ti, dh * 512:(dh + 1) * 512],
                                pb[bk][:, 0:512]),
                                  r=(PB[bk], f"mACC:{ti}"),
                                  w=(f"mACC:{ti}",), tag="mi_b2ev")
                        pipe.step(mmf, evf)
                pipe.flush()
                for ti in range(MCH // P):
                    kc.dma(SYNC, outv[:, c * (MCH // P) + ti:
                                      c * (MCH // P) + ti + 1, :],
                           accv[:, ti:ti + 1, :], r=(f"mACC:{ti}",),
                           w=(f"out:{c}",), tag="mi_st")

            pipe = EvacPipe([0, 1, 2, 3, 4, 5])
            for e_i in range(NEXP):
                for fg in range(4):
                    for t in range(NT):
                        kc.dma(GPSIMD,
                               WP1[:, t * DFF + fg * 8 * P:
                                   t * DFF + (fg + 1) * 8 * P],
                               wm1.ap()[e_i, t * P:(t + 1) * P,
                                        fg * 8 * P:(fg + 1) * 8 * P],
                               w=(f"mW1c:{fg}",), tag="ldw1")
                for f in range(NFT):
                    kc.dma(GPSIMD, WP2[:, f * D:(f + 1) * D],
                           wm2.ap()[e_i, f * P:(f + 1) * P, :],
                           w=(f"mW2:{f}",), tag="ldw2")
                B1C = B1Cs[e_i % 2]
                kc.dma(SYNC, B1C[:, :], wb1.ap()[e_i, :, :], w=(B1C.r,),
                       tag="ldb1")
                w2regs = tuple(f"mW2:{f}" for f in range(NFT))
                for c in range(NMCH):
                    c0 = c * MCH
                    XFS = XFSs[(e_i * NMCH + c) % 2]
                    xfs = v3(XFS, MCH)
                    for s2 in range(2):
                        t0, t1 = s2 * 4, (s2 + 1) * 4
                        kc.dma(SYNC, xfs[:, t0:t1, 0:MCH],
                               xfv[:, t0:t1, c0:c0 + MCH],
                               r=(f"xfD:{c0 // CH}",
                                  f"xfD:{(c0 + MCH - 1) // CH}"),
                               w=(XFS.r,), tag="me_xf")
                    for ti in range(MCH // P):
                        kc.dma(SYNC, accv[:, ti:ti + 1, :],
                               outv[:, c * (MCH // P) + ti:
                                    c * (MCH // P) + ti + 1, :],
                               r=(f"out:{c}",), w=(f"mACC:{ti}",),
                               tag="me_lda")
                    for f in range(NFT):
                        def mmf(bk, f=f, xfs=xfs, XFS=XFS):
                            kc.mmgroup(
                                [(WP1[:, t * DFF + f * P: t * DFF + (f + 1) * P],
                                  xfs[:, t, 0:MCH]) for t in range(NT)],
                                pb[bk][:, 0:MCH], PB[bk],
                                r=(f"mW1c:{f // 8}", XFS.r), tag="me_h")

                        def evf(bk, f=f, B1C=B1C):
                            kc.act(ht[:, f, 0:MCH], pb[bk][:, 0:MCH], AF.Gelu,
                                   bias=B1C[:, f:f + 1], r=(PB[bk], B1C.r),
                                   w=("mHT",), tag="me_gelu")
                        pipe.step(mmf, evf)
                    pipe.flush()
                    for ti in range(MCH // P):
                        tt = c * (MCH // P) + ti
                        for dh in range(2):
                            def mmf(bk, ti=ti, dh=dh):
                                kc.mmgroup(
                                    [(ht[:, f, ti * P:(ti + 1) * P],
                                      WP2[:, f * D + dh * 512:
                                          f * D + (dh + 1) * 512])
                                     for f in range(NFT)],
                                    pb[bk][:, 0:512], PB[bk],
                                    r=w2regs + ("mHT",), tag="me_y")

                            def evf(bk, ti=ti, dh=dh, tt=tt, e_i=e_i):
                                kc.ve(lambda e, ti=ti, dh=dh, bk=bk, tt=tt,
                                      e_i=e_i: e.scalar_tensor_tensor(
                                          accv[:, ti, dh * 512:(dh + 1) * 512],
                                          pb[bk][:, 0:512],
                                          WTE[:, tt * NEXP + e_i:
                                              tt * NEXP + e_i + 1],
                                          accv[:, ti, dh * 512:(dh + 1) * 512],
                                          ALU.mult, ALU.add),
                                      r=(PB[bk], "WTE", f"mACC:{ti}"),
                                      w=(f"mACC:{ti}",), tag="me_acc")
                            pipe.step(mmf, evf)
                    pipe.flush()
                    for ti in range(MCH // P):
                        kc.dma(SYNC, outv[:, c * (MCH // P) + ti:
                                          c * (MCH // P) + ti + 1, :],
                               accv[:, ti:ti + 1, :], r=(f"mACC:{ti}",),
                               w=(f"out:{c}",), tag="me_st")
        sch.barrier("moe")

    for l in range(LAYERS):
        phase_tmA(l)
        phase_tmB(l)
        phase_cm(l)
    phase_x1()
    phase_moe()
    sch.barrier("fin")

    sch.plan()
    if cfg.get("verbose"):
        print("SCHED:", sch.stats(), flush=True)
        print("sbuf bytes remaining:", nc.sbuf_bytes_remaining, flush=True)

    semnames = (["eng_" + e for e in ENGINES]
                + [f"dma_{e}{i}" for e in ENGINES for i in range(Sched.NLANES)]
                + ["cc"])
    sems = {}
    with contextlib.ExitStack() as ses:
        for sn in semnames:
            sems[sn] = ses.enter_context(nc.semaphore(sn))
        with nc.Block() as block:
            sch.emit(block, sems)
    kc.es.close()
    return nc



def _col8(vec):
    return np.ascontiguousarray(vec.reshape(NT, P).T)


def _pack_shared(i):
    f = np.float32
    L = 2
    ppar = np.zeros((P, 200), f)
    names = {"ln1": i["L0_attn_ln1_w"], "ln2": i["L0_attn_ln2_w"],
             "xm": i["L0_attn_x_maa"], "wm": i["L0_attn_w_maa"],
             "km": i["L0_attn_k_maa"], "vm": i["L0_attn_v_maa"],
             "rm": i["L0_attn_r_maa"], "gm": i["L0_attn_g_maa"],
             "ckm": i["L0_attn_ck_maa"], "crm": i["L0_attn_cr_maa"],
             "td": np.asarray(i["L0_attn_time_decay"], f).reshape(L, D),
             "tf": np.asarray(i["L0_attn_time_first"], f).reshape(L, D)}
    order = ["ln1", "ln2", "xm", "wm", "km", "vm", "rm", "gm", "ckm", "crm",
             "td", "tf"]
    for l in range(L):
        for pi, nm in enumerate(order):
            ppar[:, l * 96 + pi * 8:l * 96 + pi * 8 + 8] = \
                _col8(np.asarray(names[nm], f)[l])
    ppar[:, 192:200] = _col8(np.asarray(i["L0_attn_ln_out_w"], f))

    prows = np.zeros((1, 8 + P), f)
    prows[0, 0:8] = np.asarray(i["L0_ffn_br"], f)
    prows[0, 8:8 + P] = 1.0
    pgn = np.zeros((1, 4 * D), f)
    gnw = np.asarray(i["L0_attn_gn_w"], f)
    gnb = np.asarray(i["L0_attn_gn_b"], f)
    for l in range(L):
        pgn[0, 2 * l * D:(2 * l) * D + D] = gnw[l]
        pgn[0, (2 * l + 1) * D:(2 * l + 1) * D + D] = gnb[l]

    tf2 = np.asarray(i["L0_attn_time_first"], f).reshape(L, D)
    u2p = np.zeros((P, 32), f)
    for l in range(L):
        for tp in range(NT):
            blk = tf2[l, tp * P:(tp + 1) * P]
            u2p[0:64, l * 16 + tp * 2 + 0] = blk[0:64]
            u2p[64:128, l * 16 + tp * 2 + 1] = blk[64:128]

    tm1 = np.asarray(i["L0_attn_tm_w1"], f).reshape(L, D, 5, 32)
    wtm1 = np.zeros((L, D, 384), f)
    for j in range(5):
        jp, jh = j // 2, j % 2
        wtm1[:, :, jp * 128 + jh * 64:jp * 128 + jh * 64 + 32] = tm1[:, :, j]
    tm2 = np.asarray(i["L0_attn_tm_w2"], f)
    wtm2 = np.zeros((L, P, 5 * D), f)
    for j in range(5):
        jh = j % 2
        wtm2[:, jh * 64:jh * 64 + 32, j * D:(j + 1) * D] = tm2[:, j]
    td1 = np.asarray(i["L0_attn_td_w1"], f)
    wtd1 = np.zeros((L, D, P), f)
    wtd1[:, :, 0:64] = td1
    td2 = np.asarray(i["L0_attn_td_w2"], f)
    wtd2 = np.zeros((L, P, D), f)
    wtd2[:, 0:64, :] = td2

    b1 = np.asarray(i["L0_ffn_b1"], f)
    wb1 = np.ascontiguousarray(b1.reshape(8, 32, 128).transpose(0, 2, 1))

    shared = {
        "ppar": ppar, "prows": prows, "pgn": pgn, "u2p": u2p,
        "wWr": np.asarray(i["L0_attn_Wr"], f),
        "wWk": np.asarray(i["L0_attn_Wk"], f),
        "wWv": np.asarray(i["L0_attn_Wv"], f),
        "wWo": np.asarray(i["L0_attn_Wo"], f),
        "wWg": np.asarray(i["L0_attn_Wg"], f),
        "wtm1": wtm1, "wtm2": wtm2, "wtd1": wtd1, "wtd2": wtd2,
        "wcWk": np.asarray(i["L0_attn_cWk"], f),
        "wcWv": np.asarray(i["L0_attn_cWv"], f),
        "wcWr": np.asarray(i["L0_attn_cWr"], f),
        "wmWr": np.asarray(i["L0_ffn_Wr"], f),
        "wm1": np.asarray(i["L0_ffn_W1"], f),
        "wb1": wb1,
        "wm2": np.asarray(i["L0_ffn_W2"], f),
        "wb2": np.asarray(i["L0_ffn_b2"], f),
    }
    return shared


def _consts(is_second_half):
    f = np.float32
    c = np.zeros((P, 264), f)
    c[:, 0:P] = np.eye(P, dtype=f)
    ts = np.arange(P)
    c[:, P:2 * P] = (ts[:, None] < ts[None, :]).astype(f)
    c[0:64, 2 * P + 0] = 1.0
    c[64:128, 2 * P + 1] = 1.0
    c[:, 2 * P + 2] = 1.0 if is_second_half else 0.0
    c[:, 2 * P + 3] = 0.0
    c[:, 2 * P + 4] = 1.0
    c[:, 2 * P + 5] = EPS
    c[:, 2 * P + 6] = -1.0
    return c


_NC_CACHE = {}


def _get_nc(cfg_key):
    if cfg_key not in _NC_CACHE:
        cfg = dict(NTOK=4096, CH=256, LAYERS=2, MCH=512, NEXP=8, DFF=4096,
                   verbose=False)
        _NC_CACHE[cfg_key] = build_program(cfg)
    return _NC_CACHE[cfg_key]


def kernel(**inputs):
    x = np.asarray(inputs["L0_attn_x"], np.float32)
    B, T, _ = x.shape
    half = T // 2
    shared = _pack_shared(inputs)
    nc = _get_nc("full")
    in_maps = []
    cA = _consts(False)
    cB = _consts(True)
    for c in range(8):
        b, h = c // 2, c % 2
        m = dict(shared)
        m["x0"] = np.ascontiguousarray(x[b, h * half:(h + 1) * half, :].T)
        m["consts"] = cB if h else cA
        in_maps.append(m)
    res = run_bass_kernel_spmd(nc, in_maps, core_ids=list(range(8)))
    outf = np.empty((B, T, D), np.float32)
    for c in range(8):
        b, h = c // 2, c % 2
        outf[b, h * half:(h + 1) * half, :] = res.results[c]["out"]
    return outf


# version 06 (final)
# speedup vs baseline: 1.2676x; 1.0198x over previous
import contextlib
import numpy as np
import ml_dtypes

import concourse.bass as bass
import concourse.mybir as mybir
from concourse.bass_utils import run_bass_kernel_spmd

F32 = mybir.dt.float32
BF16 = mybir.dt.bfloat16
AF = mybir.ActivationFunctionType
ALU = mybir.AluOpType
AX = mybir.AxisListType

TENSOR, VECTOR, SCALAR, GPSIMD, SYNC = "tensor", "vector", "scalar", "gpsimd", "sync"
ENGINES = (TENSOR, VECTOR, SCALAR, GPSIMD, SYNC)

EPS = 1e-5
P = 128
D = 1024
NT = D // P
HS = 64
NH = 16

BF = ml_dtypes.bfloat16



class Op:
    __slots__ = ("idx", "eng", "fn", "reads", "writes", "dma", "mm_out",
                 "marked", "waits", "tag", "coll", "inc_amt", "epos",
                 "semname", "semval", "waitlist", "markable", "lane")

    def __init__(self, idx, eng, fn, reads, writes, dma, mm_out, tag, coll,
                 markable=True):
        self.idx = idx
        self.eng = eng
        self.fn = fn
        self.reads = tuple(reads)
        self.writes = tuple(writes)
        self.dma = dma
        self.mm_out = mm_out
        self.marked = False
        self.waits = []
        self.tag = tag
        self.coll = coll
        self.inc_amt = 0
        self.epos = 0
        self.semname = None
        self.semval = 0
        self.waitlist = []
        self.markable = markable
        self.lane = -1

    def kind(self):
        return (self.dma, self.coll)


class Sched:
    def __init__(self):
        self.ops = []
        self.last_write = {}
        self.readers = {}
        self.pe_mm = {}
        self.drains = []
        self.estreams = {r: [] for r in ENGINES}
        self.floor = {}
        self.dma_counts = {}

    def _new(self, eng, fn, reads, writes, dma, mm_out, tag, coll, markable=True):
        o = Op(len(self.ops), eng, fn, reads, writes, dma, mm_out, tag, coll,
               markable)
        o.epos = len(self.estreams[eng])
        self.estreams[eng].append(o)
        self.ops.append(o)
        return o

    def op(self, eng, fn, r=(), w=(), dma=False, mm_out=None, tag="", coll=False):
        o = self._new(eng, fn, r, w, dma, mm_out, tag, coll)
        if dma or coll:
            o.marked = True
        if dma:
            c = self.dma_counts.get(eng, 0)
            self.dma_counts[eng] = c + 1
            o.lane = c % self.NLANES
        deps = []
        for reg in o.reads:
            p = self.last_write.get(reg)
            if p is not None:
                deps.append(p)
            self.readers.setdefault(reg, []).append(o)
        for reg in o.writes:
            p = self.last_write.get(reg)
            if p is not None:
                deps.append(p)
            for rd in self.readers.get(reg, ()):
                deps.append(rd)
            self.readers[reg] = []
            self.last_write[reg] = o
        if o.eng != TENSOR:
            for reg in o.reads:
                mmop = self.pe_mm.get(reg)
                if mmop is not None:
                    deps.append(self._drain_after(mmop))
        if mm_out is not None:
            self.pe_mm[mm_out] = o
        fl = self.floor.get(eng)
        if fl is not None:
            deps.append(fl)
        self._edges(o, deps)
        return o

    def _drain_after(self, mmop):
        for dop in self.drains:
            if dop.epos > mmop.epos:
                return dop
        dop = self._new(TENSOR, lambda e: e.drain(), (), (), False, None,
                        "drain", False)
        mmop.marked = True
        dop.marked = True
        self.drains.append(dop)
        return dop

    def _edges(self, o, deps):
        best = {}
        for p in deps:
            if p is o or p.idx >= o.idx:
                continue
            key = (p.eng, p.kind(), p.lane)
            cur = best.get(key)
            if cur is None or p.epos > cur.epos:
                best[key] = p
        for (engname, kindk, _lane), p in best.items():
            if kindk != (False, False) or (p.marked and not p.markable):
                o.waits.append(p)
                continue
            stream = self.estreams[engname]
            mop = None
            for cand in stream[p.epos:]:
                if cand.idx >= o.idx:
                    break
                if cand.marked and cand.markable and cand.kind() == kindk:
                    mop = cand
                    break
            if mop is None:
                assert p.markable, f"unmarkable producer {p.tag} for {o.tag}"
                p.marked = True
                mop = p
            o.waits.append(mop)

    def barrier(self, tag="bar"):
        lasts = []
        for e in ENGINES:
            ncomp = 0
            ndma = 0
            ncoll = 0
            for o in reversed(self.estreams[e]):
                if not o.markable:
                    continue
                if o.dma:
                    if ndma < self.NLANES:
                        lasts.append(o)
                        ndma += 1
                elif o.coll:
                    if ncoll < 1:
                        lasts.append(o)
                        ncoll += 1
                else:
                    if ncomp < 1:
                        lasts.append(o)
                        ncomp += 1
                if ncomp and ncoll and ndma >= self.NLANES:
                    break
        outs = []
        for engname in ENGINES:
            o = self._new(engname, lambda e: e.nop(), (), (), False, None,
                          tag, False, markable=False)
            fl = self.floor.get(engname)
            deps = [x for x in lasts if x.eng != engname]
            if fl is not None:
                deps.append(fl)
            self._edges(o, deps)
            o.marked = True
            self.floor[engname] = o
            outs.append(o)
        return outs

    NLANES = 8

    def plan(self):
        counts = {}
        lane_idx = {}
        gate = {}
        for engname in ENGINES:
            for o in self.estreams[engname]:
                if not o.marked:
                    continue
                if o.coll:
                    o.semname, o.inc_amt = "cc", 1
                elif o.dma:
                    o.semname = f"dma_{engname}{o.lane}"
                    o.inc_amt = 16
                else:
                    o.semname, o.inc_amt = "eng_" + engname, 1
                c = counts.get(o.semname, 0) + o.inc_amt
                counts[o.semname] = c
                o.semval = c
                if o.semval > o.inc_amt:
                    gate[o.idx] = (o.semname, o.semval - o.inc_amt)
        for o in self.ops:
            wl = {}
            g = gate.get(o.idx)
            if g is not None:
                wl[g[0]] = g[1]
            for mop in o.waits:
                assert mop.marked, (o.tag, mop.tag)
                k = mop.semname
                if wl.get(k, 0) < mop.semval:
                    wl[k] = mop.semval
            o.waitlist = sorted(wl.items())
        self.sem_totals = counts

    def emit(self, block, sems):
        def mk(engname):
            def run(eng_obj):
                for o in self.estreams[engname]:
                    for semname, val in o.waitlist:
                        eng_obj.wait_ge(sems[semname], val)
                    inst = o.fn(eng_obj)
                    if o.marked:
                        inst.then_inc(sems[o.semname], o.inc_amt)
            return run

        block.tensor(mk(TENSOR))
        block.vector(mk(VECTOR))
        block.scalar(mk(SCALAR))
        block.gpsimd(mk(GPSIMD))
        block.sync(mk(SYNC))

    def stats(self):
        per = {e: len(self.estreams[e]) for e in ENGINES}
        nmark = sum(1 for o in self.ops if o.marked)
        nwait = sum(len(o.waitlist) for o in self.ops)
        return (f"ops={len(self.ops)} per-engine={per} marked={nmark} "
                f"waits={nwait} sems={getattr(self, 'sem_totals', None)}")



class KCtx:

    def __init__(self, nc, sch, cfg):
        self.nc = nc
        self.sch = sch
        self.cfg = cfg
        self.es = contextlib.ExitStack()
        self.dram = {}
        self.pb = []
        self.pbT = None

    def dma(self, q, out_ap, in_ap, r=(), w=(), tag="dma", slow=False):
        return self.sch.op(q, lambda e, o=out_ap, i=in_ap, s=slow:
                           e.dma_start(out=o, in_=i,
                                       allow_slow_non_contiguous=s),
                           r=r, w=w, dma=True, tag=tag)

    def mm(self, lhsT, rhs, out_ap, bank, r=(), start=True, stop=True, tag="mm"):
        return self.sch.op(TENSOR,
                           lambda e, a=lhsT, b=rhs, o=out_ap, s=start, t=stop:
                           e.matmul(o, a, b, start=s, stop=t),
                           r=r, w=(bank,), mm_out=bank, tag=tag)

    def mmgroup(self, pairs, out_ap, bank, r=(), tag="mmg"):
        def fn(e, ps=tuple(pairs), o=out_ap):
            inst = None
            n = len(ps)
            for i, (a, b) in enumerate(ps):
                inst = e.matmul(o, a, b, start=(i == 0), stop=(i == n - 1))
            return inst
        return self.sch.op(TENSOR, fn, r=r, w=(bank,), mm_out=bank, tag=tag)

    def transpose(self, out_ap, in_ap, ident, bank, r=(), tag="tr"):
        return self.sch.op(TENSOR,
                           lambda e, o=out_ap, i=in_ap, d=ident: e.transpose(o, i, d),
                           r=r, w=(bank,), mm_out=bank, tag=tag)

    def act(self, out_ap, in_ap, func, r=(), w=(), bias=0.0, scale=1.0, tag="act"):
        return self.sch.op(SCALAR,
                           lambda e, o=out_ap, i=in_ap, f=func, b=bias, s=scale:
                           e.activation(o, i, f, bias=b, scale=s),
                           r=r, w=w, tag=tag)

    def ve(self, fn, r=(), w=(), tag="ve"):
        return self.sch.op(VECTOR, fn, r=r, w=w, tag=tag)

    def gp(self, fn, r=(), w=(), tag="gp"):
        return self.sch.op(GPSIMD, fn, r=r, w=w, tag=tag)

    def sbuf(self, es, name, shape, dtype):
        return es.enter_context(self.nc.sbuf_tensor(name, shape, dtype))


class EvacPipe:

    def __init__(self, banks, lag=None):
        self.banks = banks
        self.lag = (len(banks) - 1) if lag is None else lag
        self.i = 0
        self.pend = []

    def step(self, mmfn, evfn):
        bk = self.banks[self.i % len(self.banks)]
        self.i += 1
        mmfn(bk)
        self.pend.append((evfn, bk))
        if len(self.pend) > self.lag:
            f, b = self.pend.pop(0)
            f(b)

    def flush(self):
        for f, b in self.pend:
            f(b)
        self.pend = []


def dmajor_view(dram_handle, p=P):
    return dram_handle.ap().rearrange("(t p) n -> p t n", p=p)


def tokmajor_view(dram_handle, p=P):
    return dram_handle.ap().rearrange("(a p) d -> p a d", p=p)



def load_dmajor_chunk(kc, q, sbuf_t, dram_handle, dname, c0, c1, w, tag,
                      nt=NT, split=2):
    v = dmajor_view(dram_handle)
    ops = []
    step = nt // split
    for s in range(split):
        t0, t1 = s * step, (s + 1) * step
        ops.append(kc.dma(q, sbuf_t[:, t0:t1, c0:c1], v[:, t0:t1, c0:c1],
                          r=(dname,) if isinstance(dname, str) else dname,
                          w=(w + f"/{s}",) if split > 1 else (w,), tag=tag))
    return ops


def store_dmajor_chunk(kc, q, dram_handle, dname, sbuf_t, rname, c0, c1,
                       nt=NT, split=2):
    v = dmajor_view(dram_handle)
    step = nt // split
    for s in range(split):
        t0, t1 = s * step, (s + 1) * step
        kc.dma(q, v[:, t0:t1, c0:c1], sbuf_t[:, t0:t1, c0:c1],
               r=(rname,), w=(dname,), tag="st_" + dname)





def dmajor_view(dram_handle, p=P):
    return dram_handle.ap().rearrange("(t p) n -> p t n", p=p)


def tokmajor_view(dram_handle, p=P):
    return dram_handle.ap().rearrange("(a p) d -> p a d", p=p)


class Slot:

    def __init__(self, t, name):
        self.t = t
        self.name = name

    def __getitem__(self, k):
        return self.t[k]

    @property
    def r(self):
        return self.name


def bc_nt(col_ap, C):
    return col_ap.broadcast_to([P, NT, C])


class LN:

    def __init__(self, kc, ones_bf, xb, sq, srow, tf, bank_a, bank_b, C,
                 epsc=None):
        self.kc, self.ones = kc, ones_bf
        self.xb, self.sq, self.srow, self.tf = xb, sq, srow, tf
        self.ba, self.bb = bank_a, bank_b
        self.C = C
        self.epsc = epsc

    def run(self, x_ap, xreg, out_ap, outreg, wcol=None, out2=None, out2reg=None):
        kc, C = self.kc, self.C
        xbs, sqs, srow, tfs = self.xb, self.sq, self.srow, self.tf
        xb = v3(xbs, C)
        sq = v3(sqs, C)
        tf = v3(tfs, C)
        (ban, ba), (bbn, bb) = self.ba, self.bb
        mb = srow[:, 0:C]
        vb = srow[:, C:2 * C]
        tp = srow[:, 2 * C:3 * C]
        rb = srow[:, 3 * C:4 * C]
        kc.ve(lambda e: e.tensor_copy(xb[:, :, 0:C], x_ap), r=(xreg,),
              w=(xbs.r,), tag="ln_cast")
        kc.act(sq[:, :, 0:C], xb[:, :, 0:C], AF.Square, r=(xbs.r,), w=(sqs.r,),
               tag="ln_sq")
        kc.mmgroup([(self.ones, xb[:, t, 0:C]) for t in range(NT)],
                   ba[:, 0:C], ban, r=(xbs.r,), tag="ln_sum")
        kc.mmgroup([(self.ones, sq[:, t, 0:C]) for t in range(NT)],
                   bb[:, 0:C], bbn, r=(sqs.r,), tag="ln_sumsq")
        kc.ve(lambda e: e.tensor_scalar_mul(mb, ba[:, 0:C], 1.0 / D),
              r=(ban,), w=(srow.r,), tag="ln_mean")
        kc.ve(lambda e: e.scalar_tensor_tensor(tp, mb, -1.0, mb, ALU.mult,
                                               ALU.mult),
              r=(srow.r,), w=(srow.r,), tag="ln_msq")
        kc.ve(lambda e: e.scalar_tensor_tensor(vb, bb[:, 0:C], 1.0 / D, tp,
                                               ALU.mult, ALU.add),
              r=(bbn, srow.r), w=(srow.r,), tag="ln_var")
        kc.act(tp, vb, AF.Sqrt, bias=self.epsc, r=(srow.r, "CCOL"),
               w=(srow.r,), tag="ln_sdv")
        kc.ve(lambda e: e.reciprocal(rb, tp), r=(srow.r,), w=(srow.r,),
              tag="ln_rstd")
        mb_bc = mb.unsqueeze(1).broadcast_to([P, NT, C])
        rb_bc = rb.unsqueeze(1).broadcast_to([P, NT, C])
        kc.ve(lambda e: e.tensor_sub(tf[:, :, 0:C], x_ap, mb_bc),
              r=(xreg, srow.r), w=(tfs.r,), tag="ln_sub")
        if wcol is None:
            kc.ve(lambda e: e.tensor_mul(out_ap, tf[:, :, 0:C], rb_bc),
                  r=(tfs.r, srow.r), w=(outreg,), tag="ln_out")
        else:
            kc.ve(lambda e: e.tensor_mul(tf[:, :, 0:C], tf[:, :, 0:C], rb_bc),
                  r=(tfs.r, srow.r), w=(tfs.r,), tag="ln_nrm")
            kc.ve(lambda e: e.tensor_mul(out_ap, tf[:, :, 0:C],
                                         bc_nt(wcol, C)),
                  r=(tfs.r, "PPAR"), w=(outreg,), tag="ln_scale")


def v3(slot, C=256):
    return slot.t[:, :].rearrange("p (t c) -> p t c", c=C)



def build_program(cfg):
    NTOK = cfg["NTOK"]
    CH = cfg["CH"]
    NCH = NTOK // CH
    NSUB = NTOK // P
    LAYERS = cfg["LAYERS"]
    MCH = cfg["MCH"]
    NMCH = NTOK // MCH
    NEXP = cfg["NEXP"]
    DFF = cfg["DFF"]
    NFT = DFF // P
    dbg = "ExternalOutput" if cfg.get("debug") else "Internal"

    nc = bass.Bass("TRN2", target_bir_lowering=False)
    sch = Sched()
    kc = KCtx(nc, sch, cfg)

    ext_in = {}

    def din(name, shape, dt=F32):
        t = nc.dram_tensor(name, list(shape), dt, kind="ExternalInput")
        ext_in[name] = t
        return t

    x0 = din("x0", [D, NTOK])
    consts = din("consts", [P, 264])
    ppar = din("ppar", [P, 2 * 96 + 8])
    prows = din("prows", [1, 8 + P])
    pgn = din("pgn", [1, 4 * D])
    u2p = din("u2p", [P, 32])
    wWr = din("wWr", [2, D, D]); wWk = din("wWk", [2, D, D])
    wWv = din("wWv", [2, D, D]); wWo = din("wWo", [2, D, D])
    wWg = din("wWg", [2, D, D])
    wtm1 = din("wtm1", [2, D, 384])
    wtm2 = din("wtm2", [2, P, 5 * D])
    wtd1 = din("wtd1", [2, D, P])
    wtd2 = din("wtd2", [2, P, D])
    wcWk = din("wcWk", [2, D, 3 * D])
    wcWv = din("wcWv", [2, 3 * D, D])
    wcWr = din("wcWr", [2, D, D])
    wmWr = din("wmWr", [D, NEXP])
    wm1 = din("wm1", [NEXP, D, DFF])
    wb1 = din("wb1", [NEXP, P, NFT])
    wm2 = din("wm2", [NEXP, DFF, D])
    wb2 = din("wb2", [NEXP, D])

    out = nc.dram_tensor("out", [NTOK, D], F32, kind="ExternalOutput")

    hD = nc.dram_tensor("hD", [D, NTOK], F32, kind=dbg)
    x1D = nc.dram_tensor("x1D", [D, NTOK], F32, kind=dbg)
    xfD = nc.dram_tensor("xfD", [D, NTOK], BF16, kind=dbg)
    olocD = nc.dram_tensor("olocD", [NTOK, D], F32, kind="Internal")
    ggD = nc.dram_tensor("ggD", [NTOK, D], BF16, kind="Internal")
    qaD = nc.dram_tensor("qaD", [D, NTOK], BF16, kind="Internal")
    qbD = nc.dram_tensor("qbD", [D, NTOK], BF16, kind="Internal")
    cinU = nc.dram_tensor("cinU", [P, P], F32, kind="Internal")
    coutU = nc.dram_tensor("coutU", [2 * P, P], F32, kind="Internal")
    cinS = nc.dram_tensor("cinS", [P, 512], F32, kind="Internal")
    coutS = nc.dram_tensor("coutS", [2 * P, 512], F32, kind="Internal")

    es0 = kc.es
    pb = [kc.sbuf(es0, f"pb{i}", [P, 512], F32) if False else
          es0.enter_context(nc.psum_tensor(f"pb{i}", [P, 512], F32))
          for i in range(7)]
    pbT = es0.enter_context(nc.psum_tensor("pbT", [P, 1024], BF16))
    PB = [f"pb{i}" for i in range(7)]

    _uniq = [0]

    def S(name, shape, dt, es=es0):
        _uniq[0] += 1
        uname = f"{name}_{_uniq[0]}"
        t = es.enter_context(nc.sbuf_tensor(uname, shape, dt))
        return Slot(t, name)

    CID = S("CID", [P, P], BF16)
    CIDF = S("CIDF", [P, P], F32)
    CTRI = S("CTRI", [P, P], F32)
    CTRI4 = S("CTRI4", [P, 512], F32)
    CONE = S("CONE", [P, P], BF16)
    CCOL = S("CCOL", [P, 8], F32)
    PPAR = S("PPARs", [P, 200], F32)
    PROW = S("PROWs", [1, 8 + P], F32)
    U2S = S("U2S", [P, 32], BF16)
    SST = S("SST", [P, 512], F32)
    SBFS = S("SBFS", [P, 512], BF16)
    ACUM = S("ACUM", [P, 8], F32)
    ACUMC = S("ACUMC", [P, 16], F32)
    ACTAB = S("ACTAB", [P, 8 * NSUB], F32)
    BPREV = S("BPREV", [P, 8], F32)
    CCU = S("CCU", [P, 8], F32)
    CCG2 = S("CCG2", [P, 512], F32)
    GST = S("GST", [P, 80], F32)
    RTMP = S("RTMP", [P, 16], F32)
    WTE = S("WTE", [P, (NTOK // P) * 8], F32)
    UB2 = S("UB2", [P, 16], BF16)
    HLAST = S("HLAST", [P, 8], F32)

    PAR_IDX = ["ln1", "ln2", "xm", "wm", "km", "vm", "rm", "gm", "ckm", "crm",
               "td", "tf"]

    def par(l, name):
        i = PAR_IDX.index(name)
        base = l * 96 + i * 8
        return PPAR[:, base:base + 8]

    def par_bc(l, name, C):
        return par(l, name).unsqueeze(2).broadcast_to([P, NT, C])

    def pcol(l, name, t):
        i = PAR_IDX.index(name)
        return PPAR[:, l * 96 + i * 8 + t: l * 96 + i * 8 + t + 1]

    LNOUTW = PPAR[:, 192:200]
    maskA = CCOL[:, 0:1]
    maskB = CCOL[:, 1:2]
    halfc = CCOL[:, 2:3]
    zeroc = CCOL[:, 3:4]
    epsc = CCOL[:, 5:6]
    neg1c = CCOL[:, 6:7]

    kc.dma(GPSIMD, CID[:, :], consts[:, 0:P], w=("CID",), tag="ld_cid")
    kc.dma(SYNC, CIDF[:, :], consts[:, 0:P], w=("CIDF",), tag="ld_cidf")
    kc.dma(SYNC, CTRI[:, :], consts[:, P:2 * P], w=("CTRI",), tag="ld_tri")
    kc.ve(lambda e: e.tensor_copy(
        CTRI4.t[:, :].rearrange("p (q c) -> p q c", c=P),
        CTRI[:, :].unsqueeze(1).broadcast_to([P, 4, P])),
          r=("CTRI",), w=("CTRI4",), tag="ctri4")
    kc.ve(lambda e: e.memset(CONE[:, :], 1.0), w=("CONE",), tag="ones")
    kc.dma(SYNC, CCOL[:, :], consts[:, 2 * P:2 * P + 8], w=("CCOL",), tag="ld_ccol")
    kc.dma(SYNC, PPAR[:, :], ppar[:, :], w=("PPAR",), tag="ld_par")
    kc.dma(SYNC, PROW[:, :], prows[:, :], w=("PROW",), tag="ld_prow")
    kc.dma(GPSIMD, U2S[:, :], u2p.ap(), w=("U2S",), tag="ld_u2")
    kc.ve(lambda e: e.memset(SST[:, :], 0.0), w=("SST",), tag="s0")
    kc.ve(lambda e: e.memset(SBFS[:, :], 0.0), w=("SBFS",), tag="sb0")
    kc.ve(lambda e: e.memset(ACUM[:, :], 1.0), w=("ACUM",), tag="ac0")
    kc.ve(lambda e: e.memset(CCG2[:, 0:P], 0.0), w=("CCG2",), tag="zsrow")
    kc.dma(GPSIMD, cinU.ap()[:, 8:P], CCG2[:, 8:P], r=("CCG2",), w=("cinU",),
           tag="zcinu")

    hDv = dmajor_view(hD)
    x0v = dmajor_view(x0)
    x1v = dmajor_view(x1D)
    xfv = dmajor_view(xfD)
    qav = dmajor_view(qaD)
    qbv = dmajor_view(qbD)
    olv = tokmajor_view(olocD)
    ggv = tokmajor_view(ggD)
    outv = tokmajor_view(out)

    def ld_dmaj(q, slot, C, view, dreg, c0, tag, f32=True):
        vv = v3(slot, C)
        for s2 in range(2):
            t0, t1 = s2 * 4, (s2 + 1) * 4
            kc.dma(q, vv[:, t0:t1, 0:C], view[:, t0:t1, c0:c0 + C],
                   r=(dreg,), w=(slot.r,), tag=tag)

    def st_dmaj(q, view, dreg, slot, C, c0, tag):
        vv = v3(slot, C)
        for s2 in range(2):
            t0, t1 = s2 * 4, (s2 + 1) * 4
            kc.dma(q, view[:, t0:t1, c0:c0 + C], vv[:, t0:t1, 0:C],
                   r=(slot.r,), w=(dreg,), tag=tag)

    def boundary_u(l, lnw_name, ph, UBB, tag):
        kc.ve(lambda e: e.tensor_copy(UBB[:, 0:8], HLAST[:, 0:8]),
              r=("HLAST",), w=(UBB.r,), tag=tag + "_ld")
        ubb = UBB[:, 0:8]
        ubbf = UBB[:, 8:16]
        kc.ve(lambda e: e.tensor_copy(UB2[:, 0:8], ubb), r=(UBB.r,),
              w=(UB2.r,), tag=tag + "_cast")
        kc.act(UB2[:, 8:16], UB2[:, 0:8], AF.Square, r=(UB2.r,), w=(UB2.r,),
               tag=tag + "_sq")
        kc.mmgroup([(CONE[:, :], UB2[:, t:t + 1]) for t in range(NT)],
                   pb[0][:, 0:1], PB[0], r=(UB2.r, "CONE"), tag=tag + "_sum")
        kc.mmgroup([(CONE[:, :], UB2[:, 8 + t:9 + t]) for t in range(NT)],
                   pb[1][:, 0:1], PB[1], r=(UB2.r, "CONE"), tag=tag + "_sumsq")
        kc.ve(lambda e: e.tensor_scalar_mul(RTMP[:, 0:1], pb[0][:, 0:1], 1.0 / D),
              r=(PB[0],), w=(RTMP.r,), tag=tag + "_m")
        kc.ve(lambda e: e.scalar_tensor_tensor(RTMP[:, 1:2], RTMP[:, 0:1], -1.0,
                                               RTMP[:, 0:1], ALU.mult, ALU.mult),
              r=(RTMP.r,), w=(RTMP.r,), tag=tag + "_msq")
        kc.ve(lambda e: e.scalar_tensor_tensor(RTMP[:, 2:3], pb[1][:, 0:1],
                                               1.0 / D, RTMP[:, 1:2], ALU.mult,
                                               ALU.add),
              r=(PB[1], RTMP.r), w=(RTMP.r,), tag=tag + "_var")
        kc.act(RTMP[:, 3:4], RTMP[:, 2:3], AF.Sqrt, bias=epsc,
               r=(RTMP.r, "CCOL"), w=(RTMP.r,), tag=tag + "_sdv")
        kc.ve(lambda e: e.reciprocal(RTMP[:, 4:5], RTMP[:, 3:4]),
              r=(RTMP.r,), w=(RTMP.r,), tag=tag + "_rstd")
        kc.ve(lambda e: e.tensor_scalar_sub(UBB[:, 0:8], ubb, RTMP[:, 0:1]),
              r=(UBB.r, RTMP.r), w=(UBB.r,), tag=tag + "_sub")
        kc.ve(lambda e: e.tensor_scalar_mul(UBB[:, 0:8], ubb, RTMP[:, 4:5]),
              r=(UBB.r, RTMP.r), w=(UBB.r,), tag=tag + "_nrm")
        if lnw_name is not None:
            kc.ve(lambda e: e.tensor_mul(UBB[:, 0:8], ubb, par(l, lnw_name)),
                  r=(UBB.r, "PPAR"), w=(UBB.r,), tag=tag + "_w")
        kc.dma(SYNC, cinU.ap()[:, 0:8], UBB[:, 0:8], r=(UBB.r,), w=("cinU",),
               tag=tag + "_stg")
        kc.sch.op(GPSIMD,
                  lambda e: e.collective_compute(
                      "AllGather", ALU.bypass,
                      replica_groups=[[0, 1], [2, 3], [4, 5], [6, 7]],
                      ins=[cinU.ap()], outs=[coutU.ap()]),
                  r=("cinU",), w=("coutU",), coll=True, tag=tag + "_cc")
        kc.dma(SYNC, CCU[:, 0:8], coutU.ap()[0:P, 0:8], r=("coutU",),
               w=("CCU",), tag=tag + "_g")
        kc.ve(lambda e: e.tensor_scalar_mul(BPREV[:, 0:8], CCU[:, 0:8], halfc),
              r=("CCU", "CCOL"), w=("BPREV",), tag=tag + "_bp")

    with contextlib.ExitStack() as pes:
        FXH = S("p0FXH", [P, NT * CH], F32, pes)
        LXB = S("p0LXB", [P, NT * CH], BF16, pes)
        LSQ = S("p0LSQ", [P, NT * CH], BF16, pes)
        LTF = S("p0LTF", [P, NT * CH], F32, pes)
        SROW = S("SROW", [P, 4 * CH], F32, pes)
        ln = LN(kc, CONE[:, :], LXB, LSQ, SROW, LTF,
                (PB[0], pb[0]), (PB[1], pb[1]), CH, epsc)
        for c in range(NCH):
            c0 = c * CH
            ld_dmaj(SYNC, FXH, CH, x0v, "x0", c0, "p0_ld")
            ln.run(v3(FXH, CH)[:, :, 0:CH], FXH.r,
                   v3(FXH, CH)[:, :, 0:CH], FXH.r)
            if c == NCH - 1:
                kc.ve(lambda e: e.tensor_copy(
                    HLAST[:, 0:8], v3(FXH, CH)[:, :, CH - 1:CH].squeeze(2)),
                      r=(FXH.r,), w=("HLAST",), tag="hlast")
            st_dmaj(SYNC, hDv, f"hD:{c}", FXH, CH, c0, "p0_st")
    sch.barrier("p0")

    def phase_tmA(l):
        with contextlib.ExitStack() as pes:
            WPA = S("WPA", [P, 4 * NT * D], BF16, pes)
            TM1 = S("TM1", [P, NT * 384], BF16, pes)
            TM2 = S("TM2", [P, 5 * D], BF16, pes)
            TD1 = S("TD1", [P, NT * P], BF16, pes)
            TD2 = S("TD2", [P, D], BF16, pes)
            FXH = S("FXH", [P, NT * CH], F32, pes)
            LOGW = S("LOGW", [P, NT * CH], F32, pes)
            PLAX = S("PLAX", [P, NT * CH], F32, pes)
            LTF = S("LTF", [P, NT * CH], F32, pes)
            LXB = S("LXB", [P, NT * CH], BF16, pes)
            LSQ = S("LSQ", [P, NT * CH], BF16, pes)
            U = S("U", [P, NT * CH], BF16, pes)
            SX = S("SX", [P, NT * CH], BF16, pes)
            EEXC = S("EEXC", [P, NT * CH], BF16, pes)
            EINC = S("EINC", [P, NT * CH], BF16, pes)
            KT = S("KT", [P, NT * CH], BF16, pes)
            RT = S("RT", [P, NT * CH], BF16, pes)
            QA = S("QA", [P, NT * CH], BF16, pes)
            QB = S("QB", [P, NT * CH], BF16, pes)
            RK = S("RK", [P, NT * CH], BF16, pes)
            KH = S("KH", [P, NT * CH], BF16, pes)
            PX = S("PXs", [P, NT * CH], BF16, pes)
            TA = S("TAs", [P, NT * CH], BF16, pes)
            TMPB = S("TMPB", [P, 3 * CH], BF16, pes)
            VT = S("VTs", [P, 2 * D], BF16, pes)
            GT = S("GTs", [P, 2 * D], BF16, pes)
            UBB = S("UBB", [P, 16], F32, pes)

            def wslab(i, t):
                return WPA[:, (i * NT + t) * D:(i * NT + t) * D + D]

            for i, wt in enumerate((wWr, wWk, wWv, wWg)):
                for t in range(NT):
                    kc.dma(GPSIMD, wslab(i, t), wt.ap()[l, t * P:(t + 1) * P, :],
                           w=(f"WPA{i}",), tag=f"ldw{i}")
            for t in range(NT):
                kc.dma(GPSIMD, TM1[:, t * 384:(t + 1) * 384],
                       wtm1.ap()[l, t * P:(t + 1) * P, :], w=("TM1",), tag="ldtm1")
                kc.dma(GPSIMD, TD1[:, t * P:(t + 1) * P],
                       wtd1.ap()[l, t * P:(t + 1) * P, :], w=("TD1",), tag="ldtd1")
            kc.dma(GPSIMD, TM2[:, :], wtm2.ap()[l, :, :], w=("TM2",), tag="ldtm2")
            kc.dma(GPSIMD, TD2[:, :], wtd2.ap()[l, :, :], w=("TD2",), tag="ldtd2")

            boundary_u(l, "ln1", None, UBB, f"ubA{l}")
            kc.ve(lambda e: e.memset(SST[:, :], 0.0), w=("SST",), tag="s0")
            kc.ve(lambda e: e.memset(SBFS[:, :], 0.0), w=("SBFS",), tag="sb0")
            kc.ve(lambda e: e.memset(ACUM[:, :], 1.0), w=("ACUM",), tag="ac0")

            SROW = S("SROW", [P, 4 * CH], F32, pes)
            ln = LN(kc, CONE[:, :], LXB, LSQ, SROW, LTF,
                    (PB[0], pb[0]), (PB[1], pb[1]), CH, epsc)
            fxh = v3(FXH, CH)
            uu = v3(U, CH)
            sx = v3(SX, CH)
            logw = v3(LOGW, CH)
            plax = v3(PLAX, CH)
            eexc = v3(EEXC, CH)
            einc = v3(EINC, CH)
            ktv = v3(KT, CH)
            rtv = v3(RT, CH)
            qa = v3(QA, CH)
            qb = v3(QB, CH)
            rk = v3(RK, CH)
            kh = v3(KH, CH)
            pxv = v3(PX, CH)
            tav = v3(TA, CH)
            ltf = v3(LTF, CH)
            vtv = VT.t[:, :].rearrange("p (s d) -> p s d", d=D)
            gtv = GT.t[:, :].rearrange("p (s d) -> p s d", d=D)

            for c in range(NCH):
                c0 = c * CH
                ld_dmaj(SYNC, FXH, CH, hDv, f"hD:{c}", c0, "tmA_ldh")
                ln.run(fxh[:, :, 0:CH], FXH.r, uu[:, :, 0:CH], U.r,
                       wcol=par(l, "ln1").unsqueeze(2))
                kc.ve(lambda e: e.tensor_sub(sx[:, :, 1:CH], uu[:, :, 0:CH - 1],
                                             uu[:, :, 1:CH]),
                      r=(U.r,), w=(SX.r,), tag="sx")
                kc.ve(lambda e: e.tensor_sub(sx[:, :, 0:1],
                                             BPREV[:, 0:8].unsqueeze(2),
                                             uu[:, :, 0:1]),
                      r=(U.r, "BPREV"), w=(SX.r,), tag="sx0")
                kc.ve(lambda e: e.tensor_copy(BPREV[:, 0:8],
                                              uu[:, :, CH - 1:CH].squeeze(2)),
                      r=(U.r,), w=("BPREV",), tag="bpup")
                kc.ve(lambda e: e.tensor_mul(pxv[:, :, 0:CH], sx[:, :, 0:CH],
                                             par_bc(l, "xm", CH)),
                      r=(SX.r, "PPAR"), w=(PX.r,), tag="xxx1")
                kc.ve(lambda e: e.tensor_add(pxv[:, :, 0:CH], pxv[:, :, 0:CH],
                                             uu[:, :, 0:CH]),
                      r=(PX.r, U.r), w=(PX.r,), tag="xxx2")
                pipe = EvacPipe([2, 3, 4, 5, 0, 1])
                for jp in range(3):
                    def mmf(bk, jp=jp):
                        kc.mmgroup(
                            [(TM1[:, t * 384 + jp * P: t * 384 + (jp + 1) * P],
                              pxv[:, t, 0:CH]) for t in range(NT)],
                            pb[bk][:, 0:CH], PB[bk], r=("TM1", PX.r),
                            tag="tmp_mm")

                    def evf(bk, jp=jp):
                        kc.act(TMPB[:, jp * CH:(jp + 1) * CH], pb[bk][:, 0:CH],
                               AF.Tanh, r=(PB[bk],), w=("TMPB",), tag="tmp_tanh")
                    pipe.step(mmf, evf)
                pipe.flush()
                MAAN = ["wm", "km", "vm", "rm", "gm"]
                for Pi in range(5):
                    for j in range(NT):
                        def mmf(bk, j=j, Pi=Pi):
                            kc.mm(TM2[:, Pi * D + j * P: Pi * D + (j + 1) * P],
                                  TMPB[:, (Pi // 2) * CH:(Pi // 2 + 1) * CH],
                                  pb[bk][:, 0:CH], PB[bk], r=("TM2", "TMPB"),
                                  tag="mix_mm")

                        def evf(bk, j=j, Pi=Pi):
                            kc.ve(lambda e, j=j, bk=bk, Pi=Pi:
                                  e.scalar_tensor_tensor(
                                      ltf[:, j, 0:CH], pb[bk][:, 0:CH],
                                      pcol(l, MAAN[Pi], j), sx[:, j, 0:CH],
                                      ALU.add, ALU.mult),
                                  r=(PB[bk], "PPAR", SX.r), w=(LTF.r,),
                                  tag="pxa")
                            kc.ve(lambda e, j=j: e.tensor_add(
                                pxv[:, j, 0:CH], ltf[:, j, 0:CH],
                                uu[:, j, 0:CH]),
                                  r=(LTF.r, U.r), w=(PX.r,), tag="pxb")
                        pipe.step(mmf, evf)
                    pipe.flush()
                    if Pi == 0:
                        def mmf(bk):
                            kc.mmgroup([(TD1[:, t * P:(t + 1) * P],
                                         pxv[:, t, 0:CH]) for t in range(NT)],
                                       pb[bk][:, 0:CH], PB[bk],
                                       r=("TD1", PX.r), tag="td1_mm")

                        def evf(bk):
                            kc.act(TA[:, 0:CH], pb[bk][:, 0:CH], AF.Tanh,
                                   r=(PB[bk],), w=(TA.r,), tag="td1_tanh")
                        pipe.step(mmf, evf)
                        pipe.flush()
                        for j in range(NT):
                            def mmf(bk, j=j):
                                kc.mm(TD2[:, j * P:(j + 1) * P], TA[:, 0:CH],
                                      pb[bk][:, 0:CH], PB[bk],
                                      r=("TD2", TA.r), tag="td2_mm")

                            def evf(bk, j=j):
                                kc.act(logw[:, j, 0:CH], pb[bk][:, 0:CH],
                                       AF.Exp, bias=pcol(l, "td", j),
                                       r=(PB[bk], "PPAR"), w=(LOGW.r,),
                                       tag="logw")
                            pipe.step(mmf, evf)
                    elif Pi == 1:
                        for j in range(NT):
                            def mmf(bk, j=j):
                                kc.mmgroup(
                                    [(wslab(1, t)[:, j * P:(j + 1) * P],
                                      pxv[:, t, 0:CH]) for t in range(NT)],
                                    pb[bk][:, 0:CH], PB[bk],
                                    r=("WPA1", PX.r), tag="k_mm")

                            def evf(bk, j=j):
                                kc.sch.op(SCALAR,
                                          lambda e, j=j, bk=bk: e.copy(
                                              ktv[:, j, 0:CH], pb[bk][:, 0:CH]),
                                          r=(PB[bk],), w=(KT.r,), tag="k_ev")
                            pipe.step(mmf, evf)
                    elif Pi == 2:
                        for st in range(2):
                            for nh in range(2):
                                def mmf(bk, st=st, nh=nh):
                                    kc.mmgroup(
                                        [(pxv[:, t, st * P:(st + 1) * P],
                                          wslab(2, t)[:, nh * 512:(nh + 1) * 512])
                                         for t in range(NT)],
                                        pb[bk][:, 0:512], PB[bk],
                                        r=("WPA2", PX.r), tag="v_mm")

                                def evf(bk, st=st, nh=nh):
                                    kc.sch.op(
                                        SCALAR,
                                        lambda e, st=st, nh=nh, bk=bk: e.copy(
                                            vtv[:, st, nh * 512:(nh + 1) * 512],
                                            pb[bk][:, 0:512]),
                                        r=(PB[bk],), w=(VT.r,), tag="v_ev")
                                pipe.step(mmf, evf)
                    elif Pi == 3:
                        for j in range(NT):
                            def mmf(bk, j=j):
                                kc.mmgroup(
                                    [(wslab(0, t)[:, j * P:(j + 1) * P],
                                      pxv[:, t, 0:CH]) for t in range(NT)],
                                    pb[bk][:, 0:CH], PB[bk],
                                    r=("WPA0", PX.r), tag="r_mm")

                            def evf(bk, j=j):
                                kc.act(rtv[:, j, 0:CH], pb[bk][:, 0:CH],
                                       AF.Sigmoid, r=(PB[bk],), w=(RT.r,),
                                       tag="r_ev")
                            pipe.step(mmf, evf)
                    elif Pi == 4:
                        for st in range(2):
                            for nh in range(2):
                                def mmf(bk, st=st, nh=nh):
                                    kc.mmgroup(
                                        [(pxv[:, t, st * P:(st + 1) * P],
                                          wslab(3, t)[:, nh * 512:(nh + 1) * 512])
                                         for t in range(NT)],
                                        pb[bk][:, 0:512], PB[bk],
                                        r=("WPA3", PX.r), tag="g_mm")

                                def evf(bk, st=st, nh=nh):
                                    kc.act(gtv[:, st, nh * 512:(nh + 1) * 512],
                                           pb[bk][:, 0:512], AF.Silu,
                                           r=(PB[bk],), w=(GT.r,), tag="g_ev")
                                pipe.step(mmf, evf)
                pipe.flush()
                for st in range(2):
                    kc.dma(SYNC, ggv[:, c * 2 + st, :], gtv[:, st, :],
                           r=(GT.r,), w=(f"ggD:{c}",), tag="g_sp")
                for s in range(2):
                    w0 = s * P
                    kc.ve(lambda e, w0=w0: e.memset(plax[:, :, w0:w0 + 1], 0.0),
                          w=(PLAX.r,), tag="plx0")
                    for tp in range(NT):
                        kc.ve(lambda e, w0=w0, tp=tp: e.tensor_tensor_scan(
                            plax[:, tp, w0 + 1:w0 + P],
                            logw[:, tp, w0:w0 + P - 1],
                            zeroc.broadcast_to([P, P - 1]),
                            0.0, ALU.add, ALU.add),
                              r=(LOGW.r, "CCOL"), w=(PLAX.r,), tag="scan")
                kc.ve(lambda e: e.tensor_add(logw[:, :, 0:CH], plax[:, :, 0:CH],
                                             logw[:, :, 0:CH]),
                      r=(PLAX.r, LOGW.r), w=(LOGW.r,), tag="plainc")
                for s in range(2):
                    w1 = s * P + P - 1
                    kc.act(ACUMC[:, s * 8:(s + 1) * 8].unsqueeze(2),
                           logw[:, :, w1:w1 + 1], AF.Exp, scale=neg1c,
                           r=(LOGW.r, "CCOL"), w=("ACUMC",), tag="aC")
                kc.act(eexc[:, :, 0:CH], plax[:, :, 0:CH], AF.Exp, scale=neg1c,
                       r=(PLAX.r, "CCOL"), w=(EEXC.r,), tag="eexc")
                kc.act(einc[:, :, 0:CH], logw[:, :, 0:CH], AF.Exp,
                       r=(LOGW.r,), w=(EINC.r,), tag="einc")
                kc.ve(lambda e: e.scalar_tensor_tensor(
                    v3(LXB, CH)[:, :, 0:CH], einc[:, :, 0:CH], maskA,
                    ktv[:, :, 0:CH], ALU.mult, ALU.mult),
                      r=(EINC.r, "CCOL", KT.r), w=(LXB.r,), tag="ka")
                kc.ve(lambda e: e.scalar_tensor_tensor(
                    v3(LSQ, CH)[:, :, 0:CH], einc[:, :, 0:CH], maskB,
                    ktv[:, :, 0:CH], ALU.mult, ALU.mult),
                      r=(EINC.r, "CCOL", KT.r), w=(LSQ.r,), tag="kb")
                kc.ve(lambda e: e.scalar_tensor_tensor(
                    qa[:, :, 0:CH], eexc[:, :, 0:CH], maskA,
                    rtv[:, :, 0:CH], ALU.mult, ALU.mult),
                      r=(EEXC.r, "CCOL", RT.r), w=(QA.r,), tag="qa")
                kc.ve(lambda e: e.scalar_tensor_tensor(
                    qb[:, :, 0:CH], eexc[:, :, 0:CH], maskB,
                    rtv[:, :, 0:CH], ALU.mult, ALU.mult),
                      r=(EEXC.r, "CCOL", RT.r), w=(QB.r,), tag="qb")
                kc.ve(lambda e: e.tensor_mul(rk[:, :, 0:CH], rtv[:, :, 0:CH],
                                             ktv[:, :, 0:CH]),
                      r=(RT.r, KT.r), w=(RK.r,), tag="rk")
                for s in range(2):
                    w0 = s * P
                    kc.ve(lambda e, s=s, w0=w0: e.tensor_mul(
                        tav[:, :, w0:w0 + P], einc[:, :, w0:w0 + P],
                        ACUMC[:, s * 8:(s + 1) * 8].unsqueeze(2)
                        .broadcast_to([P, NT, P])),
                          r=(EINC.r, "ACUMC"), w=(TA.r,), tag="est")
                    kc.ve(lambda e, w0=w0: e.tensor_mul(
                        kh[:, :, w0:w0 + P], ktv[:, :, w0:w0 + P],
                        tav[:, :, w0:w0 + P]),
                          r=(KT.r, TA.r), w=(KH.r,), tag="khat")
                for s2 in range(2):
                    t0, t1 = s2 * 4, (s2 + 1) * 4
                    kc.dma(SYNC, qav[:, t0:t1, c0:c0 + CH], qa[:, t0:t1, 0:CH],
                           r=(QA.r,), w=(f"qaD:{c}",), tag="qa_sp")
                    kc.dma(SYNC, qbv[:, t0:t1, c0:c0 + CH], qb[:, t0:t1, 0:CH],
                           r=(QB.r,), w=(f"qbD:{c}",), tag="qb_sp")
                for s in range(2):
                    sub = c * 2 + s
                    kc.ve(lambda e, sub=sub: e.tensor_copy(
                        ACTAB[:, sub * 8:(sub + 1) * 8], ACUM[:, 0:8]),
                          r=("ACUM",), w=("ACTAB",), tag="acst")
                    kc.ve(lambda e, s=s: e.tensor_mul(
                        ACUM[:, 0:8], ACUM[:, 0:8], ACUMC[:, s * 8:(s + 1) * 8]),
                          r=("ACUM", "ACUMC"), w=("ACUM",), tag="acup")
                sv = SST.t[:, :].rearrange("p (t v) -> p t v", v=64)
                pxc = PX.t[:, 0:1024].rearrange("p (c v) -> p c v", v=64)
                vt4 = VT.t[:, :].rearrange("p (s c v) -> p s c v", s=2, v=64)
                for s in range(2):
                    w0 = s * P
                    for tp in range(NT):
                        b, q = tp // 4, tp % 4
                        kc.mm(v3(LXB, CH)[:, tp, w0:w0 + P],
                              qa[:, tp, w0:w0 + P],
                              pb[b][:, q * P:(q + 1) * P], PB[b],
                              r=(LXB.r, QA.r), tag="ata")
                        kc.mm(v3(LSQ, CH)[:, tp, w0:w0 + P],
                              qb[:, tp, w0:w0 + P],
                              pb[2 + b][:, q * P:(q + 1) * P], PB[2 + b],
                              r=(LSQ.r, QB.r), tag="atb")
                        kc.mm(rk[:, tp, w0:w0 + P],
                              U2S[:, l * 16 + tp * 2: l * 16 + tp * 2 + 2],
                              pb[6][:, s * 64 + tp * 2: s * 64 + tp * 2 + 2],
                              PB[6], r=(RK.r, "U2S"), tag="c2")
                    for mb in range(4):
                        kc.ve(lambda e, mb=mb: e.tensor_mul(
                            TA[:, mb * 512:(mb + 1) * 512],
                            pb[mb][:, 0:512], CTRI4[:, :]),
                              r=(PB[mb], "CTRI4"), w=(TA.r,), tag="mstr")
                    kc.ve(lambda e, s=s: e.tensor_mul(
                        pxc, vt4[:, s, :, :],
                        pb[6][:, s * 64:s * 64 + 16].unsqueeze(2)
                        .broadcast_to([P, 16, 64])),
                          r=(VT.r, PB[6]), w=(PX.r,), tag="vc")
                    for tp in range(NT):
                        b, q = tp // 4, tp % 4
                        for h, qv in enumerate((qa, qb)):
                            kc.mmgroup(
                                [(TA[:, h * 1024 + tp * P:
                                     h * 1024 + (tp + 1) * P],
                                  vtv[:, s, tp * P + h * 64:
                                      tp * P + (h + 1) * 64]),
                                 (qv[:, tp, w0:w0 + P],
                                  SBFS[:, tp * 64:(tp + 1) * 64]),
                                 (CID[:, :], PX[:, tp * P + h * 64:
                                                tp * P + (h + 1) * 64])],
                                pb[4 + b][:, q * P + h * 64:
                                          q * P + (h + 1) * 64], PB[4 + b],
                                r=(TA.r, VT.r, QA.r if h == 0 else QB.r,
                                   "SBFS", "CID", PX.r),
                                tag="o_mm")
                        kc.transpose(pbT[:, tp * P:(tp + 1) * P],
                                     kh[:, tp, w0:w0 + P],
                                     CID[:, :], "pbT", r=(KH.r, "CID"),
                                     tag="khT")
                    kc.ve(lambda e: e.tensor_copy(PX.t[:, 1024:2048],
                                                  pbT[:, 0:1024]),
                          r=("pbT",), w=(PX.r,), tag="khT_ev")
                    for b in range(2):
                        kc.ve(lambda e, b=b: e.tensor_copy(
                            LTF[:, b * 512:(b + 1) * 512], pb[4 + b][:, 0:512]),
                              r=(PB[4 + b],), w=(LTF.r,), tag="o_ev")
                    kc.dma(SYNC, olv[:, c * 2 + s, :], LTF[:, 0:D],
                           r=(LTF.r,), w=(f"olocD:{c}",), tag="o_sp")
                    for tp in range(NT):
                        b, q = tp // 4, tp % 4
                        kc.mm(PX[:, 1024 + tp * P: 1024 + (tp + 1) * P],
                              vtv[:, s, tp * P:(tp + 1) * P],
                              pb[b][:, q * P:(q + 1) * P], PB[b],
                              r=(PX.r, VT.r), tag="ds")
                    for b in range(2):
                        dsv = pb[b][:, 0:512].rearrange("p (q c) -> p q c", c=P)
                        for h in range(2):
                            h0 = h * 64
                            kc.ve(lambda e, b=b, h0=h0, s=s: e.tensor_mul(
                                sv[h0:h0 + 64, b * 4:(b + 1) * 4, :],
                                sv[h0:h0 + 64, b * 4:(b + 1) * 4, :],
                                ACUMC[h0:h0 + 64, s * 8 + b * 4:
                                      s * 8 + b * 4 + 4].unsqueeze(2)
                                .broadcast_to([64, 4, 64])),
                                  r=("SST", "ACUMC"), w=("SST",), tag="sup1")
                            kc.ve(lambda e, b=b, h0=h0, dsv=dsv: e.tensor_add(
                                sv[h0:h0 + 64, b * 4:(b + 1) * 4, :],
                                sv[h0:h0 + 64, b * 4:(b + 1) * 4, :],
                                dsv[h0:h0 + 64, :, h0:h0 + 64]),
                                  r=("SST", PB[b]), w=("SST",), tag="sup2")
                    kc.ve(lambda e: e.tensor_copy(SBFS[:, 0:512],
                                                  SST[:, 0:512]),
                          r=("SST",), w=("SBFS",), tag="sbf")
            kc.dma(SYNC, cinS.ap()[:, :], SST[:, :], r=("SST",), w=("cinS",),
                   tag="st_stg")
            kc.sch.op(GPSIMD,
                      lambda e: e.collective_compute(
                          "AllGather", ALU.bypass,
                          replica_groups=[[0, 1], [2, 3], [4, 5], [6, 7]],
                          ins=[cinS.ap()], outs=[coutS.ap()]),
                      r=("cinS",), w=("coutS",), coll=True, tag="st_cc")
        sch.barrier(f"tmA{l}")

    def phase_tmB(l):
        with contextlib.ExitStack() as pes:
            WOS = S("WOS", [P, NT * D], BF16, pes)
            FXH = S("bFXH", [P, NT * CH], F32, pes)
            OLS = S("OLS", [P, 2 * D], F32, pes)
            OF = S("OFs", [P, D], F32, pes)
            OSQ = S("OSQ", [P, D], F32, pes)
            QAS = S("QAS", [P, NT * CH], BF16, pes)
            QBS = S("QBS", [P, NT * CH], BF16, pes)
            GS = S("GSs", [P, 2 * D], BF16, pes)
            SCB = S("SCB", [P, 2 * 512], BF16, pes)
            GTB = S("GTB", [P, D], BF16, pes)
            GTT = S("GTTs", [P, NT * CH], BF16, pes)
            GNREP = S("GNREP", [P, 2 * D], F32, pes)
            PGN = S("PGNs", [1, 4 * D], F32, pes)

            for t in range(NT):
                kc.dma(GPSIMD, WOS[:, t * D:(t + 1) * D],
                       wWo.ap()[l, t * P:(t + 1) * P, :], w=("WOS",), tag="ldwo")
            kc.dma(SYNC, PGN[:, :], pgn.ap()[:, :], w=("PGN",), tag="ldpgn")
            for i, base in enumerate((2 * l * D, (2 * l + 1) * D)):
                for nh in range(2):
                    kc.mm(PROW[0:1, NEXP:NEXP + P],
                          PGN[0:1, base + nh * 512: base + (nh + 1) * 512],
                          pb[i][:, 0:512], PB[i], r=("PROW", "PGN"),
                          tag="gnrep_mm")
                    kc.ve(lambda e, i=i, nh=nh: e.tensor_copy(
                        GNREP[:, i * D + nh * 512: i * D + (nh + 1) * 512],
                        pb[i][:, 0:512]),
                          r=(PB[i],), w=("GNREP",), tag="gnrep_ev")
            kc.dma(SYNC, CCG2[:, :], coutS.ap()[0:P, :], r=("coutS",),
                   w=("CCG2",), tag="g0s")
            qas = v3(QAS, CH)
            qbs = v3(QBS, CH)
            fxh = v3(FXH, CH)
            gtt = v3(GTT, CH)
            ccg2v = CCG2.t[:, :].rearrange("p (t v) -> p t v", v=64)
            scbv = SCB.t[:, :].rearrange("p (s x) -> p s x", x=512)
            for c in range(NCH):
                c0 = c * CH
                ld_dmaj(SYNC, FXH, CH, hDv, f"hD:{c}", c0, "tmB_ldh")
                for s2 in range(2):
                    t0, t1 = s2 * 4, (s2 + 1) * 4
                    kc.dma(SYNC, qas[:, t0:t1, 0:CH], qav[:, t0:t1, c0:c0 + CH],
                           r=(f"qaD:{c}",), w=(QAS.r,), tag="qa_ld")
                    kc.dma(SYNC, qbs[:, t0:t1, 0:CH], qbv[:, t0:t1, c0:c0 + CH],
                           r=(f"qbD:{c}",), w=(QBS.r,), tag="qb_ld")
                for s in range(2):
                    kc.dma(SYNC, OLS[:, s * D:(s + 1) * D], olv[:, c * 2 + s, :],
                           r=(f"olocD:{c}",), w=(OLS.r,), tag="ol_ld")
                    kc.dma(GPSIMD, GS[:, s * D:(s + 1) * D], ggv[:, c * 2 + s, :],
                           r=(f"ggD:{c}",), w=(GS.r,), tag="g_ld")
                    sub = c * 2 + s
                    kc.ve(lambda e, s=s, sub=sub: e.tensor_mul(
                        scbv[:, s, :].rearrange("p (t v) -> p t v", v=64),
                        ccg2v[:, :, :],
                        ACTAB[:, sub * 8:(sub + 1) * 8].unsqueeze(2)
                        .broadcast_to([P, NT, 64])),
                          r=("CCG2", "ACTAB"), w=(SCB.r,), tag="scb1")
                    kc.ve(lambda e, s=s: e.tensor_scalar_mul(
                        scbv[:, s, :], scbv[:, s, :], halfc),
                          r=(SCB.r, "CCOL"), w=(SCB.r,), tag="scb2")
                pipe = EvacPipe([2, 3, 4, 5, 0, 1])
                for s in range(2):
                    w0 = s * P
                    for tp in range(NT):
                        def mmf(bk, s=s, tp=tp, w0=w0):
                            for h, qv in enumerate((qas, qbs)):
                                kc.mm(qv[:, tp, w0:w0 + P],
                                      scbv[:, s, tp * 64:(tp + 1) * 64],
                                      pb[bk][:, h * 64:(h + 1) * 64], PB[bk],
                                      r=(QAS.r if h == 0 else QBS.r, SCB.r),
                                      tag="corr")

                        def evf(bk, s=s, tp=tp):
                            kc.ve(lambda e, s=s, tp=tp, bk=bk: e.tensor_add(
                                OF[:, tp * P:(tp + 1) * P],
                                OLS[:, s * D + tp * P: s * D + (tp + 1) * P],
                                pb[bk][:, 0:P]),
                                  r=(OLS.r, PB[bk]), w=(OF.r,), tag="ofull")
                        pipe.step(mmf, evf)
                    pipe.flush()
                    ofv = OF.t[:, :].rearrange("p (g v) -> p g v", v=64)
                    kc.ve(lambda e: e.reduce_sum(GST[:, 0:16].unsqueeze(2),
                                                 ofv[:, :, :], AX.X),
                          r=(OF.r,), w=("GST",), tag="gn_sum")
                    kc.act(OSQ[:, 0:D], OF[:, 0:D], AF.Square, r=(OF.r,),
                           w=(OSQ.r,), tag="gn_sq")
                    kc.ve(lambda e: e.reduce_sum(GST[:, 16:32].unsqueeze(2),
                                                 OSQ.t[:, :].rearrange(
                                                     "p (g v) -> p g v", v=64),
                                                 AX.X),
                          r=(OSQ.r,), w=("GST",), tag="gn_sumsq")
                    kc.ve(lambda e: e.tensor_scalar_mul(GST[:, 0:16],
                                                        GST[:, 0:16], 1.0 / 64),
                          r=("GST",), w=("GST",), tag="gn_mean")
                    kc.ve(lambda e: e.scalar_tensor_tensor(
                        GST[:, 32:48], GST[:, 0:16], -1.0, GST[:, 0:16],
                        ALU.mult, ALU.mult),
                          r=("GST",), w=("GST",), tag="gn_msq")
                    kc.ve(lambda e: e.scalar_tensor_tensor(
                        GST[:, 48:64], GST[:, 16:32], 1.0 / 64, GST[:, 32:48],
                        ALU.mult, ALU.add),
                          r=("GST",), w=("GST",), tag="gn_var")
                    kc.act(GST[:, 32:48], GST[:, 48:64], AF.Sqrt,
                           bias=epsc, r=("GST", "CCOL"), w=("GST",),
                           tag="gn_sdv")
                    kc.ve(lambda e: e.reciprocal(GST[:, 48:64], GST[:, 32:48]),
                          r=("GST",), w=("GST",), tag="gn_rstd")
                    kc.ve(lambda e: e.tensor_sub(
                        ofv[:, :, :], ofv[:, :, :],
                        GST[:, 0:16].unsqueeze(2).broadcast_to([P, 16, 64])),
                          r=(OF.r, "GST"), w=(OF.r,), tag="gn_sub")
                    kc.ve(lambda e: e.tensor_mul(
                        ofv[:, :, :], ofv[:, :, :],
                        GST[:, 48:64].unsqueeze(2).broadcast_to([P, 16, 64])),
                          r=(OF.r, "GST"), w=(OF.r,), tag="gn_nrm")
                    kc.ve(lambda e: e.tensor_mul(OF[:, 0:D], OF[:, 0:D],
                                                 GNREP[:, 0:D]),
                          r=(OF.r, "GNREP"), w=(OF.r,), tag="gn_w")
                    kc.ve(lambda e: e.tensor_add(OF[:, 0:D], OF[:, 0:D],
                                                 GNREP[:, D:2 * D]),
                          r=(OF.r, "GNREP"), w=(OF.r,), tag="gn_b")
                    kc.ve(lambda e, s=s: e.tensor_mul(
                        GTB[:, 0:D], OF[:, 0:D], GS[:, s * D:(s + 1) * D]),
                          r=(OF.r, GS.r), w=(GTB.r,), tag="gate")
                    for t in range(NT):
                        kc.transpose(pbT[:, t * P:(t + 1) * P],
                                     GTB[:, t * P:(t + 1) * P],
                                     CID[:, :], "pbT", r=(GTB.r, "CID"),
                                     tag="gT")
                    kc.ve(lambda e, s=s: e.tensor_copy(
                        gtt[:, :, s * P:(s + 1) * P],
                        pbT[:, 0:D].rearrange("p (t c) -> p t c", c=P)),
                          r=("pbT",), w=(GTT.r,), tag="gT_ev")
                for j in range(NT):
                    def mmf(bk, j=j):
                        kc.mmgroup([(WOS[:, t * D + j * P: t * D + (j + 1) * P],
                                     gtt[:, t, 0:CH]) for t in range(NT)],
                                   pb[bk][:, 0:CH], PB[bk], r=("WOS", GTT.r),
                                   tag="wo_mm")

                    def evf(bk, j=j):
                        kc.ve(lambda e, j=j, bk=bk: e.tensor_add(
                            fxh[:, j, 0:CH], fxh[:, j, 0:CH], pb[bk][:, 0:CH]),
                              r=(FXH.r, PB[bk]), w=(FXH.r,), tag="hup")
                    pipe.step(mmf, evf)
                pipe.flush()
                if c == NCH - 1:
                    kc.ve(lambda e: e.tensor_copy(
                        HLAST[:, 0:8], fxh[:, :, CH - 1:CH].squeeze(2)),
                          r=(FXH.r,), w=("HLAST",), tag="hlast")
                st_dmaj(SYNC, hDv, f"hD:{c}", FXH, CH, c0, "tmB_sth")
        sch.barrier(f"tmB{l}")

    def phase_cm(l):
        with contextlib.ExitStack() as pes:
            CWK = S("CWK", [P, NT * 3 * D], BF16, pes)
            CWV = S("CWV", [P, 24 * D], BF16, pes)
            CWR = S("CWR", [P, NT * D], BF16, pes)
            FXH = S("cFXH", [P, NT * CH], F32, pes)
            LXB = S("cLXB", [P, NT * CH], BF16, pes)
            LSQ = S("cLSQ", [P, NT * CH], BF16, pes)
            LTF = S("cLTF", [P, NT * CH], F32, pes)
            U2 = S("cU2", [P, NT * CH], BF16, pes)
            SX2 = S("cSX2", [P, NT * CH], BF16, pes)
            KX = S("cKX", [P, NT * CH], BF16, pes)
            RX = S("cRX", [P, NT * CH], BF16, pes)
            KK = S("cKK", [P, 24 * CH], BF16, pes)
            RG = S("cRG", [P, NT * CH], BF16, pes)
            TR = S("cTR", [P, CH], F32, pes)
            UBB = S("cUBB", [P, 16], F32, pes)

            for t in range(NT):
                kc.dma(GPSIMD, CWK[:, t * 3 * D:(t + 1) * 3 * D],
                       wcWk.ap()[l, t * P:(t + 1) * P, :], w=("CWK",), tag="ldck")
                kc.dma(GPSIMD, CWR[:, t * D:(t + 1) * D],
                       wcWr.ap()[l, t * P:(t + 1) * P, :], w=("CWR",), tag="ldcr")
            for f in range(24):
                kc.dma(GPSIMD, CWV[:, f * D:(f + 1) * D],
                       wcWv.ap()[l, f * P:(f + 1) * P, :], w=("CWV",), tag="ldcv")
            boundary_u(l, "ln2", None, UBB, f"ubC{l}")

            SROW = S("SROW", [P, 4 * CH], F32, pes)
            ln = LN(kc, CONE[:, :], LXB, LSQ, SROW, LTF,
                    (PB[0], pb[0]), (PB[1], pb[1]), CH, epsc)
            fxh = v3(FXH, CH)
            u2 = v3(U2, CH)
            sx2 = v3(SX2, CH)
            kx = v3(KX, CH)
            rx = v3(RX, CH)
            kkv = v3(KK, CH)
            rgv = v3(RG, CH)
            ltf = v3(LTF, CH)
            for c in range(NCH):
                c0 = c * CH
                ld_dmaj(SYNC, FXH, CH, hDv, f"hD:{c}", c0, "cm_ldh")
                ln.run(fxh[:, :, 0:CH], FXH.r, u2[:, :, 0:CH], U2.r,
                       wcol=par(l, "ln2").unsqueeze(2))
                kc.ve(lambda e: e.tensor_sub(sx2[:, :, 1:CH], u2[:, :, 0:CH - 1],
                                             u2[:, :, 1:CH]),
                      r=(U2.r,), w=(SX2.r,), tag="csx")
                kc.ve(lambda e: e.tensor_sub(sx2[:, :, 0:1],
                                             BPREV[:, 0:8].unsqueeze(2),
                                             u2[:, :, 0:1]),
                      r=(U2.r, "BPREV"), w=(SX2.r,), tag="csx0")
                kc.ve(lambda e: e.tensor_copy(BPREV[:, 0:8],
                                              u2[:, :, CH - 1:CH].squeeze(2)),
                      r=(U2.r,), w=("BPREV",), tag="cbpup")
                kc.ve(lambda e: e.tensor_mul(ltf[:, :, 0:CH], sx2[:, :, 0:CH],
                                             par_bc(l, "ckm", CH)),
                      r=(SX2.r, "PPAR"), w=(LTF.r,), tag="ckx1")
                kc.ve(lambda e: e.tensor_add(kx[:, :, 0:CH], ltf[:, :, 0:CH],
                                             u2[:, :, 0:CH]),
                      r=(LTF.r, U2.r), w=(KX.r,), tag="ckx2")
                kc.ve(lambda e: e.tensor_mul(ltf[:, :, 0:CH], sx2[:, :, 0:CH],
                                             par_bc(l, "crm", CH)),
                      r=(SX2.r, "PPAR"), w=(LTF.r,), tag="crx1")
                kc.ve(lambda e: e.tensor_add(rx[:, :, 0:CH], ltf[:, :, 0:CH],
                                             u2[:, :, 0:CH]),
                      r=(LTF.r, U2.r), w=(RX.r,), tag="crx2")
                pipe = EvacPipe([2, 3, 4, 5, 0, 1])
                for j in range(24):
                    def mmf(bk, j=j):
                        kc.mmgroup(
                            [(CWK[:, t * 3 * D + j * P: t * 3 * D + (j + 1) * P],
                              kx[:, t, 0:CH]) for t in range(NT)],
                            pb[bk][:, 0:CH], PB[bk], r=("CWK", KX.r),
                            tag="ck_mm")

                    def evf(bk, j=j):
                        kc.ve(lambda e, bk=bk: e.tensor_relu(TR[:, 0:CH],
                                                             pb[bk][:, 0:CH]),
                              r=(PB[bk],), w=(TR.r,), tag="ck_rl")
                        kc.act(kkv[:, j, 0:CH], TR[:, 0:CH], AF.Square,
                               r=(TR.r,), w=(KK.r,), tag="ck_sq")
                    pipe.step(mmf, evf)
                for j in range(NT):
                    def mmf(bk, j=j):
                        kc.mmgroup(
                            [(CWR[:, t * D + j * P: t * D + (j + 1) * P],
                              rx[:, t, 0:CH]) for t in range(NT)],
                            pb[bk][:, 0:CH], PB[bk], r=("CWR", RX.r),
                            tag="cr_mm")

                    def evf(bk, j=j):
                        kc.act(rgv[:, j, 0:CH], pb[bk][:, 0:CH], AF.Sigmoid,
                               r=(PB[bk],), w=(RG.r,), tag="cr_ev")
                    pipe.step(mmf, evf)
                pipe.flush()
                for j in range(NT):
                    def mmf(bk, j=j):
                        kc.mmgroup(
                            [(CWV[:, f * D + j * P: f * D + (j + 1) * P],
                              kkv[:, f, 0:CH]) for f in range(24)],
                            pb[bk][:, 0:CH], PB[bk], r=("CWV", KK.r),
                            tag="cv_mm")

                    def evf(bk, j=j):
                        kc.ve(lambda e, j=j, bk=bk: e.tensor_mul(
                            ltf[:, j, 0:CH], pb[bk][:, 0:CH], rgv[:, j, 0:CH]),
                              r=(PB[bk], RG.r), w=(LTF.r,), tag="cv_gate")
                        kc.ve(lambda e, j=j: e.tensor_add(
                            fxh[:, j, 0:CH], fxh[:, j, 0:CH], ltf[:, j, 0:CH]),
                              r=(LTF.r, FXH.r), w=(FXH.r,), tag="chup")
                    pipe.step(mmf, evf)
                pipe.flush()
                if c == NCH - 1:
                    kc.ve(lambda e: e.tensor_copy(
                        HLAST[:, 0:8], fxh[:, :, CH - 1:CH].squeeze(2)),
                          r=(FXH.r,), w=("HLAST",), tag="hlast")
                st_dmaj(SYNC, hDv, f"hD:{c}", FXH, CH, c0, "cm_sth")
        sch.barrier(f"cm{l}")

    def phase_x1():
        with contextlib.ExitStack() as pes:
            FXH = S("xFXH", [P, NT * CH], F32, pes)
            FX2 = S("xFX2", [P, NT * CH], F32, pes)
            LXB = S("xLXB", [P, NT * CH], BF16, pes)
            LSQ = S("xLSQ", [P, NT * CH], BF16, pes)
            LTF = S("xLTF", [P, NT * CH], F32, pes)
            XFB = S("xXFB", [P, NT * CH], BF16, pes)
            SROW = S("SROW", [P, 4 * CH], F32, pes)
            ln = LN(kc, CONE[:, :], LXB, LSQ, SROW, LTF,
                    (PB[0], pb[0]), (PB[1], pb[1]), CH, epsc)
            fxh = v3(FXH, CH)
            fx2 = v3(FX2, CH)
            xfb = v3(XFB, CH)
            for c in range(NCH):
                c0 = c * CH
                ld_dmaj(SYNC, FXH, CH, hDv, f"hD:{c}", c0, "x1_ldh")
                ld_dmaj(SYNC, FX2, CH, x0v, "x0", c0, "x1_ldx0")
                ln.run(fxh[:, :, 0:CH], FXH.r, fxh[:, :, 0:CH], FXH.r,
                       wcol=LNOUTW.unsqueeze(2))
                kc.ve(lambda e: e.tensor_add(fxh[:, :, 0:CH], fxh[:, :, 0:CH],
                                             fx2[:, :, 0:CH]),
                      r=(FXH.r, FX2.r), w=(FXH.r,), tag="x1add")
                st_dmaj(SYNC, x1v, f"x1D:{c}", FXH, CH, c0, "x1_st")
                ln.run(fxh[:, :, 0:CH], FXH.r, xfb[:, :, 0:CH], XFB.r)
                st_dmaj(SYNC, xfv, f"xfD:{c}", XFB, CH, c0, "xf_st")
        sch.barrier("x1")

    def phase_moe():
        with contextlib.ExitStack() as pes:
            WP1 = S("mWP1", [P, NT * DFF], BF16, pes)
            WP2 = S("mWP2", [P, NFT * D], BF16, pes)
            HT = S("mHT", [P, NFT * MCH], BF16, pes)
            XFSs = [S("mXFSa", [P, NT * MCH], BF16, pes),
                    S("mXFSb", [P, NT * MCH], BF16, pes)]
            XFS = XFSs[0]
            ACC = S("mACC", [P, (MCH // P) * D], F32, pes)
            WRM = S("mWRM", [P, NT * NEXP], BF16, pes)
            B1Cs = [S("mB1Ca", [P, NFT], F32, pes),
                    S("mB1Cb", [P, NFT], F32, pes)]
            B2S = S("mB2S", [NEXP, D], BF16, pes)
            RL = S("mRL", [P, 3 * NEXP], F32, pes)
            RWB = S("mRWB", [P, NEXP], BF16, pes)
            BRB = S("mBRB", [P, NEXP], F32, pes)

            ht = HT.t[:, :].rearrange("p (f c) -> p f c", c=MCH)
            xfs = v3(XFS, MCH)
            accv = ACC.t[:, :].rearrange("p (a d) -> p a d", d=D)
            NTT = NTOK // P
            x1sb = HT.t[:, 0:2 * NT * MCH].bitcast(F32)\
                .rearrange("p (t c) -> p t c", c=MCH)
            WTTa = HT.t[0:NEXP, (NFT - NTT // 4) * MCH:NFT * MCH]

            onesrow = PROW[0:1, NEXP:NEXP + P]
            kc.mm(onesrow, PROW[0:1, 0:NEXP], pb[6][:, 0:NEXP],
                  PB[6], r=("PROW",), tag="brb_mm")
            kc.ve(lambda e: e.tensor_copy(BRB[:, :], pb[6][:, 0:NEXP]),
                  r=(PB[6],), w=("BRB",), tag="brb_ev")
            for t in range(NT):
                kc.dma(GPSIMD, WRM[:, t * NEXP:(t + 1) * NEXP],
                       wmWr.ap()[t * P:(t + 1) * P, :], w=("WRM",), tag="ldwr")
            kc.dma(GPSIMD, B2S[:, :], wb2.ap()[:, :], w=("B2S",), tag="ldb2")

            for c in range(NMCH):
                c0 = c * MCH
                for s2 in range(2):
                    t0, t1 = s2 * 4, (s2 + 1) * 4
                    kc.dma(SYNC, xfs[:, t0:t1, 0:MCH], xfv[:, t0:t1, c0:c0 + MCH],
                           r=(f"xfD:{c0 // CH}", f"xfD:{(c0 + MCH - 1) // CH}"),
                           w=(XFS.r,), tag="rxf_ld")
                for ti in range(MCH // P):
                    tt = c * (MCH // P) + ti
                    kc.mmgroup([(xfs[:, t, ti * P:(ti + 1) * P],
                                 WRM[:, t * NEXP:(t + 1) * NEXP])
                                for t in range(NT)],
                               pb[6][:, 0:NEXP], PB[6], r=(XFS.r, "WRM"),
                               tag="rt_mm")
                    kc.ve(lambda e: e.tensor_add(RL[:, 0:NEXP], pb[6][:, 0:NEXP],
                                                 BRB[:, :]),
                          r=(PB[6], "BRB"), w=("mRL",), tag="rt_br")
                    kc.ve(lambda e: e.reduce_max(RTMP[:, 0:1], RL[:, 0:NEXP],
                                                 AX.X),
                          r=("mRL",), w=("RTMP",), tag="rt_m1")
                    kc.ve(lambda e: e.tensor_single_scalar(
                        RL[:, NEXP:2 * NEXP], RL[:, 0:NEXP], RTMP[:, 0:1],
                        ALU.is_ge),
                          r=("mRL", "RTMP"), w=("mRL",), tag="rt_t1")
                    kc.ve(lambda e: e.scalar_tensor_tensor(
                        RL[:, NEXP:2 * NEXP], RL[:, NEXP:2 * NEXP], -1e30,
                        RL[:, 0:NEXP], ALU.mult, ALU.add),
                          r=("mRL",), w=("mRL",), tag="rt_pen")
                    kc.ve(lambda e: e.reduce_max(RTMP[:, 1:2],
                                                 RL[:, NEXP:2 * NEXP], AX.X),
                          r=("mRL",), w=("RTMP",), tag="rt_m2")
                    kc.ve(lambda e: e.tensor_single_scalar(
                        RL[:, NEXP:2 * NEXP], RL[:, 0:NEXP], RTMP[:, 1:2],
                        ALU.is_ge),
                          r=("mRL", "RTMP"), w=("mRL",), tag="rt_keep")
                    kc.ve(lambda e: e.tensor_scalar_mul(RTMP[:, 2:3],
                                                        RTMP[:, 0:1], -1.0),
                          r=("RTMP",), w=("RTMP",), tag="rt_nm1")
                    kc.act(RL[:, 2 * NEXP:3 * NEXP], RL[:, 0:NEXP], AF.Exp,
                           bias=RTMP[:, 2:3], r=("mRL", "RTMP"), w=("mRL",),
                           tag="rt_exp")
                    kc.ve(lambda e: e.tensor_mul(RL[:, 2 * NEXP:3 * NEXP],
                                                 RL[:, 2 * NEXP:3 * NEXP],
                                                 RL[:, NEXP:2 * NEXP]),
                          r=("mRL",), w=("mRL",), tag="rt_w")
                    kc.ve(lambda e: e.reduce_sum(RTMP[:, 3:4],
                                                 RL[:, 2 * NEXP:3 * NEXP], AX.X),
                          r=("mRL",), w=("RTMP",), tag="rt_den")
                    kc.ve(lambda e: e.reciprocal(RTMP[:, 4:5], RTMP[:, 3:4]),
                          r=("RTMP",), w=("RTMP",), tag="rt_rd")
                    kc.ve(lambda e, tt=tt: e.tensor_scalar_mul(
                        WTE[:, tt * NEXP:(tt + 1) * NEXP],
                        RL[:, 2 * NEXP:3 * NEXP], RTMP[:, 4:5]),
                          r=("mRL", "RTMP"), w=("WTE",), tag="rt_fin")
                    kc.ve(lambda e, tt=tt: e.tensor_copy(
                        RWB[:, :], WTE[:, tt * NEXP:(tt + 1) * NEXP]),
                          r=("WTE",), w=("RWB",), tag="rt_cw")
                    kc.transpose(pbT[0:NEXP, 0:P], RWB[:, :], CID[:, :],
                                 "pbT", r=("RWB", "CID"), tag="rt_tr")
                    kc.ve(lambda e, tt=tt: e.tensor_copy(
                        WTTa[0:NEXP, tt * P:(tt + 1) * P], pbT[0:NEXP, 0:P]),
                          r=("pbT",), w=("mHT",), tag="rt_trev")

            for c in range(NMCH):
                c0 = c * MCH
                for s2 in range(2):
                    t0, t1 = s2 * 4, (s2 + 1) * 4
                    kc.dma(SYNC, x1sb[:, t0:t1, 0:MCH],
                           x1v[:, t0:t1, c0:c0 + MCH],
                           r=(f"x1D:{c0 // CH}", f"x1D:{(c0 + MCH - 1) // CH}"),
                           w=("mHT",), tag="mi_ldx1")
                pipe = EvacPipe([0, 1, 2, 3, 4, 5])
                for ti in range(MCH // P):
                    tt = c * (MCH // P) + ti
                    for t in range(NT):
                        def mmf(bk, ti=ti, t=t):
                            kc.transpose(pb[bk][:, 0:P],
                                         x1sb[:, t, ti * P:(ti + 1) * P],
                                         CIDF[:, :], PB[bk], r=("mHT", "CIDF"),
                                         tag="mi_tr")

                        def evf(bk, ti=ti, t=t):
                            kc.ve(lambda e, ti=ti, t=t, bk=bk: e.tensor_copy(
                                accv[:, ti, t * P:(t + 1) * P], pb[bk][:, 0:P]),
                                  r=(PB[bk],), w=(f"mACC:{ti}",), tag="mi_ev")
                        pipe.step(mmf, evf)
                    for dh in range(2):
                        def mmf(bk, tt=tt, dh=dh):
                            kc.mm(WTTa[0:NEXP, tt * P:(tt + 1) * P],
                                  B2S[:, dh * 512:(dh + 1) * 512],
                                  pb[bk][:, 0:512], PB[bk], r=("mHT", "B2S"),
                                  tag="mi_b2")

                        def evf(bk, ti=ti, dh=dh):
                            kc.ve(lambda e, ti=ti, dh=dh, bk=bk: e.tensor_add(
                                accv[:, ti, dh * 512:(dh + 1) * 512],
                                accv[:, ti, dh * 512:(dh + 1) * 512],
                                pb[bk][:, 0:512]),
                                  r=(PB[bk], f"mACC:{ti}"),
                                  w=(f"mACC:{ti}",), tag="mi_b2ev")
                        pipe.step(mmf, evf)
                pipe.flush()
                for ti in range(MCH // P):
                    kc.dma(SYNC, outv[:, c * (MCH // P) + ti:
                                      c * (MCH // P) + ti + 1, :],
                           accv[:, ti:ti + 1, :], r=(f"mACC:{ti}",),
                           w=(f"out:{c}",), tag="mi_st")

            pipe = EvacPipe([0, 1, 2, 3, 4, 5])
            for e_i in range(NEXP):
                for fg in range(4):
                    for t in range(NT):
                        kc.dma(GPSIMD,
                               WP1[:, t * DFF + fg * 8 * P:
                                   t * DFF + (fg + 1) * 8 * P],
                               wm1.ap()[e_i, t * P:(t + 1) * P,
                                        fg * 8 * P:(fg + 1) * 8 * P],
                               w=(f"mW1c:{fg}",), tag="ldw1")
                for f in range(NFT):
                    kc.dma(GPSIMD, WP2[:, f * D:(f + 1) * D],
                           wm2.ap()[e_i, f * P:(f + 1) * P, :],
                           w=(f"mW2:{f}",), tag="ldw2")
                B1C = B1Cs[e_i % 2]
                kc.dma(SYNC, B1C[:, :], wb1.ap()[e_i, :, :], w=(B1C.r,),
                       tag="ldb1")
                w2regs = tuple(f"mW2:{f}" for f in range(NFT))
                for c in range(NMCH):
                    c0 = c * MCH
                    XFS = XFSs[(e_i * NMCH + c) % 2]
                    xfs = v3(XFS, MCH)
                    for s2 in range(2):
                        t0, t1 = s2 * 4, (s2 + 1) * 4
                        kc.dma(SYNC, xfs[:, t0:t1, 0:MCH],
                               xfv[:, t0:t1, c0:c0 + MCH],
                               r=(f"xfD:{c0 // CH}",
                                  f"xfD:{(c0 + MCH - 1) // CH}"),
                               w=(XFS.r,), tag="me_xf")
                    for ti in range(MCH // P):
                        kc.dma(SYNC, accv[:, ti:ti + 1, :],
                               outv[:, c * (MCH // P) + ti:
                                    c * (MCH // P) + ti + 1, :],
                               r=(f"out:{c}",), w=(f"mACC:{ti}",),
                               tag="me_lda")
                    for f in range(NFT):
                        def mmf(bk, f=f, xfs=xfs, XFS=XFS):
                            kc.mmgroup(
                                [(WP1[:, t * DFF + f * P: t * DFF + (f + 1) * P],
                                  xfs[:, t, 0:MCH]) for t in range(NT)],
                                pb[bk][:, 0:MCH], PB[bk],
                                r=(f"mW1c:{f // 8}", XFS.r), tag="me_h")

                        def evf(bk, f=f, B1C=B1C):
                            kc.act(ht[:, f, 0:MCH], pb[bk][:, 0:MCH], AF.Gelu,
                                   bias=B1C[:, f:f + 1], r=(PB[bk], B1C.r),
                                   w=("mHT",), tag="me_gelu")
                        pipe.step(mmf, evf)
                    pipe.flush()
                    for ti in range(MCH // P):
                        tt = c * (MCH // P) + ti
                        for dh in range(2):
                            def mmf(bk, ti=ti, dh=dh):
                                kc.mmgroup(
                                    [(ht[:, f, ti * P:(ti + 1) * P],
                                      WP2[:, f * D + dh * 512:
                                          f * D + (dh + 1) * 512])
                                     for f in range(NFT)],
                                    pb[bk][:, 0:512], PB[bk],
                                    r=w2regs + ("mHT",), tag="me_y")

                            def evf(bk, ti=ti, dh=dh, tt=tt, e_i=e_i):
                                kc.ve(lambda e, ti=ti, dh=dh, bk=bk, tt=tt,
                                      e_i=e_i: e.scalar_tensor_tensor(
                                          accv[:, ti, dh * 512:(dh + 1) * 512],
                                          pb[bk][:, 0:512],
                                          WTE[:, tt * NEXP + e_i:
                                              tt * NEXP + e_i + 1],
                                          accv[:, ti, dh * 512:(dh + 1) * 512],
                                          ALU.mult, ALU.add),
                                      r=(PB[bk], "WTE", f"mACC:{ti}"),
                                      w=(f"mACC:{ti}",), tag="me_acc")
                            pipe.step(mmf, evf)
                    pipe.flush()
                    for ti in range(MCH // P):
                        kc.dma(SYNC, outv[:, c * (MCH // P) + ti:
                                          c * (MCH // P) + ti + 1, :],
                               accv[:, ti:ti + 1, :], r=(f"mACC:{ti}",),
                               w=(f"out:{c}",), tag="me_st")
        sch.barrier("moe")

    for l in range(LAYERS):
        phase_tmA(l)
        phase_tmB(l)
        phase_cm(l)
    phase_x1()
    phase_moe()
    sch.barrier("fin")

    sch.plan()
    if cfg.get("verbose"):
        print("SCHED:", sch.stats(), flush=True)
        print("sbuf bytes remaining:", nc.sbuf_bytes_remaining, flush=True)

    semnames = (["eng_" + e for e in ENGINES]
                + [f"dma_{e}{i}" for e in ENGINES for i in range(Sched.NLANES)]
                + ["cc"])
    sems = {}
    with contextlib.ExitStack() as ses:
        for sn in semnames:
            sems[sn] = ses.enter_context(nc.semaphore(sn))
        with nc.Block() as block:
            sch.emit(block, sems)
    kc.es.close()
    return nc



def _col8(vec):
    return np.ascontiguousarray(vec.reshape(NT, P).T)


def _pack_shared(i):
    f = np.float32
    L = 2
    ppar = np.zeros((P, 200), f)
    names = {"ln1": i["L0_attn_ln1_w"], "ln2": i["L0_attn_ln2_w"],
             "xm": i["L0_attn_x_maa"], "wm": i["L0_attn_w_maa"],
             "km": i["L0_attn_k_maa"], "vm": i["L0_attn_v_maa"],
             "rm": i["L0_attn_r_maa"], "gm": i["L0_attn_g_maa"],
             "ckm": i["L0_attn_ck_maa"], "crm": i["L0_attn_cr_maa"],
             "td": np.asarray(i["L0_attn_time_decay"], f).reshape(L, D),
             "tf": np.asarray(i["L0_attn_time_first"], f).reshape(L, D)}
    order = ["ln1", "ln2", "xm", "wm", "km", "vm", "rm", "gm", "ckm", "crm",
             "td", "tf"]
    for l in range(L):
        for pi, nm in enumerate(order):
            ppar[:, l * 96 + pi * 8:l * 96 + pi * 8 + 8] = \
                _col8(np.asarray(names[nm], f)[l])
    ppar[:, 192:200] = _col8(np.asarray(i["L0_attn_ln_out_w"], f))

    prows = np.zeros((1, 8 + P), f)
    prows[0, 0:8] = np.asarray(i["L0_ffn_br"], f)
    prows[0, 8:8 + P] = 1.0
    pgn = np.zeros((1, 4 * D), f)
    gnw = np.asarray(i["L0_attn_gn_w"], f)
    gnb = np.asarray(i["L0_attn_gn_b"], f)
    for l in range(L):
        pgn[0, 2 * l * D:(2 * l) * D + D] = gnw[l]
        pgn[0, (2 * l + 1) * D:(2 * l + 1) * D + D] = gnb[l]

    tf2 = np.asarray(i["L0_attn_time_first"], f).reshape(L, D)
    u2p = np.zeros((P, 32), f)
    for l in range(L):
        for tp in range(NT):
            blk = tf2[l, tp * P:(tp + 1) * P]
            u2p[0:64, l * 16 + tp * 2 + 0] = blk[0:64]
            u2p[64:128, l * 16 + tp * 2 + 1] = blk[64:128]

    tm1 = np.asarray(i["L0_attn_tm_w1"], f).reshape(L, D, 5, 32)
    wtm1 = np.zeros((L, D, 384), f)
    for j in range(5):
        jp, jh = j // 2, j % 2
        wtm1[:, :, jp * 128 + jh * 64:jp * 128 + jh * 64 + 32] = tm1[:, :, j]
    tm2 = np.asarray(i["L0_attn_tm_w2"], f)
    wtm2 = np.zeros((L, P, 5 * D), f)
    for j in range(5):
        jh = j % 2
        wtm2[:, jh * 64:jh * 64 + 32, j * D:(j + 1) * D] = tm2[:, j]
    td1 = np.asarray(i["L0_attn_td_w1"], f)
    wtd1 = np.zeros((L, D, P), f)
    wtd1[:, :, 0:64] = td1
    td2 = np.asarray(i["L0_attn_td_w2"], f)
    wtd2 = np.zeros((L, P, D), f)
    wtd2[:, 0:64, :] = td2

    b1 = np.asarray(i["L0_ffn_b1"], f)
    wb1 = np.ascontiguousarray(b1.reshape(8, 32, 128).transpose(0, 2, 1))

    shared = {
        "ppar": ppar, "prows": prows, "pgn": pgn, "u2p": u2p,
        "wWr": np.asarray(i["L0_attn_Wr"], f),
        "wWk": np.asarray(i["L0_attn_Wk"], f),
        "wWv": np.asarray(i["L0_attn_Wv"], f),
        "wWo": np.asarray(i["L0_attn_Wo"], f),
        "wWg": np.asarray(i["L0_attn_Wg"], f),
        "wtm1": wtm1, "wtm2": wtm2, "wtd1": wtd1, "wtd2": wtd2,
        "wcWk": np.asarray(i["L0_attn_cWk"], f),
        "wcWv": np.asarray(i["L0_attn_cWv"], f),
        "wcWr": np.asarray(i["L0_attn_cWr"], f),
        "wmWr": np.asarray(i["L0_ffn_Wr"], f),
        "wm1": np.asarray(i["L0_ffn_W1"], f),
        "wb1": wb1,
        "wm2": np.asarray(i["L0_ffn_W2"], f),
        "wb2": np.asarray(i["L0_ffn_b2"], f),
    }
    return shared


def _consts(is_second_half):
    f = np.float32
    c = np.zeros((P, 264), f)
    c[:, 0:P] = np.eye(P, dtype=f)
    ts = np.arange(P)
    c[:, P:2 * P] = (ts[:, None] < ts[None, :]).astype(f)
    c[0:64, 2 * P + 0] = 1.0
    c[64:128, 2 * P + 1] = 1.0
    c[:, 2 * P + 2] = 1.0 if is_second_half else 0.0
    c[:, 2 * P + 3] = 0.0
    c[:, 2 * P + 4] = 1.0
    c[:, 2 * P + 5] = EPS
    c[:, 2 * P + 6] = -1.0
    return c


_NC_CACHE = {}


def _get_nc(cfg_key):
    if cfg_key not in _NC_CACHE:
        cfg = dict(NTOK=4096, CH=256, LAYERS=2, MCH=512, NEXP=8, DFF=4096,
                   verbose=False)
        _NC_CACHE[cfg_key] = build_program(cfg)
    return _NC_CACHE[cfg_key]


def kernel(**inputs):
    x = np.asarray(inputs["L0_attn_x"], np.float32)
    B, T, _ = x.shape
    half = T // 2
    shared = _pack_shared(inputs)
    nc = _get_nc("full")
    in_maps = []
    cA = _consts(False)
    cB = _consts(True)
    for c in range(8):
        b, h = c // 2, c % 2
        m = dict(shared)
        m["x0"] = np.ascontiguousarray(x[b, h * half:(h + 1) * half, :].T)
        m["consts"] = cB if h else cA
        in_maps.append(m)
    res = run_bass_kernel_spmd(nc, in_maps, core_ids=list(range(8)))
    outf = np.empty((B, T, D), np.float32)
    for c in range(8):
        b, h = c // 2, c % 2
        outf[b, h * half:(h + 1) * half, :] = res.results[c]["out"]
    return outf
